# version 06
# baseline (speedup 1.0000x reference)
import math
from contextlib import ExitStack

import numpy as np
import ml_dtypes

import concourse.bass as bass
import concourse.mybir as mybir
from concourse.bass_utils import run_bass_kernel_spmd
from concourse.aot_env import allow_ici_all_to_all

F32, BF16 = mybir.dt.float32, mybir.dt.bfloat16
AF = mybir.ActivationFunctionType
OP = mybir.AluOpType
AX = mybir.AxisListType
BFNP = ml_dtypes.bfloat16

D = 2048
QH, VH = 4, 8
C = 128
INTER = 1408
SHS = 352
LN128 = float(np.log(128.0))
RS128 = float(1.0 / np.sqrt(128.0))


class Guard:
    __slots__ = ("w", "rs")
    def __init__(self, init=()):
        self.w = None
        self.rs = list(init)


class Arena:
    def __init__(self, ap_full):
        self.ap = ap_full
        self.ivs = []

    def view(self, a, b):
        for (x, y, gd) in self.ivs:
            if x == a and y == b:
                return self.ap[:, a:b], gd
        deps, keep = [], []
        for (x, y, gd) in self.ivs:
            if x < b and a < y:
                if gd.w is not None:
                    deps.append(gd.w)
                deps.extend(gd.rs)
            else:
                keep.append((x, y, gd))
        gnew = Guard(deps)
        keep.append((a, b, gnew))
        self.ivs = keep
        return self.ap[:, a:b], gnew


class Prog:
    SECTIONS = ("tensor", "vector", "scalar", "sync", "gpsimd")
    COMPUTE_SEMS = ("pe", "dve", "act", "pool")
    DMA_POOL = {"hdma": 8, "adma": 8, "pdma": 8, "cc": 4}
    SEMS = COMPUTE_SEMS + tuple(
        f"{c}{i}" for c, n in DMA_POOL.items() for i in range(n))

    def __init__(self):
        self.q = {s: [] for s in self.SECTIONS}
        self.cnt = {k: 0 for k in self.SEMS}
        self.rr = {c: 0 for c in self.DMA_POOL}

    def _op(self, section, sem, fn, reads, writes, extra, k):
        deps = list(extra)
        if sem in self.DMA_POOL:
            i = self.rr[sem]
            self.rr[sem] = (i + 1) % self.DMA_POOL[sem]
            sub = f"{sem}{i}"
            if self.cnt[sub]:
                deps.append((sub, self.cnt[sub]))
            sem = sub
        for gd in reads:
            if gd.w is not None:
                deps.append(gd.w)
        for gd in writes:
            if gd.w is not None:
                deps.append(gd.w)
            deps.extend(gd.rs)
        self.cnt[sem] += k
        tok = (sem, self.cnt[sem])
        self.q[section].append((fn, sem, k, tuple(d for d in deps if d is not None)))
        for gd in reads:
            gd.rs.append(tok)
        for gd in writes:
            gd.w = tok
            gd.rs = []
        return tok

    def pe(self, fn, reads=(), writes=(), extra=()):
        return self._op("tensor", "pe", fn, reads, writes, extra, 1)

    def dve(self, fn, reads=(), writes=(), extra=()):
        return self._op("vector", "dve", fn, reads, writes, extra, 1)

    def act(self, fn, reads=(), writes=(), extra=()):
        return self._op("scalar", "act", fn, reads, writes, extra, 1)

    def pool(self, fn, reads=(), writes=(), extra=()):
        return self._op("gpsimd", "pool", fn, reads, writes, extra, 1)

    def hdma(self, fn, reads=(), writes=(), extra=()):
        return self._op("sync", "hdma", fn, reads, writes, extra, 16)

    def adma(self, fn, reads=(), writes=(), extra=()):
        return self._op("scalar", "adma", fn, reads, writes, extra, 16)

    def pdma(self, fn, reads=(), writes=(), extra=()):
        return self._op("gpsimd", "pdma", fn, reads, writes, extra, 16)

    def cc(self, fn, reads=(), writes=(), extra=()):
        return self._op("gpsimd", "cc", fn, reads, writes, extra, 1)

    def emit(self, blk, sems, pe_fifo=True):
        for section in self.SECTIONS:
            ops = self.q[section]
            if not ops:
                continue

            def body(eng, ops=ops, section=section):
                water = {k: 0 for k in self.cnt}
                for fn, sem, k, deps in ops:
                    for dsem, dval in deps:
                        if pe_fifo and dsem == "pe" and section == "tensor":
                            continue
                        if water[dsem] < dval:
                            eng.wait_ge(sems[dsem], dval)
                            water[dsem] = dval
                    fn(eng).then_inc(sems[sem], k)
            getattr(blk, section)(body)


class Builder:
    def __init__(self, T, dev_dbg=False, pe_fifo=True):
        self.pe_fifo = pe_fifo
        self.T = T
        self.Q = T // 4
        self.J = T // C
        self.dev_dbg = dev_dbg
        self.nc = bass.Bass("TRN2", target_bir_lowering=False)
        self.p = Prog()
        self.G = {}

    def g(self, name):
        if name not in self.G:
            self.G[name] = Guard()
        return self.G[name]

    def declare(self):
        nc, T, Q = self.nc, self.T, self.Q
        din = lambda n, s, dt=F32: nc.dram_tensor(n, s, dt, kind="ExternalInput").ap()
        self.x_fm = din("x_fm", [D, Q])
        self.wqkv = din("wqkv", [D, 2048])
        self.wz = din("wz", [D, 1024])
        self.wb128 = din("wb128", [128, 128])
        self.wa128 = din("wa128", [128, 128])
        self.convw = din("convw", [128, 64])
        self.alog = din("alog", [VH, 1])
        self.dtb = din("dtb", [VH, 1])
        self.normw = din("normw", [128, 1])
        self.wout = din("wout", [VH * 128, D])
        self.gw128 = din("gw128", [128, 128])
        self.rbias = din("rbias", [1, 8])
        self.w1 = din("w1", [D, INTER])
        self.w3 = din("w3", [D, INTER])
        self.w2 = din("w2", [INTER, D])
        self.sw1T = din("sw1T", [D, SHS])
        self.sw3T = din("sw3T", [D, SHS])
        self.sw2T = din("sw2T", [SHS, D])
        self.idf_d = din("idf", [128, 128])
        self.idb_d = din("idb", [128, 128], BF16)
        self.onesb_d = din("onesb", [1, 128], BF16)
        self.onebc_d = din("onebc", [128, 1], BF16)
        self.cbias_d = din("cbias", [128, 4])
        self.onesf_d = din("onesf", [1, 128])
        self.sneg_d = din("sneg", [128, VH * 128], BF16)
        self.blkI_d = din("blkI", [128, VH * 128], BF16)

        dr = lambda n, s, dt=F32: nc.dram_tensor(n, s, dt)
        self.xn_blk = dr("xn_blk", [D, Q], BF16)
        self.xn_bat = dr("xn_bat", [4 * D, Q], BF16)
        self.xn_all = dr("xn_all", [8 * D, Q], BF16)
        self.gates_in = dr("gates_in", [8, Q], BF16)
        self.gates_x = dr("gates_x", [8, Q], BF16)
        self.qn_sc = dr("qn_sc", [QH * 128, T], BF16)
        self.kn_sc = dr("kn_sc", [QH * 128, T], BF16)
        self.v_sc = dr("v_sc", [VH * 128, T], BF16)
        self.z_sc = dr("z_sc", [VH * 128, T], BF16)
        self.nrm_sc = dr("nrm_sc", [VH * 128, T], BF16)
        self.ps_in = dr("ps_in", [4 * D, Q], F32)
        self.s_own = dr("s_own", [D, Q], F32)
        self.y_in = dr("y_in", [8 * D, Q], F32)
        self.ffn_own = dr("ffn_own", [D, Q], F32)
        self.out_fm = nc.dram_tensor("out_fm", [D, Q], F32, kind="ExternalOutput").ap()

        self.dbg = {}
        if self.dev_dbg:
            for nm, shape, dt in [
                ("dbg_xn", [D, Q], BF16), ("dbg_gates", [8, Q], BF16),
                ("dbg_q", [QH * 128, T], BF16), ("dbg_k", [QH * 128, T], BF16),
                ("dbg_v", [VH * 128, T], BF16), ("dbg_z", [VH * 128, T], BF16),
                ("dbg_bt", [128, self.J * 8], F32), ("dbg_gc", [128, self.J * 8], F32),
                ("dbg_nrm", [VH * 128, T], BF16), ("dbg_s", [D, Q], F32),
                ("dbg_ffn", [D, Q], F32),
            ]:
                self.dbg[nm] = nc.dram_tensor(nm, shape, dt, kind="ExternalOutput").ap()

    def alloc(self, es):
        nc = self.nc
        sb = lambda n, s, dt: es.enter_context(nc.sbuf_tensor(n, s, dt))
        ps = lambda n, s, dt: es.enter_context(nc.psum_tensor(n, s, dt))
        self.AW = Arena(sb("AW", [128, 36864], BF16))
        self.ABF = Arena(sb("ABF", [128, 31744], BF16))
        self.AFA = Arena(sb("AFA", [128, 8192], F32))
        self.gcumT = sb("gcumT", [128, self.J * 8], F32)
        self.betaT = sb("betaT", [128, self.J * 8], F32)
        self.gates8 = sb("gates8", [8, self.Q], BF16)
        self.c_idf = sb("c_idf", [128, 128], F32)
        self.c_idb = sb("c_idb", [128, 128], BF16)
        self.c_onesb = sb("c_onesb", [1, 128], BF16)
        self.c_onebc = sb("c_onebc", [128, 1], BF16)
        self.c_bias = sb("c_bias", [128, 4], F32)
        self.c_onesf = sb("c_onesf", [1, 128], F32)
        self.c_ones128 = sb("c_ones128", [128, 128], F32)
        self.c_sneg = sb("c_sneg", [128, VH * 128], BF16)
        self.c_blkI = sb("c_blkI", [128, VH * 128], BF16)
        self.c_convw = sb("c_convw", [128, 64], F32)
        self.c_normw = sb("c_normw", [128, 1], F32)
        self.c_alog = sb("c_alog", [VH, 1], F32)
        self.c_dtb = sb("c_dtb", [VH, 1], F32)
        self.c_rbias = sb("c_rbias", [128, 8], F32)
        self.c_nalog = sb("c_nalog", [VH, 1], F32)
        self.c_zero8 = sb("c_zero8", [8, 1], F32)
        self.s_gw = sb("s_gw", [128, 128], F32)
        self.s_wba = sb("s_wba", [128, 256], BF16)
        self.s_cols = sb("s_cols", [128, 256], F32)
        self.s_rows = sb("s_rows", [1, 1536], F32)
        self.s_rowb = sb("s_rowb", [1, 1024], BF16)
        self.s_st = sb("s_st", [128, 1024], F32)
        self.s_stb = sb("s_stb", [128, 1024], BF16)
        self.pA = ps("pA", [128, 1024], F32)
        self.pB = ps("pB", [128, 1024], F32)
        self.pC = ps("pC", [128, 1024], F32)
        self.pT0 = ps("pT0", [128, 1024], BF16)
        self.pT1 = ps("pT1", [128, 1024], BF16)
        for n in ("pA0", "pA1", "pB0", "pB1", "pC0", "pC1", "pT0", "pT1"):
            self.g(n)

    def b0_consts(self):
        p, g = self.p, self.g
        pairs = [
            (self.c_idf, self.idf_d, "c_idf"), (self.c_idb, self.idb_d, "c_idb"),
            (self.c_onesb, self.onesb_d, "c_onesb"), (self.c_onesf, self.onesf_d, "c_onesf"),
            (self.c_onebc, self.onebc_d, "c_onebc"),
            (self.c_bias, self.cbias_d, "c_bias"),
            (self.c_sneg, self.sneg_d, "c_sneg"), (self.c_blkI, self.blkI_d, "c_blkI"),
            (self.c_convw, self.convw, "c_convw"), (self.c_normw, self.normw, "c_normw"),
            (self.c_alog, self.alog, "c_alog"), (self.c_dtb, self.dtb, "c_dtb"),
            (self.s_gw, self.gw128, "s_gw"),
        ]
        for dst, src, gn in pairs:
            p.pdma(lambda gg, dst=dst, src=src: gg.dma_start(out=dst[:, :], in_=src[:, :]),
                   writes=[g(gn)])
        p.pdma(lambda gg: gg.dma_start(out=self.s_wba[:, 0:128], in_=self.wb128[:, :]),
               writes=[g("s_wba")])
        p.pdma(lambda gg: gg.dma_start(out=self.s_wba[:, 128:256], in_=self.wa128[:, :]),
               writes=[g("s_wba")])
        p.pool(lambda gg: gg.memset(self.c_ones128[:, :], 1.0), writes=[g("c_ones128")])
        p.pool(lambda gg: gg.memset(self.c_zero8[:, :], 0.0), writes=[g("c_zero8")])
        p.pool(lambda gg: gg.memset(self.s_st[:, :], 0.0), writes=[g("s_st")])
        p.pool(lambda gg: gg.memset(self.s_stb[:, :], 0.0), writes=[g("s_stb")])
        p.pdma(lambda gg: gg.dma_start(out=self.s_rows[0:1, 0:8], in_=self.rbias[:, :]),
               writes=[g("s_rows")])
        p.pe(lambda t: t.matmul(self.pC[0:128, 0:8], self.c_onesf[:, :],
                                self.s_rows[0:1, 0:8], start=True, stop=True),
             reads=[g("c_onesf"), g("s_rows")], writes=[g("pC0")])
        p.dve(lambda v: v.tensor_copy(self.c_rbias[:, :], self.pC[0:128, 0:8]),
              reads=[g("pC0")], writes=[g("c_rbias")])
        p.act(lambda s: s.activation(self.c_nalog[:, :], self.c_alog[:, :], AF.Exp),
              reads=[g("c_alog")], writes=[g("c_nalog")])
        p.dve(lambda v: v.tensor_scalar_mul(self.c_nalog[:, :], self.c_nalog[:, :], -1.0),
              reads=[g("c_nalog")], writes=[g("c_nalog")])

    def b1_ln_router(self):
        p, g = self.p, self.g
        B1T = self.Q // 512
        for tt in range(B1T):
            t0 = tt * 512
            vx, gx = self.AFA.view(0, 8192)
            vxb, gxb = self.ABF.view(0, 8192)
            vx2, gx2 = self.ABF.view(8192, 16384)
            vxnb, gxnb = self.ABF.view(16384, 24576)
            src = self.x_fm[:, t0:t0 + 512].rearrange("(c p) t -> p c t", p=128)
            p.hdma(lambda s, src=src, vx=vx: s.dma_start(
                out=vx.rearrange("p (c t) -> p c t", t=512), in_=src), writes=[gx])
            p.pdma(lambda gg, src=src, vxb=vxb: gg.dma_start(
                out=vxb.rearrange("p (c t) -> p c t", t=512), in_=src), writes=[gxb])
            p.act(lambda s, vx=vx, vx2=vx2: s.activation(vx2, vx, AF.Square),
                  reads=[gx], writes=[gx2])
            for dc in range(16):
                p.pe(lambda t, dc=dc, vxb=vxb: t.matmul(
                    self.pC[0:1, 0:512], self.c_onebc[:, :],
                    vxb[:, dc * 512:(dc + 1) * 512], start=(dc == 0), stop=(dc == 15)),
                    reads=[gxb, g("c_onebc")], writes=[g("pC0")])
            for dc in range(16):
                p.pe(lambda t, dc=dc, vx2=vx2: t.matmul(
                    self.pC[0:1, 512:1024], self.c_onebc[:, :],
                    vx2[:, dc * 512:(dc + 1) * 512], start=(dc == 0), stop=(dc == 15)),
                    reads=[gx2, g("c_onebc")], writes=[g("pC1")])
            R, gR = self.s_rows, g("s_rows")
            p.dve(lambda v: v.tensor_scalar_mul(R[0:1, 0:512], self.pC[0:1, 0:512], 1.0 / D),
                  reads=[g("pC0")], writes=[gR])
            p.dve(lambda v: v.tensor_scalar_mul(R[0:1, 512:1024], self.pC[0:1, 512:1024], 1.0 / D),
                  reads=[g("pC1")], writes=[gR])
            p.dve(lambda v: v.tensor_tensor(R[0:1, 1024:1536], R[0:1, 0:512], R[0:1, 0:512], OP.mult),
                  reads=[gR], writes=[gR])
            p.dve(lambda v: v.tensor_tensor(R[0:1, 512:1024], R[0:1, 512:1024],
                                            R[0:1, 1024:1536], OP.subtract),
                  reads=[gR], writes=[gR])
            p.act(lambda s: s.activation(R[0:1, 512:1024], R[0:1, 512:1024], AF.Sqrt,
                                         bias=self.c_bias[0:1, 0:1]),
                  reads=[gR, g("c_bias")], writes=[gR])
            p.dve(lambda v: v.reciprocal(R[0:1, 512:1024], R[0:1, 512:1024]),
                  reads=[gR], writes=[gR])
            p.pe(lambda t: t.matmul(self.pA[:, 0:512], self.c_onesf[:, :], R[0:1, 0:512],
                                    start=True, stop=True),
                 reads=[gR, g("c_onesf")], writes=[g("pA0")])
            p.pe(lambda t: t.matmul(self.pA[:, 512:1024], self.c_onesf[:, :], R[0:1, 512:1024],
                                    start=True, stop=True),
                 reads=[gR, g("c_onesf")], writes=[g("pA1")])
            for dc in range(16):
                xv = vx[:, dc * 512:(dc + 1) * 512]
                p.dve(lambda v, xv=xv: v.tensor_tensor(xv, xv, self.pA[:, 0:512], OP.subtract),
                      reads=[g("pA0")], writes=[gx])
                p.dve(lambda v, xv=xv: v.tensor_tensor(xv, xv, self.pA[:, 512:1024], OP.mult),
                      reads=[g("pA1")], writes=[gx])
                p.dve(lambda v, dc=dc, xv=xv, vxnb=vxnb: v.tensor_copy(
                    vxnb[:, dc * 512:(dc + 1) * 512], xv), reads=[gx], writes=[gxnb])
            for sub in range(4):
                for dc in range(16):
                    p.pe(lambda t, sub=sub, dc=dc, vx=vx: t.matmul(
                        self.pB[:, sub * 8:sub * 8 + 8],
                        vx[:, dc * 512 + sub * 128: dc * 512 + sub * 128 + 128],
                        self.s_gw[:, dc * 8:(dc + 1) * 8],
                        start=(dc == 0), stop=(dc == 15)),
                        reads=[gx, g("s_gw")], writes=[g("pB0")])
            SC, gS = self.s_cols, g("s_cols")
            for sub in range(4):
                pg = self.pB[:, sub * 8:sub * 8 + 8]
                p.act(lambda s, pg=pg: s.activation(SC[:, 0:8], pg, AF.Sigmoid),
                      reads=[g("pB0")], writes=[gS])
                p.dve(lambda v: v.tensor_tensor(SC[:, 8:16], SC[:, 0:8], self.c_rbias[:, :], OP.add),
                      reads=[gS, g("c_rbias")], writes=[gS])
                p.dve(lambda v: v.reduce_max(SC[:, 16:17], SC[:, 8:16], axis=AX.X),
                      reads=[gS], writes=[gS])
                p.dve(lambda v: v.tensor_scalar(SC[:, 24:32], SC[:, 8:16], SC[:, 16:17],
                                                None, OP.is_ge),
                      reads=[gS], writes=[gS])
                p.dve(lambda v: v.scalar_tensor_tensor(SC[:, 32:40], SC[:, 24:32], -1e9,
                                                       SC[:, 8:16], OP.mult, OP.add),
                      reads=[gS], writes=[gS])
                p.dve(lambda v: v.reduce_max(SC[:, 40:41], SC[:, 32:40], axis=AX.X),
                      reads=[gS], writes=[gS])
                p.dve(lambda v: v.tensor_scalar(SC[:, 48:56], SC[:, 8:16], SC[:, 40:41],
                                                None, OP.is_ge),
                      reads=[gS], writes=[gS])
                p.dve(lambda v: v.tensor_tensor(SC[:, 56:64], SC[:, 0:8], SC[:, 48:56], OP.mult),
                      reads=[gS], writes=[gS])
                p.dve(lambda v: v.reduce_sum(SC[:, 64:65], SC[:, 56:64], axis=AX.X),
                      reads=[gS], writes=[gS])
                p.dve(lambda v: v.tensor_scalar_max(SC[:, 64:65], SC[:, 64:65], 1e-12),
                      reads=[gS], writes=[gS])
                p.dve(lambda v: v.reciprocal(SC[:, 64:65], SC[:, 64:65]),
                      reads=[gS], writes=[gS])
                p.dve(lambda v: v.tensor_scalar(SC[:, 72:80], SC[:, 56:64], SC[:, 64:65],
                                                None, OP.mult),
                      reads=[gS], writes=[gS])
                vgb, ggb = self.ABF.view(24576, 24584)
                p.dve(lambda v, vgb=vgb: v.tensor_copy(vgb, SC[:, 72:80]),
                      reads=[gS], writes=[ggb])
                p.pe(lambda t, vgb=vgb: t.transpose(self.pT0[0:8, 0:128], vgb, self.c_idb[:, :]),
                     reads=[ggb, g("c_idb")], writes=[g("pT0")])
                vgf, ggf = self.ABF.view(24704, 24832)
                p.dve(lambda v, vgf=vgf: v.tensor_copy(vgf[0:8, :], self.pT0[0:8, 0:128]),
                      reads=[g("pT0")], writes=[ggf])
                dst = self.gates_in.ap()[:, t0 + sub * 128: t0 + (sub + 1) * 128]
                p.hdma(lambda s, dst=dst, vgf=vgf: s.dma_start(out=dst, in_=vgf[0:8, :]),
                       reads=[ggf], writes=[g("d_gates_in")])
            dst = self.xn_blk.ap()[:, t0:t0 + 512].rearrange("(c p) t -> p c t", p=128)
            p.hdma(lambda s, dst=dst, vxnb=vxnb: s.dma_start(
                out=dst, in_=vxnb.rearrange("p (c t) -> p c t", t=512)),
                   reads=[gxnb], writes=[g("d_xn_blk")])

    def b2_collectives(self):
        p, g = self.p, self.g
        p.cc(lambda gg: gg.collective_compute(
            "AllGather", OP.bypass, replica_groups=[[0, 1, 2, 3], [4, 5, 6, 7]],
            ins=[self.xn_blk.ap().opt()], outs=[self.xn_bat.ap().opt()]),
            reads=[g("d_xn_blk")], writes=[g("d_xn_bat")])
        p.cc(lambda gg: gg.collective_compute(
            "AllGather", OP.bypass, replica_groups=[list(range(8))],
            ins=[self.xn_blk.ap().opt()], outs=[self.xn_all.ap().opt()]),
            reads=[g("d_xn_blk")], writes=[g("d_xn_all")])

        def _a2a(gg):
            with allow_ici_all_to_all():
                return gg.collective_compute(
                    "AllToAll", OP.bypass, replica_groups=[list(range(8))],
                    ins=[self.gates_in.ap().opt()], outs=[self.gates_x.ap().opt()])
        p.cc(_a2a, reads=[g("d_gates_in")], writes=[g("d_gates_x")])
        if self.dev_dbg:
            p.pdma(lambda gg: gg.dma_start(out=self.dbg["dbg_xn"][:, :],
                                           in_=self.xn_blk.ap()[:, :]),
                   reads=[g("d_xn_blk")], writes=[g("dbg")])
            p.pdma(lambda gg: gg.dma_start(out=self.dbg["dbg_gates"][:, :],
                                           in_=self.gates_x.ap()[:, :]),
                   reads=[g("d_gates_x")], writes=[g("dbg")])

    def b7_ffn(self):
        p, g = self.p, self.g
        Q = self.Q
        B7T = (2 * self.T) // 512
        p.pdma(lambda gg: gg.dma_start(out=self.gates8[:, :], in_=self.gates_x.ap()[:, :]),
               reads=[g("d_gates_x")], writes=[g("gates8")])
        sweeps = [("e", 0, 6), ("e", 6, 11), ("sh", 0, 3)]
        for sw, (kind, h0, h1) in enumerate(sweeps):
            nh = h1 - h0
            if kind == "e":
                hszs = [128] * nh
                WROW = nh * 128
                w1v, gw1 = self.AW.view(0, 16 * WROW)
                w3v, gw3 = self.AW.view(16 * WROW, 32 * WROW)
                w2v, gw2 = self.AW.view(32 * WROW, 32 * WROW + nh * 2048)
                for dc in range(16):
                    src = self.w1[dc * 128:(dc + 1) * 128, h0 * 128:h1 * 128]
                    p.pdma(lambda gg, src=src, dc=dc, w1v=w1v, WROW=WROW: gg.dma_start(
                        out=w1v[:, dc * WROW:(dc + 1) * WROW], in_=src), writes=[gw1])
                for dc in range(16):
                    src = self.w3[dc * 128:(dc + 1) * 128, h0 * 128:h1 * 128]
                    p.pdma(lambda gg, src=src, dc=dc, w3v=w3v, WROW=WROW: gg.dma_start(
                        out=w3v[:, dc * WROW:(dc + 1) * WROW], in_=src), writes=[gw3])
                for hc in range(nh):
                    src = self.w2[(h0 + hc) * 128:(h0 + hc + 1) * 128, :]
                    p.pdma(lambda gg, src=src, hc=hc, w2v=w2v: gg.dma_start(
                        out=w2v[:, hc * 2048:(hc + 1) * 2048], in_=src), writes=[gw2])
            else:
                hszs = [128, 128, 96]
                WROW = SHS
                w1v, gw1 = self.AW.view(0, 16 * SHS)
                w3v, gw3 = self.AW.view(16 * SHS, 32 * SHS)
                w2v, gw2 = self.AW.view(32 * SHS, 32 * SHS + 3 * 2048)
                for dc in range(16):
                    src = self.sw1T[dc * 128:(dc + 1) * 128, :]
                    p.pdma(lambda gg, src=src, dc=dc, w1v=w1v: gg.dma_start(
                        out=w1v[:, dc * SHS:(dc + 1) * SHS], in_=src), writes=[gw1])
                for dc in range(16):
                    src = self.sw3T[dc * 128:(dc + 1) * 128, :]
                    p.pdma(lambda gg, src=src, dc=dc, w3v=w3v: gg.dma_start(
                        out=w3v[:, dc * SHS:(dc + 1) * SHS], in_=src), writes=[gw3])
                for hc in range(3):
                    hs = hszs[hc]
                    src = self.sw2T[hc * 128:hc * 128 + hs, :]
                    p.pdma(lambda gg, src=src, hc=hc, hs=hs, w2v=w2v: gg.dma_start(
                        out=w2v[0:hs, hc * 2048:(hc + 1) * 2048], in_=src), writes=[gw2])
            nh_ = len(hszs)
            for gt in range(B7T):
                qq, lc = (gt * 512) // Q, (gt * 512) % Q
                par = gt % 2
                vxn, gxn = self.ABF.view(par * 8192, par * 8192 + 8192)
                src = self.xn_all.ap()[qq * 2048:(qq + 1) * 2048, lc:lc + 512].rearrange(
                    "(c p) t -> p c t", p=128)
                p.hdma(lambda s, src=src, vxn=vxn: s.dma_start(
                    out=vxn.rearrange("p (c t) -> p c t", t=512), in_=src),
                    writes=[gxn], extra=[g("d_xn_all").w])
                gate_sb, ggsb = self.ABF.view(19968 + par * 512, 19968 + par * 512 + 512)
                if kind == "e":
                    p.pe(lambda t, qq=qq, lc=lc: t.matmul(
                        self.pB[0:1, 512:1024], self.c_idb[0:8, qq:qq + 1],
                        self.gates8[:, lc:lc + 512], start=True, stop=True),
                        reads=[g("gates8"), g("c_idb")], writes=[g("pB1")])
                    p.dve(lambda v: v.tensor_copy(self.s_rowb[0:1, 0:512],
                                                  self.pB[0:1, 512:1024]),
                          reads=[g("pB1")], writes=[g("s_rowb")])
                    p.pe(lambda t: t.matmul(self.pB[:, 512:1024], self.c_onesb[:, :],
                                            self.s_rowb[0:1, 0:512], start=True, stop=True),
                         reads=[g("s_rowb"), g("c_onesb")], writes=[g("pB1")])
                    p.dve(lambda v, gate_sb=gate_sb: v.tensor_copy(
                        gate_sb, self.pB[:, 512:1024]),
                        reads=[g("pB1")], writes=[ggsb])
                g13v, g13g = self.ABF.view(16896, 16896 + 6 * 512)
                silv, gsil = self.ABF.view(16384, 16896)
                hoff = 0
                for hc in range(nh_):
                    hs = hszs[hc]
                    slot = hc % 2
                    pH = self.pA if slot == 0 else self.pB
                    gh1 = g("pA0") if slot == 0 else g("pB0")
                    gh3 = g("pA1") if slot == 0 else g("pB1")
                    for dc in range(16):
                        p.pe(lambda t, dc=dc, hs=hs, hoff=hoff, vxn=vxn, w1v=w1v,
                             WROW=WROW, pH=pH: t.matmul(
                            pH[0:hs, 0:512], w1v[:, dc * WROW + hoff: dc * WROW + hoff + hs],
                            vxn[:, dc * 512:(dc + 1) * 512], start=(dc == 0), stop=(dc == 15)),
                            reads=[gw1, gxn], writes=[gh1])
                    for dc in range(16):
                        p.pe(lambda t, dc=dc, hs=hs, hoff=hoff, vxn=vxn, w3v=w3v,
                             WROW=WROW, pH=pH: t.matmul(
                            pH[0:hs, 512:1024], w3v[:, dc * WROW + hoff: dc * WROW + hoff + hs],
                            vxn[:, dc * 512:(dc + 1) * 512], start=(dc == 0), stop=(dc == 15)),
                            reads=[gw3, gxn], writes=[gh3])
                    p.act(lambda s, hs=hs, pH=pH: s.activation(
                        silv[0:hs, 0:512], pH[0:hs, 0:512], AF.Silu),
                        reads=[gh1], writes=[gsil])
                    p.dve(lambda v, hs=hs, pH=pH: v.tensor_tensor(
                        silv[0:hs, 0:512], pH[0:hs, 512:1024], silv[0:hs, 0:512], OP.mult),
                        reads=[gh3, gsil], writes=[gsil])
                    if kind == "e":
                        p.dve(lambda v, hc=hc, hs=hs, gate_sb=gate_sb: v.tensor_tensor(
                            g13v[0:hs, hc * 512:(hc + 1) * 512], silv[0:hs, 0:512],
                            gate_sb[0:hs, :], OP.mult),
                            reads=[gsil, ggsb], writes=[g13g])
                    else:
                        p.dve(lambda v, hc=hc, hs=hs: v.tensor_copy(
                            g13v[0:hs, hc * 512:(hc + 1) * 512], silv[0:hs, 0:512]),
                            reads=[gsil], writes=[g13g])
                    hoff += hs
                yst, gyst = self.AFA.view(0, 8192)
                for dc in range(16):
                    half = dc % 2
                    gy = g("pC0") if half == 0 else g("pC1")
                    for hc in range(nh_):
                        hs = hszs[hc]
                        p.pe(lambda t, dc=dc, hc=hc, hs=hs, half=half, w2v=w2v,
                             nh_=nh_: t.matmul(
                            self.pC[:, half * 512:half * 512 + 512],
                            w2v[0:hs, hc * 2048 + dc * 128: hc * 2048 + (dc + 1) * 128],
                            g13v[0:hs, hc * 512:(hc + 1) * 512],
                            start=(hc == 0), stop=(hc == nh_ - 1)),
                            reads=[gw2, g13g], writes=[gy])
                    p.act(lambda s, dc=dc, half=half, yst=yst: s.activation(
                        yst[:, dc * 512:(dc + 1) * 512],
                        self.pC[:, half * 512:half * 512 + 512], AF.Copy),
                        reads=[gy], writes=[gyst])
                dst = self.y_in.ap()[qq * 2048:(qq + 1) * 2048, lc:lc + 512].rearrange(
                    "(c p) t -> p c t", p=128)
                ysr = yst.rearrange("p (c t) -> p c t", t=512)
                if sw == 0:
                    p.pdma(lambda gg, dst=dst, ysr=ysr: gg.dma_start(out=dst, in_=ysr),
                           reads=[gyst], writes=[g("d_y_%d_%d" % (qq, lc))])
                else:
                    p.pdma(lambda gg, dst=dst, ysr=ysr: gg.dma_start(
                        out=dst, in_=ysr, accum_op=OP.add),
                        reads=[gyst], writes=[g("d_y_%d_%d" % (qq, lc))])
        ally = [self.g("d_y_%d_%d" % (qq, lc)) for qq in range(8) for lc in range(0, Q, 512)]
        p.cc(lambda gg: gg.collective_compute(
            "ReduceScatter", OP.add, replica_groups=[list(range(8))],
            ins=[self.y_in.ap().opt()], outs=[self.ffn_own.ap().opt()]),
            reads=ally, writes=[g("d_ffn_own")])
        if self.dev_dbg:
            p.pdma(lambda gg: gg.dma_start(out=self.dbg["dbg_ffn"][:, :],
                                           in_=self.ffn_own.ap()[:, :]),
                   reads=[g("d_ffn_own")], writes=[g("dbg")])

    def b3_proj(self):
        p, g = self.p, self.g
        T, Q = self.T, self.Q
        B3T = T // 512
        wv, gwv = self.AW.view(0, 32768)
        for dc in range(16):
            src = self.wqkv[dc * 128:(dc + 1) * 128, :]
            p.pdma(lambda gg, src=src, dc=dc, wv=wv: gg.dma_start(
                out=wv[:, dc * 2048:(dc + 1) * 2048], in_=src), writes=[gwv])
        rings = []
        for cc in range(16):
            rv, gr = self.ABF.view(16384 + cc * 520, 16384 + cc * 520 + 515)
            rings.append((rv, gr))
            p.pool(lambda gg, rv=rv: gg.memset(rv[:, 0:3], 0.0), writes=[gr])
        for tt in range(B3T):
            par = tt % 2
            t0g = tt * 512
            r, col = t0g // Q, t0g % Q
            vxn, gxn = self.ABF.view(par * 8192, par * 8192 + 8192)
            src = self.xn_bat.ap()[r * 2048:(r + 1) * 2048, col:col + 512].rearrange(
                "(c p) t -> p c t", p=128)
            p.hdma(lambda s, src=src, vxn=vxn: s.dma_start(
                out=vxn.rearrange("p (c t) -> p c t", t=512), in_=src),
                   writes=[gxn], extra=[self.g("d_xn_bat").w])
            for cc in range(16):
                a0, ga0 = self.ABF.view(26752, 27264) if cc % 2 == 0 else self.ABF.view(27776, 28288)
                a1, ga1 = self.ABF.view(27264, 27776) if cc % 2 == 0 else self.ABF.view(28288, 28800)
                rv, gr = rings[cc]
                ph = (cc % 2) * 512
                gph = g("pA0") if cc % 2 == 0 else g("pA1")
                for dc in range(16):
                    p.pe(lambda t, dc=dc, cc=cc, ph=ph, vxn=vxn, wv=wv: t.matmul(
                        self.pA[:, ph:ph + 512],
                        wv[:, dc * 2048 + cc * 128: dc * 2048 + (cc + 1) * 128],
                        vxn[:, dc * 512:(dc + 1) * 512], start=(dc == 0), stop=(dc == 15)),
                        reads=[gwv, gxn], writes=[gph])
                p.dve(lambda v, rv=rv, ph=ph: v.tensor_copy(rv[:, 3:515],
                                                            self.pA[:, ph:ph + 512]),
                      reads=[gph], writes=[gr])
                p.dve(lambda v, rv=rv, cc=cc, a0=a0: v.tensor_scalar(
                    a0, rv[:, 0:512], self.c_convw[:, cc * 4:cc * 4 + 1], None, OP.mult),
                    reads=[gr, g("c_convw")], writes=[ga0])
                p.dve(lambda v, rv=rv, cc=cc, a0=a0, a1=a1: v.scalar_tensor_tensor(
                    a1, rv[:, 1:513], self.c_convw[:, cc * 4 + 1:cc * 4 + 2], a0, OP.mult, OP.add),
                    reads=[gr, ga0, g("c_convw")], writes=[ga1])
                p.dve(lambda v, rv=rv, cc=cc, a0=a0, a1=a1: v.scalar_tensor_tensor(
                    a0, rv[:, 2:514], self.c_convw[:, cc * 4 + 2:cc * 4 + 3], a1, OP.mult, OP.add),
                    reads=[gr, ga1, g("c_convw")], writes=[ga0])
                p.dve(lambda v, rv=rv, cc=cc, a0=a0, a1=a1: v.scalar_tensor_tensor(
                    a1, rv[:, 3:515], self.c_convw[:, cc * 4 + 3:cc * 4 + 4], a0, OP.mult, OP.add),
                    reads=[gr, ga0, g("c_convw")], writes=[ga1])
                p.dve(lambda v, rv=rv: v.tensor_copy(rv[:, 0:3], rv[:, 512:515]),
                      reads=[gr], writes=[gr])
                if cc < 4:
                    qf, gqf = self.AFA.view(1024, 1536)
                    p.act(lambda s, qf=qf, a1=a1: s.activation(qf, a1, AF.Silu),
                          reads=[ga1], writes=[gqf])
                    sq, gsq = self.ABF.view(24704, 25216)
                    p.act(lambda s, sq=sq, qf=qf: s.activation(sq, qf, AF.Square),
                          reads=[gqf], writes=[gsq])
                    p.pe(lambda t, sq=sq: t.matmul(self.pC[0:1, 0:512], self.c_onebc[:, :],
                                                   sq, start=True, stop=True),
                         reads=[gsq, g("c_onebc")], writes=[g("pC0")])
                    R, gR = self.s_rows, g("s_rows")
                    p.dve(lambda v: v.tensor_scalar_add(R[0:1, 0:512], self.pC[0:1, 0:512], 1e-6),
                          reads=[g("pC0")], writes=[gR])
                    p.act(lambda s: s.activation(R[0:1, 0:512], R[0:1, 0:512], AF.Sqrt),
                          reads=[gR], writes=[gR])
                    p.dve(lambda v: v.reciprocal(R[0:1, 0:512], R[0:1, 0:512]),
                          reads=[gR], writes=[gR])
                    p.dve(lambda v: v.tensor_copy(self.s_rowb[0:1, 0:512], R[0:1, 0:512]),
                          reads=[gR], writes=[g("s_rowb")])
                    p.pe(lambda t: t.matmul(self.pB[:, 0:512], self.c_onesb[:, :],
                                            self.s_rowb[0:1, 0:512], start=True, stop=True),
                         reads=[g("s_rowb"), g("c_onesb")], writes=[g("pB0")])
                    qn, gqn = self.ABF.view(25216, 25728)
                    p.dve(lambda v, qn=qn, qf=qf: v.tensor_tensor(qn, qf, self.pB[:, 0:512],
                                                                  OP.mult),
                          reads=[gqf, g("pB0")], writes=[gqn])
                    dst = self.qn_sc.ap()[cc * 128:(cc + 1) * 128, t0g:t0g + 512]
                    p.hdma(lambda s, dst=dst, qn=qn: s.dma_start(out=dst, in_=qn),
                           reads=[gqn], writes=[g("d_qn")])
                elif cc < 8:
                    kb_, gkb = self.ABF.view(25728, 26240)
                    p.act(lambda s, kb_=kb_, a1=a1: s.activation(kb_, a1, AF.Silu),
                          reads=[ga1], writes=[gkb])
                    dst = self.kn_sc.ap()[(cc - 4) * 128:(cc - 3) * 128, t0g:t0g + 512]
                    p.hdma(lambda s, dst=dst, kb_=kb_: s.dma_start(out=dst, in_=kb_),
                           reads=[gkb], writes=[g("d_kn")])
                else:
                    vv, gvv = self.ABF.view(26240, 26752)
                    p.act(lambda s, vv=vv, a1=a1: s.activation(vv, a1, AF.Silu),
                          reads=[ga1], writes=[gvv])
                    dst = self.v_sc.ap()[(cc - 8) * 128:(cc - 7) * 128, t0g:t0g + 512]
                    p.hdma(lambda s, dst=dst, vv=vv: s.dma_start(out=dst, in_=vv),
                           reads=[gvv], writes=[g("d_v")])
            for dc in range(16):
                p.pe(lambda t, dc=dc, vxn=vxn: t.matmul(
                    self.pC[0:8, 0:512], self.s_wba[:, dc * 8:(dc + 1) * 8],
                    vxn[:, dc * 512:(dc + 1) * 512], start=(dc == 0), stop=(dc == 15)),
                    reads=[g("s_wba"), gxn], writes=[g("pC0")])
            for dc in range(16):
                p.pe(lambda t, dc=dc, vxn=vxn: t.matmul(
                    self.pC[0:8, 512:1024], self.s_wba[:, 128 + dc * 8:128 + (dc + 1) * 8],
                    vxn[:, dc * 512:(dc + 1) * 512], start=(dc == 0), stop=(dc == 15)),
                    reads=[g("s_wba"), gxn], writes=[g("pC1")])
            bt, gbt = self.AFA.view(1536, 2048)
            ga_, gga = self.AFA.view(2048, 2560)
            tx, gtx = self.AFA.view(2560, 3072)
            ty, gty = self.AFA.view(3072, 3584)
            p.act(lambda s, bt=bt: s.activation(bt[0:8, :], self.pC[0:8, 0:512], AF.Sigmoid),
                  reads=[g("pC0")], writes=[gbt])
            p.dve(lambda v, ga_=ga_: v.tensor_scalar(ga_[0:8, :], self.pC[0:8, 512:1024],
                                                     self.c_dtb[:, :], None, OP.add),
                  reads=[g("pC1"), g("c_dtb")], writes=[gga])
            p.act(lambda s, ga_=ga_, tx=tx: s.activation(tx[0:8, :], ga_[0:8, :], AF.Abs),
                  reads=[gga], writes=[gtx])
            p.act(lambda s, tx=tx: s.activation(tx[0:8, :], tx[0:8, :], AF.Exp, scale=-1.0),
                  reads=[gtx], writes=[gtx])
            p.act(lambda s, tx=tx: s.activation(tx[0:8, :], tx[0:8, :], AF.Ln,
                                                bias=self.c_bias[0:8, 1:2]),
                  reads=[gtx, g("c_bias")], writes=[gtx])
            p.dve(lambda v, ga_=ga_, ty=ty: v.tensor_scalar_max(ty[0:8, :], ga_[0:8, :], 0.0),
                  reads=[gga], writes=[gty])
            p.dve(lambda v, tx=tx, ty=ty: v.tensor_tensor(ty[0:8, :], ty[0:8, :], tx[0:8, :],
                                                          OP.add),
                  reads=[gtx], writes=[gty])
            p.dve(lambda v, ty=ty: v.tensor_scalar(ty[0:8, :], ty[0:8, :], self.c_nalog[:, :],
                                                   None, OP.mult),
                  reads=[g("c_nalog")], writes=[gty])
            gc_, ggc = self.AFA.view(3584, 4096)
            for sub in range(4):
                p.dve(lambda v, sub=sub, ty=ty, gc_=gc_: v.tensor_tensor_scan(
                    gc_[0:8, sub * 128:(sub + 1) * 128], ty[0:8, sub * 128:(sub + 1) * 128],
                    ty[0:8, sub * 128:(sub + 1) * 128], self.c_zero8[:, :], OP.add, OP.bypass),
                    reads=[gty, g("c_zero8")], writes=[ggc])
            for sub in range(4):
                p.pe(lambda t, sub=sub, gc_=gc_: t.transpose(
                    self.pA[:, sub * 8:sub * 8 + 8], gc_[0:8, sub * 128:(sub + 1) * 128],
                    self.c_idf[0:8, 0:8]),
                    reads=[ggc, g("c_idf")], writes=[g("pA0")])
                p.pe(lambda t, sub=sub, bt=bt: t.transpose(
                    self.pA[:, 32 + sub * 8:32 + sub * 8 + 8], bt[0:8, sub * 128:(sub + 1) * 128],
                    self.c_idf[0:8, 0:8]),
                    reads=[gbt, g("c_idf")], writes=[g("pA0")])
            p.dve(lambda v, tt=tt: v.tensor_copy(self.gcumT[:, tt * 32:tt * 32 + 32],
                                                 self.pA[:, 0:32]),
                  reads=[g("pA0")], writes=[g("gcumT")])
            p.dve(lambda v, tt=tt: v.tensor_copy(self.betaT[:, tt * 32:tt * 32 + 32],
                                                 self.pA[:, 32:64]),
                  reads=[g("pA0")], writes=[g("betaT")])
        wzv, gwz = self.AW.view(0, 16384)
        for dc in range(16):
            src = self.wz[dc * 128:(dc + 1) * 128, :]
            p.pdma(lambda gg, src=src, dc=dc, wzv=wzv: gg.dma_start(
                out=wzv[:, dc * 1024:(dc + 1) * 1024], in_=src), writes=[gwz])
        for tt in range(B3T):
            par = tt % 2
            t0g = tt * 512
            r, col = t0g // Q, t0g % Q
            vxn, gxn = self.ABF.view(par * 8192, par * 8192 + 8192)
            src = self.xn_bat.ap()[r * 2048:(r + 1) * 2048, col:col + 512].rearrange(
                "(c p) t -> p c t", p=128)
            p.hdma(lambda s, src=src, vxn=vxn: s.dma_start(
                out=vxn.rearrange("p (c t) -> p c t", t=512), in_=src), writes=[gxn])
            for zc in range(8):
                for dc in range(16):
                    p.pe(lambda t, zc=zc, dc=dc, vxn=vxn, wzv=wzv: t.matmul(
                        self.pA[:, (zc % 2) * 512:(zc % 2) * 512 + 512],
                        wzv[:, dc * 1024 + zc * 128: dc * 1024 + (zc + 1) * 128],
                        vxn[:, dc * 512:(dc + 1) * 512], start=(dc == 0), stop=(dc == 15)),
                        reads=[gwz, gxn], writes=[g("pA0") if zc % 2 == 0 else g("pA1")])
                zst, gzst = self.ABF.view(16384 + (zc % 2) * 512, 16384 + (zc % 2) * 512 + 512)
                p.act(lambda s, zst=zst, zc=zc: s.activation(
                    zst, self.pA[:, (zc % 2) * 512:(zc % 2) * 512 + 512], AF.Copy),
                    reads=[g("pA0") if zc % 2 == 0 else g("pA1")], writes=[gzst])
                dst = self.z_sc.ap()[zc * 128:(zc + 1) * 128, t0g:t0g + 512]
                p.hdma(lambda s, dst=dst, zst=zst: s.dma_start(out=dst, in_=zst),
                       reads=[gzst], writes=[g("d_z")])
        if self.dev_dbg:
            for nm, t_ in [("dbg_q", self.qn_sc), ("dbg_k", self.kn_sc),
                           ("dbg_v", self.v_sc), ("dbg_z", self.z_sc)]:
                p.pdma(lambda gg, nm=nm, t_=t_: gg.dma_start(
                    out=self.dbg[nm][:, :], in_=t_.ap()[:, :]),
                    reads=[g("d_qn"), g("d_kn"), g("d_v"), g("d_z")], writes=[g("dbg")])
            p.pdma(lambda gg: gg.dma_start(out=self.dbg["dbg_bt"][:, :], in_=self.betaT[:, :]),
                   reads=[g("betaT")], writes=[g("dbg")])
            p.pdma(lambda gg: gg.dma_start(out=self.dbg["dbg_gc"][:, :], in_=self.gcumT[:, :]),
                   reads=[g("gcumT")], writes=[g("dbg")])

    def b4_scan(self):
        p, g = self.p, self.g
        T, Q, J = self.T, self.Q, self.J
        SC = self.s_cols
        gSC = g("s_cols")

        def emit_pre(j):
            par = j % 2
            AB = self.ABF if par == 0 else self.AW
            jc = j * 8

            def ab(a, b, AB=AB):
                return AB.view(a, b)

            def afv(a, b, par=par):
                return self.AFA.view(a + par * 4096, b + par * 4096)

            ctx = {"j": j, "jc": jc, "par": par, "ab": ab, "afv": afv}
            qv, gq = ab(0, 512)
            src = self.qn_sc.ap()[:, j * 128:(j + 1) * 128].rearrange(
                "(h p) t -> p h t", p=128)
            p.hdma(lambda s, src=src, qv=qv: s.dma_start(
                out=qv.rearrange("p (h t) -> p h t", t=128), in_=src),
                writes=[gq], extra=[g("d_qn").w])
            kfm_r, gkfr = ab(1024, 1536)
            src = self.kn_sc.ap()[:, j * 128:(j + 1) * 128].rearrange(
                "(h p) t -> p h t", p=128)
            p.hdma(lambda s, src=src, kfm_r=kfm_r: s.dma_start(
                out=kfm_r.rearrange("p (h t) -> p h t", t=128), in_=src),
                writes=[gkfr], extra=[g("d_kn").w])
            vfm_r, gvfr = ab(2048, 3072)
            src = self.v_sc.ap()[:, j * 128:(j + 1) * 128].rearrange(
                "(u p) t -> p u t", p=128)
            p.hdma(lambda s, src=src, vfm_r=vfm_r: s.dma_start(
                out=vfm_r.rearrange("p (u t) -> p u t", t=128), in_=src),
                writes=[gvfr], extra=[g("d_v").w])
            zfm_r, gzfr = ab(4096, 5120)
            src = self.z_sc.ap()[:, j * 128:(j + 1) * 128].rearrange(
                "(u p) t -> p u t", p=128)
            p.adma(lambda s, src=src, zfm_r=zfm_r: s.dma_start(
                out=zfm_r.rearrange("p (u t) -> p u t", t=128), in_=src),
                writes=[gzfr], extra=[g("d_z").w])
            ktm, gkt = ab(1536, 2048)
            for h in range(4):
                p.pe(lambda t, h=h, kfm_r=kfm_r: t.transpose(
                    self.pT0[:, h * 128:(h + 1) * 128], kfm_r[:, h * 128:(h + 1) * 128],
                    self.c_idb[:, :]),
                    reads=[gkfr, g("c_idb")], writes=[g("pT0")])
            p.dve(lambda v, ktm=ktm: v.tensor_copy(ktm, self.pT0[:, 0:512]),
                  reads=[g("pT0")], writes=[gkt])
            vtm, gvt = ab(3072, 4096)
            for u in range(8):
                p.pe(lambda t, u=u, vfm_r=vfm_r: t.transpose(
                    self.pT1[:, u * 128:(u + 1) * 128], vfm_r[:, u * 128:(u + 1) * 128],
                    self.c_idb[:, :]),
                    reads=[gvfr, g("c_idb")], writes=[g("pT1")])
            p.dve(lambda v, vtm=vtm: v.tensor_copy(vtm, self.pT1[:, :]),
                  reads=[g("pT1")], writes=[gvt])
            ztm, gzt = ab(5120, 6144)
            for u in range(8):
                p.pe(lambda t, u=u, zfm_r=zfm_r: t.transpose(
                    self.pT0[:, u * 128:(u + 1) * 128], zfm_r[:, u * 128:(u + 1) * 128],
                    self.c_idb[:, :]),
                    reads=[gzfr, g("c_idb")], writes=[g("pT0")])
            p.dve(lambda v, ztm=ztm: v.tensor_copy(ztm, self.pT0[:, :]),
                  reads=[g("pT0")], writes=[gzt])
            ctx["qv"], ctx["gq"] = qv, gq
            ctx["ktm"], ctx["gkt"] = ktm, gkt
            ctx["ztm"], ctx["gzt"] = ztm, gzt
            k2, gk2 = afv(3072, 3584)
            p.dve(lambda v, ktm=ktm, k2=k2: v.tensor_tensor(k2, ktm, ktm, OP.mult),
                  reads=[gkt], writes=[gk2])
            p.dve(lambda v, k2=k2: v.reduce_sum(
                SC[:, 0:4], k2.rearrange("p (h a) -> p h a", a=128), axis=AX.X),
                reads=[gk2], writes=[gSC])
            p.dve(lambda v: v.tensor_scalar_add(SC[:, 4:8], SC[:, 0:4], 1e-6),
                  reads=[gSC], writes=[gSC])
            p.act(lambda s: s.activation(SC[:, 4:8], SC[:, 4:8], AF.Sqrt),
                  reads=[gSC], writes=[gSC])
            p.dve(lambda v: v.reciprocal(SC[:, 4:8], SC[:, 4:8]),
                  reads=[gSC], writes=[gSC])
            p.dve(lambda v: v.tensor_copy(
                SC[:, 8:16].rearrange("p (u o) -> p u o", o=2)[:, :, 0:1],
                SC[:, 4:8].rearrange("p (u o) -> p u o", o=1)),
                reads=[gSC], writes=[gSC])
            p.dve(lambda v: v.tensor_copy(
                SC[:, 8:16].rearrange("p (u o) -> p u o", o=2)[:, :, 1:2],
                SC[:, 4:8].rearrange("p (u o) -> p u o", o=1)),
                reads=[gSC], writes=[gSC])
            p.act(lambda s, jc=jc: s.activation(SC[:, 16:24], self.gcumT[:, jc:jc + 8], AF.Exp),
                  reads=[g("gcumT")], writes=[gSC])
            p.dve(lambda v, jc=jc: v.tensor_tensor(SC[:, 48:56], self.betaT[:, jc:jc + 8],
                                                   SC[:, 8:16], OP.mult),
                  reads=[g("betaT"), gSC], writes=[gSC])
            p.dve(lambda v: v.tensor_tensor(SC[:, 56:64], SC[:, 48:56], SC[:, 16:24], OP.mult),
                  reads=[gSC], writes=[gSC])
            p.dve(lambda v, jc=jc: v.tensor_scalar_mul(SC[:, 72:80], self.gcumT[:, jc:jc + 8],
                                                       -1.0),
                  reads=[g("gcumT")], writes=[gSC])
            kbtm, gkb = ab(6144, 7168)
            kbgtm, gkbg = ab(7168, 8192)
            vbtm, gvb = ab(9216, 10240)
            kntm, gknt = ab(10240, 10752)
            for u in range(8):
                h = u // 2
                p.dve(lambda v, u=u, h=h, ktm=ktm, kbtm=kbtm: v.tensor_scalar(
                    kbtm[:, u * 128:(u + 1) * 128], ktm[:, h * 128:(h + 1) * 128],
                    SC[:, 48 + u:49 + u], None, OP.mult),
                    reads=[gkt, gSC], writes=[gkb])
                p.dve(lambda v, u=u, h=h, ktm=ktm, kbgtm=kbgtm: v.tensor_scalar(
                    kbgtm[:, u * 128:(u + 1) * 128], ktm[:, h * 128:(h + 1) * 128],
                    SC[:, 56 + u:57 + u], None, OP.mult),
                    reads=[gkt, gSC], writes=[gkbg])
                p.dve(lambda v, u=u, jc=jc, vtm=vtm, vbtm=vbtm: v.tensor_scalar(
                    vbtm[:, u * 128:(u + 1) * 128], vtm[:, u * 128:(u + 1) * 128],
                    self.betaT[:, jc + u:jc + u + 1], None, OP.mult),
                    reads=[gvt, g("betaT")], writes=[gvb])
            for h in range(4):
                p.dve(lambda v, h=h, ktm=ktm, kntm=kntm: v.tensor_scalar(
                    kntm[:, h * 128:(h + 1) * 128], ktm[:, h * 128:(h + 1) * 128],
                    SC[:, 4 + h:5 + h], None, OP.mult),
                    reads=[gkt, gSC], writes=[gknt])
            kbfm, gkbf = ab(10752, 11776)
            for u in range(8):
                p.pe(lambda t, u=u, kbtm=kbtm: t.transpose(
                    self.pT0[:, u * 128:(u + 1) * 128], kbtm[:, u * 128:(u + 1) * 128],
                    self.c_idb[:, :]),
                    reads=[gkb, g("c_idb")], writes=[g("pT0")])
            p.dve(lambda v, kbfm=kbfm: v.tensor_copy(kbfm, self.pT0[:, :]),
                  reads=[g("pT0")], writes=[gkbf])
            knfm, gknf = ab(11776, 12288)
            for h in range(4):
                p.pe(lambda t, h=h, kntm=kntm: t.transpose(
                    self.pT1[:, h * 128:(h + 1) * 128], kntm[:, h * 128:(h + 1) * 128],
                    self.c_idb[:, :]),
                    reads=[gknt, g("c_idb")], writes=[g("pT1")])
            p.dve(lambda v, knfm=knfm: v.tensor_copy(knfm, self.pT1[:, 0:512]),
                  reads=[g("pT1")], writes=[gknf])
            ctx["vbtm"], ctx["gvb"] = vbtm, gvb
            ctx["kbgtm"], ctx["gkbg"] = kbgtm, gkbg
            epre, gep = afv(0, 1024)
            ebc, gebc = afv(3584, 3712)
            for u in range(8):
                p.dve(lambda v, u=u, ebc=ebc: v.tensor_scalar(
                    ebc, self.c_ones128[:, :], SC[:, 72 + u:73 + u], None, OP.mult),
                    reads=[g("c_ones128"), gSC], writes=[gebc])
                p.pe(lambda t, u=u, ebc=ebc: t.transpose(
                    self.pB[:, u * 128:(u + 1) * 128], ebc, self.c_idf[:, :]),
                    reads=[gebc, g("c_idf")], writes=[g("pB0") if u < 4 else g("pB1")])
                p.dve(lambda v, u=u, jc=jc, epre=epre: v.tensor_scalar(
                    epre[:, u * 128:(u + 1) * 128], self.pB[:, u * 128:(u + 1) * 128],
                    self.gcumT[:, jc + u:jc + u + 1], 0.0, OP.add, OP.min),
                    reads=[g("pB0") if u < 4 else g("pB1"), g("gcumT")], writes=[gep])
            eexp, gee = ab(12288, 13312)
            p.act(lambda s, epre=epre, eexp=eexp: s.activation(eexp, epre, AF.Exp),
                  reads=[gep], writes=[gee])
            eneg, gen = ab(13312, 14336)
            p.dve(lambda v, eexp=eexp, eneg=eneg: v.tensor_tensor(
                eneg, eexp, self.c_sneg[:, :], OP.mult),
                reads=[gee, g("c_sneg")], writes=[gen])
            for u in range(8):
                p.pe(lambda t, u=u, kbfm=kbfm, knfm=knfm: t.matmul(
                    self.pA[:, u * 128:(u + 1) * 128], kbfm[:, u * 128:(u + 1) * 128],
                    knfm[:, (u // 2) * 128:(u // 2 + 1) * 128], start=True, stop=True),
                    reads=[gkbf, gknf], writes=[g("pA0") if u < 4 else g("pA1")])
            sl, gsl = ab(15360, 16384)
            p.dve(lambda v, sl=sl, eneg=eneg: v.tensor_tensor(sl, self.pA[:, :], eneg, OP.mult),
                  reads=[g("pA0"), g("pA1"), gen], writes=[gsl])
            st_, gst = ab(16384, 17408)
            for u in range(8):
                p.pe(lambda t, u=u, sl=sl: t.transpose(
                    self.pT0[:, u * 128:(u + 1) * 128], sl[:, u * 128:(u + 1) * 128],
                    self.c_idb[:, :]),
                    reads=[gsl, g("c_idb")], writes=[g("pT0")])
            p.dve(lambda v, st_=st_: v.tensor_copy(st_, self.pT0[:, :]),
                  reads=[g("pT0")], writes=[gst])
            eit, geit = ab(14336, 15360)
            for u in range(8):
                p.pe(lambda t, u=u, eneg=eneg: t.transpose(
                    self.pT1[:, u * 128:(u + 1) * 128], eneg[:, u * 128:(u + 1) * 128],
                    self.c_idb[:, :]),
                    reads=[gen, g("c_idb")], writes=[g("pT1")])
            p.dve(lambda v, eit=eit: v.tensor_tensor(eit, self.c_blkI[:, :], self.pT1[:, :],
                                                     OP.subtract),
                  reads=[g("c_blkI"), g("pT1")], writes=[geit])
            for u in range(8):
                p.pe(lambda t, u=u, knfm=knfm, qv=qv: t.matmul(
                    self.pB[:, u * 128:(u + 1) * 128], knfm[:, (u // 2) * 128:(u // 2 + 1) * 128],
                    qv[:, (u // 2) * 128:(u // 2 + 1) * 128], start=True, stop=True),
                    reads=[gknf, gq], writes=[g("pB0") if u < 4 else g("pB1")])
            intq, gint = ab(17408, 18432)
            p.dve(lambda v, intq=intq, eit=eit: v.scalar_tensor_tensor(
                intq, self.pB[:, :], RS128, eit, OP.mult, OP.mult),
                reads=[g("pB0"), g("pB1"), geit], writes=[gint])
            ctx["intq"], ctx["gint"] = intq, gint
            Pb = [ab(22528, 23552), ab(23552, 24576)]
            p.dve(lambda v, P0=Pb[0][0], st_=st_: v.tensor_tensor(
                P0, self.c_blkI[:, :], st_, OP.add),
                  reads=[g("c_blkI"), gst], writes=[Pb[0][1]])
            cur = 0
            spl, gspl = sl, gsl
            spt, gspt = st_, gst
            lv = [(ab(18432, 19456), ab(19456, 20480)), (ab(20480, 21504), ab(21504, 22528))]
            for lev in range(1, 7):
                for u in range(8):
                    p.pe(lambda t, u=u, spl=spl, spt=spt: t.matmul(
                        self.pA[:, u * 128:(u + 1) * 128], spt[:, u * 128:(u + 1) * 128],
                        spl[:, u * 128:(u + 1) * 128], start=True, stop=True),
                        reads=[gspl, gspt], writes=[g("pA0") if u < 4 else g("pA1")])
                (nl, gnl), (nt, gnt) = lv[lev % 2]
                p.act(lambda s, nl=nl: s.activation(nl, self.pA[:, :], AF.Copy),
                      reads=[g("pA0"), g("pA1")], writes=[gnl])
                if lev < 6:
                    for u in range(8):
                        p.pe(lambda t, u=u, spl=spl, spt=spt: t.matmul(
                            self.pC[:, u * 128:(u + 1) * 128], spl[:, u * 128:(u + 1) * 128],
                            spt[:, u * 128:(u + 1) * 128], start=True, stop=True),
                            reads=[gspl, gspt], writes=[g("pC0") if u < 4 else g("pC1")])
                    p.act(lambda s, nt=nt: s.activation(nt, self.pC[:, :], AF.Copy),
                          reads=[g("pC0"), g("pC1")], writes=[gnt])
                for u in range(8):
                    p.pe(lambda t, u=u, nl=nl, Pc=Pb[cur][0]: t.matmul(
                        self.pB[:, u * 128:(u + 1) * 128], nl[:, u * 128:(u + 1) * 128],
                        Pc[:, u * 128:(u + 1) * 128], start=True, stop=True),
                        reads=[gnl, Pb[cur][1]], writes=[g("pB0") if u < 4 else g("pB1")])
                p.dve(lambda v, Pn=Pb[1 - cur][0], Pc=Pb[cur][0]: v.tensor_tensor(
                    Pn[:, :], self.pB[:, :], Pc[:, :], OP.add),
                    reads=[g("pB0"), g("pB1"), Pb[cur][1]], writes=[Pb[1 - cur][1]])
                cur = 1 - cur
                spl, gspl = nl, gnl
                spt, gspt = nt, gnt
            Pf, gPf = Pb[cur]
            ctx["Pf"], ctx["gPf"] = Pf, gPf
            co2 = 96 + par * 48
            ctx["co2"] = co2
            p.act(lambda s, jc=jc, co2=co2: s.activation(
                SC[:, co2:co2 + 8], self.gcumT[:, jc:jc + 8], AF.Exp, bias=self.c_bias[:, 2:3]),
                  reads=[g("gcumT"), g("c_bias")], writes=[gSC])
            p.pe(lambda t, jc=jc: t.matmul(self.pC[0:1, 896:904], self.c_idf[:, 127:128],
                                           self.gcumT[:, jc:jc + 8], start=True, stop=True),
                 reads=[g("gcumT"), g("c_idf")], writes=[g("pC1")])
            p.dve(lambda v: v.tensor_copy(self.s_rows[0:1, 16:24], self.pC[0:1, 896:904]),
                  reads=[g("pC1")], writes=[g("s_rows")])
            p.pe(lambda t: t.matmul(self.pC[:, 904:912], self.c_onesf[:, :],
                                    self.s_rows[0:1, 16:24], start=True, stop=True),
                 reads=[g("s_rows"), g("c_onesf")], writes=[g("pC1")])
            p.act(lambda s, co2=co2: s.activation(SC[:, co2 + 8:co2 + 16], self.pC[:, 904:912],
                                                  AF.Exp),
                  reads=[g("pC1")], writes=[gSC])
            p.dve(lambda v, jc=jc, co2=co2: v.tensor_tensor(
                SC[:, co2 + 16:co2 + 24], self.pC[:, 904:912],
                self.gcumT[:, jc:jc + 8], OP.subtract),
                  reads=[g("pC1"), g("gcumT")], writes=[gSC])
            p.act(lambda s, co2=co2: s.activation(SC[:, co2 + 16:co2 + 24],
                                                  SC[:, co2 + 16:co2 + 24], AF.Exp),
                  reads=[gSC], writes=[gSC])
            p.dve(lambda v, co2=co2: v.tensor_tensor(SC[:, co2 + 24:co2 + 32],
                                                     SC[:, co2 + 16:co2 + 24],
                                                     SC[:, 8:16], OP.mult),
                  reads=[gSC], writes=[gSC])
            kdtm, gkd = ab(8192, 9216)
            for u in range(8):
                h = u // 2
                p.dve(lambda v, u=u, h=h, ktm=ktm, kdtm=kdtm, co2=co2: v.tensor_scalar(
                    kdtm[:, u * 128:(u + 1) * 128], ktm[:, h * 128:(h + 1) * 128],
                    SC[:, co2 + 24 + u:co2 + 25 + u], None, OP.mult),
                    reads=[gkt, gSC], writes=[gkd])
            ctx["kdtm"], ctx["gkd"] = kdtm, gkd
            nkcd, gnk = ab(24576, 25600)
            for u in range(8):
                p.pe(lambda t, u=u, kbgtm=kbgtm, Pf=Pf: t.matmul(
                    self.pB[:, u * 128:(u + 1) * 128], kbgtm[:, u * 128:(u + 1) * 128],
                    Pf[:, u * 128:(u + 1) * 128], start=True, stop=True),
                    reads=[gkbg, gPf], writes=[g("pB0") if u < 4 else g("pB1")])
            p.dve(lambda v, nkcd=nkcd: v.tensor_scalar_mul(nkcd, self.pB[:, :], -1.0),
                  reads=[g("pB0"), g("pB1")], writes=[gnk])
            ctx["nkcd"], ctx["gnk"] = nkcd, gnk
            return ctx

        def emit_post(ctx):
            j, jc, par, ab, afv = ctx["j"], ctx["jc"], ctx["par"], ctx["ab"], ctx["afv"]
            qv, gq = ctx["qv"], ctx["gq"]
            ktm, gkt = ctx["ktm"], ctx["gkt"]
            ztm, gzt = ctx["ztm"], ctx["gzt"]
            vbtm, gvb = ctx["vbtm"], ctx["gvb"]
            Pf, gPf = ctx["Pf"], ctx["gPf"]
            nkcd, gnk = ctx["nkcd"], ctx["gnk"]
            intq, gint = ctx["intq"], ctx["gint"]
            co2 = ctx["co2"]
            kdtm, gkd = ctx["kdtm"], ctx["gkd"]
            for u in range(8):
                p.pe(lambda t, u=u, Pf=Pf, vbtm=vbtm: t.matmul(
                    self.pA[:, u * 128:(u + 1) * 128], Pf[:, u * 128:(u + 1) * 128],
                    vbtm[:, u * 128:(u + 1) * 128], start=True, stop=False),
                    reads=[gPf, gvb], writes=[g("pA0") if u < 4 else g("pA1")])
                p.pe(lambda t, u=u, nkcd=nkcd: t.matmul(
                    self.pA[:, u * 128:(u + 1) * 128], nkcd[:, u * 128:(u + 1) * 128],
                    self.s_stb[:, u * 128:(u + 1) * 128], start=False, stop=True),
                    reads=[gnk, g("s_stb")], writes=[g("pA0") if u < 4 else g("pA1")])
            vnew, gvn = ab(25600, 26624)
            p.dve(lambda v, vnew=vnew: v.tensor_copy(vnew, self.pA[:, :]),
                  reads=[g("pA0"), g("pA1")], writes=[gvn])
            interf, gitf = afv(1024, 2048)
            for u in range(8):
                p.pe(lambda t, u=u, qv=qv: t.matmul(
                    self.pB[:, u * 128:(u + 1) * 128], qv[:, (u // 2) * 128:(u // 2 + 1) * 128],
                    self.s_stb[:, u * 128:(u + 1) * 128], start=True, stop=True),
                    reads=[gq, g("s_stb")], writes=[g("pB0") if u < 4 else g("pB1")])
            for u in range(8):
                p.act(lambda s, u=u, interf=interf, co2=co2: s.activation(
                    interf[:, u * 128:(u + 1) * 128], self.pB[:, u * 128:(u + 1) * 128],
                    AF.Copy, scale=SC[:, co2 + u:co2 + u + 1]),
                    reads=[g("pB0") if u < 4 else g("pB1"), gSC], writes=[gitf])
            for u in range(8):
                p.pe(lambda t, u=u, intq=intq, vnew=vnew: t.matmul(
                    self.pC[:, u * 128:(u + 1) * 128], intq[:, u * 128:(u + 1) * 128],
                    vnew[:, u * 128:(u + 1) * 128], start=True, stop=True),
                    reads=[gint, gvn], writes=[g("pC0") if u < 4 else g("pC1")])
            outc, goc = afv(2048, 3072)
            p.dve(lambda v, outc=outc, interf=interf: v.tensor_tensor(
                outc, self.pC[:, :], interf, OP.add),
                reads=[g("pC0"), g("pC1"), gitf], writes=[goc])
            for u in range(8):
                p.pe(lambda t, u=u, kdtm=kdtm, vnew=vnew: t.matmul(
                    self.pA[:, u * 128:(u + 1) * 128], kdtm[:, u * 128:(u + 1) * 128],
                    vnew[:, u * 128:(u + 1) * 128], start=True, stop=True),
                    reads=[gkd, gvn], writes=[g("pA0") if u < 4 else g("pA1")])
            for u in range(8):
                p.dve(lambda v, u=u, co2=co2: v.tensor_scalar(
                    self.s_st[:, u * 128:(u + 1) * 128], self.s_st[:, u * 128:(u + 1) * 128],
                    SC[:, co2 + 8 + u:co2 + 9 + u], None, OP.mult),
                    reads=[gSC], writes=[g("s_st")])
            p.dve(lambda v: v.tensor_tensor(self.s_st[:, :], self.s_st[:, :], self.pA[:, :],
                                            OP.add),
                  reads=[g("pA0"), g("pA1")], writes=[g("s_st")])
            p.dve(lambda v: v.tensor_copy(self.s_stb[:, :], self.s_st[:, :]),
                  reads=[g("s_st")], writes=[g("s_stb")])
            zsil, gzs = ab(26624, 27648)
            p.act(lambda s, ztm=ztm, zsil=zsil: s.activation(zsil, ztm, AF.Silu),
                  reads=[gzt], writes=[gzs])
            xg, gxg = afv(1024, 2048)
            p.dve(lambda v, outc=outc, zsil=zsil, xg=xg: v.tensor_tensor(
                xg, outc, zsil, OP.mult),
                reads=[goc, gzs], writes=[gxg])
            sqb, gsq = ab(27648, 28672)
            p.act(lambda s, xg=xg, sqb=sqb: s.activation(sqb, xg, AF.Square),
                  reads=[gxg], writes=[gsq])
            p.dve(lambda v, sqb=sqb: v.reduce_sum(
                SC[:, 80:88], sqb.rearrange("p (u l) -> p u l", l=128), axis=AX.X),
                reads=[gsq], writes=[gSC])
            p.dve(lambda v: v.tensor_scalar(SC[:, 88:96], SC[:, 80:88], 1.0 / 128.0, 1e-6,
                                            OP.mult, OP.add),
                  reads=[gSC], writes=[gSC])
            p.act(lambda s: s.activation(SC[:, 88:96], SC[:, 88:96], AF.Sqrt),
                  reads=[gSC], writes=[gSC])
            p.dve(lambda v: v.reciprocal(SC[:, 88:96], SC[:, 88:96]),
                  reads=[gSC], writes=[gSC])
            ntm, gnt2 = ab(28672, 29696)
            for u in range(8):
                p.dve(lambda v, u=u, xg=xg, ntm=ntm: v.tensor_scalar(
                    ntm[:, u * 128:(u + 1) * 128], xg[:, u * 128:(u + 1) * 128],
                    SC[:, 88 + u:89 + u], None, OP.mult),
                    reads=[gxg, gSC], writes=[gnt2])
            nst, gnst = ab(29696, 30720)
            for u in range(8):
                p.pe(lambda t, u=u, ntm=ntm: t.transpose(
                    self.pT0[:, u * 128:(u + 1) * 128], ntm[:, u * 128:(u + 1) * 128],
                    self.c_idb[:, :]),
                    reads=[gnt2, g("c_idb")], writes=[g("pT0")])
            for u in range(8):
                p.act(lambda s, u=u, nst=nst: s.activation(
                    nst[:, u * 128:(u + 1) * 128],
                    self.pT0[:, u * 128:(u + 1) * 128], AF.Copy, scale=self.c_normw[:, :]),
                    reads=[g("pT0"), g("c_normw")], writes=[gnst])
            dst = self.nrm_sc.ap()[:, j * 128:(j + 1) * 128].rearrange(
                "(u p) l -> p u l", p=128)
            p.hdma(lambda s, dst=dst, nst=nst: s.dma_start(
                out=dst, in_=nst.rearrange("p (u l) -> p u l", l=128)),
                   reads=[gnst], writes=[g("d_nrm")])

        ctx = emit_pre(0)
        for j in range(J):
            nxt = emit_pre(j + 1) if j + 1 < J else None
            emit_post(ctx)
            ctx = nxt
        if self.dev_dbg:
            p.pdma(lambda gg: gg.dma_start(out=self.dbg["dbg_nrm"][:, :],
                                           in_=self.nrm_sc.ap()[:, :]),
                   reads=[g("d_nrm")], writes=[g("dbg")])

    def b6_wout(self):
        p, g = self.p, self.g
        T, Q = self.T, self.Q
        wo, gwo = self.AW.view(0, 16384)
        for vdc in range(8):
            src = self.wout[vdc * 128:(vdc + 1) * 128, :]
            p.pdma(lambda gg, src=src, vdc=vdc, wo=wo: gg.dma_start(
                out=wo[:, vdc * 2048:(vdc + 1) * 2048], in_=src), writes=[gwo])
        for tt in range(T // 512):
            par = tt % 2
            t0 = tt * 512
            q, lc = t0 // Q, t0 % Q
            vn, gvn = self.ABF.view(par * 4096, par * 4096 + 4096)
            src = self.nrm_sc.ap()[:, t0:t0 + 512].rearrange("(c p) t -> p c t", p=128)
            p.hdma(lambda s, src=src, vn=vn: s.dma_start(
                out=vn.rearrange("p (c t) -> p c t", t=512), in_=src),
                   writes=[gvn], extra=[g("d_nrm").w])
            yst, gyst = self.AFA.view(0, 8192)
            for dc in range(16):
                half = dc % 2
                for vdc in range(8):
                    p.pe(lambda t, dc=dc, vdc=vdc, half=half, vn=vn, wo=wo: t.matmul(
                        self.pA[:, half * 512:half * 512 + 512],
                        wo[:, vdc * 2048 + dc * 128: vdc * 2048 + (dc + 1) * 128],
                        vn[:, vdc * 512:(vdc + 1) * 512], start=(vdc == 0), stop=(vdc == 7)),
                        reads=[gwo, gvn], writes=[g("pA0") if half == 0 else g("pA1")])
                p.dve(lambda v, dc=dc, half=half, yst=yst: v.tensor_copy(
                    yst[:, dc * 512:(dc + 1) * 512], self.pA[:, half * 512:half * 512 + 512]),
                    reads=[g("pA0") if half == 0 else g("pA1")], writes=[gyst])
            dst = self.ps_in.ap()[q * 2048:(q + 1) * 2048, lc:lc + 512].rearrange(
                "(c p) t -> p c t", p=128)
            p.hdma(lambda s, dst=dst, yst=yst: s.dma_start(
                out=dst, in_=yst.rearrange("p (c t) -> p c t", t=512)),
                   reads=[gyst], writes=[g("d_ps")])
        p.cc(lambda gg: gg.collective_compute(
            "ReduceScatter", OP.add, replica_groups=[[0, 1, 2, 3], [4, 5, 6, 7]],
            ins=[self.ps_in.ap().opt()], outs=[self.s_own.ap().opt()]),
            reads=[g("d_ps")], writes=[g("d_s_own")])
        if self.dev_dbg:
            p.pdma(lambda gg: gg.dma_start(out=self.dbg["dbg_s"][:, :],
                                           in_=self.s_own.ap()[:, :]),
                   reads=[g("d_s_own")], writes=[g("dbg")])

    def b9_combine(self):
        p, g = self.p, self.g
        Q = self.Q
        for tt in range(Q // 512):
            t0 = tt * 512
            for dc in range(16):
                i = (tt * 16 + dc) % 2
                vx, gx = self.AFA.view(i * 2048, i * 2048 + 512)
                vs, gs = self.AFA.view(i * 2048 + 512, i * 2048 + 1024)
                vf, gf = self.AFA.view(i * 2048 + 1024, i * 2048 + 1536)
                vo, go = self.AFA.view(i * 2048 + 1536, i * 2048 + 2048)
                p.adma(lambda s, dc=dc, t0=t0, vx=vx: s.dma_start(
                    out=vx, in_=self.x_fm[dc * 128:(dc + 1) * 128, t0:t0 + 512]),
                    writes=[gx])
                p.adma(lambda s, dc=dc, t0=t0, vs=vs: s.dma_start(
                    out=vs, in_=self.s_own.ap()[dc * 128:(dc + 1) * 128, t0:t0 + 512]),
                    writes=[gs], extra=[g("d_s_own").w])
                p.adma(lambda s, dc=dc, t0=t0, vf=vf: s.dma_start(
                    out=vf, in_=self.ffn_own.ap()[dc * 128:(dc + 1) * 128, t0:t0 + 512]),
                    writes=[gf], extra=[g("d_ffn_own").w])
                p.act(lambda s, vf=vf: s.activation(vf, vf, AF.Silu),
                      reads=[gf], writes=[gf])
                p.dve(lambda v, vo=vo, vs=vs, vf=vf: v.tensor_tensor(vo, vs, vf, OP.mult),
                      reads=[gs, gf], writes=[go])
                p.dve(lambda v, vo=vo, vx=vx: v.tensor_tensor(vo, vo, vx, OP.add),
                      reads=[gx], writes=[go])
                p.hdma(lambda s, dc=dc, t0=t0, vo=vo: s.dma_start(
                    out=self.out_fm[dc * 128:(dc + 1) * 128, t0:t0 + 512], in_=vo),
                    reads=[go], writes=[g("d_out")])

    def build(self):
        self.declare()
        with ExitStack() as es:
            self.alloc(es)
            self.b0_consts()
            self.b1_ln_router()
            self.b2_collectives()
            self.b7_ffn()
            self.b3_proj()
            self.b4_scan()
            self.b6_wout()
            self.b9_combine()
            with ExitStack() as es2:
                sems = {k: es2.enter_context(self.nc.semaphore("sem_" + k))
                        for k in Prog.SEMS}
                blk = es2.enter_context(self.nc.Block("mk"))
                self.p.emit(blk, sems, pe_fifo=self.pe_fifo)
                final = dict(self.p.cnt)

                def tail(eng):
                    for k, vv in final.items():
                        if vv:
                            eng.wait_ge(sems[k], vv)
                blk.sync(tail)
        return self.nc



def _host_inputs(inputs, T):
    Q = T // 4
    x = np.asarray(inputs["L0_attn_hidden"])
    Wqkv = np.asarray(inputs["L0_attn_W_qkv"])
    Wz = np.asarray(inputs["L0_attn_W_z"])
    Wb = np.asarray(inputs["L0_attn_W_b"])
    Wa = np.asarray(inputs["L0_attn_W_a"])
    cw = np.asarray(inputs["L0_attn_conv_weight"])
    dtbias = np.asarray(inputs["L0_attn_dt_bias"])
    Alog = np.asarray(inputs["L0_attn_A_log"])
    nw = np.asarray(inputs["L0_attn_norm_w"])
    Wout = np.asarray(inputs["L0_attn_W_out"])
    gw = np.asarray(inputs["L0_ffn_gate_w"])
    rb = np.asarray(inputs["L0_ffn_router_bias"])
    W1 = np.asarray(inputs["L0_ffn_W1"])
    W3 = np.asarray(inputs["L0_ffn_W3"])
    W2 = np.asarray(inputs["L0_ffn_W2"])
    sw1 = np.asarray(inputs["L0_ffn_sw1"])
    sw3 = np.asarray(inputs["L0_ffn_sw3"])
    sw2 = np.asarray(inputs["L0_ffn_sw2"])

    Xf = np.ascontiguousarray(x.reshape(2 * T, D))
    idf = np.eye(128, dtype=np.float32)
    idb = idf.astype(BFNP)
    onesb = np.ones((1, 128), np.float32).astype(BFNP)
    onebc = np.ones((128, 1), np.float32).astype(BFNP)
    cbias = np.tile(np.array([[1e-5, 1.0, -0.5 * np.log(128.0), 1e-6]], np.float32), (128, 1))
    onesf = np.ones((1, 128), np.float32)
    sneg1 = (-np.tril(np.ones((128, 128), np.float32), -1)).astype(BFNP)
    sneg = np.concatenate([sneg1] * VH, axis=1)
    blkI = np.concatenate([idb] * VH, axis=1)
    gw128 = np.ascontiguousarray(
        gw.T.reshape(16, 128, 8).transpose(1, 0, 2).reshape(128, 128)).astype(np.float32)

    maps = []
    for c in range(8):
        b = c // 4
        hq0 = (c % 4) * QH
        qcols = np.arange(hq0 * 128, (hq0 + QH) * 128)
        kcols = 2048 + qcols
        vcols = 4096 + np.arange(hq0 * 2 * 128, (hq0 + QH) * 2 * 128)
        cols = np.concatenate([qcols, kcols, vcols])
        vh = np.arange(hq0 * 2, (hq0 + QH) * 2)
        vdcols = np.arange(vh[0] * 128, (vh[-1] + 1) * 128)

        x_fm = np.ascontiguousarray(Xf[c * Q:(c + 1) * Q].T)
        wqkv_s = np.ascontiguousarray(Wqkv[:, cols])
        wz_s = np.ascontiguousarray(Wz[:, vdcols])
        wb128 = np.ascontiguousarray(
            Wb[:, vh].reshape(16, 128, VH).transpose(1, 0, 2).reshape(128, 128))
        wa128 = np.ascontiguousarray(
            Wa[:, vh].reshape(16, 128, VH).transpose(1, 0, 2).reshape(128, 128))
        convw_s = np.ascontiguousarray(
            cw[cols].reshape(16, 128, 4).transpose(1, 0, 2).reshape(128, 64))
        m = dict(
            x_fm=x_fm, wqkv=wqkv_s, wz=wz_s, wb128=wb128, wa128=wa128,
            convw=convw_s,
            alog=np.ascontiguousarray(Alog[vh].reshape(VH, 1)),
            dtb=np.ascontiguousarray(dtbias[vh].reshape(VH, 1)),
            normw=np.ascontiguousarray(nw.reshape(128, 1)),
            wout=np.ascontiguousarray(Wout[vdcols, :]),
            gw128=gw128, rbias=np.ascontiguousarray(rb.reshape(1, 8)),
            w1=np.ascontiguousarray(W1[c]), w3=np.ascontiguousarray(W3[c]),
            w2=np.ascontiguousarray(W2[c]),
            sw1T=np.ascontiguousarray(sw1.T[:, c * SHS:(c + 1) * SHS]),
            sw3T=np.ascontiguousarray(sw3.T[:, c * SHS:(c + 1) * SHS]),
            sw2T=np.ascontiguousarray(sw2.T[c * SHS:(c + 1) * SHS, :]),
            idf=idf, idb=idb, onesb=onesb, onebc=onebc, onesf=onesf, sneg=sneg, blkI=blkI,
            cbias=cbias,
        )
        maps.append(m)
    return maps


_NC_CACHE = {}


def kernel(**inputs):
    T = int(np.asarray(inputs["L0_attn_hidden"]).shape[1])
    key = ("nc", T)
    if key not in _NC_CACHE:
        _NC_CACHE[key] = Builder(T).build()
    nc = _NC_CACHE[key]
    maps = _host_inputs(inputs, T)
    res = run_bass_kernel_spmd(nc, maps, core_ids=list(range(8)))
    Q = T // 4
    full = np.concatenate([np.asarray(res.results[c]["out_fm"]) for c in range(8)], axis=1)
    return np.ascontiguousarray(full.T).reshape(2, T, D).astype(np.float32)


# version 07
# speedup vs baseline: 1.0110x; 1.0110x over previous
import math
from contextlib import ExitStack

import numpy as np
import ml_dtypes

import concourse.bass as bass
import concourse.mybir as mybir
from concourse.bass_utils import run_bass_kernel_spmd
from concourse.aot_env import allow_ici_all_to_all

F32, BF16 = mybir.dt.float32, mybir.dt.bfloat16
AF = mybir.ActivationFunctionType
OP = mybir.AluOpType
AX = mybir.AxisListType
BFNP = ml_dtypes.bfloat16

D = 2048
QH, VH = 4, 8
C = 128
INTER = 1408
SHS = 352
LN128 = float(np.log(128.0))
RS128 = float(1.0 / np.sqrt(128.0))


class Guard:
    __slots__ = ("w", "rs")
    def __init__(self, init=()):
        self.w = None
        self.rs = list(init)


class Arena:
    def __init__(self, ap_full):
        self.ap = ap_full
        self.ivs = []

    def view(self, a, b):
        for (x, y, gd) in self.ivs:
            if x == a and y == b:
                return self.ap[:, a:b], gd
        deps, keep = [], []
        for (x, y, gd) in self.ivs:
            if x < b and a < y:
                if gd.w is not None:
                    deps.append(gd.w)
                deps.extend(gd.rs)
            else:
                keep.append((x, y, gd))
        gnew = Guard(deps)
        keep.append((a, b, gnew))
        self.ivs = keep
        return self.ap[:, a:b], gnew


class Prog:
    SECTIONS = ("tensor", "vector", "scalar", "sync", "gpsimd")
    COMPUTE_SEMS = ("pe", "dve", "act", "pool")
    DMA_POOL = {"hdma": 8, "adma": 8, "pdma": 8, "cc": 4}
    SEMS = COMPUTE_SEMS + tuple(
        f"{c}{i}" for c, n in DMA_POOL.items() for i in range(n))

    def __init__(self):
        self.q = {s: [] for s in self.SECTIONS}
        self.cnt = {k: 0 for k in self.SEMS}
        self.rr = {c: 0 for c in self.DMA_POOL}

    def _op(self, section, sem, fn, reads, writes, extra, k):
        deps = list(extra)
        if sem in self.DMA_POOL:
            i = self.rr[sem]
            self.rr[sem] = (i + 1) % self.DMA_POOL[sem]
            sub = f"{sem}{i}"
            if self.cnt[sub]:
                deps.append((sub, self.cnt[sub]))
            sem = sub
        for gd in reads:
            if gd.w is not None:
                deps.append(gd.w)
        for gd in writes:
            if gd.w is not None:
                deps.append(gd.w)
            deps.extend(gd.rs)
        self.cnt[sem] += k
        tok = (sem, self.cnt[sem])
        self.q[section].append((fn, sem, k, tuple(d for d in deps if d is not None)))
        for gd in reads:
            gd.rs.append(tok)
        for gd in writes:
            gd.w = tok
            gd.rs = []
        return tok

    def pe(self, fn, reads=(), writes=(), extra=()):
        return self._op("tensor", "pe", fn, reads, writes, extra, 1)

    def dve(self, fn, reads=(), writes=(), extra=()):
        return self._op("vector", "dve", fn, reads, writes, extra, 1)

    def act(self, fn, reads=(), writes=(), extra=()):
        return self._op("scalar", "act", fn, reads, writes, extra, 1)

    def pool(self, fn, reads=(), writes=(), extra=()):
        return self._op("gpsimd", "pool", fn, reads, writes, extra, 1)

    def hdma(self, fn, reads=(), writes=(), extra=()):
        return self._op("sync", "hdma", fn, reads, writes, extra, 16)

    def adma(self, fn, reads=(), writes=(), extra=()):
        return self._op("scalar", "adma", fn, reads, writes, extra, 16)

    def pdma(self, fn, reads=(), writes=(), extra=()):
        return self._op("gpsimd", "pdma", fn, reads, writes, extra, 16)

    def cc(self, fn, reads=(), writes=(), extra=()):
        return self._op("gpsimd", "cc", fn, reads, writes, extra, 1)

    def emit(self, blk, sems, pe_fifo=True):
        for section in self.SECTIONS:
            ops = self.q[section]
            if not ops:
                continue

            def body(eng, ops=ops, section=section):
                water = {k: 0 for k in self.cnt}
                for fn, sem, k, deps in ops:
                    for dsem, dval in deps:
                        if pe_fifo and dsem == "pe" and section == "tensor":
                            continue
                        if water[dsem] < dval:
                            eng.wait_ge(sems[dsem], dval)
                            water[dsem] = dval
                    fn(eng).then_inc(sems[sem], k)
            getattr(blk, section)(body)


class Builder:
    def __init__(self, T, dev_dbg=False, pe_fifo=True):
        self.pe_fifo = pe_fifo
        self.T = T
        self.Q = T // 4
        self.J = T // C
        self.dev_dbg = dev_dbg
        self.nc = bass.Bass("TRN2", target_bir_lowering=False)
        self.p = Prog()
        self.G = {}

    def g(self, name):
        if name not in self.G:
            self.G[name] = Guard()
        return self.G[name]

    def declare(self):
        nc, T, Q = self.nc, self.T, self.Q
        din = lambda n, s, dt=F32: nc.dram_tensor(n, s, dt, kind="ExternalInput").ap()
        self.x_fm = din("x_fm", [D, Q])
        self.wqkv = din("wqkv", [D, 2048])
        self.wz = din("wz", [D, 1024])
        self.wb128 = din("wb128", [128, 128])
        self.wa128 = din("wa128", [128, 128])
        self.convw = din("convw", [128, 64])
        self.alog = din("alog", [VH, 1])
        self.dtb = din("dtb", [VH, 1])
        self.normw = din("normw", [128, 1])
        self.wout = din("wout", [VH * 128, D])
        self.gw128 = din("gw128", [128, 128])
        self.rbias = din("rbias", [1, 8])
        self.w1 = din("w1", [D, INTER])
        self.w3 = din("w3", [D, INTER])
        self.w2 = din("w2", [INTER, D])
        self.sw1T = din("sw1T", [D, SHS])
        self.sw3T = din("sw3T", [D, SHS])
        self.sw2T = din("sw2T", [SHS, D])
        self.idf_d = din("idf", [128, 128])
        self.idb_d = din("idb", [128, 128], BF16)
        self.onesb_d = din("onesb", [1, 128], BF16)
        self.onebc_d = din("onebc", [128, 1], BF16)
        self.cbias_d = din("cbias", [128, 4])
        self.onesf_d = din("onesf", [1, 128])
        self.sneg_d = din("sneg", [128, VH * 128], BF16)
        self.blkI_d = din("blkI", [128, VH * 128], BF16)

        dr = lambda n, s, dt=F32: nc.dram_tensor(n, s, dt)
        self.xn_blk = dr("xn_blk", [D, Q], BF16)
        self.xn_bat = dr("xn_bat", [4 * D, Q], BF16)
        self.xn_all = dr("xn_all", [8 * D, Q], BF16)
        self.gates_in = dr("gates_in", [8, Q], BF16)
        self.gates_x = dr("gates_x", [8, Q], BF16)
        self.qn_sc = dr("qn_sc", [QH * 128, T], BF16)
        self.kn_sc = dr("kn_sc", [QH * 128, T], BF16)
        self.v_sc = dr("v_sc", [VH * 128, T], BF16)
        self.z_sc = dr("z_sc", [VH * 128, T], BF16)
        self.nrm_sc = dr("nrm_sc", [VH * 128, T], BF16)
        self.ps_in = dr("ps_in", [4 * D, Q], F32)
        self.s_own = dr("s_own", [D, Q], F32)
        self.y_in = dr("y_in", [8 * D, Q], F32)
        self.ffn_own = dr("ffn_own", [D, Q], F32)
        self.out_fm = nc.dram_tensor("out_fm", [D, Q], F32, kind="ExternalOutput").ap()

        self.dbg = {}
        if self.dev_dbg:
            for nm, shape, dt in [
                ("dbg_xn", [D, Q], BF16), ("dbg_gates", [8, Q], BF16),
                ("dbg_q", [QH * 128, T], BF16), ("dbg_k", [QH * 128, T], BF16),
                ("dbg_v", [VH * 128, T], BF16), ("dbg_z", [VH * 128, T], BF16),
                ("dbg_bt", [128, self.J * 8], F32), ("dbg_gc", [128, self.J * 8], F32),
                ("dbg_nrm", [VH * 128, T], BF16), ("dbg_s", [D, Q], F32),
                ("dbg_ffn", [D, Q], F32),
            ]:
                self.dbg[nm] = nc.dram_tensor(nm, shape, dt, kind="ExternalOutput").ap()

    def alloc(self, es):
        nc = self.nc
        sb = lambda n, s, dt: es.enter_context(nc.sbuf_tensor(n, s, dt))
        ps = lambda n, s, dt: es.enter_context(nc.psum_tensor(n, s, dt))
        self.AW = Arena(sb("AW", [128, 36864], BF16))
        self.ABF = Arena(sb("ABF", [128, 31744], BF16))
        self.AFA = Arena(sb("AFA", [128, 8192], F32))
        self.gcumT = sb("gcumT", [128, self.J * 8], F32)
        self.betaT = sb("betaT", [128, self.J * 8], F32)
        self.gates8 = sb("gates8", [8, self.Q], BF16)
        self.c_idf = sb("c_idf", [128, 128], F32)
        self.c_idb = sb("c_idb", [128, 128], BF16)
        self.c_onesb = sb("c_onesb", [1, 128], BF16)
        self.c_onebc = sb("c_onebc", [128, 1], BF16)
        self.c_bias = sb("c_bias", [128, 4], F32)
        self.c_onesf = sb("c_onesf", [1, 128], F32)
        self.c_ones128 = sb("c_ones128", [128, 128], F32)
        self.c_sneg = sb("c_sneg", [128, VH * 128], BF16)
        self.c_blkI = sb("c_blkI", [128, VH * 128], BF16)
        self.c_convw = sb("c_convw", [128, 64], F32)
        self.c_normw = sb("c_normw", [128, 1], F32)
        self.c_alog = sb("c_alog", [VH, 1], F32)
        self.c_dtb = sb("c_dtb", [VH, 1], F32)
        self.c_rbias = sb("c_rbias", [128, 8], F32)
        self.c_nalog = sb("c_nalog", [VH, 1], F32)
        self.c_zero8 = sb("c_zero8", [8, 1], F32)
        self.s_gw = sb("s_gw", [128, 128], F32)
        self.s_wba = sb("s_wba", [128, 256], BF16)
        self.s_cols = sb("s_cols", [128, 256], F32)
        self.s_rows = sb("s_rows", [1, 1536], F32)
        self.s_rowb = sb("s_rowb", [1, 1024], BF16)
        self.s_st = sb("s_st", [128, 1024], F32)
        self.s_stb = sb("s_stb", [128, 1024], BF16)
        self.pA = ps("pA", [128, 1024], F32)
        self.pB = ps("pB", [128, 1024], F32)
        self.pC = ps("pC", [128, 1024], F32)
        self.pT0 = ps("pT0", [128, 1024], BF16)
        self.pT1 = ps("pT1", [128, 1024], BF16)
        for n in ("pA0", "pA1", "pB0", "pB1", "pC0", "pC1", "pT0", "pT1"):
            self.g(n)

    def b0_consts(self):
        p, g = self.p, self.g
        pairs = [
            (self.c_idf, self.idf_d, "c_idf"), (self.c_idb, self.idb_d, "c_idb"),
            (self.c_onesb, self.onesb_d, "c_onesb"), (self.c_onesf, self.onesf_d, "c_onesf"),
            (self.c_onebc, self.onebc_d, "c_onebc"),
            (self.c_bias, self.cbias_d, "c_bias"),
            (self.c_sneg, self.sneg_d, "c_sneg"), (self.c_blkI, self.blkI_d, "c_blkI"),
            (self.c_convw, self.convw, "c_convw"), (self.c_normw, self.normw, "c_normw"),
            (self.c_alog, self.alog, "c_alog"), (self.c_dtb, self.dtb, "c_dtb"),
            (self.s_gw, self.gw128, "s_gw"),
        ]
        for dst, src, gn in pairs:
            p.pdma(lambda gg, dst=dst, src=src: gg.dma_start(out=dst[:, :], in_=src[:, :]),
                   writes=[g(gn)])
        p.pdma(lambda gg: gg.dma_start(out=self.s_wba[:, 0:128], in_=self.wb128[:, :]),
               writes=[g("s_wba")])
        p.pdma(lambda gg: gg.dma_start(out=self.s_wba[:, 128:256], in_=self.wa128[:, :]),
               writes=[g("s_wba")])
        p.pool(lambda gg: gg.memset(self.c_ones128[:, :], 1.0), writes=[g("c_ones128")])
        p.pool(lambda gg: gg.memset(self.c_zero8[:, :], 0.0), writes=[g("c_zero8")])
        p.pool(lambda gg: gg.memset(self.s_st[:, :], 0.0), writes=[g("s_st")])
        p.pool(lambda gg: gg.memset(self.s_stb[:, :], 0.0), writes=[g("s_stb")])
        p.pdma(lambda gg: gg.dma_start(out=self.s_rows[0:1, 0:8], in_=self.rbias[:, :]),
               writes=[g("s_rows")])
        p.pe(lambda t: t.matmul(self.pC[0:128, 0:8], self.c_onesf[:, :],
                                self.s_rows[0:1, 0:8], start=True, stop=True),
             reads=[g("c_onesf"), g("s_rows")], writes=[g("pC0")])
        p.dve(lambda v: v.tensor_copy(self.c_rbias[:, :], self.pC[0:128, 0:8]),
              reads=[g("pC0")], writes=[g("c_rbias")])
        p.act(lambda s: s.activation(self.c_nalog[:, :], self.c_alog[:, :], AF.Exp),
              reads=[g("c_alog")], writes=[g("c_nalog")])
        p.dve(lambda v: v.tensor_scalar_mul(self.c_nalog[:, :], self.c_nalog[:, :], -1.0),
              reads=[g("c_nalog")], writes=[g("c_nalog")])

    def b1_ln_router(self):
        p, g = self.p, self.g
        B1T = self.Q // 512
        for tt in range(B1T):
            t0 = tt * 512
            vx, gx = self.AFA.view(0, 8192)
            vxb, gxb = self.ABF.view(0, 8192)
            vx2, gx2 = self.ABF.view(8192, 16384)
            vxnb, gxnb = self.ABF.view(16384, 24576)
            src = self.x_fm[:, t0:t0 + 512].rearrange("(c p) t -> p c t", p=128)
            p.hdma(lambda s, src=src, vx=vx: s.dma_start(
                out=vx.rearrange("p (c t) -> p c t", t=512), in_=src), writes=[gx])
            p.pdma(lambda gg, src=src, vxb=vxb: gg.dma_start(
                out=vxb.rearrange("p (c t) -> p c t", t=512), in_=src), writes=[gxb])
            p.act(lambda s, vx=vx, vx2=vx2: s.activation(vx2, vx, AF.Square),
                  reads=[gx], writes=[gx2])
            for dc in range(16):
                p.pe(lambda t, dc=dc, vxb=vxb: t.matmul(
                    self.pC[0:1, 0:512], self.c_onebc[:, :],
                    vxb[:, dc * 512:(dc + 1) * 512], start=(dc == 0), stop=(dc == 15)),
                    reads=[gxb, g("c_onebc")], writes=[g("pC0")])
            for dc in range(16):
                p.pe(lambda t, dc=dc, vx2=vx2: t.matmul(
                    self.pC[0:1, 512:1024], self.c_onebc[:, :],
                    vx2[:, dc * 512:(dc + 1) * 512], start=(dc == 0), stop=(dc == 15)),
                    reads=[gx2, g("c_onebc")], writes=[g("pC1")])
            R, gR = self.s_rows, g("s_rows")
            p.dve(lambda v: v.tensor_scalar_mul(R[0:1, 0:512], self.pC[0:1, 0:512], 1.0 / D),
                  reads=[g("pC0")], writes=[gR])
            p.dve(lambda v: v.tensor_scalar_mul(R[0:1, 512:1024], self.pC[0:1, 512:1024], 1.0 / D),
                  reads=[g("pC1")], writes=[gR])
            p.dve(lambda v: v.tensor_tensor(R[0:1, 1024:1536], R[0:1, 0:512], R[0:1, 0:512], OP.mult),
                  reads=[gR], writes=[gR])
            p.dve(lambda v: v.tensor_tensor(R[0:1, 512:1024], R[0:1, 512:1024],
                                            R[0:1, 1024:1536], OP.subtract),
                  reads=[gR], writes=[gR])
            p.act(lambda s: s.activation(R[0:1, 512:1024], R[0:1, 512:1024], AF.Sqrt,
                                         bias=self.c_bias[0:1, 0:1]),
                  reads=[gR, g("c_bias")], writes=[gR])
            p.dve(lambda v: v.reciprocal(R[0:1, 512:1024], R[0:1, 512:1024]),
                  reads=[gR], writes=[gR])
            p.pe(lambda t: t.matmul(self.pA[:, 0:512], self.c_onesf[:, :], R[0:1, 0:512],
                                    start=True, stop=True),
                 reads=[gR, g("c_onesf")], writes=[g("pA0")])
            p.pe(lambda t: t.matmul(self.pA[:, 512:1024], self.c_onesf[:, :], R[0:1, 512:1024],
                                    start=True, stop=True),
                 reads=[gR, g("c_onesf")], writes=[g("pA1")])
            mean_x = bass.AP(self.pA[:, 0:512].tensor, 0, [[1024, 128], [0, 16], [1, 512]])
            rstd_x = bass.AP(self.pA[:, 0:512].tensor, 512, [[1024, 128], [0, 16], [1, 512]])
            vx3 = vx.rearrange("p (c t) -> p c t", t=512)
            p.dve(lambda v, vx3=vx3, mean_x=mean_x: v.tensor_tensor(
                vx3, vx3, mean_x, OP.subtract), reads=[g("pA0")], writes=[gx])
            p.dve(lambda v, vx3=vx3, rstd_x=rstd_x: v.tensor_tensor(
                vx3, vx3, rstd_x, OP.mult), reads=[g("pA1")], writes=[gx])
            p.dve(lambda v, vx=vx, vxnb=vxnb: v.tensor_copy(vxnb, vx),
                  reads=[gx], writes=[gxnb])
            for sub in range(4):
                for dc in range(16):
                    p.pe(lambda t, sub=sub, dc=dc, vx=vx: t.matmul(
                        self.pB[:, sub * 8:sub * 8 + 8],
                        vx[:, dc * 512 + sub * 128: dc * 512 + sub * 128 + 128],
                        self.s_gw[:, dc * 8:(dc + 1) * 8],
                        start=(dc == 0), stop=(dc == 15)),
                        reads=[gx, g("s_gw")], writes=[g("pB0")])
            SC, gS = self.s_cols, g("s_cols")
            for sub in range(4):
                pg = self.pB[:, sub * 8:sub * 8 + 8]
                p.act(lambda s, pg=pg: s.activation(SC[:, 0:8], pg, AF.Sigmoid),
                      reads=[g("pB0")], writes=[gS])
                p.dve(lambda v: v.tensor_tensor(SC[:, 8:16], SC[:, 0:8], self.c_rbias[:, :], OP.add),
                      reads=[gS, g("c_rbias")], writes=[gS])
                p.dve(lambda v: v.reduce_max(SC[:, 16:17], SC[:, 8:16], axis=AX.X),
                      reads=[gS], writes=[gS])
                p.dve(lambda v: v.tensor_scalar(SC[:, 24:32], SC[:, 8:16], SC[:, 16:17],
                                                None, OP.is_ge),
                      reads=[gS], writes=[gS])
                p.dve(lambda v: v.scalar_tensor_tensor(SC[:, 32:40], SC[:, 24:32], -1e9,
                                                       SC[:, 8:16], OP.mult, OP.add),
                      reads=[gS], writes=[gS])
                p.dve(lambda v: v.reduce_max(SC[:, 40:41], SC[:, 32:40], axis=AX.X),
                      reads=[gS], writes=[gS])
                p.dve(lambda v: v.tensor_scalar(SC[:, 48:56], SC[:, 8:16], SC[:, 40:41],
                                                None, OP.is_ge),
                      reads=[gS], writes=[gS])
                p.dve(lambda v: v.tensor_tensor(SC[:, 56:64], SC[:, 0:8], SC[:, 48:56], OP.mult),
                      reads=[gS], writes=[gS])
                p.dve(lambda v: v.reduce_sum(SC[:, 64:65], SC[:, 56:64], axis=AX.X),
                      reads=[gS], writes=[gS])
                p.dve(lambda v: v.tensor_scalar_max(SC[:, 64:65], SC[:, 64:65], 1e-12),
                      reads=[gS], writes=[gS])
                p.dve(lambda v: v.reciprocal(SC[:, 64:65], SC[:, 64:65]),
                      reads=[gS], writes=[gS])
                p.dve(lambda v: v.tensor_scalar(SC[:, 72:80], SC[:, 56:64], SC[:, 64:65],
                                                None, OP.mult),
                      reads=[gS], writes=[gS])
                vgb, ggb = self.ABF.view(24576, 24584)
                p.dve(lambda v, vgb=vgb: v.tensor_copy(vgb, SC[:, 72:80]),
                      reads=[gS], writes=[ggb])
                p.pe(lambda t, vgb=vgb: t.transpose(self.pT0[0:8, 0:128], vgb, self.c_idb[:, :]),
                     reads=[ggb, g("c_idb")], writes=[g("pT0")])
                vgf, ggf = self.ABF.view(24704, 24832)
                p.dve(lambda v, vgf=vgf: v.tensor_copy(vgf[0:8, :], self.pT0[0:8, 0:128]),
                      reads=[g("pT0")], writes=[ggf])
                dst = self.gates_in.ap()[:, t0 + sub * 128: t0 + (sub + 1) * 128]
                p.hdma(lambda s, dst=dst, vgf=vgf: s.dma_start(out=dst, in_=vgf[0:8, :]),
                       reads=[ggf], writes=[g("d_gates_in")])
            dst = self.xn_blk.ap()[:, t0:t0 + 512].rearrange("(c p) t -> p c t", p=128)
            p.hdma(lambda s, dst=dst, vxnb=vxnb: s.dma_start(
                out=dst, in_=vxnb.rearrange("p (c t) -> p c t", t=512)),
                   reads=[gxnb], writes=[g("d_xn_blk")])

    def b2_collectives(self):
        p, g = self.p, self.g
        p.cc(lambda gg: gg.collective_compute(
            "AllGather", OP.bypass, replica_groups=[[0, 1, 2, 3], [4, 5, 6, 7]],
            ins=[self.xn_blk.ap().opt()], outs=[self.xn_bat.ap().opt()]),
            reads=[g("d_xn_blk")], writes=[g("d_xn_bat")])
        p.cc(lambda gg: gg.collective_compute(
            "AllGather", OP.bypass, replica_groups=[list(range(8))],
            ins=[self.xn_blk.ap().opt()], outs=[self.xn_all.ap().opt()]),
            reads=[g("d_xn_blk")], writes=[g("d_xn_all")])

        def _a2a(gg):
            with allow_ici_all_to_all():
                return gg.collective_compute(
                    "AllToAll", OP.bypass, replica_groups=[list(range(8))],
                    ins=[self.gates_in.ap().opt()], outs=[self.gates_x.ap().opt()])
        p.cc(_a2a, reads=[g("d_gates_in")], writes=[g("d_gates_x")])
        if self.dev_dbg:
            p.pdma(lambda gg: gg.dma_start(out=self.dbg["dbg_xn"][:, :],
                                           in_=self.xn_blk.ap()[:, :]),
                   reads=[g("d_xn_blk")], writes=[g("dbg")])
            p.pdma(lambda gg: gg.dma_start(out=self.dbg["dbg_gates"][:, :],
                                           in_=self.gates_x.ap()[:, :]),
                   reads=[g("d_gates_x")], writes=[g("dbg")])

    def b7_ffn(self):
        p, g = self.p, self.g
        Q = self.Q
        B7T = (2 * self.T) // 512
        p.pdma(lambda gg: gg.dma_start(out=self.gates8[:, :], in_=self.gates_x.ap()[:, :]),
               reads=[g("d_gates_x")], writes=[g("gates8")])
        sweeps = [("e", 0, 6), ("e", 6, 11), ("sh", 0, 3)]
        for sw, (kind, h0, h1) in enumerate(sweeps):
            nh = h1 - h0
            if kind == "e":
                hszs = [128] * nh
                WROW = nh * 128
                w1v, gw1 = self.AW.view(0, 16 * WROW)
                w3v, gw3 = self.AW.view(16 * WROW, 32 * WROW)
                w2v, gw2 = self.AW.view(32 * WROW, 32 * WROW + nh * 2048)
                for dc in range(16):
                    src = self.w1[dc * 128:(dc + 1) * 128, h0 * 128:h1 * 128]
                    p.pdma(lambda gg, src=src, dc=dc, w1v=w1v, WROW=WROW: gg.dma_start(
                        out=w1v[:, dc * WROW:(dc + 1) * WROW], in_=src), writes=[gw1])
                for dc in range(16):
                    src = self.w3[dc * 128:(dc + 1) * 128, h0 * 128:h1 * 128]
                    p.pdma(lambda gg, src=src, dc=dc, w3v=w3v, WROW=WROW: gg.dma_start(
                        out=w3v[:, dc * WROW:(dc + 1) * WROW], in_=src), writes=[gw3])
                for hc in range(nh):
                    src = self.w2[(h0 + hc) * 128:(h0 + hc + 1) * 128, :]
                    p.pdma(lambda gg, src=src, hc=hc, w2v=w2v: gg.dma_start(
                        out=w2v[:, hc * 2048:(hc + 1) * 2048], in_=src), writes=[gw2])
            else:
                hszs = [128, 128, 96]
                WROW = SHS
                w1v, gw1 = self.AW.view(0, 16 * SHS)
                w3v, gw3 = self.AW.view(16 * SHS, 32 * SHS)
                w2v, gw2 = self.AW.view(32 * SHS, 32 * SHS + 3 * 2048)
                for dc in range(16):
                    src = self.sw1T[dc * 128:(dc + 1) * 128, :]
                    p.pdma(lambda gg, src=src, dc=dc, w1v=w1v: gg.dma_start(
                        out=w1v[:, dc * SHS:(dc + 1) * SHS], in_=src), writes=[gw1])
                for dc in range(16):
                    src = self.sw3T[dc * 128:(dc + 1) * 128, :]
                    p.pdma(lambda gg, src=src, dc=dc, w3v=w3v: gg.dma_start(
                        out=w3v[:, dc * SHS:(dc + 1) * SHS], in_=src), writes=[gw3])
                for hc in range(3):
                    hs = hszs[hc]
                    src = self.sw2T[hc * 128:hc * 128 + hs, :]
                    p.pdma(lambda gg, src=src, hc=hc, hs=hs, w2v=w2v: gg.dma_start(
                        out=w2v[0:hs, hc * 2048:(hc + 1) * 2048], in_=src), writes=[gw2])
            nh_ = len(hszs)
            for gt in range(B7T):
                qq, lc = (gt * 512) // Q, (gt * 512) % Q
                par = gt % 2
                vxn, gxn = self.ABF.view(par * 8192, par * 8192 + 8192)
                src = self.xn_all.ap()[qq * 2048:(qq + 1) * 2048, lc:lc + 512].rearrange(
                    "(c p) t -> p c t", p=128)
                p.hdma(lambda s, src=src, vxn=vxn: s.dma_start(
                    out=vxn.rearrange("p (c t) -> p c t", t=512), in_=src),
                    writes=[gxn], extra=[g("d_xn_all").w])
                gate_sb, ggsb = self.ABF.view(19968 + par * 512, 19968 + par * 512 + 512)
                if kind == "e":
                    p.pe(lambda t, qq=qq, lc=lc: t.matmul(
                        self.pB[0:1, 512:1024], self.c_idb[0:8, qq:qq + 1],
                        self.gates8[:, lc:lc + 512], start=True, stop=True),
                        reads=[g("gates8"), g("c_idb")], writes=[g("pB1")])
                    p.dve(lambda v: v.tensor_copy(self.s_rowb[0:1, 0:512],
                                                  self.pB[0:1, 512:1024]),
                          reads=[g("pB1")], writes=[g("s_rowb")])
                    p.pe(lambda t: t.matmul(self.pB[:, 512:1024], self.c_onesb[:, :],
                                            self.s_rowb[0:1, 0:512], start=True, stop=True),
                         reads=[g("s_rowb"), g("c_onesb")], writes=[g("pB1")])
                    p.dve(lambda v, gate_sb=gate_sb: v.tensor_copy(
                        gate_sb, self.pB[:, 512:1024]),
                        reads=[g("pB1")], writes=[ggsb])
                g13v, g13g = self.ABF.view(16896, 16896 + 6 * 512)
                silv, gsil = self.ABF.view(16384, 16896)
                hoff = 0
                for hc in range(nh_):
                    hs = hszs[hc]
                    slot = hc % 2
                    pH = self.pA if slot == 0 else self.pB
                    gh1 = g("pA0") if slot == 0 else g("pB0")
                    gh3 = g("pA1") if slot == 0 else g("pB1")
                    for dc in range(16):
                        p.pe(lambda t, dc=dc, hs=hs, hoff=hoff, vxn=vxn, w1v=w1v,
                             WROW=WROW, pH=pH: t.matmul(
                            pH[0:hs, 0:512], w1v[:, dc * WROW + hoff: dc * WROW + hoff + hs],
                            vxn[:, dc * 512:(dc + 1) * 512], start=(dc == 0), stop=(dc == 15)),
                            reads=[gw1, gxn], writes=[gh1])
                    for dc in range(16):
                        p.pe(lambda t, dc=dc, hs=hs, hoff=hoff, vxn=vxn, w3v=w3v,
                             WROW=WROW, pH=pH: t.matmul(
                            pH[0:hs, 512:1024], w3v[:, dc * WROW + hoff: dc * WROW + hoff + hs],
                            vxn[:, dc * 512:(dc + 1) * 512], start=(dc == 0), stop=(dc == 15)),
                            reads=[gw3, gxn], writes=[gh3])
                    p.act(lambda s, hs=hs, pH=pH: s.activation(
                        silv[0:hs, 0:512], pH[0:hs, 0:512], AF.Silu),
                        reads=[gh1], writes=[gsil])
                    p.dve(lambda v, hs=hs, pH=pH: v.tensor_tensor(
                        silv[0:hs, 0:512], pH[0:hs, 512:1024], silv[0:hs, 0:512], OP.mult),
                        reads=[gh3, gsil], writes=[gsil])
                    if kind == "e":
                        p.dve(lambda v, hc=hc, hs=hs, gate_sb=gate_sb: v.tensor_tensor(
                            g13v[0:hs, hc * 512:(hc + 1) * 512], silv[0:hs, 0:512],
                            gate_sb[0:hs, :], OP.mult),
                            reads=[gsil, ggsb], writes=[g13g])
                    else:
                        p.dve(lambda v, hc=hc, hs=hs: v.tensor_copy(
                            g13v[0:hs, hc * 512:(hc + 1) * 512], silv[0:hs, 0:512]),
                            reads=[gsil], writes=[g13g])
                    hoff += hs
                yst, gyst = self.AFA.view(0, 8192)
                for dc in range(16):
                    half = dc % 2
                    gy = g("pC0") if half == 0 else g("pC1")
                    for hc in range(nh_):
                        hs = hszs[hc]
                        p.pe(lambda t, dc=dc, hc=hc, hs=hs, half=half, w2v=w2v,
                             nh_=nh_: t.matmul(
                            self.pC[:, half * 512:half * 512 + 512],
                            w2v[0:hs, hc * 2048 + dc * 128: hc * 2048 + (dc + 1) * 128],
                            g13v[0:hs, hc * 512:(hc + 1) * 512],
                            start=(hc == 0), stop=(hc == nh_ - 1)),
                            reads=[gw2, g13g], writes=[gy])
                    p.act(lambda s, dc=dc, half=half, yst=yst: s.activation(
                        yst[:, dc * 512:(dc + 1) * 512],
                        self.pC[:, half * 512:half * 512 + 512], AF.Copy),
                        reads=[gy], writes=[gyst])
                dst = self.y_in.ap()[qq * 2048:(qq + 1) * 2048, lc:lc + 512].rearrange(
                    "(c p) t -> p c t", p=128)
                ysr = yst.rearrange("p (c t) -> p c t", t=512)
                if sw == 0:
                    p.pdma(lambda gg, dst=dst, ysr=ysr: gg.dma_start(out=dst, in_=ysr),
                           reads=[gyst], writes=[g("d_y_%d_%d" % (qq, lc))])
                else:
                    p.pdma(lambda gg, dst=dst, ysr=ysr: gg.dma_start(
                        out=dst, in_=ysr, accum_op=OP.add),
                        reads=[gyst], writes=[g("d_y_%d_%d" % (qq, lc))])
        ally = [self.g("d_y_%d_%d" % (qq, lc)) for qq in range(8) for lc in range(0, Q, 512)]
        p.cc(lambda gg: gg.collective_compute(
            "ReduceScatter", OP.add, replica_groups=[list(range(8))],
            ins=[self.y_in.ap().opt()], outs=[self.ffn_own.ap().opt()]),
            reads=ally, writes=[g("d_ffn_own")])
        if self.dev_dbg:
            p.pdma(lambda gg: gg.dma_start(out=self.dbg["dbg_ffn"][:, :],
                                           in_=self.ffn_own.ap()[:, :]),
                   reads=[g("d_ffn_own")], writes=[g("dbg")])

    def b3_proj(self):
        p, g = self.p, self.g
        T, Q = self.T, self.Q
        B3T = T // 512
        wv, gwv = self.AW.view(0, 32768)
        for dc in range(16):
            src = self.wqkv[dc * 128:(dc + 1) * 128, :]
            p.pdma(lambda gg, src=src, dc=dc, wv=wv: gg.dma_start(
                out=wv[:, dc * 2048:(dc + 1) * 2048], in_=src), writes=[gwv])
        rings = []
        for cc in range(16):
            rv, gr = self.ABF.view(16384 + cc * 520, 16384 + cc * 520 + 515)
            rings.append((rv, gr))
            p.pool(lambda gg, rv=rv: gg.memset(rv[:, 0:3], 0.0), writes=[gr])
        for tt in range(B3T):
            par = tt % 2
            t0g = tt * 512
            r, col = t0g // Q, t0g % Q
            vxn, gxn = self.ABF.view(par * 8192, par * 8192 + 8192)
            src = self.xn_bat.ap()[r * 2048:(r + 1) * 2048, col:col + 512].rearrange(
                "(c p) t -> p c t", p=128)
            p.hdma(lambda s, src=src, vxn=vxn: s.dma_start(
                out=vxn.rearrange("p (c t) -> p c t", t=512), in_=src),
                   writes=[gxn], extra=[self.g("d_xn_bat").w])
            for cc in range(16):
                a0, ga0 = self.ABF.view(26752, 27264) if cc % 2 == 0 else self.ABF.view(27776, 28288)
                a1, ga1 = self.ABF.view(27264, 27776) if cc % 2 == 0 else self.ABF.view(28288, 28800)
                rv, gr = rings[cc]
                ph = (cc % 2) * 512
                gph = g("pA0") if cc % 2 == 0 else g("pA1")
                for dc in range(16):
                    p.pe(lambda t, dc=dc, cc=cc, ph=ph, vxn=vxn, wv=wv: t.matmul(
                        self.pA[:, ph:ph + 512],
                        wv[:, dc * 2048 + cc * 128: dc * 2048 + (cc + 1) * 128],
                        vxn[:, dc * 512:(dc + 1) * 512], start=(dc == 0), stop=(dc == 15)),
                        reads=[gwv, gxn], writes=[gph])
                p.dve(lambda v, rv=rv, ph=ph: v.tensor_copy(rv[:, 3:515],
                                                            self.pA[:, ph:ph + 512]),
                      reads=[gph], writes=[gr])
                p.dve(lambda v, rv=rv, cc=cc, a0=a0: v.tensor_scalar(
                    a0, rv[:, 0:512], self.c_convw[:, cc * 4:cc * 4 + 1], None, OP.mult),
                    reads=[gr, g("c_convw")], writes=[ga0])
                p.dve(lambda v, rv=rv, cc=cc, a0=a0, a1=a1: v.scalar_tensor_tensor(
                    a1, rv[:, 1:513], self.c_convw[:, cc * 4 + 1:cc * 4 + 2], a0, OP.mult, OP.add),
                    reads=[gr, ga0, g("c_convw")], writes=[ga1])
                p.dve(lambda v, rv=rv, cc=cc, a0=a0, a1=a1: v.scalar_tensor_tensor(
                    a0, rv[:, 2:514], self.c_convw[:, cc * 4 + 2:cc * 4 + 3], a1, OP.mult, OP.add),
                    reads=[gr, ga1, g("c_convw")], writes=[ga0])
                p.dve(lambda v, rv=rv, cc=cc, a0=a0, a1=a1: v.scalar_tensor_tensor(
                    a1, rv[:, 3:515], self.c_convw[:, cc * 4 + 3:cc * 4 + 4], a0, OP.mult, OP.add),
                    reads=[gr, ga0, g("c_convw")], writes=[ga1])
                p.dve(lambda v, rv=rv: v.tensor_copy(rv[:, 0:3], rv[:, 512:515]),
                      reads=[gr], writes=[gr])
                if cc < 4:
                    qf, gqf = self.AFA.view(1024, 1536)
                    p.act(lambda s, qf=qf, a1=a1: s.activation(qf, a1, AF.Silu),
                          reads=[ga1], writes=[gqf])
                    sq, gsq = self.ABF.view(24704, 25216)
                    p.act(lambda s, sq=sq, qf=qf: s.activation(sq, qf, AF.Square),
                          reads=[gqf], writes=[gsq])
                    p.pe(lambda t, sq=sq: t.matmul(self.pC[0:1, 0:512], self.c_onebc[:, :],
                                                   sq, start=True, stop=True),
                         reads=[gsq, g("c_onebc")], writes=[g("pC0")])
                    R, gR = self.s_rows, g("s_rows")
                    p.dve(lambda v: v.tensor_scalar_add(R[0:1, 0:512], self.pC[0:1, 0:512], 1e-6),
                          reads=[g("pC0")], writes=[gR])
                    p.act(lambda s: s.activation(R[0:1, 0:512], R[0:1, 0:512], AF.Sqrt),
                          reads=[gR], writes=[gR])
                    p.dve(lambda v: v.reciprocal(R[0:1, 0:512], R[0:1, 0:512]),
                          reads=[gR], writes=[gR])
                    p.dve(lambda v: v.tensor_copy(self.s_rowb[0:1, 0:512], R[0:1, 0:512]),
                          reads=[gR], writes=[g("s_rowb")])
                    p.pe(lambda t: t.matmul(self.pB[:, 0:512], self.c_onesb[:, :],
                                            self.s_rowb[0:1, 0:512], start=True, stop=True),
                         reads=[g("s_rowb"), g("c_onesb")], writes=[g("pB0")])
                    qn, gqn = self.ABF.view(25216, 25728)
                    p.dve(lambda v, qn=qn, qf=qf: v.tensor_tensor(qn, qf, self.pB[:, 0:512],
                                                                  OP.mult),
                          reads=[gqf, g("pB0")], writes=[gqn])
                    dst = self.qn_sc.ap()[cc * 128:(cc + 1) * 128, t0g:t0g + 512]
                    p.hdma(lambda s, dst=dst, qn=qn: s.dma_start(out=dst, in_=qn),
                           reads=[gqn], writes=[g("d_qn")])
                elif cc < 8:
                    kb_, gkb = self.ABF.view(25728, 26240)
                    p.act(lambda s, kb_=kb_, a1=a1: s.activation(kb_, a1, AF.Silu),
                          reads=[ga1], writes=[gkb])
                    dst = self.kn_sc.ap()[(cc - 4) * 128:(cc - 3) * 128, t0g:t0g + 512]
                    p.hdma(lambda s, dst=dst, kb_=kb_: s.dma_start(out=dst, in_=kb_),
                           reads=[gkb], writes=[g("d_kn")])
                else:
                    vv, gvv = self.ABF.view(26240, 26752)
                    p.act(lambda s, vv=vv, a1=a1: s.activation(vv, a1, AF.Silu),
                          reads=[ga1], writes=[gvv])
                    dst = self.v_sc.ap()[(cc - 8) * 128:(cc - 7) * 128, t0g:t0g + 512]
                    p.hdma(lambda s, dst=dst, vv=vv: s.dma_start(out=dst, in_=vv),
                           reads=[gvv], writes=[g("d_v")])
            for dc in range(16):
                p.pe(lambda t, dc=dc, vxn=vxn: t.matmul(
                    self.pC[0:8, 0:512], self.s_wba[:, dc * 8:(dc + 1) * 8],
                    vxn[:, dc * 512:(dc + 1) * 512], start=(dc == 0), stop=(dc == 15)),
                    reads=[g("s_wba"), gxn], writes=[g("pC0")])
            for dc in range(16):
                p.pe(lambda t, dc=dc, vxn=vxn: t.matmul(
                    self.pC[0:8, 512:1024], self.s_wba[:, 128 + dc * 8:128 + (dc + 1) * 8],
                    vxn[:, dc * 512:(dc + 1) * 512], start=(dc == 0), stop=(dc == 15)),
                    reads=[g("s_wba"), gxn], writes=[g("pC1")])
            bt, gbt = self.AFA.view(1536, 2048)
            ga_, gga = self.AFA.view(2048, 2560)
            tx, gtx = self.AFA.view(2560, 3072)
            ty, gty = self.AFA.view(3072, 3584)
            p.act(lambda s, bt=bt: s.activation(bt[0:8, :], self.pC[0:8, 0:512], AF.Sigmoid),
                  reads=[g("pC0")], writes=[gbt])
            p.dve(lambda v, ga_=ga_: v.tensor_scalar(ga_[0:8, :], self.pC[0:8, 512:1024],
                                                     self.c_dtb[:, :], None, OP.add),
                  reads=[g("pC1"), g("c_dtb")], writes=[gga])
            p.act(lambda s, ga_=ga_, tx=tx: s.activation(tx[0:8, :], ga_[0:8, :], AF.Abs),
                  reads=[gga], writes=[gtx])
            p.act(lambda s, tx=tx: s.activation(tx[0:8, :], tx[0:8, :], AF.Exp, scale=-1.0),
                  reads=[gtx], writes=[gtx])
            p.act(lambda s, tx=tx: s.activation(tx[0:8, :], tx[0:8, :], AF.Ln,
                                                bias=self.c_bias[0:8, 1:2]),
                  reads=[gtx, g("c_bias")], writes=[gtx])
            p.dve(lambda v, ga_=ga_, ty=ty: v.tensor_scalar_max(ty[0:8, :], ga_[0:8, :], 0.0),
                  reads=[gga], writes=[gty])
            p.dve(lambda v, tx=tx, ty=ty: v.tensor_tensor(ty[0:8, :], ty[0:8, :], tx[0:8, :],
                                                          OP.add),
                  reads=[gtx], writes=[gty])
            p.dve(lambda v, ty=ty: v.tensor_scalar(ty[0:8, :], ty[0:8, :], self.c_nalog[:, :],
                                                   None, OP.mult),
                  reads=[g("c_nalog")], writes=[gty])
            gc_, ggc = self.AFA.view(3584, 4096)
            for sub in range(4):
                p.dve(lambda v, sub=sub, ty=ty, gc_=gc_: v.tensor_tensor_scan(
                    gc_[0:8, sub * 128:(sub + 1) * 128], ty[0:8, sub * 128:(sub + 1) * 128],
                    ty[0:8, sub * 128:(sub + 1) * 128], self.c_zero8[:, :], OP.add, OP.bypass),
                    reads=[gty, g("c_zero8")], writes=[ggc])
            for sub in range(4):
                p.pe(lambda t, sub=sub, gc_=gc_: t.transpose(
                    self.pA[:, sub * 8:sub * 8 + 8], gc_[0:8, sub * 128:(sub + 1) * 128],
                    self.c_idf[0:8, 0:8]),
                    reads=[ggc, g("c_idf")], writes=[g("pA0")])
                p.pe(lambda t, sub=sub, bt=bt: t.transpose(
                    self.pA[:, 32 + sub * 8:32 + sub * 8 + 8], bt[0:8, sub * 128:(sub + 1) * 128],
                    self.c_idf[0:8, 0:8]),
                    reads=[gbt, g("c_idf")], writes=[g("pA0")])
            p.dve(lambda v, tt=tt: v.tensor_copy(self.gcumT[:, tt * 32:tt * 32 + 32],
                                                 self.pA[:, 0:32]),
                  reads=[g("pA0")], writes=[g("gcumT")])
            p.dve(lambda v, tt=tt: v.tensor_copy(self.betaT[:, tt * 32:tt * 32 + 32],
                                                 self.pA[:, 32:64]),
                  reads=[g("pA0")], writes=[g("betaT")])
        wzv, gwz = self.AW.view(0, 16384)
        for dc in range(16):
            src = self.wz[dc * 128:(dc + 1) * 128, :]
            p.pdma(lambda gg, src=src, dc=dc, wzv=wzv: gg.dma_start(
                out=wzv[:, dc * 1024:(dc + 1) * 1024], in_=src), writes=[gwz])
        for tt in range(B3T):
            par = tt % 2
            t0g = tt * 512
            r, col = t0g // Q, t0g % Q
            vxn, gxn = self.ABF.view(par * 8192, par * 8192 + 8192)
            src = self.xn_bat.ap()[r * 2048:(r + 1) * 2048, col:col + 512].rearrange(
                "(c p) t -> p c t", p=128)
            p.hdma(lambda s, src=src, vxn=vxn: s.dma_start(
                out=vxn.rearrange("p (c t) -> p c t", t=512), in_=src), writes=[gxn])
            for zc in range(8):
                for dc in range(16):
                    p.pe(lambda t, zc=zc, dc=dc, vxn=vxn, wzv=wzv: t.matmul(
                        self.pA[:, (zc % 2) * 512:(zc % 2) * 512 + 512],
                        wzv[:, dc * 1024 + zc * 128: dc * 1024 + (zc + 1) * 128],
                        vxn[:, dc * 512:(dc + 1) * 512], start=(dc == 0), stop=(dc == 15)),
                        reads=[gwz, gxn], writes=[g("pA0") if zc % 2 == 0 else g("pA1")])
                zst, gzst = self.ABF.view(16384 + (zc % 2) * 512, 16384 + (zc % 2) * 512 + 512)
                p.act(lambda s, zst=zst, zc=zc: s.activation(
                    zst, self.pA[:, (zc % 2) * 512:(zc % 2) * 512 + 512], AF.Copy),
                    reads=[g("pA0") if zc % 2 == 0 else g("pA1")], writes=[gzst])
                dst = self.z_sc.ap()[zc * 128:(zc + 1) * 128, t0g:t0g + 512]
                p.hdma(lambda s, dst=dst, zst=zst: s.dma_start(out=dst, in_=zst),
                       reads=[gzst], writes=[g("d_z")])
        if self.dev_dbg:
            for nm, t_ in [("dbg_q", self.qn_sc), ("dbg_k", self.kn_sc),
                           ("dbg_v", self.v_sc), ("dbg_z", self.z_sc)]:
                p.pdma(lambda gg, nm=nm, t_=t_: gg.dma_start(
                    out=self.dbg[nm][:, :], in_=t_.ap()[:, :]),
                    reads=[g("d_qn"), g("d_kn"), g("d_v"), g("d_z")], writes=[g("dbg")])
            p.pdma(lambda gg: gg.dma_start(out=self.dbg["dbg_bt"][:, :], in_=self.betaT[:, :]),
                   reads=[g("betaT")], writes=[g("dbg")])
            p.pdma(lambda gg: gg.dma_start(out=self.dbg["dbg_gc"][:, :], in_=self.gcumT[:, :]),
                   reads=[g("gcumT")], writes=[g("dbg")])

    def b4_scan(self):
        p, g = self.p, self.g
        T, Q, J = self.T, self.Q, self.J
        SC = self.s_cols
        gSC = g("s_cols")

        def emit_pre(j):
            par = j % 2
            AB = self.ABF if par == 0 else self.AW
            jc = j * 8

            def ab(a, b, AB=AB):
                return AB.view(a, b)

            def afv(a, b, par=par):
                return self.AFA.view(a + par * 4096, b + par * 4096)

            ctx = {"j": j, "jc": jc, "par": par, "ab": ab, "afv": afv}
            qv, gq = ab(0, 512)
            src = self.qn_sc.ap()[:, j * 128:(j + 1) * 128].rearrange(
                "(h p) t -> p h t", p=128)
            p.hdma(lambda s, src=src, qv=qv: s.dma_start(
                out=qv.rearrange("p (h t) -> p h t", t=128), in_=src),
                writes=[gq], extra=[g("d_qn").w])
            kfm_r, gkfr = ab(1024, 1536)
            src = self.kn_sc.ap()[:, j * 128:(j + 1) * 128].rearrange(
                "(h p) t -> p h t", p=128)
            p.hdma(lambda s, src=src, kfm_r=kfm_r: s.dma_start(
                out=kfm_r.rearrange("p (h t) -> p h t", t=128), in_=src),
                writes=[gkfr], extra=[g("d_kn").w])
            vfm_r, gvfr = ab(2048, 3072)
            src = self.v_sc.ap()[:, j * 128:(j + 1) * 128].rearrange(
                "(u p) t -> p u t", p=128)
            p.hdma(lambda s, src=src, vfm_r=vfm_r: s.dma_start(
                out=vfm_r.rearrange("p (u t) -> p u t", t=128), in_=src),
                writes=[gvfr], extra=[g("d_v").w])
            zfm_r, gzfr = ab(4096, 5120)
            src = self.z_sc.ap()[:, j * 128:(j + 1) * 128].rearrange(
                "(u p) t -> p u t", p=128)
            p.adma(lambda s, src=src, zfm_r=zfm_r: s.dma_start(
                out=zfm_r.rearrange("p (u t) -> p u t", t=128), in_=src),
                writes=[gzfr], extra=[g("d_z").w])
            ktm, gkt = ab(1536, 2048)
            for h in range(4):
                p.pe(lambda t, h=h, kfm_r=kfm_r: t.transpose(
                    self.pT0[:, h * 128:(h + 1) * 128], kfm_r[:, h * 128:(h + 1) * 128],
                    self.c_idb[:, :]),
                    reads=[gkfr, g("c_idb")], writes=[g("pT0")])
            p.dve(lambda v, ktm=ktm: v.tensor_copy(ktm, self.pT0[:, 0:512]),
                  reads=[g("pT0")], writes=[gkt])
            vtm, gvt = ab(3072, 4096)
            for u in range(8):
                p.pe(lambda t, u=u, vfm_r=vfm_r: t.transpose(
                    self.pT1[:, u * 128:(u + 1) * 128], vfm_r[:, u * 128:(u + 1) * 128],
                    self.c_idb[:, :]),
                    reads=[gvfr, g("c_idb")], writes=[g("pT1")])
            p.dve(lambda v, vtm=vtm: v.tensor_copy(vtm, self.pT1[:, :]),
                  reads=[g("pT1")], writes=[gvt])
            ztm, gzt = ab(5120, 6144)
            for u in range(8):
                p.pe(lambda t, u=u, zfm_r=zfm_r: t.transpose(
                    self.pT0[:, u * 128:(u + 1) * 128], zfm_r[:, u * 128:(u + 1) * 128],
                    self.c_idb[:, :]),
                    reads=[gzfr, g("c_idb")], writes=[g("pT0")])
            p.dve(lambda v, ztm=ztm: v.tensor_copy(ztm, self.pT0[:, :]),
                  reads=[g("pT0")], writes=[gzt])
            ctx["qv"], ctx["gq"] = qv, gq
            ctx["ktm"], ctx["gkt"] = ktm, gkt
            ctx["ztm"], ctx["gzt"] = ztm, gzt
            k2, gk2 = afv(3072, 3584)
            p.dve(lambda v, ktm=ktm, k2=k2: v.tensor_tensor(k2, ktm, ktm, OP.mult),
                  reads=[gkt], writes=[gk2])
            p.dve(lambda v, k2=k2: v.reduce_sum(
                SC[:, 0:4], k2.rearrange("p (h a) -> p h a", a=128), axis=AX.X),
                reads=[gk2], writes=[gSC])
            p.dve(lambda v: v.tensor_scalar_add(SC[:, 4:8], SC[:, 0:4], 1e-6),
                  reads=[gSC], writes=[gSC])
            p.act(lambda s: s.activation(SC[:, 4:8], SC[:, 4:8], AF.Sqrt),
                  reads=[gSC], writes=[gSC])
            p.dve(lambda v: v.reciprocal(SC[:, 4:8], SC[:, 4:8]),
                  reads=[gSC], writes=[gSC])
            p.dve(lambda v: v.tensor_copy(
                SC[:, 8:16].rearrange("p (u o) -> p u o", o=2)[:, :, 0:1],
                SC[:, 4:8].rearrange("p (u o) -> p u o", o=1)),
                reads=[gSC], writes=[gSC])
            p.dve(lambda v: v.tensor_copy(
                SC[:, 8:16].rearrange("p (u o) -> p u o", o=2)[:, :, 1:2],
                SC[:, 4:8].rearrange("p (u o) -> p u o", o=1)),
                reads=[gSC], writes=[gSC])
            p.act(lambda s, jc=jc: s.activation(SC[:, 16:24], self.gcumT[:, jc:jc + 8], AF.Exp),
                  reads=[g("gcumT")], writes=[gSC])
            p.dve(lambda v, jc=jc: v.tensor_tensor(SC[:, 48:56], self.betaT[:, jc:jc + 8],
                                                   SC[:, 8:16], OP.mult),
                  reads=[g("betaT"), gSC], writes=[gSC])
            p.dve(lambda v: v.tensor_tensor(SC[:, 56:64], SC[:, 48:56], SC[:, 16:24], OP.mult),
                  reads=[gSC], writes=[gSC])
            p.dve(lambda v, jc=jc: v.tensor_scalar_mul(SC[:, 72:80], self.gcumT[:, jc:jc + 8],
                                                       -1.0),
                  reads=[g("gcumT")], writes=[gSC])
            kbtm, gkb = ab(6144, 7168)
            kbgtm, gkbg = ab(7168, 8192)
            vbtm, gvb = ab(9216, 10240)
            kntm, gknt = ab(10240, 10752)
            for u in range(8):
                h = u // 2
                p.dve(lambda v, u=u, h=h, ktm=ktm, kbtm=kbtm: v.tensor_scalar(
                    kbtm[:, u * 128:(u + 1) * 128], ktm[:, h * 128:(h + 1) * 128],
                    SC[:, 48 + u:49 + u], None, OP.mult),
                    reads=[gkt, gSC], writes=[gkb])
                p.dve(lambda v, u=u, h=h, ktm=ktm, kbgtm=kbgtm: v.tensor_scalar(
                    kbgtm[:, u * 128:(u + 1) * 128], ktm[:, h * 128:(h + 1) * 128],
                    SC[:, 56 + u:57 + u], None, OP.mult),
                    reads=[gkt, gSC], writes=[gkbg])
                p.dve(lambda v, u=u, jc=jc, vtm=vtm, vbtm=vbtm: v.tensor_scalar(
                    vbtm[:, u * 128:(u + 1) * 128], vtm[:, u * 128:(u + 1) * 128],
                    self.betaT[:, jc + u:jc + u + 1], None, OP.mult),
                    reads=[gvt, g("betaT")], writes=[gvb])
            for h in range(4):
                p.dve(lambda v, h=h, ktm=ktm, kntm=kntm: v.tensor_scalar(
                    kntm[:, h * 128:(h + 1) * 128], ktm[:, h * 128:(h + 1) * 128],
                    SC[:, 4 + h:5 + h], None, OP.mult),
                    reads=[gkt, gSC], writes=[gknt])
            kbfm, gkbf = ab(10752, 11776)
            for u in range(8):
                p.pe(lambda t, u=u, kbtm=kbtm: t.transpose(
                    self.pT0[:, u * 128:(u + 1) * 128], kbtm[:, u * 128:(u + 1) * 128],
                    self.c_idb[:, :]),
                    reads=[gkb, g("c_idb")], writes=[g("pT0")])
            p.dve(lambda v, kbfm=kbfm: v.tensor_copy(kbfm, self.pT0[:, :]),
                  reads=[g("pT0")], writes=[gkbf])
            knfm, gknf = ab(11776, 12288)
            for h in range(4):
                p.pe(lambda t, h=h, kntm=kntm: t.transpose(
                    self.pT1[:, h * 128:(h + 1) * 128], kntm[:, h * 128:(h + 1) * 128],
                    self.c_idb[:, :]),
                    reads=[gknt, g("c_idb")], writes=[g("pT1")])
            p.dve(lambda v, knfm=knfm: v.tensor_copy(knfm, self.pT1[:, 0:512]),
                  reads=[g("pT1")], writes=[gknf])
            ctx["vbtm"], ctx["gvb"] = vbtm, gvb
            ctx["kbgtm"], ctx["gkbg"] = kbgtm, gkbg
            epre, gep = afv(0, 1024)
            ebc, gebc = afv(3584, 3712)
            for u in range(8):
                p.dve(lambda v, u=u, ebc=ebc: v.tensor_scalar(
                    ebc, self.c_ones128[:, :], SC[:, 72 + u:73 + u], None, OP.mult),
                    reads=[g("c_ones128"), gSC], writes=[gebc])
                p.pe(lambda t, u=u, ebc=ebc: t.transpose(
                    self.pB[:, u * 128:(u + 1) * 128], ebc, self.c_idf[:, :]),
                    reads=[gebc, g("c_idf")], writes=[g("pB0") if u < 4 else g("pB1")])
                p.dve(lambda v, u=u, jc=jc, epre=epre: v.tensor_scalar(
                    epre[:, u * 128:(u + 1) * 128], self.pB[:, u * 128:(u + 1) * 128],
                    self.gcumT[:, jc + u:jc + u + 1], 0.0, OP.add, OP.min),
                    reads=[g("pB0") if u < 4 else g("pB1"), g("gcumT")], writes=[gep])
            eexp, gee = ab(12288, 13312)
            p.act(lambda s, epre=epre, eexp=eexp: s.activation(eexp, epre, AF.Exp),
                  reads=[gep], writes=[gee])
            eneg, gen = ab(13312, 14336)
            p.dve(lambda v, eexp=eexp, eneg=eneg: v.tensor_tensor(
                eneg, eexp, self.c_sneg[:, :], OP.mult),
                reads=[gee, g("c_sneg")], writes=[gen])
            for u in range(8):
                p.pe(lambda t, u=u, kbfm=kbfm, knfm=knfm: t.matmul(
                    self.pA[:, u * 128:(u + 1) * 128], kbfm[:, u * 128:(u + 1) * 128],
                    knfm[:, (u // 2) * 128:(u // 2 + 1) * 128], start=True, stop=True),
                    reads=[gkbf, gknf], writes=[g("pA0") if u < 4 else g("pA1")])
            sl, gsl = ab(15360, 16384)
            p.dve(lambda v, sl=sl, eneg=eneg: v.tensor_tensor(sl, self.pA[:, :], eneg, OP.mult),
                  reads=[g("pA0"), g("pA1"), gen], writes=[gsl])
            st_, gst = ab(16384, 17408)
            for u in range(8):
                p.pe(lambda t, u=u, sl=sl: t.transpose(
                    self.pT0[:, u * 128:(u + 1) * 128], sl[:, u * 128:(u + 1) * 128],
                    self.c_idb[:, :]),
                    reads=[gsl, g("c_idb")], writes=[g("pT0")])
            p.dve(lambda v, st_=st_: v.tensor_copy(st_, self.pT0[:, :]),
                  reads=[g("pT0")], writes=[gst])
            eit, geit = ab(14336, 15360)
            for u in range(8):
                p.pe(lambda t, u=u, eneg=eneg: t.transpose(
                    self.pT1[:, u * 128:(u + 1) * 128], eneg[:, u * 128:(u + 1) * 128],
                    self.c_idb[:, :]),
                    reads=[gen, g("c_idb")], writes=[g("pT1")])
            p.dve(lambda v, eit=eit: v.tensor_tensor(eit, self.c_blkI[:, :], self.pT1[:, :],
                                                     OP.subtract),
                  reads=[g("c_blkI"), g("pT1")], writes=[geit])
            for u in range(8):
                p.pe(lambda t, u=u, knfm=knfm, qv=qv: t.matmul(
                    self.pB[:, u * 128:(u + 1) * 128], knfm[:, (u // 2) * 128:(u // 2 + 1) * 128],
                    qv[:, (u // 2) * 128:(u // 2 + 1) * 128], start=True, stop=True),
                    reads=[gknf, gq], writes=[g("pB0") if u < 4 else g("pB1")])
            intq, gint = ab(17408, 18432)
            p.dve(lambda v, intq=intq, eit=eit: v.scalar_tensor_tensor(
                intq, self.pB[:, :], RS128, eit, OP.mult, OP.mult),
                reads=[g("pB0"), g("pB1"), geit], writes=[gint])
            ctx["intq"], ctx["gint"] = intq, gint
            Pb = [ab(22528, 23552), ab(23552, 24576)]
            p.dve(lambda v, P0=Pb[0][0], st_=st_: v.tensor_tensor(
                P0, self.c_blkI[:, :], st_, OP.add),
                  reads=[g("c_blkI"), gst], writes=[Pb[0][1]])
            cur = 0
            spl, gspl = sl, gsl
            spt, gspt = st_, gst
            lv = [(ab(18432, 19456), ab(19456, 20480)), (ab(20480, 21504), ab(21504, 22528))]
            for lev in range(1, 7):
                for u in range(8):
                    p.pe(lambda t, u=u, spl=spl, spt=spt: t.matmul(
                        self.pA[:, u * 128:(u + 1) * 128], spt[:, u * 128:(u + 1) * 128],
                        spl[:, u * 128:(u + 1) * 128], start=True, stop=True),
                        reads=[gspl, gspt], writes=[g("pA0") if u < 4 else g("pA1")])
                (nl, gnl), (nt, gnt) = lv[lev % 2]
                p.act(lambda s, nl=nl: s.activation(nl, self.pA[:, :], AF.Copy),
                      reads=[g("pA0"), g("pA1")], writes=[gnl])
                if lev < 6:
                    for u in range(8):
                        p.pe(lambda t, u=u, spl=spl, spt=spt: t.matmul(
                            self.pC[:, u * 128:(u + 1) * 128], spl[:, u * 128:(u + 1) * 128],
                            spt[:, u * 128:(u + 1) * 128], start=True, stop=True),
                            reads=[gspl, gspt], writes=[g("pC0") if u < 4 else g("pC1")])
                    p.act(lambda s, nt=nt: s.activation(nt, self.pC[:, :], AF.Copy),
                          reads=[g("pC0"), g("pC1")], writes=[gnt])
                for u in range(8):
                    p.pe(lambda t, u=u, nl=nl, Pc=Pb[cur][0]: t.matmul(
                        self.pB[:, u * 128:(u + 1) * 128], nl[:, u * 128:(u + 1) * 128],
                        Pc[:, u * 128:(u + 1) * 128], start=True, stop=True),
                        reads=[gnl, Pb[cur][1]], writes=[g("pB0") if u < 4 else g("pB1")])
                p.dve(lambda v, Pn=Pb[1 - cur][0], Pc=Pb[cur][0]: v.tensor_tensor(
                    Pn[:, :], self.pB[:, :], Pc[:, :], OP.add),
                    reads=[g("pB0"), g("pB1"), Pb[cur][1]], writes=[Pb[1 - cur][1]])
                cur = 1 - cur
                spl, gspl = nl, gnl
                spt, gspt = nt, gnt
            Pf, gPf = Pb[cur]
            ctx["Pf"], ctx["gPf"] = Pf, gPf
            co2 = 96 + par * 48
            ctx["co2"] = co2
            p.act(lambda s, jc=jc, co2=co2: s.activation(
                SC[:, co2:co2 + 8], self.gcumT[:, jc:jc + 8], AF.Exp, bias=self.c_bias[:, 2:3]),
                  reads=[g("gcumT"), g("c_bias")], writes=[gSC])
            p.pe(lambda t, jc=jc: t.matmul(self.pC[0:1, 896:904], self.c_idf[:, 127:128],
                                           self.gcumT[:, jc:jc + 8], start=True, stop=True),
                 reads=[g("gcumT"), g("c_idf")], writes=[g("pC1")])
            p.dve(lambda v: v.tensor_copy(self.s_rows[0:1, 16:24], self.pC[0:1, 896:904]),
                  reads=[g("pC1")], writes=[g("s_rows")])
            p.pe(lambda t: t.matmul(self.pC[:, 904:912], self.c_onesf[:, :],
                                    self.s_rows[0:1, 16:24], start=True, stop=True),
                 reads=[g("s_rows"), g("c_onesf")], writes=[g("pC1")])
            p.act(lambda s, co2=co2: s.activation(SC[:, co2 + 8:co2 + 16], self.pC[:, 904:912],
                                                  AF.Exp),
                  reads=[g("pC1")], writes=[gSC])
            p.dve(lambda v, jc=jc, co2=co2: v.tensor_tensor(
                SC[:, co2 + 16:co2 + 24], self.pC[:, 904:912],
                self.gcumT[:, jc:jc + 8], OP.subtract),
                  reads=[g("pC1"), g("gcumT")], writes=[gSC])
            p.act(lambda s, co2=co2: s.activation(SC[:, co2 + 16:co2 + 24],
                                                  SC[:, co2 + 16:co2 + 24], AF.Exp),
                  reads=[gSC], writes=[gSC])
            p.dve(lambda v, co2=co2: v.tensor_tensor(SC[:, co2 + 24:co2 + 32],
                                                     SC[:, co2 + 16:co2 + 24],
                                                     SC[:, 8:16], OP.mult),
                  reads=[gSC], writes=[gSC])
            kdtm, gkd = ab(8192, 9216)
            for u in range(8):
                h = u // 2
                p.dve(lambda v, u=u, h=h, ktm=ktm, kdtm=kdtm, co2=co2: v.tensor_scalar(
                    kdtm[:, u * 128:(u + 1) * 128], ktm[:, h * 128:(h + 1) * 128],
                    SC[:, co2 + 24 + u:co2 + 25 + u], None, OP.mult),
                    reads=[gkt, gSC], writes=[gkd])
            ctx["kdtm"], ctx["gkd"] = kdtm, gkd
            nkcd, gnk = ab(24576, 25600)
            for u in range(8):
                p.pe(lambda t, u=u, kbgtm=kbgtm, Pf=Pf: t.matmul(
                    self.pB[:, u * 128:(u + 1) * 128], kbgtm[:, u * 128:(u + 1) * 128],
                    Pf[:, u * 128:(u + 1) * 128], start=True, stop=True),
                    reads=[gkbg, gPf], writes=[g("pB0") if u < 4 else g("pB1")])
            p.dve(lambda v, nkcd=nkcd: v.tensor_scalar_mul(nkcd, self.pB[:, :], -1.0),
                  reads=[g("pB0"), g("pB1")], writes=[gnk])
            ctx["nkcd"], ctx["gnk"] = nkcd, gnk
            return ctx

        def emit_post(ctx):
            j, jc, par, ab, afv = ctx["j"], ctx["jc"], ctx["par"], ctx["ab"], ctx["afv"]
            qv, gq = ctx["qv"], ctx["gq"]
            ktm, gkt = ctx["ktm"], ctx["gkt"]
            ztm, gzt = ctx["ztm"], ctx["gzt"]
            vbtm, gvb = ctx["vbtm"], ctx["gvb"]
            Pf, gPf = ctx["Pf"], ctx["gPf"]
            nkcd, gnk = ctx["nkcd"], ctx["gnk"]
            intq, gint = ctx["intq"], ctx["gint"]
            co2 = ctx["co2"]
            kdtm, gkd = ctx["kdtm"], ctx["gkd"]
            for u in range(8):
                p.pe(lambda t, u=u, Pf=Pf, vbtm=vbtm: t.matmul(
                    self.pA[:, u * 128:(u + 1) * 128], Pf[:, u * 128:(u + 1) * 128],
                    vbtm[:, u * 128:(u + 1) * 128], start=True, stop=False),
                    reads=[gPf, gvb], writes=[g("pA0") if u < 4 else g("pA1")])
                p.pe(lambda t, u=u, nkcd=nkcd: t.matmul(
                    self.pA[:, u * 128:(u + 1) * 128], nkcd[:, u * 128:(u + 1) * 128],
                    self.s_stb[:, u * 128:(u + 1) * 128], start=False, stop=True),
                    reads=[gnk, g("s_stb")], writes=[g("pA0") if u < 4 else g("pA1")])
            vnew, gvn = ab(25600, 26624)
            p.dve(lambda v, vnew=vnew: v.tensor_copy(vnew, self.pA[:, :]),
                  reads=[g("pA0"), g("pA1")], writes=[gvn])
            interf, gitf = afv(1024, 2048)
            for u in range(8):
                p.pe(lambda t, u=u, qv=qv: t.matmul(
                    self.pB[:, u * 128:(u + 1) * 128], qv[:, (u // 2) * 128:(u // 2 + 1) * 128],
                    self.s_stb[:, u * 128:(u + 1) * 128], start=True, stop=True),
                    reads=[gq, g("s_stb")], writes=[g("pB0") if u < 4 else g("pB1")])
            for u in range(8):
                p.act(lambda s, u=u, interf=interf, co2=co2: s.activation(
                    interf[:, u * 128:(u + 1) * 128], self.pB[:, u * 128:(u + 1) * 128],
                    AF.Copy, scale=SC[:, co2 + u:co2 + u + 1]),
                    reads=[g("pB0") if u < 4 else g("pB1"), gSC], writes=[gitf])
            for u in range(8):
                p.pe(lambda t, u=u, intq=intq, vnew=vnew: t.matmul(
                    self.pC[:, u * 128:(u + 1) * 128], intq[:, u * 128:(u + 1) * 128],
                    vnew[:, u * 128:(u + 1) * 128], start=True, stop=True),
                    reads=[gint, gvn], writes=[g("pC0") if u < 4 else g("pC1")])
            outc, goc = afv(2048, 3072)
            p.dve(lambda v, outc=outc, interf=interf: v.tensor_tensor(
                outc, self.pC[:, :], interf, OP.add),
                reads=[g("pC0"), g("pC1"), gitf], writes=[goc])
            for u in range(8):
                p.pe(lambda t, u=u, kdtm=kdtm, vnew=vnew: t.matmul(
                    self.pA[:, u * 128:(u + 1) * 128], kdtm[:, u * 128:(u + 1) * 128],
                    vnew[:, u * 128:(u + 1) * 128], start=True, stop=True),
                    reads=[gkd, gvn], writes=[g("pA0") if u < 4 else g("pA1")])
            for u in range(8):
                p.dve(lambda v, u=u, co2=co2: v.tensor_scalar(
                    self.s_st[:, u * 128:(u + 1) * 128], self.s_st[:, u * 128:(u + 1) * 128],
                    SC[:, co2 + 8 + u:co2 + 9 + u], None, OP.mult),
                    reads=[gSC], writes=[g("s_st")])
            p.dve(lambda v: v.tensor_tensor(self.s_st[:, :], self.s_st[:, :], self.pA[:, :],
                                            OP.add),
                  reads=[g("pA0"), g("pA1")], writes=[g("s_st")])
            p.dve(lambda v: v.tensor_copy(self.s_stb[:, :], self.s_st[:, :]),
                  reads=[g("s_st")], writes=[g("s_stb")])
            zsil, gzs = ab(26624, 27648)
            p.act(lambda s, ztm=ztm, zsil=zsil: s.activation(zsil, ztm, AF.Silu),
                  reads=[gzt], writes=[gzs])
            xg, gxg = afv(1024, 2048)
            p.dve(lambda v, outc=outc, zsil=zsil, xg=xg: v.tensor_tensor(
                xg, outc, zsil, OP.mult),
                reads=[goc, gzs], writes=[gxg])
            sqb, gsq = ab(27648, 28672)
            p.act(lambda s, xg=xg, sqb=sqb: s.activation(sqb, xg, AF.Square),
                  reads=[gxg], writes=[gsq])
            p.dve(lambda v, sqb=sqb: v.reduce_sum(
                SC[:, 80:88], sqb.rearrange("p (u l) -> p u l", l=128), axis=AX.X),
                reads=[gsq], writes=[gSC])
            p.dve(lambda v: v.tensor_scalar(SC[:, 88:96], SC[:, 80:88], 1.0 / 128.0, 1e-6,
                                            OP.mult, OP.add),
                  reads=[gSC], writes=[gSC])
            p.act(lambda s: s.activation(SC[:, 88:96], SC[:, 88:96], AF.Sqrt),
                  reads=[gSC], writes=[gSC])
            p.dve(lambda v: v.reciprocal(SC[:, 88:96], SC[:, 88:96]),
                  reads=[gSC], writes=[gSC])
            ntm, gnt2 = ab(28672, 29696)
            for u in range(8):
                p.dve(lambda v, u=u, xg=xg, ntm=ntm: v.tensor_scalar(
                    ntm[:, u * 128:(u + 1) * 128], xg[:, u * 128:(u + 1) * 128],
                    SC[:, 88 + u:89 + u], None, OP.mult),
                    reads=[gxg, gSC], writes=[gnt2])
            nst, gnst = ab(29696, 30720)
            for u in range(8):
                p.pe(lambda t, u=u, ntm=ntm: t.transpose(
                    self.pT0[:, u * 128:(u + 1) * 128], ntm[:, u * 128:(u + 1) * 128],
                    self.c_idb[:, :]),
                    reads=[gnt2, g("c_idb")], writes=[g("pT0")])
            p.dve(lambda v, nst=nst: v.tensor_scalar(
                nst, self.pT0[:, :], self.c_normw[:, :], None, OP.mult),
                reads=[g("pT0"), g("c_normw")], writes=[gnst])
            dst = self.nrm_sc.ap()[:, j * 128:(j + 1) * 128].rearrange(
                "(u p) l -> p u l", p=128)
            p.hdma(lambda s, dst=dst, nst=nst: s.dma_start(
                out=dst, in_=nst.rearrange("p (u l) -> p u l", l=128)),
                   reads=[gnst], writes=[g("d_nrm")])

        ctx = emit_pre(0)
        for j in range(J):
            nxt = emit_pre(j + 1) if j + 1 < J else None
            emit_post(ctx)
            ctx = nxt
        if self.dev_dbg:
            p.pdma(lambda gg: gg.dma_start(out=self.dbg["dbg_nrm"][:, :],
                                           in_=self.nrm_sc.ap()[:, :]),
                   reads=[g("d_nrm")], writes=[g("dbg")])

    def b6_wout(self):
        p, g = self.p, self.g
        T, Q = self.T, self.Q
        wo, gwo = self.AW.view(0, 16384)
        for vdc in range(8):
            src = self.wout[vdc * 128:(vdc + 1) * 128, :]
            p.pdma(lambda gg, src=src, vdc=vdc, wo=wo: gg.dma_start(
                out=wo[:, vdc * 2048:(vdc + 1) * 2048], in_=src), writes=[gwo])
        for tt in range(T // 512):
            par = tt % 2
            t0 = tt * 512
            q, lc = t0 // Q, t0 % Q
            vn, gvn = self.ABF.view(par * 4096, par * 4096 + 4096)
            src = self.nrm_sc.ap()[:, t0:t0 + 512].rearrange("(c p) t -> p c t", p=128)
            p.hdma(lambda s, src=src, vn=vn: s.dma_start(
                out=vn.rearrange("p (c t) -> p c t", t=512), in_=src),
                   writes=[gvn], extra=[g("d_nrm").w])
            yst, gyst = self.AFA.view(0, 8192)
            for dc in range(16):
                half = dc % 2
                for vdc in range(8):
                    p.pe(lambda t, dc=dc, vdc=vdc, half=half, vn=vn, wo=wo: t.matmul(
                        self.pA[:, half * 512:half * 512 + 512],
                        wo[:, vdc * 2048 + dc * 128: vdc * 2048 + (dc + 1) * 128],
                        vn[:, vdc * 512:(vdc + 1) * 512], start=(vdc == 0), stop=(vdc == 7)),
                        reads=[gwo, gvn], writes=[g("pA0") if half == 0 else g("pA1")])
                p.dve(lambda v, dc=dc, half=half, yst=yst: v.tensor_copy(
                    yst[:, dc * 512:(dc + 1) * 512], self.pA[:, half * 512:half * 512 + 512]),
                    reads=[g("pA0") if half == 0 else g("pA1")], writes=[gyst])
            dst = self.ps_in.ap()[q * 2048:(q + 1) * 2048, lc:lc + 512].rearrange(
                "(c p) t -> p c t", p=128)
            p.hdma(lambda s, dst=dst, yst=yst: s.dma_start(
                out=dst, in_=yst.rearrange("p (c t) -> p c t", t=512)),
                   reads=[gyst], writes=[g("d_ps")])
        p.cc(lambda gg: gg.collective_compute(
            "ReduceScatter", OP.add, replica_groups=[[0, 1, 2, 3], [4, 5, 6, 7]],
            ins=[self.ps_in.ap().opt()], outs=[self.s_own.ap().opt()]),
            reads=[g("d_ps")], writes=[g("d_s_own")])
        if self.dev_dbg:
            p.pdma(lambda gg: gg.dma_start(out=self.dbg["dbg_s"][:, :],
                                           in_=self.s_own.ap()[:, :]),
                   reads=[g("d_s_own")], writes=[g("dbg")])

    def b9_combine(self):
        p, g = self.p, self.g
        Q = self.Q
        for tt in range(Q // 512):
            t0 = tt * 512
            for dc in range(16):
                i = (tt * 16 + dc) % 2
                vx, gx = self.AFA.view(i * 2048, i * 2048 + 512)
                vs, gs = self.AFA.view(i * 2048 + 512, i * 2048 + 1024)
                vf, gf = self.AFA.view(i * 2048 + 1024, i * 2048 + 1536)
                vo, go = self.AFA.view(i * 2048 + 1536, i * 2048 + 2048)
                p.adma(lambda s, dc=dc, t0=t0, vx=vx: s.dma_start(
                    out=vx, in_=self.x_fm[dc * 128:(dc + 1) * 128, t0:t0 + 512]),
                    writes=[gx])
                p.adma(lambda s, dc=dc, t0=t0, vs=vs: s.dma_start(
                    out=vs, in_=self.s_own.ap()[dc * 128:(dc + 1) * 128, t0:t0 + 512]),
                    writes=[gs], extra=[g("d_s_own").w])
                p.adma(lambda s, dc=dc, t0=t0, vf=vf: s.dma_start(
                    out=vf, in_=self.ffn_own.ap()[dc * 128:(dc + 1) * 128, t0:t0 + 512]),
                    writes=[gf], extra=[g("d_ffn_own").w])
                p.act(lambda s, vf=vf: s.activation(vf, vf, AF.Silu),
                      reads=[gf], writes=[gf])
                p.dve(lambda v, vo=vo, vs=vs, vf=vf: v.tensor_tensor(vo, vs, vf, OP.mult),
                      reads=[gs, gf], writes=[go])
                p.dve(lambda v, vo=vo, vx=vx: v.tensor_tensor(vo, vo, vx, OP.add),
                      reads=[gx], writes=[go])
                p.hdma(lambda s, dc=dc, t0=t0, vo=vo: s.dma_start(
                    out=self.out_fm[dc * 128:(dc + 1) * 128, t0:t0 + 512], in_=vo),
                    reads=[go], writes=[g("d_out")])

    def build(self):
        self.declare()
        with ExitStack() as es:
            self.alloc(es)
            self.b0_consts()
            self.b1_ln_router()
            self.b2_collectives()
            self.b7_ffn()
            self.b3_proj()
            self.b4_scan()
            self.b6_wout()
            self.b9_combine()
            with ExitStack() as es2:
                sems = {k: es2.enter_context(self.nc.semaphore("sem_" + k))
                        for k in Prog.SEMS}
                blk = es2.enter_context(self.nc.Block("mk"))
                self.p.emit(blk, sems, pe_fifo=self.pe_fifo)
                final = dict(self.p.cnt)

                def tail(eng):
                    for k, vv in final.items():
                        if vv:
                            eng.wait_ge(sems[k], vv)
                blk.sync(tail)
        return self.nc



def _host_inputs(inputs, T):
    Q = T // 4
    x = np.asarray(inputs["L0_attn_hidden"])
    Wqkv = np.asarray(inputs["L0_attn_W_qkv"])
    Wz = np.asarray(inputs["L0_attn_W_z"])
    Wb = np.asarray(inputs["L0_attn_W_b"])
    Wa = np.asarray(inputs["L0_attn_W_a"])
    cw = np.asarray(inputs["L0_attn_conv_weight"])
    dtbias = np.asarray(inputs["L0_attn_dt_bias"])
    Alog = np.asarray(inputs["L0_attn_A_log"])
    nw = np.asarray(inputs["L0_attn_norm_w"])
    Wout = np.asarray(inputs["L0_attn_W_out"])
    gw = np.asarray(inputs["L0_ffn_gate_w"])
    rb = np.asarray(inputs["L0_ffn_router_bias"])
    W1 = np.asarray(inputs["L0_ffn_W1"])
    W3 = np.asarray(inputs["L0_ffn_W3"])
    W2 = np.asarray(inputs["L0_ffn_W2"])
    sw1 = np.asarray(inputs["L0_ffn_sw1"])
    sw3 = np.asarray(inputs["L0_ffn_sw3"])
    sw2 = np.asarray(inputs["L0_ffn_sw2"])

    Xf = np.ascontiguousarray(x.reshape(2 * T, D))
    idf = np.eye(128, dtype=np.float32)
    idb = idf.astype(BFNP)
    onesb = np.ones((1, 128), np.float32).astype(BFNP)
    onebc = np.ones((128, 1), np.float32).astype(BFNP)
    cbias = np.tile(np.array([[1e-5, 1.0, -0.5 * np.log(128.0), 1e-6]], np.float32), (128, 1))
    onesf = np.ones((1, 128), np.float32)
    sneg1 = (-np.tril(np.ones((128, 128), np.float32), -1)).astype(BFNP)
    sneg = np.concatenate([sneg1] * VH, axis=1)
    blkI = np.concatenate([idb] * VH, axis=1)
    gw128 = np.ascontiguousarray(
        gw.T.reshape(16, 128, 8).transpose(1, 0, 2).reshape(128, 128)).astype(np.float32)

    maps = []
    for c in range(8):
        b = c // 4
        hq0 = (c % 4) * QH
        qcols = np.arange(hq0 * 128, (hq0 + QH) * 128)
        kcols = 2048 + qcols
        vcols = 4096 + np.arange(hq0 * 2 * 128, (hq0 + QH) * 2 * 128)
        cols = np.concatenate([qcols, kcols, vcols])
        vh = np.arange(hq0 * 2, (hq0 + QH) * 2)
        vdcols = np.arange(vh[0] * 128, (vh[-1] + 1) * 128)

        x_fm = np.ascontiguousarray(Xf[c * Q:(c + 1) * Q].T)
        wqkv_s = np.ascontiguousarray(Wqkv[:, cols])
        wz_s = np.ascontiguousarray(Wz[:, vdcols])
        wb128 = np.ascontiguousarray(
            Wb[:, vh].reshape(16, 128, VH).transpose(1, 0, 2).reshape(128, 128))
        wa128 = np.ascontiguousarray(
            Wa[:, vh].reshape(16, 128, VH).transpose(1, 0, 2).reshape(128, 128))
        convw_s = np.ascontiguousarray(
            cw[cols].reshape(16, 128, 4).transpose(1, 0, 2).reshape(128, 64))
        m = dict(
            x_fm=x_fm, wqkv=wqkv_s, wz=wz_s, wb128=wb128, wa128=wa128,
            convw=convw_s,
            alog=np.ascontiguousarray(Alog[vh].reshape(VH, 1)),
            dtb=np.ascontiguousarray(dtbias[vh].reshape(VH, 1)),
            normw=np.ascontiguousarray(nw.reshape(128, 1)),
            wout=np.ascontiguousarray(Wout[vdcols, :]),
            gw128=gw128, rbias=np.ascontiguousarray(rb.reshape(1, 8)),
            w1=np.ascontiguousarray(W1[c]), w3=np.ascontiguousarray(W3[c]),
            w2=np.ascontiguousarray(W2[c]),
            sw1T=np.ascontiguousarray(sw1.T[:, c * SHS:(c + 1) * SHS]),
            sw3T=np.ascontiguousarray(sw3.T[:, c * SHS:(c + 1) * SHS]),
            sw2T=np.ascontiguousarray(sw2.T[c * SHS:(c + 1) * SHS, :]),
            idf=idf, idb=idb, onesb=onesb, onebc=onebc, onesf=onesf, sneg=sneg, blkI=blkI,
            cbias=cbias,
        )
        maps.append(m)
    return maps


_NC_CACHE = {}


def kernel(**inputs):
    T = int(np.asarray(inputs["L0_attn_hidden"]).shape[1])
    key = ("nc", T)
    if key not in _NC_CACHE:
        _NC_CACHE[key] = Builder(T).build()
    nc = _NC_CACHE[key]
    maps = _host_inputs(inputs, T)
    res = run_bass_kernel_spmd(nc, maps, core_ids=list(range(8)))
    Q = T // 4
    full = np.concatenate([np.asarray(res.results[c]["out_fm"]) for c in range(8)], axis=1)
    return np.ascontiguousarray(full.T).reshape(2, T, D).astype(np.float32)


# version 08
# speedup vs baseline: 1.0711x; 1.0594x over previous
import math
from contextlib import ExitStack

import numpy as np
import ml_dtypes

import concourse.bass as bass
import concourse.mybir as mybir
from concourse.bass_utils import run_bass_kernel_spmd
from concourse.aot_env import allow_ici_all_to_all

F32, BF16 = mybir.dt.float32, mybir.dt.bfloat16
AF = mybir.ActivationFunctionType
OP = mybir.AluOpType
AX = mybir.AxisListType
BFNP = ml_dtypes.bfloat16

D = 2048
QH, VH = 4, 8
C = 128
INTER = 1408
SHS = 352
LN128 = float(np.log(128.0))
RS128 = float(1.0 / np.sqrt(128.0))


class Guard:
    __slots__ = ("w", "rs")
    def __init__(self, init=()):
        self.w = None
        self.rs = list(init)


class Arena:
    def __init__(self, ap_full):
        self.ap = ap_full
        self.ivs = []

    def view(self, a, b):
        for (x, y, gd) in self.ivs:
            if x == a and y == b:
                return self.ap[:, a:b], gd
        deps, keep = [], []
        for (x, y, gd) in self.ivs:
            if x < b and a < y:
                if gd.w is not None:
                    deps.append(gd.w)
                deps.extend(gd.rs)
            else:
                keep.append((x, y, gd))
        gnew = Guard(deps)
        keep.append((a, b, gnew))
        self.ivs = keep
        return self.ap[:, a:b], gnew


class Prog:
    SECTIONS = ("tensor", "vector", "scalar", "sync", "gpsimd")
    COMPUTE_SEMS = ("pe", "dve", "act", "pool")
    DMA_POOL = {"hdma": 8, "adma": 8, "pdma": 8, "cc": 4}
    SEMS = COMPUTE_SEMS + tuple(
        f"{c}{i}" for c, n in DMA_POOL.items() for i in range(n))

    def __init__(self):
        self.q = {s: [] for s in self.SECTIONS}
        self.cnt = {k: 0 for k in self.SEMS}
        self.rr = {c: 0 for c in self.DMA_POOL}

    def _op(self, section, sem, fn, reads, writes, extra, k):
        deps = list(extra)
        if sem in self.DMA_POOL:
            i = self.rr[sem]
            self.rr[sem] = (i + 1) % self.DMA_POOL[sem]
            sub = f"{sem}{i}"
            if self.cnt[sub]:
                deps.append((sub, self.cnt[sub]))
            sem = sub
        for gd in reads:
            if gd.w is not None:
                deps.append(gd.w)
        for gd in writes:
            if gd.w is not None:
                deps.append(gd.w)
            deps.extend(gd.rs)
        self.cnt[sem] += k
        tok = (sem, self.cnt[sem])
        self.q[section].append((fn, sem, k, tuple(d for d in deps if d is not None)))
        for gd in reads:
            gd.rs.append(tok)
        for gd in writes:
            gd.w = tok
            gd.rs = []
        return tok

    def pe(self, fn, reads=(), writes=(), extra=()):
        return self._op("tensor", "pe", fn, reads, writes, extra, 1)

    def dve(self, fn, reads=(), writes=(), extra=()):
        return self._op("vector", "dve", fn, reads, writes, extra, 1)

    def act(self, fn, reads=(), writes=(), extra=()):
        return self._op("scalar", "act", fn, reads, writes, extra, 1)

    def pool(self, fn, reads=(), writes=(), extra=()):
        return self._op("gpsimd", "pool", fn, reads, writes, extra, 1)

    def hdma(self, fn, reads=(), writes=(), extra=()):
        return self._op("sync", "hdma", fn, reads, writes, extra, 16)

    def adma(self, fn, reads=(), writes=(), extra=()):
        return self._op("scalar", "adma", fn, reads, writes, extra, 16)

    def pdma(self, fn, reads=(), writes=(), extra=()):
        return self._op("gpsimd", "pdma", fn, reads, writes, extra, 16)

    def cc(self, fn, reads=(), writes=(), extra=()):
        return self._op("gpsimd", "cc", fn, reads, writes, extra, 1)

    def emit(self, blk, sems, pe_fifo=True):
        for section in self.SECTIONS:
            ops = self.q[section]
            if not ops:
                continue

            def body(eng, ops=ops, section=section):
                water = {k: 0 for k in self.cnt}
                for fn, sem, k, deps in ops:
                    for dsem, dval in deps:
                        if pe_fifo and dsem == "pe" and section == "tensor":
                            continue
                        if water[dsem] < dval:
                            eng.wait_ge(sems[dsem], dval)
                            water[dsem] = dval
                    fn(eng).then_inc(sems[sem], k)
            getattr(blk, section)(body)


class Builder:
    def __init__(self, T, dev_dbg=False, pe_fifo=True):
        self.pe_fifo = pe_fifo
        self.T = T
        self.Q = T // 4
        self.J = T // C
        self.dev_dbg = dev_dbg
        self.nc = bass.Bass("TRN2", target_bir_lowering=False)
        self.p = Prog()
        self.G = {}

    def g(self, name):
        if name not in self.G:
            self.G[name] = Guard()
        return self.G[name]

    def declare(self):
        nc, T, Q = self.nc, self.T, self.Q
        din = lambda n, s, dt=F32: nc.dram_tensor(n, s, dt, kind="ExternalInput").ap()
        self.x_fm = din("x_fm", [D, Q])
        self.wqkv = din("wqkv", [D, 2048])
        self.wz = din("wz", [D, 1024])
        self.wb128 = din("wb128", [128, 128])
        self.wa128 = din("wa128", [128, 128])
        self.convw = din("convw", [128, 64])
        self.alog = din("alog", [VH, 1])
        self.dtb = din("dtb", [VH, 1])
        self.normw = din("normw", [128, 1])
        self.wout = din("wout", [VH * 128, D])
        self.gw128 = din("gw128", [128, 128])
        self.rbias = din("rbias", [1, 8])
        self.w1 = din("w1", [D, INTER])
        self.w3 = din("w3", [D, INTER])
        self.w2 = din("w2", [INTER, D])
        self.sw1T = din("sw1T", [D, SHS])
        self.sw3T = din("sw3T", [D, SHS])
        self.sw2T = din("sw2T", [SHS, D])
        self.idf_d = din("idf", [128, 128])
        self.idb_d = din("idb", [128, 128], BF16)
        self.onesb_d = din("onesb", [1, 128], BF16)
        self.onebc_d = din("onebc", [128, 1], BF16)
        self.cbias_d = din("cbias", [128, 4])
        self.onesf_d = din("onesf", [1, 128])
        self.sneg_d = din("sneg", [128, VH * 128], BF16)
        self.blkI_d = din("blkI", [128, VH * 128], BF16)

        dr = lambda n, s, dt=F32: nc.dram_tensor(n, s, dt)
        self.xn_blk = dr("xn_blk", [D, Q], BF16)
        self.xn_bat = dr("xn_bat", [4 * D, Q], BF16)
        self.xn_all = dr("xn_all", [8 * D, Q], BF16)
        self.gates_in = dr("gates_in", [8, Q], BF16)
        self.gates_x = dr("gates_x", [8, Q], BF16)
        self.qn_sc = dr("qn_sc", [QH * 128, T], BF16)
        self.kn_sc = dr("kn_sc", [QH * 128, T], BF16)
        self.v_sc = dr("v_sc", [VH * 128, T], BF16)
        self.z_sc = dr("z_sc", [VH * 128, T], BF16)
        self.nrm_sc = dr("nrm_sc", [VH * 128, T], BF16)
        self.ps_in = dr("ps_in", [4 * D, Q], F32)
        self.s_own = dr("s_own", [D, Q], F32)
        self.y_in = dr("y_in", [8 * D, Q], F32)
        self.ffn_own = dr("ffn_own", [D, Q], F32)
        self.out_fm = nc.dram_tensor("out_fm", [D, Q], F32, kind="ExternalOutput").ap()

        self.dbg = {}
        if self.dev_dbg:
            for nm, shape, dt in [
                ("dbg_xn", [D, Q], BF16), ("dbg_gates", [8, Q], BF16),
                ("dbg_q", [QH * 128, T], BF16), ("dbg_k", [QH * 128, T], BF16),
                ("dbg_v", [VH * 128, T], BF16), ("dbg_z", [VH * 128, T], BF16),
                ("dbg_bt", [128, self.J * 8], F32), ("dbg_gc", [128, self.J * 8], F32),
                ("dbg_nrm", [VH * 128, T], BF16), ("dbg_s", [D, Q], F32),
                ("dbg_ffn", [D, Q], F32),
            ]:
                self.dbg[nm] = nc.dram_tensor(nm, shape, dt, kind="ExternalOutput").ap()

    def alloc(self, es):
        nc = self.nc
        sb = lambda n, s, dt: es.enter_context(nc.sbuf_tensor(n, s, dt))
        ps = lambda n, s, dt: es.enter_context(nc.psum_tensor(n, s, dt))
        self.AW = Arena(sb("AW", [128, 36864], BF16))
        self.ABF = Arena(sb("ABF", [128, 31744], BF16))
        self.AFA = Arena(sb("AFA", [128, 8192], F32))
        self.gcumT = sb("gcumT", [128, self.J * 8], F32)
        self.betaT = sb("betaT", [128, self.J * 8], F32)
        self.gates8 = sb("gates8", [8, self.Q], BF16)
        self.c_idf = sb("c_idf", [128, 128], F32)
        self.c_idb = sb("c_idb", [128, 128], BF16)
        self.c_onesb = sb("c_onesb", [1, 128], BF16)
        self.c_onebc = sb("c_onebc", [128, 1], BF16)
        self.c_bias = sb("c_bias", [128, 4], F32)
        self.c_onesf = sb("c_onesf", [1, 128], F32)
        self.c_ones128 = sb("c_ones128", [128, 128], F32)
        self.c_sneg = sb("c_sneg", [128, VH * 128], BF16)
        self.c_blkI = sb("c_blkI", [128, VH * 128], BF16)
        self.c_convw = sb("c_convw", [128, 64], F32)
        self.c_normw = sb("c_normw", [128, 1], F32)
        self.c_alog = sb("c_alog", [VH, 1], F32)
        self.c_dtb = sb("c_dtb", [VH, 1], F32)
        self.c_rbias = sb("c_rbias", [128, 8], F32)
        self.c_nalog = sb("c_nalog", [VH, 1], F32)
        self.c_zero8 = sb("c_zero8", [8, 1], F32)
        self.s_gw = sb("s_gw", [128, 128], F32)
        self.s_wba = sb("s_wba", [128, 256], BF16)
        self.s_cols = sb("s_cols", [128, 256], F32)
        self.s_rows = sb("s_rows", [1, 1536], F32)
        self.s_rowb = sb("s_rowb", [1, 1024], BF16)
        self.s_st = sb("s_st", [128, 1024], F32)
        self.s_stb = sb("s_stb", [128, 1024], BF16)
        self.pA = ps("pA", [128, 1024], F32)
        self.pB = ps("pB", [128, 1024], F32)
        self.pC = ps("pC", [128, 1024], F32)
        self.pT0 = ps("pT0", [128, 1024], BF16)
        self.pT1 = ps("pT1", [128, 1024], BF16)
        for n in ("pA0", "pA1", "pB0", "pB1", "pC0", "pC1", "pT0", "pT1"):
            self.g(n)

    def b0_consts(self):
        p, g = self.p, self.g
        pairs = [
            (self.c_idf, self.idf_d, "c_idf"), (self.c_idb, self.idb_d, "c_idb"),
            (self.c_onesb, self.onesb_d, "c_onesb"), (self.c_onesf, self.onesf_d, "c_onesf"),
            (self.c_onebc, self.onebc_d, "c_onebc"),
            (self.c_bias, self.cbias_d, "c_bias"),
            (self.c_sneg, self.sneg_d, "c_sneg"), (self.c_blkI, self.blkI_d, "c_blkI"),
            (self.c_convw, self.convw, "c_convw"), (self.c_normw, self.normw, "c_normw"),
            (self.c_alog, self.alog, "c_alog"), (self.c_dtb, self.dtb, "c_dtb"),
            (self.s_gw, self.gw128, "s_gw"),
        ]
        for dst, src, gn in pairs:
            p.pdma(lambda gg, dst=dst, src=src: gg.dma_start(out=dst[:, :], in_=src[:, :]),
                   writes=[g(gn)])
        p.pdma(lambda gg: gg.dma_start(out=self.s_wba[:, 0:128], in_=self.wb128[:, :]),
               writes=[g("s_wba")])
        p.pdma(lambda gg: gg.dma_start(out=self.s_wba[:, 128:256], in_=self.wa128[:, :]),
               writes=[g("s_wba")])
        p.pool(lambda gg: gg.memset(self.c_ones128[:, :], 1.0), writes=[g("c_ones128")])
        p.pool(lambda gg: gg.memset(self.c_zero8[:, :], 0.0), writes=[g("c_zero8")])
        p.pool(lambda gg: gg.memset(self.s_st[:, :], 0.0), writes=[g("s_st")])
        p.pool(lambda gg: gg.memset(self.s_stb[:, :], 0.0), writes=[g("s_stb")])
        p.pdma(lambda gg: gg.dma_start(out=self.s_rows[0:1, 0:8], in_=self.rbias[:, :]),
               writes=[g("s_rows")])
        p.pe(lambda t: t.matmul(self.pC[0:128, 0:8], self.c_onesf[:, :],
                                self.s_rows[0:1, 0:8], start=True, stop=True),
             reads=[g("c_onesf"), g("s_rows")], writes=[g("pC0")])
        p.dve(lambda v: v.tensor_copy(self.c_rbias[:, :], self.pC[0:128, 0:8]),
              reads=[g("pC0")], writes=[g("c_rbias")])
        p.act(lambda s: s.activation(self.c_nalog[:, :], self.c_alog[:, :], AF.Exp),
              reads=[g("c_alog")], writes=[g("c_nalog")])
        p.dve(lambda v: v.tensor_scalar_mul(self.c_nalog[:, :], self.c_nalog[:, :], -1.0),
              reads=[g("c_nalog")], writes=[g("c_nalog")])

    def b1_ln_router(self):
        p, g = self.p, self.g
        B1T = self.Q // 512
        for tt in range(B1T):
            t0 = tt * 512
            vx, gx = self.AFA.view(0, 8192)
            vxb, gxb = self.ABF.view(0, 8192)
            vx2, gx2 = self.ABF.view(8192, 16384)
            vxnb, gxnb = self.ABF.view(16384, 24576)
            src = self.x_fm[:, t0:t0 + 512].rearrange("(c p) t -> p c t", p=128)
            p.hdma(lambda s, src=src, vx=vx: s.dma_start(
                out=vx.rearrange("p (c t) -> p c t", t=512), in_=src), writes=[gx])
            p.pdma(lambda gg, src=src, vxb=vxb: gg.dma_start(
                out=vxb.rearrange("p (c t) -> p c t", t=512), in_=src), writes=[gxb])
            p.act(lambda s, vx=vx, vx2=vx2: s.activation(vx2, vx, AF.Square),
                  reads=[gx], writes=[gx2])
            for dc in range(16):
                p.pe(lambda t, dc=dc, vxb=vxb: t.matmul(
                    self.pC[0:1, 0:512], self.c_onebc[:, :],
                    vxb[:, dc * 512:(dc + 1) * 512], start=(dc == 0), stop=(dc == 15)),
                    reads=[gxb, g("c_onebc")], writes=[g("pC0")])
            for dc in range(16):
                p.pe(lambda t, dc=dc, vx2=vx2: t.matmul(
                    self.pC[0:1, 512:1024], self.c_onebc[:, :],
                    vx2[:, dc * 512:(dc + 1) * 512], start=(dc == 0), stop=(dc == 15)),
                    reads=[gx2, g("c_onebc")], writes=[g("pC1")])
            R, gR = self.s_rows, g("s_rows")
            p.dve(lambda v: v.tensor_scalar_mul(R[0:1, 0:512], self.pC[0:1, 0:512], 1.0 / D),
                  reads=[g("pC0")], writes=[gR])
            p.dve(lambda v: v.tensor_scalar_mul(R[0:1, 512:1024], self.pC[0:1, 512:1024], 1.0 / D),
                  reads=[g("pC1")], writes=[gR])
            p.dve(lambda v: v.tensor_tensor(R[0:1, 1024:1536], R[0:1, 0:512], R[0:1, 0:512], OP.mult),
                  reads=[gR], writes=[gR])
            p.dve(lambda v: v.tensor_tensor(R[0:1, 512:1024], R[0:1, 512:1024],
                                            R[0:1, 1024:1536], OP.subtract),
                  reads=[gR], writes=[gR])
            p.act(lambda s: s.activation(R[0:1, 512:1024], R[0:1, 512:1024], AF.Sqrt,
                                         bias=self.c_bias[0:1, 0:1]),
                  reads=[gR, g("c_bias")], writes=[gR])
            p.dve(lambda v: v.reciprocal(R[0:1, 512:1024], R[0:1, 512:1024]),
                  reads=[gR], writes=[gR])
            p.pe(lambda t: t.matmul(self.pA[:, 0:512], self.c_onesf[:, :], R[0:1, 0:512],
                                    start=True, stop=True),
                 reads=[gR, g("c_onesf")], writes=[g("pA0")])
            p.pe(lambda t: t.matmul(self.pA[:, 512:1024], self.c_onesf[:, :], R[0:1, 512:1024],
                                    start=True, stop=True),
                 reads=[gR, g("c_onesf")], writes=[g("pA1")])
            mean_x = bass.AP(self.pA[:, 0:512].tensor, 0, [[1024, 128], [0, 16], [1, 512]])
            rstd_x = bass.AP(self.pA[:, 0:512].tensor, 512, [[1024, 128], [0, 16], [1, 512]])
            vx3 = vx.rearrange("p (c t) -> p c t", t=512)
            p.dve(lambda v, vx3=vx3, mean_x=mean_x: v.tensor_tensor(
                vx3, vx3, mean_x, OP.subtract), reads=[g("pA0")], writes=[gx])
            p.dve(lambda v, vx3=vx3, rstd_x=rstd_x: v.tensor_tensor(
                vx3, vx3, rstd_x, OP.mult), reads=[g("pA1")], writes=[gx])
            p.dve(lambda v, vx=vx, vxnb=vxnb: v.tensor_copy(vxnb, vx),
                  reads=[gx], writes=[gxnb])
            for sub in range(4):
                for dc in range(16):
                    p.pe(lambda t, sub=sub, dc=dc, vx=vx: t.matmul(
                        self.pB[:, sub * 8:sub * 8 + 8],
                        vx[:, dc * 512 + sub * 128: dc * 512 + sub * 128 + 128],
                        self.s_gw[:, dc * 8:(dc + 1) * 8],
                        start=(dc == 0), stop=(dc == 15)),
                        reads=[gx, g("s_gw")], writes=[g("pB0")])
            SC, gS = self.s_cols, g("s_cols")
            for sub in range(4):
                pg = self.pB[:, sub * 8:sub * 8 + 8]
                p.act(lambda s, pg=pg: s.activation(SC[:, 0:8], pg, AF.Sigmoid),
                      reads=[g("pB0")], writes=[gS])
                p.dve(lambda v: v.tensor_tensor(SC[:, 8:16], SC[:, 0:8], self.c_rbias[:, :], OP.add),
                      reads=[gS, g("c_rbias")], writes=[gS])
                p.dve(lambda v: v.reduce_max(SC[:, 16:17], SC[:, 8:16], axis=AX.X),
                      reads=[gS], writes=[gS])
                p.dve(lambda v: v.tensor_scalar(SC[:, 24:32], SC[:, 8:16], SC[:, 16:17],
                                                None, OP.is_ge),
                      reads=[gS], writes=[gS])
                p.dve(lambda v: v.scalar_tensor_tensor(SC[:, 32:40], SC[:, 24:32], -1e9,
                                                       SC[:, 8:16], OP.mult, OP.add),
                      reads=[gS], writes=[gS])
                p.dve(lambda v: v.reduce_max(SC[:, 40:41], SC[:, 32:40], axis=AX.X),
                      reads=[gS], writes=[gS])
                p.dve(lambda v: v.tensor_scalar(SC[:, 48:56], SC[:, 8:16], SC[:, 40:41],
                                                None, OP.is_ge),
                      reads=[gS], writes=[gS])
                p.dve(lambda v: v.tensor_tensor(SC[:, 56:64], SC[:, 0:8], SC[:, 48:56], OP.mult),
                      reads=[gS], writes=[gS])
                p.dve(lambda v: v.reduce_sum(SC[:, 64:65], SC[:, 56:64], axis=AX.X),
                      reads=[gS], writes=[gS])
                p.dve(lambda v: v.tensor_scalar_max(SC[:, 64:65], SC[:, 64:65], 1e-12),
                      reads=[gS], writes=[gS])
                p.dve(lambda v: v.reciprocal(SC[:, 64:65], SC[:, 64:65]),
                      reads=[gS], writes=[gS])
                p.dve(lambda v: v.tensor_scalar(SC[:, 72:80], SC[:, 56:64], SC[:, 64:65],
                                                None, OP.mult),
                      reads=[gS], writes=[gS])
                vgb, ggb = self.ABF.view(24576, 24584)
                p.dve(lambda v, vgb=vgb: v.tensor_copy(vgb, SC[:, 72:80]),
                      reads=[gS], writes=[ggb])
                p.pe(lambda t, vgb=vgb: t.transpose(self.pT0[0:8, 0:128], vgb, self.c_idb[:, :]),
                     reads=[ggb, g("c_idb")], writes=[g("pT0")])
                vgf, ggf = self.ABF.view(24704, 24832)
                p.dve(lambda v, vgf=vgf: v.tensor_copy(vgf[0:8, :], self.pT0[0:8, 0:128]),
                      reads=[g("pT0")], writes=[ggf])
                dst = self.gates_in.ap()[:, t0 + sub * 128: t0 + (sub + 1) * 128]
                p.adma(lambda s, dst=dst, vgf=vgf: s.dma_start(out=dst, in_=vgf[0:8, :]),
                       reads=[ggf], writes=[g("d_gates_in")])
            dst = self.xn_blk.ap()[:, t0:t0 + 512].rearrange("(c p) t -> p c t", p=128)
            p.adma(lambda s, dst=dst, vxnb=vxnb: s.dma_start(
                out=dst, in_=vxnb.rearrange("p (c t) -> p c t", t=512)),
                   reads=[gxnb], writes=[g("d_xn_blk")])

    def b2_collectives(self):
        p, g = self.p, self.g
        p.cc(lambda gg: gg.collective_compute(
            "AllGather", OP.bypass, replica_groups=[[0, 1, 2, 3], [4, 5, 6, 7]],
            ins=[self.xn_blk.ap().opt()], outs=[self.xn_bat.ap().opt()]),
            reads=[g("d_xn_blk")], writes=[g("d_xn_bat")])
        p.cc(lambda gg: gg.collective_compute(
            "AllGather", OP.bypass, replica_groups=[list(range(8))],
            ins=[self.xn_blk.ap().opt()], outs=[self.xn_all.ap().opt()]),
            reads=[g("d_xn_blk")], writes=[g("d_xn_all")])

        def _a2a(gg):
            with allow_ici_all_to_all():
                return gg.collective_compute(
                    "AllToAll", OP.bypass, replica_groups=[list(range(8))],
                    ins=[self.gates_in.ap().opt()], outs=[self.gates_x.ap().opt()])
        p.cc(_a2a, reads=[g("d_gates_in")], writes=[g("d_gates_x")])
        if self.dev_dbg:
            p.pdma(lambda gg: gg.dma_start(out=self.dbg["dbg_xn"][:, :],
                                           in_=self.xn_blk.ap()[:, :]),
                   reads=[g("d_xn_blk")], writes=[g("dbg")])
            p.pdma(lambda gg: gg.dma_start(out=self.dbg["dbg_gates"][:, :],
                                           in_=self.gates_x.ap()[:, :]),
                   reads=[g("d_gates_x")], writes=[g("dbg")])

    def b7_ffn(self):
        p, g = self.p, self.g
        Q = self.Q
        B7T = (2 * self.T) // 512
        p.pdma(lambda gg: gg.dma_start(out=self.gates8[:, :], in_=self.gates_x.ap()[:, :]),
               reads=[g("d_gates_x")], writes=[g("gates8")])
        sweeps = [("e", 0, 6), ("e", 6, 11), ("sh", 0, 3)]
        for sw, (kind, h0, h1) in enumerate(sweeps):
            nh = h1 - h0
            if kind == "e":
                hszs = [128] * nh
                WROW = nh * 128
                w1v, gw1 = self.AW.view(0, 16 * WROW)
                w3v, gw3 = self.AW.view(16 * WROW, 32 * WROW)
                w2v, gw2 = self.AW.view(32 * WROW, 32 * WROW + nh * 2048)
                for dc in range(16):
                    src = self.w1[dc * 128:(dc + 1) * 128, h0 * 128:h1 * 128]
                    p.pdma(lambda gg, src=src, dc=dc, w1v=w1v, WROW=WROW: gg.dma_start(
                        out=w1v[:, dc * WROW:(dc + 1) * WROW], in_=src), writes=[gw1])
                for dc in range(16):
                    src = self.w3[dc * 128:(dc + 1) * 128, h0 * 128:h1 * 128]
                    p.pdma(lambda gg, src=src, dc=dc, w3v=w3v, WROW=WROW: gg.dma_start(
                        out=w3v[:, dc * WROW:(dc + 1) * WROW], in_=src), writes=[gw3])
                for hc in range(nh):
                    src = self.w2[(h0 + hc) * 128:(h0 + hc + 1) * 128, :]
                    p.pdma(lambda gg, src=src, hc=hc, w2v=w2v: gg.dma_start(
                        out=w2v[:, hc * 2048:(hc + 1) * 2048], in_=src), writes=[gw2])
            else:
                hszs = [128, 128, 96]
                WROW = SHS
                w1v, gw1 = self.AW.view(0, 16 * SHS)
                w3v, gw3 = self.AW.view(16 * SHS, 32 * SHS)
                w2v, gw2 = self.AW.view(32 * SHS, 32 * SHS + 3 * 2048)
                for dc in range(16):
                    src = self.sw1T[dc * 128:(dc + 1) * 128, :]
                    p.pdma(lambda gg, src=src, dc=dc, w1v=w1v: gg.dma_start(
                        out=w1v[:, dc * SHS:(dc + 1) * SHS], in_=src), writes=[gw1])
                for dc in range(16):
                    src = self.sw3T[dc * 128:(dc + 1) * 128, :]
                    p.pdma(lambda gg, src=src, dc=dc, w3v=w3v: gg.dma_start(
                        out=w3v[:, dc * SHS:(dc + 1) * SHS], in_=src), writes=[gw3])
                for hc in range(3):
                    hs = hszs[hc]
                    src = self.sw2T[hc * 128:hc * 128 + hs, :]
                    p.pdma(lambda gg, src=src, hc=hc, hs=hs, w2v=w2v: gg.dma_start(
                        out=w2v[0:hs, hc * 2048:(hc + 1) * 2048], in_=src), writes=[gw2])
            nh_ = len(hszs)
            for gt in range(B7T):
                qq, lc = (gt * 512) // Q, (gt * 512) % Q
                par = gt % 2
                vxn, gxn = self.ABF.view(par * 8192, par * 8192 + 8192)
                src = self.xn_all.ap()[qq * 2048:(qq + 1) * 2048, lc:lc + 512].rearrange(
                    "(c p) t -> p c t", p=128)
                p.hdma(lambda s, src=src, vxn=vxn: s.dma_start(
                    out=vxn.rearrange("p (c t) -> p c t", t=512), in_=src),
                    writes=[gxn], extra=[g("d_xn_all").w])
                gate_sb, ggsb = self.ABF.view(19968 + par * 512, 19968 + par * 512 + 512)
                if kind == "e":
                    p.pe(lambda t, qq=qq, lc=lc: t.matmul(
                        self.pB[0:1, 512:1024], self.c_idb[0:8, qq:qq + 1],
                        self.gates8[:, lc:lc + 512], start=True, stop=True),
                        reads=[g("gates8"), g("c_idb")], writes=[g("pB1")])
                    p.dve(lambda v: v.tensor_copy(self.s_rowb[0:1, 0:512],
                                                  self.pB[0:1, 512:1024]),
                          reads=[g("pB1")], writes=[g("s_rowb")])
                    p.pe(lambda t: t.matmul(self.pB[:, 512:1024], self.c_onesb[:, :],
                                            self.s_rowb[0:1, 0:512], start=True, stop=True),
                         reads=[g("s_rowb"), g("c_onesb")], writes=[g("pB1")])
                    p.dve(lambda v, gate_sb=gate_sb: v.tensor_copy(
                        gate_sb, self.pB[:, 512:1024]),
                        reads=[g("pB1")], writes=[ggsb])
                g13v, g13g = self.ABF.view(16896, 16896 + 6 * 512)
                silv, gsil = self.ABF.view(16384, 16896)
                hoff = 0
                for hc in range(nh_):
                    hs = hszs[hc]
                    slot = hc % 2
                    pH = self.pA if slot == 0 else self.pB
                    gh1 = g("pA0") if slot == 0 else g("pB0")
                    gh3 = g("pA1") if slot == 0 else g("pB1")
                    for dc in range(16):
                        p.pe(lambda t, dc=dc, hs=hs, hoff=hoff, vxn=vxn, w1v=w1v,
                             WROW=WROW, pH=pH: t.matmul(
                            pH[0:hs, 0:512], w1v[:, dc * WROW + hoff: dc * WROW + hoff + hs],
                            vxn[:, dc * 512:(dc + 1) * 512], start=(dc == 0), stop=(dc == 15)),
                            reads=[gw1, gxn], writes=[gh1])
                    for dc in range(16):
                        p.pe(lambda t, dc=dc, hs=hs, hoff=hoff, vxn=vxn, w3v=w3v,
                             WROW=WROW, pH=pH: t.matmul(
                            pH[0:hs, 512:1024], w3v[:, dc * WROW + hoff: dc * WROW + hoff + hs],
                            vxn[:, dc * 512:(dc + 1) * 512], start=(dc == 0), stop=(dc == 15)),
                            reads=[gw3, gxn], writes=[gh3])
                    p.act(lambda s, hs=hs, pH=pH: s.activation(
                        silv[0:hs, 0:512], pH[0:hs, 0:512], AF.Silu),
                        reads=[gh1], writes=[gsil])
                    p.dve(lambda v, hs=hs, pH=pH: v.tensor_tensor(
                        silv[0:hs, 0:512], pH[0:hs, 512:1024], silv[0:hs, 0:512], OP.mult),
                        reads=[gh3, gsil], writes=[gsil])
                    if kind == "e":
                        p.dve(lambda v, hc=hc, hs=hs, gate_sb=gate_sb: v.tensor_tensor(
                            g13v[0:hs, hc * 512:(hc + 1) * 512], silv[0:hs, 0:512],
                            gate_sb[0:hs, :], OP.mult),
                            reads=[gsil, ggsb], writes=[g13g])
                    else:
                        p.dve(lambda v, hc=hc, hs=hs: v.tensor_copy(
                            g13v[0:hs, hc * 512:(hc + 1) * 512], silv[0:hs, 0:512]),
                            reads=[gsil], writes=[g13g])
                    hoff += hs
                yst, gyst = self.AFA.view(0, 8192)
                for dc in range(16):
                    half = dc % 2
                    gy = g("pC0") if half == 0 else g("pC1")
                    for hc in range(nh_):
                        hs = hszs[hc]
                        p.pe(lambda t, dc=dc, hc=hc, hs=hs, half=half, w2v=w2v,
                             nh_=nh_: t.matmul(
                            self.pC[:, half * 512:half * 512 + 512],
                            w2v[0:hs, hc * 2048 + dc * 128: hc * 2048 + (dc + 1) * 128],
                            g13v[0:hs, hc * 512:(hc + 1) * 512],
                            start=(hc == 0), stop=(hc == nh_ - 1)),
                            reads=[gw2, g13g], writes=[gy])
                    p.act(lambda s, dc=dc, half=half, yst=yst: s.activation(
                        yst[:, dc * 512:(dc + 1) * 512],
                        self.pC[:, half * 512:half * 512 + 512], AF.Copy),
                        reads=[gy], writes=[gyst])
                dst = self.y_in.ap()[qq * 2048:(qq + 1) * 2048, lc:lc + 512].rearrange(
                    "(c p) t -> p c t", p=128)
                ysr = yst.rearrange("p (c t) -> p c t", t=512)
                if sw == 0:
                    p.pdma(lambda gg, dst=dst, ysr=ysr: gg.dma_start(out=dst, in_=ysr),
                           reads=[gyst], writes=[g("d_y_%d_%d" % (qq, lc))])
                else:
                    p.pdma(lambda gg, dst=dst, ysr=ysr: gg.dma_start(
                        out=dst, in_=ysr, accum_op=OP.add),
                        reads=[gyst], writes=[g("d_y_%d_%d" % (qq, lc))])
        ally = [self.g("d_y_%d_%d" % (qq, lc)) for qq in range(8) for lc in range(0, Q, 512)]
        p.cc(lambda gg: gg.collective_compute(
            "ReduceScatter", OP.add, replica_groups=[list(range(8))],
            ins=[self.y_in.ap().opt()], outs=[self.ffn_own.ap().opt()]),
            reads=ally, writes=[g("d_ffn_own")])
        if self.dev_dbg:
            p.pdma(lambda gg: gg.dma_start(out=self.dbg["dbg_ffn"][:, :],
                                           in_=self.ffn_own.ap()[:, :]),
                   reads=[g("d_ffn_own")], writes=[g("dbg")])

    def b3_proj(self):
        p, g = self.p, self.g
        T, Q = self.T, self.Q
        B3T = T // 512
        wv, gwv = self.AW.view(0, 32768)
        for dc in range(16):
            src = self.wqkv[dc * 128:(dc + 1) * 128, :]
            p.pdma(lambda gg, src=src, dc=dc, wv=wv: gg.dma_start(
                out=wv[:, dc * 2048:(dc + 1) * 2048], in_=src), writes=[gwv])
        rings = []
        for cc in range(16):
            rv, gr = self.ABF.view(16384 + cc * 520, 16384 + cc * 520 + 515)
            rings.append((rv, gr))
            p.pool(lambda gg, rv=rv: gg.memset(rv[:, 0:3], 0.0), writes=[gr])
        for tt in range(B3T):
            par = tt % 2
            t0g = tt * 512
            r, col = t0g // Q, t0g % Q
            vxn, gxn = self.ABF.view(par * 8192, par * 8192 + 8192)
            src = self.xn_bat.ap()[r * 2048:(r + 1) * 2048, col:col + 512].rearrange(
                "(c p) t -> p c t", p=128)
            p.hdma(lambda s, src=src, vxn=vxn: s.dma_start(
                out=vxn.rearrange("p (c t) -> p c t", t=512), in_=src),
                   writes=[gxn], extra=[self.g("d_xn_bat").w])
            for cc in range(16):
                a0, ga0 = self.ABF.view(26752, 27264) if cc % 2 == 0 else self.ABF.view(27776, 28288)
                a1, ga1 = self.ABF.view(27264, 27776) if cc % 2 == 0 else self.ABF.view(28288, 28800)
                rv, gr = rings[cc]
                ph = (cc % 2) * 512
                gph = g("pA0") if cc % 2 == 0 else g("pA1")
                for dc in range(16):
                    p.pe(lambda t, dc=dc, cc=cc, ph=ph, vxn=vxn, wv=wv: t.matmul(
                        self.pA[:, ph:ph + 512],
                        wv[:, dc * 2048 + cc * 128: dc * 2048 + (cc + 1) * 128],
                        vxn[:, dc * 512:(dc + 1) * 512], start=(dc == 0), stop=(dc == 15)),
                        reads=[gwv, gxn], writes=[gph])
                p.dve(lambda v, rv=rv, ph=ph: v.tensor_copy(rv[:, 3:515],
                                                            self.pA[:, ph:ph + 512]),
                      reads=[gph], writes=[gr])
                p.dve(lambda v, rv=rv, cc=cc, a0=a0: v.tensor_scalar(
                    a0, rv[:, 0:512], self.c_convw[:, cc * 4:cc * 4 + 1], None, OP.mult),
                    reads=[gr, g("c_convw")], writes=[ga0])
                p.dve(lambda v, rv=rv, cc=cc, a0=a0, a1=a1: v.scalar_tensor_tensor(
                    a1, rv[:, 1:513], self.c_convw[:, cc * 4 + 1:cc * 4 + 2], a0, OP.mult, OP.add),
                    reads=[gr, ga0, g("c_convw")], writes=[ga1])
                p.dve(lambda v, rv=rv, cc=cc, a0=a0, a1=a1: v.scalar_tensor_tensor(
                    a0, rv[:, 2:514], self.c_convw[:, cc * 4 + 2:cc * 4 + 3], a1, OP.mult, OP.add),
                    reads=[gr, ga1, g("c_convw")], writes=[ga0])
                p.dve(lambda v, rv=rv, cc=cc, a0=a0, a1=a1: v.scalar_tensor_tensor(
                    a1, rv[:, 3:515], self.c_convw[:, cc * 4 + 3:cc * 4 + 4], a0, OP.mult, OP.add),
                    reads=[gr, ga0, g("c_convw")], writes=[ga1])
                p.dve(lambda v, rv=rv: v.tensor_copy(rv[:, 0:3], rv[:, 512:515]),
                      reads=[gr], writes=[gr])
                if cc < 4:
                    qf, gqf = self.AFA.view(1024, 1536)
                    p.act(lambda s, qf=qf, a1=a1: s.activation(qf, a1, AF.Silu),
                          reads=[ga1], writes=[gqf])
                    sq, gsq = self.ABF.view(24704, 25216)
                    p.act(lambda s, sq=sq, qf=qf: s.activation(sq, qf, AF.Square),
                          reads=[gqf], writes=[gsq])
                    p.pe(lambda t, sq=sq: t.matmul(self.pC[0:1, 0:512], self.c_onebc[:, :],
                                                   sq, start=True, stop=True),
                         reads=[gsq, g("c_onebc")], writes=[g("pC0")])
                    R, gR = self.s_rows, g("s_rows")
                    p.dve(lambda v: v.tensor_scalar_add(R[0:1, 0:512], self.pC[0:1, 0:512], 1e-6),
                          reads=[g("pC0")], writes=[gR])
                    p.act(lambda s: s.activation(R[0:1, 0:512], R[0:1, 0:512], AF.Sqrt),
                          reads=[gR], writes=[gR])
                    p.dve(lambda v: v.reciprocal(R[0:1, 0:512], R[0:1, 0:512]),
                          reads=[gR], writes=[gR])
                    p.dve(lambda v: v.tensor_copy(self.s_rowb[0:1, 0:512], R[0:1, 0:512]),
                          reads=[gR], writes=[g("s_rowb")])
                    p.pe(lambda t: t.matmul(self.pB[:, 0:512], self.c_onesb[:, :],
                                            self.s_rowb[0:1, 0:512], start=True, stop=True),
                         reads=[g("s_rowb"), g("c_onesb")], writes=[g("pB0")])
                    qn, gqn = self.ABF.view(25216, 25728)
                    p.dve(lambda v, qn=qn, qf=qf: v.tensor_tensor(qn, qf, self.pB[:, 0:512],
                                                                  OP.mult),
                          reads=[gqf, g("pB0")], writes=[gqn])
                    dst = self.qn_sc.ap()[cc * 128:(cc + 1) * 128, t0g:t0g + 512]
                    p.adma(lambda s, dst=dst, qn=qn: s.dma_start(out=dst, in_=qn),
                           reads=[gqn], writes=[g("d_qn")])
                elif cc < 8:
                    kb_, gkb = self.ABF.view(25728, 26240)
                    p.act(lambda s, kb_=kb_, a1=a1: s.activation(kb_, a1, AF.Silu),
                          reads=[ga1], writes=[gkb])
                    dst = self.kn_sc.ap()[(cc - 4) * 128:(cc - 3) * 128, t0g:t0g + 512]
                    p.adma(lambda s, dst=dst, kb_=kb_: s.dma_start(out=dst, in_=kb_),
                           reads=[gkb], writes=[g("d_kn")])
                else:
                    vv, gvv = self.ABF.view(26240, 26752)
                    p.act(lambda s, vv=vv, a1=a1: s.activation(vv, a1, AF.Silu),
                          reads=[ga1], writes=[gvv])
                    dst = self.v_sc.ap()[(cc - 8) * 128:(cc - 7) * 128, t0g:t0g + 512]
                    p.adma(lambda s, dst=dst, vv=vv: s.dma_start(out=dst, in_=vv),
                           reads=[gvv], writes=[g("d_v")])
            for dc in range(16):
                p.pe(lambda t, dc=dc, vxn=vxn: t.matmul(
                    self.pC[0:8, 0:512], self.s_wba[:, dc * 8:(dc + 1) * 8],
                    vxn[:, dc * 512:(dc + 1) * 512], start=(dc == 0), stop=(dc == 15)),
                    reads=[g("s_wba"), gxn], writes=[g("pC0")])
            for dc in range(16):
                p.pe(lambda t, dc=dc, vxn=vxn: t.matmul(
                    self.pC[0:8, 512:1024], self.s_wba[:, 128 + dc * 8:128 + (dc + 1) * 8],
                    vxn[:, dc * 512:(dc + 1) * 512], start=(dc == 0), stop=(dc == 15)),
                    reads=[g("s_wba"), gxn], writes=[g("pC1")])
            bt, gbt = self.AFA.view(1536, 2048)
            ga_, gga = self.AFA.view(2048, 2560)
            tx, gtx = self.AFA.view(2560, 3072)
            ty, gty = self.AFA.view(3072, 3584)
            p.act(lambda s, bt=bt: s.activation(bt[0:8, :], self.pC[0:8, 0:512], AF.Sigmoid),
                  reads=[g("pC0")], writes=[gbt])
            p.dve(lambda v, ga_=ga_: v.tensor_scalar(ga_[0:8, :], self.pC[0:8, 512:1024],
                                                     self.c_dtb[:, :], None, OP.add),
                  reads=[g("pC1"), g("c_dtb")], writes=[gga])
            p.act(lambda s, ga_=ga_, tx=tx: s.activation(tx[0:8, :], ga_[0:8, :], AF.Abs),
                  reads=[gga], writes=[gtx])
            p.act(lambda s, tx=tx: s.activation(tx[0:8, :], tx[0:8, :], AF.Exp, scale=-1.0),
                  reads=[gtx], writes=[gtx])
            p.act(lambda s, tx=tx: s.activation(tx[0:8, :], tx[0:8, :], AF.Ln,
                                                bias=self.c_bias[0:8, 1:2]),
                  reads=[gtx, g("c_bias")], writes=[gtx])
            p.dve(lambda v, ga_=ga_, ty=ty: v.tensor_scalar_max(ty[0:8, :], ga_[0:8, :], 0.0),
                  reads=[gga], writes=[gty])
            p.dve(lambda v, tx=tx, ty=ty: v.tensor_tensor(ty[0:8, :], ty[0:8, :], tx[0:8, :],
                                                          OP.add),
                  reads=[gtx], writes=[gty])
            p.dve(lambda v, ty=ty: v.tensor_scalar(ty[0:8, :], ty[0:8, :], self.c_nalog[:, :],
                                                   None, OP.mult),
                  reads=[g("c_nalog")], writes=[gty])
            gc_, ggc = self.AFA.view(3584, 4096)
            for sub in range(4):
                p.dve(lambda v, sub=sub, ty=ty, gc_=gc_: v.tensor_tensor_scan(
                    gc_[0:8, sub * 128:(sub + 1) * 128], ty[0:8, sub * 128:(sub + 1) * 128],
                    ty[0:8, sub * 128:(sub + 1) * 128], self.c_zero8[:, :], OP.add, OP.bypass),
                    reads=[gty, g("c_zero8")], writes=[ggc])
            for sub in range(4):
                p.pe(lambda t, sub=sub, gc_=gc_: t.transpose(
                    self.pA[:, sub * 8:sub * 8 + 8], gc_[0:8, sub * 128:(sub + 1) * 128],
                    self.c_idf[0:8, 0:8]),
                    reads=[ggc, g("c_idf")], writes=[g("pA0")])
                p.pe(lambda t, sub=sub, bt=bt: t.transpose(
                    self.pA[:, 32 + sub * 8:32 + sub * 8 + 8], bt[0:8, sub * 128:(sub + 1) * 128],
                    self.c_idf[0:8, 0:8]),
                    reads=[gbt, g("c_idf")], writes=[g("pA0")])
            p.dve(lambda v, tt=tt: v.tensor_copy(self.gcumT[:, tt * 32:tt * 32 + 32],
                                                 self.pA[:, 0:32]),
                  reads=[g("pA0")], writes=[g("gcumT")])
            p.dve(lambda v, tt=tt: v.tensor_copy(self.betaT[:, tt * 32:tt * 32 + 32],
                                                 self.pA[:, 32:64]),
                  reads=[g("pA0")], writes=[g("betaT")])
        wzv, gwz = self.AW.view(0, 16384)
        for dc in range(16):
            src = self.wz[dc * 128:(dc + 1) * 128, :]
            p.pdma(lambda gg, src=src, dc=dc, wzv=wzv: gg.dma_start(
                out=wzv[:, dc * 1024:(dc + 1) * 1024], in_=src), writes=[gwz])
        for tt in range(B3T):
            par = tt % 2
            t0g = tt * 512
            r, col = t0g // Q, t0g % Q
            vxn, gxn = self.ABF.view(par * 8192, par * 8192 + 8192)
            src = self.xn_bat.ap()[r * 2048:(r + 1) * 2048, col:col + 512].rearrange(
                "(c p) t -> p c t", p=128)
            p.hdma(lambda s, src=src, vxn=vxn: s.dma_start(
                out=vxn.rearrange("p (c t) -> p c t", t=512), in_=src), writes=[gxn])
            for zc in range(8):
                for dc in range(16):
                    p.pe(lambda t, zc=zc, dc=dc, vxn=vxn, wzv=wzv: t.matmul(
                        self.pA[:, (zc % 2) * 512:(zc % 2) * 512 + 512],
                        wzv[:, dc * 1024 + zc * 128: dc * 1024 + (zc + 1) * 128],
                        vxn[:, dc * 512:(dc + 1) * 512], start=(dc == 0), stop=(dc == 15)),
                        reads=[gwz, gxn], writes=[g("pA0") if zc % 2 == 0 else g("pA1")])
                zst, gzst = self.ABF.view(16384 + (zc % 2) * 512, 16384 + (zc % 2) * 512 + 512)
                p.act(lambda s, zst=zst, zc=zc: s.activation(
                    zst, self.pA[:, (zc % 2) * 512:(zc % 2) * 512 + 512], AF.Copy),
                    reads=[g("pA0") if zc % 2 == 0 else g("pA1")], writes=[gzst])
                dst = self.z_sc.ap()[zc * 128:(zc + 1) * 128, t0g:t0g + 512]
                p.adma(lambda s, dst=dst, zst=zst: s.dma_start(out=dst, in_=zst),
                       reads=[gzst], writes=[g("d_z")])
        if self.dev_dbg:
            for nm, t_ in [("dbg_q", self.qn_sc), ("dbg_k", self.kn_sc),
                           ("dbg_v", self.v_sc), ("dbg_z", self.z_sc)]:
                p.pdma(lambda gg, nm=nm, t_=t_: gg.dma_start(
                    out=self.dbg[nm][:, :], in_=t_.ap()[:, :]),
                    reads=[g("d_qn"), g("d_kn"), g("d_v"), g("d_z")], writes=[g("dbg")])
            p.pdma(lambda gg: gg.dma_start(out=self.dbg["dbg_bt"][:, :], in_=self.betaT[:, :]),
                   reads=[g("betaT")], writes=[g("dbg")])
            p.pdma(lambda gg: gg.dma_start(out=self.dbg["dbg_gc"][:, :], in_=self.gcumT[:, :]),
                   reads=[g("gcumT")], writes=[g("dbg")])

    def b4_scan(self):
        p, g = self.p, self.g
        T, Q, J = self.T, self.Q, self.J
        SC = self.s_cols
        gSC = g("s_cols")

        def emit_pre(j):
            par = j % 2
            AB = self.ABF if par == 0 else self.AW
            jc = j * 8

            def ab(a, b, AB=AB):
                return AB.view(a, b)

            def afv(a, b, par=par):
                return self.AFA.view(a + par * 4096, b + par * 4096)

            ctx = {"j": j, "jc": jc, "par": par, "ab": ab, "afv": afv}
            qv, gq = ab(0, 512)
            src = self.qn_sc.ap()[:, j * 128:(j + 1) * 128].rearrange(
                "(h p) t -> p h t", p=128)
            p.hdma(lambda s, src=src, qv=qv: s.dma_start(
                out=qv.rearrange("p (h t) -> p h t", t=128), in_=src),
                writes=[gq], extra=[g("d_qn").w])
            kfm_r, gkfr = ab(1024, 1536)
            src = self.kn_sc.ap()[:, j * 128:(j + 1) * 128].rearrange(
                "(h p) t -> p h t", p=128)
            p.hdma(lambda s, src=src, kfm_r=kfm_r: s.dma_start(
                out=kfm_r.rearrange("p (h t) -> p h t", t=128), in_=src),
                writes=[gkfr], extra=[g("d_kn").w])
            vfm_r, gvfr = ab(2048, 3072)
            src = self.v_sc.ap()[:, j * 128:(j + 1) * 128].rearrange(
                "(u p) t -> p u t", p=128)
            p.hdma(lambda s, src=src, vfm_r=vfm_r: s.dma_start(
                out=vfm_r.rearrange("p (u t) -> p u t", t=128), in_=src),
                writes=[gvfr], extra=[g("d_v").w])
            zfm_r, gzfr = ab(4096, 5120)
            src = self.z_sc.ap()[:, j * 128:(j + 1) * 128].rearrange(
                "(u p) t -> p u t", p=128)
            p.adma(lambda s, src=src, zfm_r=zfm_r: s.dma_start(
                out=zfm_r.rearrange("p (u t) -> p u t", t=128), in_=src),
                writes=[gzfr], extra=[g("d_z").w])
            ktm, gkt = ab(1536, 2048)
            for h in range(4):
                p.pe(lambda t, h=h, kfm_r=kfm_r: t.transpose(
                    self.pT0[:, h * 128:(h + 1) * 128], kfm_r[:, h * 128:(h + 1) * 128],
                    self.c_idb[:, :]),
                    reads=[gkfr, g("c_idb")], writes=[g("pT0")])
            p.dve(lambda v, ktm=ktm: v.tensor_copy(ktm, self.pT0[:, 0:512]),
                  reads=[g("pT0")], writes=[gkt])
            vtm, gvt = ab(3072, 4096)
            for u in range(8):
                p.pe(lambda t, u=u, vfm_r=vfm_r: t.transpose(
                    self.pT1[:, u * 128:(u + 1) * 128], vfm_r[:, u * 128:(u + 1) * 128],
                    self.c_idb[:, :]),
                    reads=[gvfr, g("c_idb")], writes=[g("pT1")])
            p.dve(lambda v, vtm=vtm: v.tensor_copy(vtm, self.pT1[:, :]),
                  reads=[g("pT1")], writes=[gvt])
            ztm, gzt = ab(5120, 6144)
            for u in range(8):
                p.pe(lambda t, u=u, zfm_r=zfm_r: t.transpose(
                    self.pT0[:, u * 128:(u + 1) * 128], zfm_r[:, u * 128:(u + 1) * 128],
                    self.c_idb[:, :]),
                    reads=[gzfr, g("c_idb")], writes=[g("pT0")])
            p.dve(lambda v, ztm=ztm: v.tensor_copy(ztm, self.pT0[:, :]),
                  reads=[g("pT0")], writes=[gzt])
            ctx["qv"], ctx["gq"] = qv, gq
            ctx["ktm"], ctx["gkt"] = ktm, gkt
            ctx["ztm"], ctx["gzt"] = ztm, gzt
            k2, gk2 = afv(3072, 3584)
            p.dve(lambda v, ktm=ktm, k2=k2: v.tensor_tensor(k2, ktm, ktm, OP.mult),
                  reads=[gkt], writes=[gk2])
            p.dve(lambda v, k2=k2: v.reduce_sum(
                SC[:, 0:4], k2.rearrange("p (h a) -> p h a", a=128), axis=AX.X),
                reads=[gk2], writes=[gSC])
            p.dve(lambda v: v.tensor_scalar_add(SC[:, 4:8], SC[:, 0:4], 1e-6),
                  reads=[gSC], writes=[gSC])
            p.act(lambda s: s.activation(SC[:, 4:8], SC[:, 4:8], AF.Sqrt),
                  reads=[gSC], writes=[gSC])
            p.dve(lambda v: v.reciprocal(SC[:, 4:8], SC[:, 4:8]),
                  reads=[gSC], writes=[gSC])
            p.dve(lambda v: v.tensor_copy(
                SC[:, 8:16].rearrange("p (u o) -> p u o", o=2)[:, :, 0:1],
                SC[:, 4:8].rearrange("p (u o) -> p u o", o=1)),
                reads=[gSC], writes=[gSC])
            p.dve(lambda v: v.tensor_copy(
                SC[:, 8:16].rearrange("p (u o) -> p u o", o=2)[:, :, 1:2],
                SC[:, 4:8].rearrange("p (u o) -> p u o", o=1)),
                reads=[gSC], writes=[gSC])
            p.act(lambda s, jc=jc: s.activation(SC[:, 16:24], self.gcumT[:, jc:jc + 8], AF.Exp),
                  reads=[g("gcumT")], writes=[gSC])
            p.dve(lambda v, jc=jc: v.tensor_tensor(SC[:, 48:56], self.betaT[:, jc:jc + 8],
                                                   SC[:, 8:16], OP.mult),
                  reads=[g("betaT"), gSC], writes=[gSC])
            p.dve(lambda v: v.tensor_tensor(SC[:, 56:64], SC[:, 48:56], SC[:, 16:24], OP.mult),
                  reads=[gSC], writes=[gSC])
            p.dve(lambda v, jc=jc: v.tensor_scalar_mul(SC[:, 72:80], self.gcumT[:, jc:jc + 8],
                                                       -1.0),
                  reads=[g("gcumT")], writes=[gSC])
            kbtm, gkb = ab(6144, 7168)
            kbgtm, gkbg = ab(7168, 8192)
            vbtm, gvb = ab(9216, 10240)
            kntm, gknt = ab(10240, 10752)
            for u in range(8):
                h = u // 2
                p.dve(lambda v, u=u, h=h, ktm=ktm, kbtm=kbtm: v.tensor_scalar(
                    kbtm[:, u * 128:(u + 1) * 128], ktm[:, h * 128:(h + 1) * 128],
                    SC[:, 48 + u:49 + u], None, OP.mult),
                    reads=[gkt, gSC], writes=[gkb])
                p.dve(lambda v, u=u, h=h, ktm=ktm, kbgtm=kbgtm: v.tensor_scalar(
                    kbgtm[:, u * 128:(u + 1) * 128], ktm[:, h * 128:(h + 1) * 128],
                    SC[:, 56 + u:57 + u], None, OP.mult),
                    reads=[gkt, gSC], writes=[gkbg])
                p.dve(lambda v, u=u, jc=jc, vtm=vtm, vbtm=vbtm: v.tensor_scalar(
                    vbtm[:, u * 128:(u + 1) * 128], vtm[:, u * 128:(u + 1) * 128],
                    self.betaT[:, jc + u:jc + u + 1], None, OP.mult),
                    reads=[gvt, g("betaT")], writes=[gvb])
            for h in range(4):
                p.dve(lambda v, h=h, ktm=ktm, kntm=kntm: v.tensor_scalar(
                    kntm[:, h * 128:(h + 1) * 128], ktm[:, h * 128:(h + 1) * 128],
                    SC[:, 4 + h:5 + h], None, OP.mult),
                    reads=[gkt, gSC], writes=[gknt])
            kbfm, gkbf = ab(10752, 11776)
            for u in range(8):
                p.pe(lambda t, u=u, kbtm=kbtm: t.transpose(
                    self.pT0[:, u * 128:(u + 1) * 128], kbtm[:, u * 128:(u + 1) * 128],
                    self.c_idb[:, :]),
                    reads=[gkb, g("c_idb")], writes=[g("pT0")])
            p.dve(lambda v, kbfm=kbfm: v.tensor_copy(kbfm, self.pT0[:, :]),
                  reads=[g("pT0")], writes=[gkbf])
            knfm, gknf = ab(11776, 12288)
            for h in range(4):
                p.pe(lambda t, h=h, kntm=kntm: t.transpose(
                    self.pT1[:, h * 128:(h + 1) * 128], kntm[:, h * 128:(h + 1) * 128],
                    self.c_idb[:, :]),
                    reads=[gknt, g("c_idb")], writes=[g("pT1")])
            p.dve(lambda v, knfm=knfm: v.tensor_copy(knfm, self.pT1[:, 0:512]),
                  reads=[g("pT1")], writes=[gknf])
            ctx["vbtm"], ctx["gvb"] = vbtm, gvb
            ctx["kbgtm"], ctx["gkbg"] = kbgtm, gkbg
            epre, gep = afv(0, 1024)
            ebc, gebc = afv(3584, 3712)
            for u in range(8):
                p.dve(lambda v, u=u, ebc=ebc: v.tensor_scalar(
                    ebc, self.c_ones128[:, :], SC[:, 72 + u:73 + u], None, OP.mult),
                    reads=[g("c_ones128"), gSC], writes=[gebc])
                p.pe(lambda t, u=u, ebc=ebc: t.transpose(
                    self.pB[:, u * 128:(u + 1) * 128], ebc, self.c_idf[:, :]),
                    reads=[gebc, g("c_idf")], writes=[g("pB0") if u < 4 else g("pB1")])
                p.dve(lambda v, u=u, jc=jc, epre=epre: v.tensor_scalar(
                    epre[:, u * 128:(u + 1) * 128], self.pB[:, u * 128:(u + 1) * 128],
                    self.gcumT[:, jc + u:jc + u + 1], 0.0, OP.add, OP.min),
                    reads=[g("pB0") if u < 4 else g("pB1"), g("gcumT")], writes=[gep])
            eexp, gee = ab(12288, 13312)
            p.act(lambda s, epre=epre, eexp=eexp: s.activation(eexp, epre, AF.Exp),
                  reads=[gep], writes=[gee])
            eneg, gen = ab(13312, 14336)
            p.dve(lambda v, eexp=eexp, eneg=eneg: v.tensor_tensor(
                eneg, eexp, self.c_sneg[:, :], OP.mult),
                reads=[gee, g("c_sneg")], writes=[gen])
            for u in range(8):
                p.pe(lambda t, u=u, kbfm=kbfm, knfm=knfm: t.matmul(
                    self.pA[:, u * 128:(u + 1) * 128], kbfm[:, u * 128:(u + 1) * 128],
                    knfm[:, (u // 2) * 128:(u // 2 + 1) * 128], start=True, stop=True),
                    reads=[gkbf, gknf], writes=[g("pA0") if u < 4 else g("pA1")])
            sl, gsl = ab(15360, 16384)
            p.dve(lambda v, sl=sl, eneg=eneg: v.tensor_tensor(sl, self.pA[:, :], eneg, OP.mult),
                  reads=[g("pA0"), g("pA1"), gen], writes=[gsl])
            st_, gst = ab(16384, 17408)
            for u in range(8):
                p.pe(lambda t, u=u, sl=sl: t.transpose(
                    self.pT0[:, u * 128:(u + 1) * 128], sl[:, u * 128:(u + 1) * 128],
                    self.c_idb[:, :]),
                    reads=[gsl, g("c_idb")], writes=[g("pT0")])
            p.dve(lambda v, st_=st_: v.tensor_copy(st_, self.pT0[:, :]),
                  reads=[g("pT0")], writes=[gst])
            eit, geit = ab(14336, 15360)
            for u in range(8):
                p.pe(lambda t, u=u, eneg=eneg: t.transpose(
                    self.pT1[:, u * 128:(u + 1) * 128], eneg[:, u * 128:(u + 1) * 128],
                    self.c_idb[:, :]),
                    reads=[gen, g("c_idb")], writes=[g("pT1")])
            p.dve(lambda v, eit=eit: v.tensor_tensor(eit, self.c_blkI[:, :], self.pT1[:, :],
                                                     OP.subtract),
                  reads=[g("c_blkI"), g("pT1")], writes=[geit])
            for u in range(8):
                p.pe(lambda t, u=u, knfm=knfm, qv=qv: t.matmul(
                    self.pB[:, u * 128:(u + 1) * 128], knfm[:, (u // 2) * 128:(u // 2 + 1) * 128],
                    qv[:, (u // 2) * 128:(u // 2 + 1) * 128], start=True, stop=True),
                    reads=[gknf, gq], writes=[g("pB0") if u < 4 else g("pB1")])
            intq, gint = ab(17408, 18432)
            p.dve(lambda v, intq=intq, eit=eit: v.scalar_tensor_tensor(
                intq, self.pB[:, :], RS128, eit, OP.mult, OP.mult),
                reads=[g("pB0"), g("pB1"), geit], writes=[gint])
            ctx["intq"], ctx["gint"] = intq, gint
            Pb = [ab(22528, 23552), ab(23552, 24576)]
            p.dve(lambda v, P0=Pb[0][0], st_=st_: v.tensor_tensor(
                P0, self.c_blkI[:, :], st_, OP.add),
                  reads=[g("c_blkI"), gst], writes=[Pb[0][1]])
            cur = 0
            spl, gspl = sl, gsl
            spt, gspt = st_, gst
            lv = [(ab(18432, 19456), ab(19456, 20480)), (ab(20480, 21504), ab(21504, 22528))]
            for lev in range(1, 7):
                for u in range(8):
                    p.pe(lambda t, u=u, spl=spl, spt=spt: t.matmul(
                        self.pA[:, u * 128:(u + 1) * 128], spt[:, u * 128:(u + 1) * 128],
                        spl[:, u * 128:(u + 1) * 128], start=True, stop=True),
                        reads=[gspl, gspt], writes=[g("pA0") if u < 4 else g("pA1")])
                (nl, gnl), (nt, gnt) = lv[lev % 2]
                p.act(lambda s, nl=nl: s.activation(nl, self.pA[:, :], AF.Copy),
                      reads=[g("pA0"), g("pA1")], writes=[gnl])
                if lev < 6:
                    for u in range(8):
                        p.pe(lambda t, u=u, spl=spl, spt=spt: t.matmul(
                            self.pC[:, u * 128:(u + 1) * 128], spl[:, u * 128:(u + 1) * 128],
                            spt[:, u * 128:(u + 1) * 128], start=True, stop=True),
                            reads=[gspl, gspt], writes=[g("pC0") if u < 4 else g("pC1")])
                    p.act(lambda s, nt=nt: s.activation(nt, self.pC[:, :], AF.Copy),
                          reads=[g("pC0"), g("pC1")], writes=[gnt])
                for u in range(8):
                    p.pe(lambda t, u=u, nl=nl, Pc=Pb[cur][0]: t.matmul(
                        self.pB[:, u * 128:(u + 1) * 128], nl[:, u * 128:(u + 1) * 128],
                        Pc[:, u * 128:(u + 1) * 128], start=True, stop=True),
                        reads=[gnl, Pb[cur][1]], writes=[g("pB0") if u < 4 else g("pB1")])
                p.dve(lambda v, Pn=Pb[1 - cur][0], Pc=Pb[cur][0]: v.tensor_tensor(
                    Pn[:, :], self.pB[:, :], Pc[:, :], OP.add),
                    reads=[g("pB0"), g("pB1"), Pb[cur][1]], writes=[Pb[1 - cur][1]])
                cur = 1 - cur
                spl, gspl = nl, gnl
                spt, gspt = nt, gnt
            Pf, gPf = Pb[cur]
            ctx["Pf"], ctx["gPf"] = Pf, gPf
            co2 = 96 + par * 48
            ctx["co2"] = co2
            p.act(lambda s, jc=jc, co2=co2: s.activation(
                SC[:, co2:co2 + 8], self.gcumT[:, jc:jc + 8], AF.Exp, bias=self.c_bias[:, 2:3]),
                  reads=[g("gcumT"), g("c_bias")], writes=[gSC])
            p.pe(lambda t, jc=jc: t.matmul(self.pC[0:1, 896:904], self.c_idf[:, 127:128],
                                           self.gcumT[:, jc:jc + 8], start=True, stop=True),
                 reads=[g("gcumT"), g("c_idf")], writes=[g("pC1")])
            p.dve(lambda v: v.tensor_copy(self.s_rows[0:1, 16:24], self.pC[0:1, 896:904]),
                  reads=[g("pC1")], writes=[g("s_rows")])
            p.pe(lambda t: t.matmul(self.pC[:, 904:912], self.c_onesf[:, :],
                                    self.s_rows[0:1, 16:24], start=True, stop=True),
                 reads=[g("s_rows"), g("c_onesf")], writes=[g("pC1")])
            p.act(lambda s, co2=co2: s.activation(SC[:, co2 + 8:co2 + 16], self.pC[:, 904:912],
                                                  AF.Exp),
                  reads=[g("pC1")], writes=[gSC])
            p.dve(lambda v, jc=jc, co2=co2: v.tensor_tensor(
                SC[:, co2 + 16:co2 + 24], self.pC[:, 904:912],
                self.gcumT[:, jc:jc + 8], OP.subtract),
                  reads=[g("pC1"), g("gcumT")], writes=[gSC])
            p.act(lambda s, co2=co2: s.activation(SC[:, co2 + 16:co2 + 24],
                                                  SC[:, co2 + 16:co2 + 24], AF.Exp),
                  reads=[gSC], writes=[gSC])
            p.dve(lambda v, co2=co2: v.tensor_tensor(SC[:, co2 + 24:co2 + 32],
                                                     SC[:, co2 + 16:co2 + 24],
                                                     SC[:, 8:16], OP.mult),
                  reads=[gSC], writes=[gSC])
            kdtm, gkd = ab(8192, 9216)
            for u in range(8):
                h = u // 2
                p.dve(lambda v, u=u, h=h, ktm=ktm, kdtm=kdtm, co2=co2: v.tensor_scalar(
                    kdtm[:, u * 128:(u + 1) * 128], ktm[:, h * 128:(h + 1) * 128],
                    SC[:, co2 + 24 + u:co2 + 25 + u], None, OP.mult),
                    reads=[gkt, gSC], writes=[gkd])
            ctx["kdtm"], ctx["gkd"] = kdtm, gkd
            nkcd, gnk = ab(24576, 25600)
            for u in range(8):
                p.pe(lambda t, u=u, kbgtm=kbgtm, Pf=Pf: t.matmul(
                    self.pB[:, u * 128:(u + 1) * 128], kbgtm[:, u * 128:(u + 1) * 128],
                    Pf[:, u * 128:(u + 1) * 128], start=True, stop=True),
                    reads=[gkbg, gPf], writes=[g("pB0") if u < 4 else g("pB1")])
            p.dve(lambda v, nkcd=nkcd: v.tensor_scalar_mul(nkcd, self.pB[:, :], -1.0),
                  reads=[g("pB0"), g("pB1")], writes=[gnk])
            ctx["nkcd"], ctx["gnk"] = nkcd, gnk
            return ctx

        def emit_post(ctx):
            j, jc, par, ab, afv = ctx["j"], ctx["jc"], ctx["par"], ctx["ab"], ctx["afv"]
            qv, gq = ctx["qv"], ctx["gq"]
            ktm, gkt = ctx["ktm"], ctx["gkt"]
            ztm, gzt = ctx["ztm"], ctx["gzt"]
            vbtm, gvb = ctx["vbtm"], ctx["gvb"]
            Pf, gPf = ctx["Pf"], ctx["gPf"]
            nkcd, gnk = ctx["nkcd"], ctx["gnk"]
            intq, gint = ctx["intq"], ctx["gint"]
            co2 = ctx["co2"]
            kdtm, gkd = ctx["kdtm"], ctx["gkd"]
            for u in range(8):
                p.pe(lambda t, u=u, Pf=Pf, vbtm=vbtm: t.matmul(
                    self.pA[:, u * 128:(u + 1) * 128], Pf[:, u * 128:(u + 1) * 128],
                    vbtm[:, u * 128:(u + 1) * 128], start=True, stop=False),
                    reads=[gPf, gvb], writes=[g("pA0") if u < 4 else g("pA1")])
                p.pe(lambda t, u=u, nkcd=nkcd: t.matmul(
                    self.pA[:, u * 128:(u + 1) * 128], nkcd[:, u * 128:(u + 1) * 128],
                    self.s_stb[:, u * 128:(u + 1) * 128], start=False, stop=True),
                    reads=[gnk, g("s_stb")], writes=[g("pA0") if u < 4 else g("pA1")])
            vnew, gvn = ab(25600, 26624)
            p.dve(lambda v, vnew=vnew: v.tensor_copy(vnew, self.pA[:, :]),
                  reads=[g("pA0"), g("pA1")], writes=[gvn])
            interf, gitf = afv(1024, 2048)
            for u in range(8):
                p.pe(lambda t, u=u, qv=qv: t.matmul(
                    self.pB[:, u * 128:(u + 1) * 128], qv[:, (u // 2) * 128:(u // 2 + 1) * 128],
                    self.s_stb[:, u * 128:(u + 1) * 128], start=True, stop=True),
                    reads=[gq, g("s_stb")], writes=[g("pB0") if u < 4 else g("pB1")])
            for u in range(8):
                p.act(lambda s, u=u, interf=interf, co2=co2: s.activation(
                    interf[:, u * 128:(u + 1) * 128], self.pB[:, u * 128:(u + 1) * 128],
                    AF.Copy, scale=SC[:, co2 + u:co2 + u + 1]),
                    reads=[g("pB0") if u < 4 else g("pB1"), gSC], writes=[gitf])
            for u in range(8):
                p.pe(lambda t, u=u, intq=intq, vnew=vnew: t.matmul(
                    self.pC[:, u * 128:(u + 1) * 128], intq[:, u * 128:(u + 1) * 128],
                    vnew[:, u * 128:(u + 1) * 128], start=True, stop=True),
                    reads=[gint, gvn], writes=[g("pC0") if u < 4 else g("pC1")])
            outc, goc = afv(2048, 3072)
            p.dve(lambda v, outc=outc, interf=interf: v.tensor_tensor(
                outc, self.pC[:, :], interf, OP.add),
                reads=[g("pC0"), g("pC1"), gitf], writes=[goc])
            for u in range(8):
                p.pe(lambda t, u=u, kdtm=kdtm, vnew=vnew: t.matmul(
                    self.pA[:, u * 128:(u + 1) * 128], kdtm[:, u * 128:(u + 1) * 128],
                    vnew[:, u * 128:(u + 1) * 128], start=True, stop=True),
                    reads=[gkd, gvn], writes=[g("pA0") if u < 4 else g("pA1")])
            for u in range(8):
                p.dve(lambda v, u=u, co2=co2: v.tensor_scalar(
                    self.s_st[:, u * 128:(u + 1) * 128], self.s_st[:, u * 128:(u + 1) * 128],
                    SC[:, co2 + 8 + u:co2 + 9 + u], None, OP.mult),
                    reads=[gSC], writes=[g("s_st")])
            p.dve(lambda v: v.tensor_tensor(self.s_st[:, :], self.s_st[:, :], self.pA[:, :],
                                            OP.add),
                  reads=[g("pA0"), g("pA1")], writes=[g("s_st")])
            p.dve(lambda v: v.tensor_copy(self.s_stb[:, :], self.s_st[:, :]),
                  reads=[g("s_st")], writes=[g("s_stb")])
            zsil, gzs = ab(26624, 27648)
            p.act(lambda s, ztm=ztm, zsil=zsil: s.activation(zsil, ztm, AF.Silu),
                  reads=[gzt], writes=[gzs])
            xg, gxg = afv(1024, 2048)
            p.dve(lambda v, outc=outc, zsil=zsil, xg=xg: v.tensor_tensor(
                xg, outc, zsil, OP.mult),
                reads=[goc, gzs], writes=[gxg])
            sqb, gsq = ab(27648, 28672)
            p.act(lambda s, xg=xg, sqb=sqb: s.activation(sqb, xg, AF.Square),
                  reads=[gxg], writes=[gsq])
            p.dve(lambda v, sqb=sqb: v.reduce_sum(
                SC[:, 80:88], sqb.rearrange("p (u l) -> p u l", l=128), axis=AX.X),
                reads=[gsq], writes=[gSC])
            p.dve(lambda v: v.tensor_scalar(SC[:, 88:96], SC[:, 80:88], 1.0 / 128.0, 1e-6,
                                            OP.mult, OP.add),
                  reads=[gSC], writes=[gSC])
            p.act(lambda s: s.activation(SC[:, 88:96], SC[:, 88:96], AF.Sqrt),
                  reads=[gSC], writes=[gSC])
            p.dve(lambda v: v.reciprocal(SC[:, 88:96], SC[:, 88:96]),
                  reads=[gSC], writes=[gSC])
            ntm, gnt2 = ab(28672, 29696)
            for u in range(8):
                p.dve(lambda v, u=u, xg=xg, ntm=ntm: v.tensor_scalar(
                    ntm[:, u * 128:(u + 1) * 128], xg[:, u * 128:(u + 1) * 128],
                    SC[:, 88 + u:89 + u], None, OP.mult),
                    reads=[gxg, gSC], writes=[gnt2])
            nst, gnst = ab(29696, 30720)
            for u in range(8):
                p.pe(lambda t, u=u, ntm=ntm: t.transpose(
                    self.pT0[:, u * 128:(u + 1) * 128], ntm[:, u * 128:(u + 1) * 128],
                    self.c_idb[:, :]),
                    reads=[gnt2, g("c_idb")], writes=[g("pT0")])
            p.dve(lambda v, nst=nst: v.tensor_scalar(
                nst, self.pT0[:, :], self.c_normw[:, :], None, OP.mult),
                reads=[g("pT0"), g("c_normw")], writes=[gnst])
            dst = self.nrm_sc.ap()[:, j * 128:(j + 1) * 128].rearrange(
                "(u p) l -> p u l", p=128)
            p.pdma(lambda s, dst=dst, nst=nst: s.dma_start(
                out=dst, in_=nst.rearrange("p (u l) -> p u l", l=128)),
                   reads=[gnst], writes=[g("d_nrm")])

        ctx = emit_pre(0)
        for j in range(J):
            nxt = emit_pre(j + 1) if j + 1 < J else None
            emit_post(ctx)
            ctx = nxt
        if self.dev_dbg:
            p.pdma(lambda gg: gg.dma_start(out=self.dbg["dbg_nrm"][:, :],
                                           in_=self.nrm_sc.ap()[:, :]),
                   reads=[g("d_nrm")], writes=[g("dbg")])

    def b6_wout(self):
        p, g = self.p, self.g
        T, Q = self.T, self.Q
        wo, gwo = self.AW.view(0, 16384)
        for vdc in range(8):
            src = self.wout[vdc * 128:(vdc + 1) * 128, :]
            p.pdma(lambda gg, src=src, vdc=vdc, wo=wo: gg.dma_start(
                out=wo[:, vdc * 2048:(vdc + 1) * 2048], in_=src), writes=[gwo])
        for tt in range(T // 512):
            par = tt % 2
            t0 = tt * 512
            q, lc = t0 // Q, t0 % Q
            vn, gvn = self.ABF.view(par * 4096, par * 4096 + 4096)
            src = self.nrm_sc.ap()[:, t0:t0 + 512].rearrange("(c p) t -> p c t", p=128)
            p.hdma(lambda s, src=src, vn=vn: s.dma_start(
                out=vn.rearrange("p (c t) -> p c t", t=512), in_=src),
                   writes=[gvn], extra=[g("d_nrm").w])
            yst, gyst = self.AFA.view(0, 8192)
            for dc in range(16):
                half = dc % 2
                for vdc in range(8):
                    p.pe(lambda t, dc=dc, vdc=vdc, half=half, vn=vn, wo=wo: t.matmul(
                        self.pA[:, half * 512:half * 512 + 512],
                        wo[:, vdc * 2048 + dc * 128: vdc * 2048 + (dc + 1) * 128],
                        vn[:, vdc * 512:(vdc + 1) * 512], start=(vdc == 0), stop=(vdc == 7)),
                        reads=[gwo, gvn], writes=[g("pA0") if half == 0 else g("pA1")])
                p.dve(lambda v, dc=dc, half=half, yst=yst: v.tensor_copy(
                    yst[:, dc * 512:(dc + 1) * 512], self.pA[:, half * 512:half * 512 + 512]),
                    reads=[g("pA0") if half == 0 else g("pA1")], writes=[gyst])
            dst = self.ps_in.ap()[q * 2048:(q + 1) * 2048, lc:lc + 512].rearrange(
                "(c p) t -> p c t", p=128)
            p.pdma(lambda s, dst=dst, yst=yst: s.dma_start(
                out=dst, in_=yst.rearrange("p (c t) -> p c t", t=512)),
                   reads=[gyst], writes=[g("d_ps")])
        p.cc(lambda gg: gg.collective_compute(
            "ReduceScatter", OP.add, replica_groups=[[0, 1, 2, 3], [4, 5, 6, 7]],
            ins=[self.ps_in.ap().opt()], outs=[self.s_own.ap().opt()]),
            reads=[g("d_ps")], writes=[g("d_s_own")])
        if self.dev_dbg:
            p.pdma(lambda gg: gg.dma_start(out=self.dbg["dbg_s"][:, :],
                                           in_=self.s_own.ap()[:, :]),
                   reads=[g("d_s_own")], writes=[g("dbg")])

    def b9_combine(self):
        p, g = self.p, self.g
        Q = self.Q
        for tt in range(Q // 512):
            t0 = tt * 512
            for dc in range(16):
                i = (tt * 16 + dc) % 2
                vx, gx = self.AFA.view(i * 2048, i * 2048 + 512)
                vs, gs = self.AFA.view(i * 2048 + 512, i * 2048 + 1024)
                vf, gf = self.AFA.view(i * 2048 + 1024, i * 2048 + 1536)
                vo, go = self.AFA.view(i * 2048 + 1536, i * 2048 + 2048)
                p.adma(lambda s, dc=dc, t0=t0, vx=vx: s.dma_start(
                    out=vx, in_=self.x_fm[dc * 128:(dc + 1) * 128, t0:t0 + 512]),
                    writes=[gx])
                p.adma(lambda s, dc=dc, t0=t0, vs=vs: s.dma_start(
                    out=vs, in_=self.s_own.ap()[dc * 128:(dc + 1) * 128, t0:t0 + 512]),
                    writes=[gs], extra=[g("d_s_own").w])
                p.adma(lambda s, dc=dc, t0=t0, vf=vf: s.dma_start(
                    out=vf, in_=self.ffn_own.ap()[dc * 128:(dc + 1) * 128, t0:t0 + 512]),
                    writes=[gf], extra=[g("d_ffn_own").w])
                p.act(lambda s, vf=vf: s.activation(vf, vf, AF.Silu),
                      reads=[gf], writes=[gf])
                p.dve(lambda v, vo=vo, vs=vs, vf=vf: v.tensor_tensor(vo, vs, vf, OP.mult),
                      reads=[gs, gf], writes=[go])
                p.dve(lambda v, vo=vo, vx=vx: v.tensor_tensor(vo, vo, vx, OP.add),
                      reads=[gx], writes=[go])
                p.hdma(lambda s, dc=dc, t0=t0, vo=vo: s.dma_start(
                    out=self.out_fm[dc * 128:(dc + 1) * 128, t0:t0 + 512], in_=vo),
                    reads=[go], writes=[g("d_out")])

    def build(self):
        self.declare()
        with ExitStack() as es:
            self.alloc(es)
            self.b0_consts()
            self.b1_ln_router()
            self.b2_collectives()
            self.b7_ffn()
            self.b3_proj()
            self.b4_scan()
            self.b6_wout()
            self.b9_combine()
            with ExitStack() as es2:
                sems = {k: es2.enter_context(self.nc.semaphore("sem_" + k))
                        for k in Prog.SEMS}
                blk = es2.enter_context(self.nc.Block("mk"))
                self.p.emit(blk, sems, pe_fifo=self.pe_fifo)
                final = dict(self.p.cnt)

                def tail(eng):
                    for k, vv in final.items():
                        if vv:
                            eng.wait_ge(sems[k], vv)
                blk.sync(tail)
        return self.nc



def _host_inputs(inputs, T):
    Q = T // 4
    x = np.asarray(inputs["L0_attn_hidden"])
    Wqkv = np.asarray(inputs["L0_attn_W_qkv"])
    Wz = np.asarray(inputs["L0_attn_W_z"])
    Wb = np.asarray(inputs["L0_attn_W_b"])
    Wa = np.asarray(inputs["L0_attn_W_a"])
    cw = np.asarray(inputs["L0_attn_conv_weight"])
    dtbias = np.asarray(inputs["L0_attn_dt_bias"])
    Alog = np.asarray(inputs["L0_attn_A_log"])
    nw = np.asarray(inputs["L0_attn_norm_w"])
    Wout = np.asarray(inputs["L0_attn_W_out"])
    gw = np.asarray(inputs["L0_ffn_gate_w"])
    rb = np.asarray(inputs["L0_ffn_router_bias"])
    W1 = np.asarray(inputs["L0_ffn_W1"])
    W3 = np.asarray(inputs["L0_ffn_W3"])
    W2 = np.asarray(inputs["L0_ffn_W2"])
    sw1 = np.asarray(inputs["L0_ffn_sw1"])
    sw3 = np.asarray(inputs["L0_ffn_sw3"])
    sw2 = np.asarray(inputs["L0_ffn_sw2"])

    Xf = np.ascontiguousarray(x.reshape(2 * T, D))
    idf = np.eye(128, dtype=np.float32)
    idb = idf.astype(BFNP)
    onesb = np.ones((1, 128), np.float32).astype(BFNP)
    onebc = np.ones((128, 1), np.float32).astype(BFNP)
    cbias = np.tile(np.array([[1e-5, 1.0, -0.5 * np.log(128.0), 1e-6]], np.float32), (128, 1))
    onesf = np.ones((1, 128), np.float32)
    sneg1 = (-np.tril(np.ones((128, 128), np.float32), -1)).astype(BFNP)
    sneg = np.concatenate([sneg1] * VH, axis=1)
    blkI = np.concatenate([idb] * VH, axis=1)
    gw128 = np.ascontiguousarray(
        gw.T.reshape(16, 128, 8).transpose(1, 0, 2).reshape(128, 128)).astype(np.float32)

    maps = []
    for c in range(8):
        b = c // 4
        hq0 = (c % 4) * QH
        qcols = np.arange(hq0 * 128, (hq0 + QH) * 128)
        kcols = 2048 + qcols
        vcols = 4096 + np.arange(hq0 * 2 * 128, (hq0 + QH) * 2 * 128)
        cols = np.concatenate([qcols, kcols, vcols])
        vh = np.arange(hq0 * 2, (hq0 + QH) * 2)
        vdcols = np.arange(vh[0] * 128, (vh[-1] + 1) * 128)

        x_fm = np.ascontiguousarray(Xf[c * Q:(c + 1) * Q].T)
        wqkv_s = np.ascontiguousarray(Wqkv[:, cols])
        wz_s = np.ascontiguousarray(Wz[:, vdcols])
        wb128 = np.ascontiguousarray(
            Wb[:, vh].reshape(16, 128, VH).transpose(1, 0, 2).reshape(128, 128))
        wa128 = np.ascontiguousarray(
            Wa[:, vh].reshape(16, 128, VH).transpose(1, 0, 2).reshape(128, 128))
        convw_s = np.ascontiguousarray(
            cw[cols].reshape(16, 128, 4).transpose(1, 0, 2).reshape(128, 64))
        m = dict(
            x_fm=x_fm, wqkv=wqkv_s, wz=wz_s, wb128=wb128, wa128=wa128,
            convw=convw_s,
            alog=np.ascontiguousarray(Alog[vh].reshape(VH, 1)),
            dtb=np.ascontiguousarray(dtbias[vh].reshape(VH, 1)),
            normw=np.ascontiguousarray(nw.reshape(128, 1)),
            wout=np.ascontiguousarray(Wout[vdcols, :]),
            gw128=gw128, rbias=np.ascontiguousarray(rb.reshape(1, 8)),
            w1=np.ascontiguousarray(W1[c]), w3=np.ascontiguousarray(W3[c]),
            w2=np.ascontiguousarray(W2[c]),
            sw1T=np.ascontiguousarray(sw1.T[:, c * SHS:(c + 1) * SHS]),
            sw3T=np.ascontiguousarray(sw3.T[:, c * SHS:(c + 1) * SHS]),
            sw2T=np.ascontiguousarray(sw2.T[c * SHS:(c + 1) * SHS, :]),
            idf=idf, idb=idb, onesb=onesb, onebc=onebc, onesf=onesf, sneg=sneg, blkI=blkI,
            cbias=cbias,
        )
        maps.append(m)
    return maps


_NC_CACHE = {}


def kernel(**inputs):
    T = int(np.asarray(inputs["L0_attn_hidden"]).shape[1])
    key = ("nc", T)
    if key not in _NC_CACHE:
        _NC_CACHE[key] = Builder(T).build()
    nc = _NC_CACHE[key]
    maps = _host_inputs(inputs, T)
    res = run_bass_kernel_spmd(nc, maps, core_ids=list(range(8)))
    Q = T // 4
    full = np.concatenate([np.asarray(res.results[c]["out_fm"]) for c in range(8)], axis=1)
    return np.ascontiguousarray(full.T).reshape(2, T, D).astype(np.float32)


# version 09
# speedup vs baseline: 1.1252x; 1.0505x over previous
import math
from contextlib import ExitStack

import numpy as np
import ml_dtypes

import concourse.bass as bass
import concourse.mybir as mybir
from concourse.bass_utils import run_bass_kernel_spmd
from concourse.aot_env import allow_ici_all_to_all

F32, BF16 = mybir.dt.float32, mybir.dt.bfloat16
AF = mybir.ActivationFunctionType
OP = mybir.AluOpType
AX = mybir.AxisListType
BFNP = ml_dtypes.bfloat16

D = 2048
QH, VH = 4, 8
C = 128
INTER = 1408
SHS = 352
LN128 = float(np.log(128.0))
RS128 = float(1.0 / np.sqrt(128.0))


class Guard:
    __slots__ = ("w", "rs")
    def __init__(self, init=()):
        self.w = None
        self.rs = list(init)


class Arena:
    def __init__(self, ap_full):
        self.ap = ap_full
        self.ivs = []

    def view(self, a, b):
        for (x, y, gd) in self.ivs:
            if x == a and y == b:
                return self.ap[:, a:b], gd
        deps, keep = [], []
        for (x, y, gd) in self.ivs:
            if x < b and a < y:
                if gd.w is not None:
                    deps.append(gd.w)
                deps.extend(gd.rs)
            else:
                keep.append((x, y, gd))
        gnew = Guard(deps)
        keep.append((a, b, gnew))
        self.ivs = keep
        return self.ap[:, a:b], gnew


class Prog:
    SECTIONS = ("tensor", "vector", "scalar", "sync", "gpsimd")
    COMPUTE_SEMS = ("pe", "dve", "act", "pool")
    DMA_POOL = {"hdma": 8, "adma": 8, "pdma": 8, "cc": 4}
    SEMS = COMPUTE_SEMS + tuple(
        f"{c}{i}" for c, n in DMA_POOL.items() for i in range(n))

    def __init__(self):
        self.q = {s: [] for s in self.SECTIONS}
        self.cnt = {k: 0 for k in self.SEMS}
        self.rr = {c: 0 for c in self.DMA_POOL}

    def _op(self, section, sem, fn, reads, writes, extra, k):
        deps = list(extra)
        if sem in self.DMA_POOL:
            i = self.rr[sem]
            self.rr[sem] = (i + 1) % self.DMA_POOL[sem]
            sub = f"{sem}{i}"
            if self.cnt[sub]:
                deps.append((sub, self.cnt[sub]))
            sem = sub
        for gd in reads:
            if gd.w is not None:
                deps.append(gd.w)
        for gd in writes:
            if gd.w is not None:
                deps.append(gd.w)
            deps.extend(gd.rs)
        self.cnt[sem] += k
        tok = (sem, self.cnt[sem])
        self.q[section].append((fn, sem, k, tuple(d for d in deps if d is not None)))
        for gd in reads:
            gd.rs.append(tok)
        for gd in writes:
            gd.w = tok
            gd.rs = []
        return tok

    def pe(self, fn, reads=(), writes=(), extra=()):
        return self._op("tensor", "pe", fn, reads, writes, extra, 1)

    def dve(self, fn, reads=(), writes=(), extra=()):
        return self._op("vector", "dve", fn, reads, writes, extra, 1)

    def act(self, fn, reads=(), writes=(), extra=()):
        return self._op("scalar", "act", fn, reads, writes, extra, 1)

    def pool(self, fn, reads=(), writes=(), extra=()):
        return self._op("gpsimd", "pool", fn, reads, writes, extra, 1)

    def hdma(self, fn, reads=(), writes=(), extra=()):
        return self._op("sync", "hdma", fn, reads, writes, extra, 16)

    def adma(self, fn, reads=(), writes=(), extra=()):
        return self._op("scalar", "adma", fn, reads, writes, extra, 16)

    def pdma(self, fn, reads=(), writes=(), extra=()):
        return self._op("gpsimd", "pdma", fn, reads, writes, extra, 16)

    def cc(self, fn, reads=(), writes=(), extra=()):
        return self._op("gpsimd", "cc", fn, reads, writes, extra, 1)

    def emit(self, blk, sems, pe_fifo=True):
        for section in self.SECTIONS:
            ops = self.q[section]
            if not ops:
                continue

            def body(eng, ops=ops, section=section):
                water = {k: 0 for k in self.cnt}
                for fn, sem, k, deps in ops:
                    for dsem, dval in deps:
                        if pe_fifo and dsem == "pe" and section == "tensor":
                            continue
                        if water[dsem] < dval:
                            eng.wait_ge(sems[dsem], dval)
                            water[dsem] = dval
                    fn(eng).then_inc(sems[sem], k)
            getattr(blk, section)(body)


class Builder:
    def __init__(self, T, dev_dbg=False, pe_fifo=True):
        self.pe_fifo = pe_fifo
        self.T = T
        self.Q = T // 4
        self.J = T // C
        self.dev_dbg = dev_dbg
        self.nc = bass.Bass("TRN2", target_bir_lowering=False)
        self.p = Prog()
        self.G = {}

    def g(self, name):
        if name not in self.G:
            self.G[name] = Guard()
        return self.G[name]

    @staticmethod
    def ex_u(colap, n=8, rep=128):
        return bass.AP(colap.tensor, colap.offset,
                       [list(colap.ap[0]), [1, n], [0, rep]])

    @staticmethod
    def ex_pair(ap2d, nh=4, r=2, inner=128):
        return bass.AP(ap2d.tensor, ap2d.offset,
                       [list(ap2d.ap[0]), [inner, nh], [0, r], [1, inner]])

    def declare(self):
        nc, T, Q = self.nc, self.T, self.Q
        din = lambda n, s, dt=F32: nc.dram_tensor(n, s, dt, kind="ExternalInput").ap()
        self.x_fm = din("x_fm", [D, Q])
        self.wqkv = din("wqkv", [D, 2048])
        self.wz = din("wz", [D, 1024])
        self.wb128 = din("wb128", [128, 128])
        self.wa128 = din("wa128", [128, 128])
        self.convw = din("convw", [128, 64])
        self.alog = din("alog", [VH, 1])
        self.dtb = din("dtb", [VH, 1])
        self.normw = din("normw", [128, 1])
        self.wout = din("wout", [VH * 128, D])
        self.gw128 = din("gw128", [128, 128])
        self.rbias = din("rbias", [1, 8])
        self.w1 = din("w1", [D, INTER])
        self.w3 = din("w3", [D, INTER])
        self.w2 = din("w2", [INTER, D])
        self.sw1T = din("sw1T", [D, SHS])
        self.sw3T = din("sw3T", [D, SHS])
        self.sw2T = din("sw2T", [SHS, D])
        self.idf_d = din("idf", [128, 128])
        self.idb_d = din("idb", [128, 128], BF16)
        self.onesb_d = din("onesb", [1, 128], BF16)
        self.onebc_d = din("onebc", [128, 1], BF16)
        self.cbias_d = din("cbias", [128, 4])
        self.onesf_d = din("onesf", [1, 128])
        self.sneg_d = din("sneg", [128, VH * 128], BF16)
        self.blkI_d = din("blkI", [128, VH * 128], BF16)

        dr = lambda n, s, dt=F32: nc.dram_tensor(n, s, dt)
        self.xn_blk = dr("xn_blk", [D, Q], BF16)
        self.xn_bat = dr("xn_bat", [4 * D, Q], BF16)
        self.xn_all = dr("xn_all", [8 * D, Q], BF16)
        self.gates_in = dr("gates_in", [8, Q], BF16)
        self.gates_x = dr("gates_x", [8, Q], BF16)
        self.qn_sc = dr("qn_sc", [QH * 128, T], BF16)
        self.kn_sc = dr("kn_sc", [QH * 128, T], BF16)
        self.v_sc = dr("v_sc", [VH * 128, T], BF16)
        self.z_sc = dr("z_sc", [VH * 128, T], BF16)
        self.nrm_sc = dr("nrm_sc", [VH * 128, T], BF16)
        self.ps_in = dr("ps_in", [4 * D, Q], F32)
        self.s_own = dr("s_own", [D, Q], F32)
        self.y_in = dr("y_in", [8 * D, Q], F32)
        self.ffn_own = dr("ffn_own", [D, Q], F32)
        self.out_fm = nc.dram_tensor("out_fm", [D, Q], F32, kind="ExternalOutput").ap()

        self.dbg = {}
        if self.dev_dbg:
            for nm, shape, dt in [
                ("dbg_xn", [D, Q], BF16), ("dbg_gates", [8, Q], BF16),
                ("dbg_q", [QH * 128, T], BF16), ("dbg_k", [QH * 128, T], BF16),
                ("dbg_v", [VH * 128, T], BF16), ("dbg_z", [VH * 128, T], BF16),
                ("dbg_bt", [128, self.J * 8], F32), ("dbg_gc", [128, self.J * 8], F32),
                ("dbg_nrm", [VH * 128, T], BF16), ("dbg_s", [D, Q], F32),
                ("dbg_ffn", [D, Q], F32),
            ]:
                self.dbg[nm] = nc.dram_tensor(nm, shape, dt, kind="ExternalOutput").ap()

    def alloc(self, es):
        nc = self.nc
        sb = lambda n, s, dt: es.enter_context(nc.sbuf_tensor(n, s, dt))
        ps = lambda n, s, dt: es.enter_context(nc.psum_tensor(n, s, dt))
        self.AW = Arena(sb("AW", [128, 36864], BF16))
        self.ABF = Arena(sb("ABF", [128, 31744], BF16))
        self.AFA = Arena(sb("AFA", [128, 8192], F32))
        self.gcumT = sb("gcumT", [128, self.J * 8], F32)
        self.betaT = sb("betaT", [128, self.J * 8], F32)
        self.gates8 = sb("gates8", [8, self.Q], BF16)
        self.c_idf = sb("c_idf", [128, 128], F32)
        self.c_idb = sb("c_idb", [128, 128], BF16)
        self.c_onesb = sb("c_onesb", [1, 128], BF16)
        self.c_onebc = sb("c_onebc", [128, 1], BF16)
        self.c_bias = sb("c_bias", [128, 4], F32)
        self.c_onesf = sb("c_onesf", [1, 128], F32)
        self.c_ones128 = sb("c_ones128", [128, 128], F32)
        self.c_sneg = sb("c_sneg", [128, VH * 128], BF16)
        self.c_blkI = sb("c_blkI", [128, VH * 128], BF16)
        self.c_convw = sb("c_convw", [128, 64], F32)
        self.c_normw = sb("c_normw", [128, 1], F32)
        self.c_alog = sb("c_alog", [VH, 1], F32)
        self.c_dtb = sb("c_dtb", [VH, 1], F32)
        self.c_rbias = sb("c_rbias", [128, 8], F32)
        self.c_nalog = sb("c_nalog", [VH, 1], F32)
        self.c_zero8 = sb("c_zero8", [8, 1], F32)
        self.s_gw = sb("s_gw", [128, 128], F32)
        self.s_wba = sb("s_wba", [128, 256], BF16)
        self.s_cols = sb("s_cols", [128, 256], F32)
        self.s_rows = sb("s_rows", [1, 1536], F32)
        self.s_rowb = sb("s_rowb", [1, 1024], BF16)
        self.s_st = sb("s_st", [128, 1024], F32)
        self.s_stb = sb("s_stb", [128, 1024], BF16)
        self.pA = ps("pA", [128, 1024], F32)
        self.pB = ps("pB", [128, 1024], F32)
        self.pC = ps("pC", [128, 1024], F32)
        self.pT0 = ps("pT0", [128, 1024], BF16)
        self.pT1 = ps("pT1", [128, 1024], BF16)
        for n in ("pA0", "pA1", "pB0", "pB1", "pC0", "pC1", "pT0", "pT1"):
            self.g(n)

    def b0_consts(self):
        p, g = self.p, self.g
        pairs = [
            (self.c_idf, self.idf_d, "c_idf"), (self.c_idb, self.idb_d, "c_idb"),
            (self.c_onesb, self.onesb_d, "c_onesb"), (self.c_onesf, self.onesf_d, "c_onesf"),
            (self.c_onebc, self.onebc_d, "c_onebc"),
            (self.c_bias, self.cbias_d, "c_bias"),
            (self.c_sneg, self.sneg_d, "c_sneg"), (self.c_blkI, self.blkI_d, "c_blkI"),
            (self.c_convw, self.convw, "c_convw"), (self.c_normw, self.normw, "c_normw"),
            (self.c_alog, self.alog, "c_alog"), (self.c_dtb, self.dtb, "c_dtb"),
            (self.s_gw, self.gw128, "s_gw"),
        ]
        for dst, src, gn in pairs:
            p.pdma(lambda gg, dst=dst, src=src: gg.dma_start(out=dst[:, :], in_=src[:, :]),
                   writes=[g(gn)])
        p.pdma(lambda gg: gg.dma_start(out=self.s_wba[:, 0:128], in_=self.wb128[:, :]),
               writes=[g("s_wba")])
        p.pdma(lambda gg: gg.dma_start(out=self.s_wba[:, 128:256], in_=self.wa128[:, :]),
               writes=[g("s_wba")])
        p.pool(lambda gg: gg.memset(self.c_ones128[:, :], 1.0), writes=[g("c_ones128")])
        p.pool(lambda gg: gg.memset(self.c_zero8[:, :], 0.0), writes=[g("c_zero8")])
        p.pool(lambda gg: gg.memset(self.s_st[:, :], 0.0), writes=[g("s_st")])
        p.pool(lambda gg: gg.memset(self.s_stb[:, :], 0.0), writes=[g("s_stb")])
        p.pdma(lambda gg: gg.dma_start(out=self.s_rows[0:1, 0:8], in_=self.rbias[:, :]),
               writes=[g("s_rows")])
        p.pe(lambda t: t.matmul(self.pC[0:128, 0:8], self.c_onesf[:, :],
                                self.s_rows[0:1, 0:8], start=True, stop=True),
             reads=[g("c_onesf"), g("s_rows")], writes=[g("pC0")])
        p.dve(lambda v: v.tensor_copy(self.c_rbias[:, :], self.pC[0:128, 0:8]),
              reads=[g("pC0")], writes=[g("c_rbias")])
        p.act(lambda s: s.activation(self.c_nalog[:, :], self.c_alog[:, :], AF.Exp),
              reads=[g("c_alog")], writes=[g("c_nalog")])
        p.dve(lambda v: v.tensor_scalar_mul(self.c_nalog[:, :], self.c_nalog[:, :], -1.0),
              reads=[g("c_nalog")], writes=[g("c_nalog")])

    def b1_ln_router(self):
        p, g = self.p, self.g
        B1T = self.Q // 512
        for tt in range(B1T):
            t0 = tt * 512
            vx, gx = self.AFA.view(0, 8192)
            vxb, gxb = self.ABF.view(0, 8192)
            vx2, gx2 = self.ABF.view(8192, 16384)
            vxnb, gxnb = self.ABF.view(16384, 24576)
            src = self.x_fm[:, t0:t0 + 512].rearrange("(c p) t -> p c t", p=128)
            p.hdma(lambda s, src=src, vx=vx: s.dma_start(
                out=vx.rearrange("p (c t) -> p c t", t=512), in_=src), writes=[gx])
            p.pdma(lambda gg, src=src, vxb=vxb: gg.dma_start(
                out=vxb.rearrange("p (c t) -> p c t", t=512), in_=src), writes=[gxb])
            p.act(lambda s, vx=vx, vx2=vx2: s.activation(vx2, vx, AF.Square),
                  reads=[gx], writes=[gx2])
            for dc in range(16):
                p.pe(lambda t, dc=dc, vxb=vxb: t.matmul(
                    self.pC[0:1, 0:512], self.c_onebc[:, :],
                    vxb[:, dc * 512:(dc + 1) * 512], start=(dc == 0), stop=(dc == 15)),
                    reads=[gxb, g("c_onebc")], writes=[g("pC0")])
            for dc in range(16):
                p.pe(lambda t, dc=dc, vx2=vx2: t.matmul(
                    self.pC[0:1, 512:1024], self.c_onebc[:, :],
                    vx2[:, dc * 512:(dc + 1) * 512], start=(dc == 0), stop=(dc == 15)),
                    reads=[gx2, g("c_onebc")], writes=[g("pC1")])
            R, gR = self.s_rows, g("s_rows")
            p.dve(lambda v: v.tensor_scalar_mul(R[0:1, 0:512], self.pC[0:1, 0:512], 1.0 / D),
                  reads=[g("pC0")], writes=[gR])
            p.dve(lambda v: v.tensor_scalar_mul(R[0:1, 512:1024], self.pC[0:1, 512:1024], 1.0 / D),
                  reads=[g("pC1")], writes=[gR])
            p.dve(lambda v: v.tensor_tensor(R[0:1, 1024:1536], R[0:1, 0:512], R[0:1, 0:512], OP.mult),
                  reads=[gR], writes=[gR])
            p.dve(lambda v: v.tensor_tensor(R[0:1, 512:1024], R[0:1, 512:1024],
                                            R[0:1, 1024:1536], OP.subtract),
                  reads=[gR], writes=[gR])
            p.act(lambda s: s.activation(R[0:1, 512:1024], R[0:1, 512:1024], AF.Sqrt,
                                         bias=self.c_bias[0:1, 0:1]),
                  reads=[gR, g("c_bias")], writes=[gR])
            p.dve(lambda v: v.reciprocal(R[0:1, 512:1024], R[0:1, 512:1024]),
                  reads=[gR], writes=[gR])
            p.pe(lambda t: t.matmul(self.pA[:, 0:512], self.c_onesf[:, :], R[0:1, 0:512],
                                    start=True, stop=True),
                 reads=[gR, g("c_onesf")], writes=[g("pA0")])
            p.pe(lambda t: t.matmul(self.pA[:, 512:1024], self.c_onesf[:, :], R[0:1, 512:1024],
                                    start=True, stop=True),
                 reads=[gR, g("c_onesf")], writes=[g("pA1")])
            mean_x = bass.AP(self.pA[:, 0:512].tensor, 0, [[1024, 128], [0, 16], [1, 512]])
            rstd_x = bass.AP(self.pA[:, 0:512].tensor, 512, [[1024, 128], [0, 16], [1, 512]])
            vx3 = vx.rearrange("p (c t) -> p c t", t=512)
            p.dve(lambda v, vx3=vx3, mean_x=mean_x: v.tensor_tensor(
                vx3, vx3, mean_x, OP.subtract), reads=[g("pA0")], writes=[gx])
            p.dve(lambda v, vx3=vx3, rstd_x=rstd_x: v.tensor_tensor(
                vx3, vx3, rstd_x, OP.mult), reads=[g("pA1")], writes=[gx])
            p.dve(lambda v, vx=vx, vxnb=vxnb: v.tensor_copy(vxnb, vx),
                  reads=[gx], writes=[gxnb])
            for sub in range(4):
                for dc in range(16):
                    p.pe(lambda t, sub=sub, dc=dc, vx=vx: t.matmul(
                        self.pB[:, sub * 8:sub * 8 + 8],
                        vx[:, dc * 512 + sub * 128: dc * 512 + sub * 128 + 128],
                        self.s_gw[:, dc * 8:(dc + 1) * 8],
                        start=(dc == 0), stop=(dc == 15)),
                        reads=[gx, g("s_gw")], writes=[g("pB0")])
            SC, gS = self.s_cols, g("s_cols")
            for sub in range(4):
                pg = self.pB[:, sub * 8:sub * 8 + 8]
                p.act(lambda s, pg=pg: s.activation(SC[:, 0:8], pg, AF.Sigmoid),
                      reads=[g("pB0")], writes=[gS])
                p.dve(lambda v: v.tensor_tensor(SC[:, 8:16], SC[:, 0:8], self.c_rbias[:, :], OP.add),
                      reads=[gS, g("c_rbias")], writes=[gS])
                p.dve(lambda v: v.reduce_max(SC[:, 16:17], SC[:, 8:16], axis=AX.X),
                      reads=[gS], writes=[gS])
                p.dve(lambda v: v.tensor_scalar(SC[:, 24:32], SC[:, 8:16], SC[:, 16:17],
                                                None, OP.is_ge),
                      reads=[gS], writes=[gS])
                p.dve(lambda v: v.scalar_tensor_tensor(SC[:, 32:40], SC[:, 24:32], -1e9,
                                                       SC[:, 8:16], OP.mult, OP.add),
                      reads=[gS], writes=[gS])
                p.dve(lambda v: v.reduce_max(SC[:, 40:41], SC[:, 32:40], axis=AX.X),
                      reads=[gS], writes=[gS])
                p.dve(lambda v: v.tensor_scalar(SC[:, 48:56], SC[:, 8:16], SC[:, 40:41],
                                                None, OP.is_ge),
                      reads=[gS], writes=[gS])
                p.dve(lambda v: v.tensor_tensor(SC[:, 56:64], SC[:, 0:8], SC[:, 48:56], OP.mult),
                      reads=[gS], writes=[gS])
                p.dve(lambda v: v.reduce_sum(SC[:, 64:65], SC[:, 56:64], axis=AX.X),
                      reads=[gS], writes=[gS])
                p.dve(lambda v: v.tensor_scalar_max(SC[:, 64:65], SC[:, 64:65], 1e-12),
                      reads=[gS], writes=[gS])
                p.dve(lambda v: v.reciprocal(SC[:, 64:65], SC[:, 64:65]),
                      reads=[gS], writes=[gS])
                p.dve(lambda v: v.tensor_scalar(SC[:, 72:80], SC[:, 56:64], SC[:, 64:65],
                                                None, OP.mult),
                      reads=[gS], writes=[gS])
                vgb, ggb = self.ABF.view(24576, 24584)
                p.dve(lambda v, vgb=vgb: v.tensor_copy(vgb, SC[:, 72:80]),
                      reads=[gS], writes=[ggb])
                p.pe(lambda t, vgb=vgb: t.transpose(self.pT0[0:8, 0:128], vgb, self.c_idb[:, :]),
                     reads=[ggb, g("c_idb")], writes=[g("pT0")])
                vgf, ggf = self.ABF.view(24704, 24832)
                p.dve(lambda v, vgf=vgf: v.tensor_copy(vgf[0:8, :], self.pT0[0:8, 0:128]),
                      reads=[g("pT0")], writes=[ggf])
                dst = self.gates_in.ap()[:, t0 + sub * 128: t0 + (sub + 1) * 128]
                p.adma(lambda s, dst=dst, vgf=vgf: s.dma_start(out=dst, in_=vgf[0:8, :]),
                       reads=[ggf], writes=[g("d_gates_in")])
            dst = self.xn_blk.ap()[:, t0:t0 + 512].rearrange("(c p) t -> p c t", p=128)
            p.adma(lambda s, dst=dst, vxnb=vxnb: s.dma_start(
                out=dst, in_=vxnb.rearrange("p (c t) -> p c t", t=512)),
                   reads=[gxnb], writes=[g("d_xn_blk")])

    def b2_collectives(self):
        p, g = self.p, self.g
        p.cc(lambda gg: gg.collective_compute(
            "AllGather", OP.bypass, replica_groups=[[0, 1, 2, 3], [4, 5, 6, 7]],
            ins=[self.xn_blk.ap().opt()], outs=[self.xn_bat.ap().opt()]),
            reads=[g("d_xn_blk")], writes=[g("d_xn_bat")])
        p.cc(lambda gg: gg.collective_compute(
            "AllGather", OP.bypass, replica_groups=[list(range(8))],
            ins=[self.xn_blk.ap().opt()], outs=[self.xn_all.ap().opt()]),
            reads=[g("d_xn_blk")], writes=[g("d_xn_all")])

        def _a2a(gg):
            with allow_ici_all_to_all():
                return gg.collective_compute(
                    "AllToAll", OP.bypass, replica_groups=[list(range(8))],
                    ins=[self.gates_in.ap().opt()], outs=[self.gates_x.ap().opt()])
        p.cc(_a2a, reads=[g("d_gates_in")], writes=[g("d_gates_x")])
        if self.dev_dbg:
            p.pdma(lambda gg: gg.dma_start(out=self.dbg["dbg_xn"][:, :],
                                           in_=self.xn_blk.ap()[:, :]),
                   reads=[g("d_xn_blk")], writes=[g("dbg")])
            p.pdma(lambda gg: gg.dma_start(out=self.dbg["dbg_gates"][:, :],
                                           in_=self.gates_x.ap()[:, :]),
                   reads=[g("d_gates_x")], writes=[g("dbg")])

    def b7_ffn(self):
        p, g = self.p, self.g
        Q = self.Q
        B7T = (2 * self.T) // 512
        p.pdma(lambda gg: gg.dma_start(out=self.gates8[:, :], in_=self.gates_x.ap()[:, :]),
               reads=[g("d_gates_x")], writes=[g("gates8")])
        sweeps = [("e", 0, 6), ("e", 6, 11), ("sh", 0, 3)]
        for sw, (kind, h0, h1) in enumerate(sweeps):
            nh = h1 - h0
            if kind == "e":
                hszs = [128] * nh
                WROW = nh * 128
                w1v, gw1 = self.AW.view(0, 16 * WROW)
                w3v, gw3 = self.AW.view(16 * WROW, 32 * WROW)
                w2v, gw2 = self.AW.view(32 * WROW, 32 * WROW + nh * 2048)
                for dc in range(16):
                    src = self.w1[dc * 128:(dc + 1) * 128, h0 * 128:h1 * 128]
                    p.pdma(lambda gg, src=src, dc=dc, w1v=w1v, WROW=WROW: gg.dma_start(
                        out=w1v[:, dc * WROW:(dc + 1) * WROW], in_=src), writes=[gw1])
                for dc in range(16):
                    src = self.w3[dc * 128:(dc + 1) * 128, h0 * 128:h1 * 128]
                    p.pdma(lambda gg, src=src, dc=dc, w3v=w3v, WROW=WROW: gg.dma_start(
                        out=w3v[:, dc * WROW:(dc + 1) * WROW], in_=src), writes=[gw3])
                for hc in range(nh):
                    src = self.w2[(h0 + hc) * 128:(h0 + hc + 1) * 128, :]
                    p.pdma(lambda gg, src=src, hc=hc, w2v=w2v: gg.dma_start(
                        out=w2v[:, hc * 2048:(hc + 1) * 2048], in_=src), writes=[gw2])
            else:
                hszs = [128, 128, 96]
                WROW = SHS
                w1v, gw1 = self.AW.view(0, 16 * SHS)
                w3v, gw3 = self.AW.view(16 * SHS, 32 * SHS)
                w2v, gw2 = self.AW.view(32 * SHS, 32 * SHS + 3 * 2048)
                for dc in range(16):
                    src = self.sw1T[dc * 128:(dc + 1) * 128, :]
                    p.pdma(lambda gg, src=src, dc=dc, w1v=w1v: gg.dma_start(
                        out=w1v[:, dc * SHS:(dc + 1) * SHS], in_=src), writes=[gw1])
                for dc in range(16):
                    src = self.sw3T[dc * 128:(dc + 1) * 128, :]
                    p.pdma(lambda gg, src=src, dc=dc, w3v=w3v: gg.dma_start(
                        out=w3v[:, dc * SHS:(dc + 1) * SHS], in_=src), writes=[gw3])
                for hc in range(3):
                    hs = hszs[hc]
                    src = self.sw2T[hc * 128:hc * 128 + hs, :]
                    p.pdma(lambda gg, src=src, hc=hc, hs=hs, w2v=w2v: gg.dma_start(
                        out=w2v[0:hs, hc * 2048:(hc + 1) * 2048], in_=src), writes=[gw2])
            nh_ = len(hszs)
            for gt in range(B7T):
                qq, lc = (gt * 512) // Q, (gt * 512) % Q
                par = gt % 2
                vxn, gxn = self.ABF.view(par * 8192, par * 8192 + 8192)
                src = self.xn_all.ap()[qq * 2048:(qq + 1) * 2048, lc:lc + 512].rearrange(
                    "(c p) t -> p c t", p=128)
                p.hdma(lambda s, src=src, vxn=vxn: s.dma_start(
                    out=vxn.rearrange("p (c t) -> p c t", t=512), in_=src),
                    writes=[gxn], extra=[g("d_xn_all").w])
                gate_sb, ggsb = self.ABF.view(19968 + par * 512, 19968 + par * 512 + 512)
                if kind == "e":
                    p.pe(lambda t, qq=qq, lc=lc: t.matmul(
                        self.pB[0:1, 512:1024], self.c_idb[0:8, qq:qq + 1],
                        self.gates8[:, lc:lc + 512], start=True, stop=True),
                        reads=[g("gates8"), g("c_idb")], writes=[g("pB1")])
                    p.dve(lambda v: v.tensor_copy(self.s_rowb[0:1, 0:512],
                                                  self.pB[0:1, 512:1024]),
                          reads=[g("pB1")], writes=[g("s_rowb")])
                    p.pe(lambda t: t.matmul(self.pB[:, 512:1024], self.c_onesb[:, :],
                                            self.s_rowb[0:1, 0:512], start=True, stop=True),
                         reads=[g("s_rowb"), g("c_onesb")], writes=[g("pB1")])
                    p.dve(lambda v, gate_sb=gate_sb: v.tensor_copy(
                        gate_sb, self.pB[:, 512:1024]),
                        reads=[g("pB1")], writes=[ggsb])
                g13v, g13g = self.ABF.view(16896, 16896 + 6 * 512)
                silv, gsil = self.ABF.view(16384, 16896)
                hoff = 0
                for hc in range(nh_):
                    hs = hszs[hc]
                    slot = hc % 2
                    pH = self.pA if slot == 0 else self.pB
                    gh1 = g("pA0") if slot == 0 else g("pB0")
                    gh3 = g("pA1") if slot == 0 else g("pB1")
                    for dc in range(16):
                        p.pe(lambda t, dc=dc, hs=hs, hoff=hoff, vxn=vxn, w1v=w1v,
                             WROW=WROW, pH=pH: t.matmul(
                            pH[0:hs, 0:512], w1v[:, dc * WROW + hoff: dc * WROW + hoff + hs],
                            vxn[:, dc * 512:(dc + 1) * 512], start=(dc == 0), stop=(dc == 15)),
                            reads=[gw1, gxn], writes=[gh1])
                    for dc in range(16):
                        p.pe(lambda t, dc=dc, hs=hs, hoff=hoff, vxn=vxn, w3v=w3v,
                             WROW=WROW, pH=pH: t.matmul(
                            pH[0:hs, 512:1024], w3v[:, dc * WROW + hoff: dc * WROW + hoff + hs],
                            vxn[:, dc * 512:(dc + 1) * 512], start=(dc == 0), stop=(dc == 15)),
                            reads=[gw3, gxn], writes=[gh3])
                    p.act(lambda s, hs=hs, pH=pH: s.activation(
                        silv[0:hs, 0:512], pH[0:hs, 0:512], AF.Silu),
                        reads=[gh1], writes=[gsil])
                    p.dve(lambda v, hs=hs, pH=pH: v.tensor_tensor(
                        silv[0:hs, 0:512], pH[0:hs, 512:1024], silv[0:hs, 0:512], OP.mult),
                        reads=[gh3, gsil], writes=[gsil])
                    if kind == "e":
                        p.dve(lambda v, hc=hc, hs=hs, gate_sb=gate_sb: v.tensor_tensor(
                            g13v[0:hs, hc * 512:(hc + 1) * 512], silv[0:hs, 0:512],
                            gate_sb[0:hs, :], OP.mult),
                            reads=[gsil, ggsb], writes=[g13g])
                    else:
                        p.dve(lambda v, hc=hc, hs=hs: v.tensor_copy(
                            g13v[0:hs, hc * 512:(hc + 1) * 512], silv[0:hs, 0:512]),
                            reads=[gsil], writes=[g13g])
                    hoff += hs
                yst, gyst = self.AFA.view(0, 8192)
                for dc in range(16):
                    half = dc % 2
                    gy = g("pC0") if half == 0 else g("pC1")
                    for hc in range(nh_):
                        hs = hszs[hc]
                        p.pe(lambda t, dc=dc, hc=hc, hs=hs, half=half, w2v=w2v,
                             nh_=nh_: t.matmul(
                            self.pC[:, half * 512:half * 512 + 512],
                            w2v[0:hs, hc * 2048 + dc * 128: hc * 2048 + (dc + 1) * 128],
                            g13v[0:hs, hc * 512:(hc + 1) * 512],
                            start=(hc == 0), stop=(hc == nh_ - 1)),
                            reads=[gw2, g13g], writes=[gy])
                    p.act(lambda s, dc=dc, half=half, yst=yst: s.activation(
                        yst[:, dc * 512:(dc + 1) * 512],
                        self.pC[:, half * 512:half * 512 + 512], AF.Copy),
                        reads=[gy], writes=[gyst])
                dst = self.y_in.ap()[qq * 2048:(qq + 1) * 2048, lc:lc + 512].rearrange(
                    "(c p) t -> p c t", p=128)
                ysr = yst.rearrange("p (c t) -> p c t", t=512)
                if sw == 0:
                    p.pdma(lambda gg, dst=dst, ysr=ysr: gg.dma_start(out=dst, in_=ysr),
                           reads=[gyst], writes=[g("d_y_%d_%d" % (qq, lc))])
                else:
                    p.pdma(lambda gg, dst=dst, ysr=ysr: gg.dma_start(
                        out=dst, in_=ysr, accum_op=OP.add),
                        reads=[gyst], writes=[g("d_y_%d_%d" % (qq, lc))])
        ally = [self.g("d_y_%d_%d" % (qq, lc)) for qq in range(8) for lc in range(0, Q, 512)]
        p.cc(lambda gg: gg.collective_compute(
            "ReduceScatter", OP.add, replica_groups=[list(range(8))],
            ins=[self.y_in.ap().opt()], outs=[self.ffn_own.ap().opt()]),
            reads=ally, writes=[g("d_ffn_own")])
        if self.dev_dbg:
            p.pdma(lambda gg: gg.dma_start(out=self.dbg["dbg_ffn"][:, :],
                                           in_=self.ffn_own.ap()[:, :]),
                   reads=[g("d_ffn_own")], writes=[g("dbg")])

    def b3_proj(self):
        p, g = self.p, self.g
        T, Q = self.T, self.Q
        B3T = T // 512
        wv, gwv = self.AW.view(0, 32768)
        for dc in range(16):
            src = self.wqkv[dc * 128:(dc + 1) * 128, :]
            p.pdma(lambda gg, src=src, dc=dc, wv=wv: gg.dma_start(
                out=wv[:, dc * 2048:(dc + 1) * 2048], in_=src), writes=[gwv])
        rings = []
        for cc in range(16):
            rv, gr = self.ABF.view(16384 + cc * 520, 16384 + cc * 520 + 515)
            rings.append((rv, gr))
            p.pool(lambda gg, rv=rv: gg.memset(rv[:, 0:3], 0.0), writes=[gr])
        for tt in range(B3T):
            par = tt % 2
            t0g = tt * 512
            r, col = t0g // Q, t0g % Q
            vxn, gxn = self.ABF.view(par * 8192, par * 8192 + 8192)
            src = self.xn_bat.ap()[r * 2048:(r + 1) * 2048, col:col + 512].rearrange(
                "(c p) t -> p c t", p=128)
            p.hdma(lambda s, src=src, vxn=vxn: s.dma_start(
                out=vxn.rearrange("p (c t) -> p c t", t=512), in_=src),
                   writes=[gxn], extra=[self.g("d_xn_bat").w])
            for cc in range(16):
                a0, ga0 = self.ABF.view(26752, 27264) if cc % 2 == 0 else self.ABF.view(27776, 28288)
                a1, ga1 = self.ABF.view(27264, 27776) if cc % 2 == 0 else self.ABF.view(28288, 28800)
                rv, gr = rings[cc]
                ph = (cc % 2) * 512
                gph = g("pA0") if cc % 2 == 0 else g("pA1")
                for dc in range(16):
                    p.pe(lambda t, dc=dc, cc=cc, ph=ph, vxn=vxn, wv=wv: t.matmul(
                        self.pA[:, ph:ph + 512],
                        wv[:, dc * 2048 + cc * 128: dc * 2048 + (cc + 1) * 128],
                        vxn[:, dc * 512:(dc + 1) * 512], start=(dc == 0), stop=(dc == 15)),
                        reads=[gwv, gxn], writes=[gph])
                p.dve(lambda v, rv=rv, ph=ph: v.tensor_copy(rv[:, 3:515],
                                                            self.pA[:, ph:ph + 512]),
                      reads=[gph], writes=[gr])
                p.dve(lambda v, rv=rv, cc=cc, a0=a0: v.tensor_scalar(
                    a0, rv[:, 0:512], self.c_convw[:, cc * 4:cc * 4 + 1], None, OP.mult),
                    reads=[gr, g("c_convw")], writes=[ga0])
                p.dve(lambda v, rv=rv, cc=cc, a0=a0, a1=a1: v.scalar_tensor_tensor(
                    a1, rv[:, 1:513], self.c_convw[:, cc * 4 + 1:cc * 4 + 2], a0, OP.mult, OP.add),
                    reads=[gr, ga0, g("c_convw")], writes=[ga1])
                p.dve(lambda v, rv=rv, cc=cc, a0=a0, a1=a1: v.scalar_tensor_tensor(
                    a0, rv[:, 2:514], self.c_convw[:, cc * 4 + 2:cc * 4 + 3], a1, OP.mult, OP.add),
                    reads=[gr, ga1, g("c_convw")], writes=[ga0])
                p.dve(lambda v, rv=rv, cc=cc, a0=a0, a1=a1: v.scalar_tensor_tensor(
                    a1, rv[:, 3:515], self.c_convw[:, cc * 4 + 3:cc * 4 + 4], a0, OP.mult, OP.add),
                    reads=[gr, ga0, g("c_convw")], writes=[ga1])
                p.dve(lambda v, rv=rv: v.tensor_copy(rv[:, 0:3], rv[:, 512:515]),
                      reads=[gr], writes=[gr])
                if cc < 4:
                    qf, gqf = self.AFA.view(1024, 1536)
                    p.act(lambda s, qf=qf, a1=a1: s.activation(qf, a1, AF.Silu),
                          reads=[ga1], writes=[gqf])
                    sq, gsq = self.ABF.view(24704, 25216)
                    p.act(lambda s, sq=sq, qf=qf: s.activation(sq, qf, AF.Square),
                          reads=[gqf], writes=[gsq])
                    p.pe(lambda t, sq=sq: t.matmul(self.pC[0:1, 0:512], self.c_onebc[:, :],
                                                   sq, start=True, stop=True),
                         reads=[gsq, g("c_onebc")], writes=[g("pC0")])
                    R, gR = self.s_rows, g("s_rows")
                    p.dve(lambda v: v.tensor_scalar_add(R[0:1, 0:512], self.pC[0:1, 0:512], 1e-6),
                          reads=[g("pC0")], writes=[gR])
                    p.act(lambda s: s.activation(R[0:1, 0:512], R[0:1, 0:512], AF.Sqrt),
                          reads=[gR], writes=[gR])
                    p.dve(lambda v: v.reciprocal(R[0:1, 0:512], R[0:1, 0:512]),
                          reads=[gR], writes=[gR])
                    p.dve(lambda v: v.tensor_copy(self.s_rowb[0:1, 0:512], R[0:1, 0:512]),
                          reads=[gR], writes=[g("s_rowb")])
                    p.pe(lambda t: t.matmul(self.pB[:, 0:512], self.c_onesb[:, :],
                                            self.s_rowb[0:1, 0:512], start=True, stop=True),
                         reads=[g("s_rowb"), g("c_onesb")], writes=[g("pB0")])
                    qn, gqn = self.ABF.view(25216, 25728)
                    p.dve(lambda v, qn=qn, qf=qf: v.tensor_tensor(qn, qf, self.pB[:, 0:512],
                                                                  OP.mult),
                          reads=[gqf, g("pB0")], writes=[gqn])
                    dst = self.qn_sc.ap()[cc * 128:(cc + 1) * 128, t0g:t0g + 512]
                    p.adma(lambda s, dst=dst, qn=qn: s.dma_start(out=dst, in_=qn),
                           reads=[gqn], writes=[g("d_qn")])
                elif cc < 8:
                    kb_, gkb = self.ABF.view(25728, 26240)
                    p.act(lambda s, kb_=kb_, a1=a1: s.activation(kb_, a1, AF.Silu),
                          reads=[ga1], writes=[gkb])
                    dst = self.kn_sc.ap()[(cc - 4) * 128:(cc - 3) * 128, t0g:t0g + 512]
                    p.adma(lambda s, dst=dst, kb_=kb_: s.dma_start(out=dst, in_=kb_),
                           reads=[gkb], writes=[g("d_kn")])
                else:
                    vv, gvv = self.ABF.view(26240, 26752)
                    p.act(lambda s, vv=vv, a1=a1: s.activation(vv, a1, AF.Silu),
                          reads=[ga1], writes=[gvv])
                    dst = self.v_sc.ap()[(cc - 8) * 128:(cc - 7) * 128, t0g:t0g + 512]
                    p.adma(lambda s, dst=dst, vv=vv: s.dma_start(out=dst, in_=vv),
                           reads=[gvv], writes=[g("d_v")])
            for dc in range(16):
                p.pe(lambda t, dc=dc, vxn=vxn: t.matmul(
                    self.pC[0:8, 0:512], self.s_wba[:, dc * 8:(dc + 1) * 8],
                    vxn[:, dc * 512:(dc + 1) * 512], start=(dc == 0), stop=(dc == 15)),
                    reads=[g("s_wba"), gxn], writes=[g("pC0")])
            for dc in range(16):
                p.pe(lambda t, dc=dc, vxn=vxn: t.matmul(
                    self.pC[0:8, 512:1024], self.s_wba[:, 128 + dc * 8:128 + (dc + 1) * 8],
                    vxn[:, dc * 512:(dc + 1) * 512], start=(dc == 0), stop=(dc == 15)),
                    reads=[g("s_wba"), gxn], writes=[g("pC1")])
            bt, gbt = self.AFA.view(1536, 2048)
            ga_, gga = self.AFA.view(2048, 2560)
            tx, gtx = self.AFA.view(2560, 3072)
            ty, gty = self.AFA.view(3072, 3584)
            p.act(lambda s, bt=bt: s.activation(bt[0:8, :], self.pC[0:8, 0:512], AF.Sigmoid),
                  reads=[g("pC0")], writes=[gbt])
            p.dve(lambda v, ga_=ga_: v.tensor_scalar(ga_[0:8, :], self.pC[0:8, 512:1024],
                                                     self.c_dtb[:, :], None, OP.add),
                  reads=[g("pC1"), g("c_dtb")], writes=[gga])
            p.act(lambda s, ga_=ga_, tx=tx: s.activation(tx[0:8, :], ga_[0:8, :], AF.Abs),
                  reads=[gga], writes=[gtx])
            p.act(lambda s, tx=tx: s.activation(tx[0:8, :], tx[0:8, :], AF.Exp, scale=-1.0),
                  reads=[gtx], writes=[gtx])
            p.act(lambda s, tx=tx: s.activation(tx[0:8, :], tx[0:8, :], AF.Ln,
                                                bias=self.c_bias[0:8, 1:2]),
                  reads=[gtx, g("c_bias")], writes=[gtx])
            p.dve(lambda v, ga_=ga_, ty=ty: v.tensor_scalar_max(ty[0:8, :], ga_[0:8, :], 0.0),
                  reads=[gga], writes=[gty])
            p.dve(lambda v, tx=tx, ty=ty: v.tensor_tensor(ty[0:8, :], ty[0:8, :], tx[0:8, :],
                                                          OP.add),
                  reads=[gtx], writes=[gty])
            p.dve(lambda v, ty=ty: v.tensor_scalar(ty[0:8, :], ty[0:8, :], self.c_nalog[:, :],
                                                   None, OP.mult),
                  reads=[g("c_nalog")], writes=[gty])
            gc_, ggc = self.AFA.view(3584, 4096)
            for sub in range(4):
                p.dve(lambda v, sub=sub, ty=ty, gc_=gc_: v.tensor_tensor_scan(
                    gc_[0:8, sub * 128:(sub + 1) * 128], ty[0:8, sub * 128:(sub + 1) * 128],
                    ty[0:8, sub * 128:(sub + 1) * 128], self.c_zero8[:, :], OP.add, OP.bypass),
                    reads=[gty, g("c_zero8")], writes=[ggc])
            for sub in range(4):
                p.pe(lambda t, sub=sub, gc_=gc_: t.transpose(
                    self.pA[:, sub * 8:sub * 8 + 8], gc_[0:8, sub * 128:(sub + 1) * 128],
                    self.c_idf[0:8, 0:8]),
                    reads=[ggc, g("c_idf")], writes=[g("pA0")])
                p.pe(lambda t, sub=sub, bt=bt: t.transpose(
                    self.pA[:, 32 + sub * 8:32 + sub * 8 + 8], bt[0:8, sub * 128:(sub + 1) * 128],
                    self.c_idf[0:8, 0:8]),
                    reads=[gbt, g("c_idf")], writes=[g("pA0")])
            p.dve(lambda v, tt=tt: v.tensor_copy(self.gcumT[:, tt * 32:tt * 32 + 32],
                                                 self.pA[:, 0:32]),
                  reads=[g("pA0")], writes=[g("gcumT")])
            p.dve(lambda v, tt=tt: v.tensor_copy(self.betaT[:, tt * 32:tt * 32 + 32],
                                                 self.pA[:, 32:64]),
                  reads=[g("pA0")], writes=[g("betaT")])
        wzv, gwz = self.AW.view(0, 16384)
        for dc in range(16):
            src = self.wz[dc * 128:(dc + 1) * 128, :]
            p.pdma(lambda gg, src=src, dc=dc, wzv=wzv: gg.dma_start(
                out=wzv[:, dc * 1024:(dc + 1) * 1024], in_=src), writes=[gwz])
        for tt in range(B3T):
            par = tt % 2
            t0g = tt * 512
            r, col = t0g // Q, t0g % Q
            vxn, gxn = self.ABF.view(par * 8192, par * 8192 + 8192)
            src = self.xn_bat.ap()[r * 2048:(r + 1) * 2048, col:col + 512].rearrange(
                "(c p) t -> p c t", p=128)
            p.hdma(lambda s, src=src, vxn=vxn: s.dma_start(
                out=vxn.rearrange("p (c t) -> p c t", t=512), in_=src), writes=[gxn])
            for zc in range(8):
                for dc in range(16):
                    p.pe(lambda t, zc=zc, dc=dc, vxn=vxn, wzv=wzv: t.matmul(
                        self.pA[:, (zc % 2) * 512:(zc % 2) * 512 + 512],
                        wzv[:, dc * 1024 + zc * 128: dc * 1024 + (zc + 1) * 128],
                        vxn[:, dc * 512:(dc + 1) * 512], start=(dc == 0), stop=(dc == 15)),
                        reads=[gwz, gxn], writes=[g("pA0") if zc % 2 == 0 else g("pA1")])
                zst, gzst = self.ABF.view(16384 + (zc % 2) * 512, 16384 + (zc % 2) * 512 + 512)
                p.act(lambda s, zst=zst, zc=zc: s.activation(
                    zst, self.pA[:, (zc % 2) * 512:(zc % 2) * 512 + 512], AF.Copy),
                    reads=[g("pA0") if zc % 2 == 0 else g("pA1")], writes=[gzst])
                dst = self.z_sc.ap()[zc * 128:(zc + 1) * 128, t0g:t0g + 512]
                p.adma(lambda s, dst=dst, zst=zst: s.dma_start(out=dst, in_=zst),
                       reads=[gzst], writes=[g("d_z")])
        if self.dev_dbg:
            for nm, t_ in [("dbg_q", self.qn_sc), ("dbg_k", self.kn_sc),
                           ("dbg_v", self.v_sc), ("dbg_z", self.z_sc)]:
                p.pdma(lambda gg, nm=nm, t_=t_: gg.dma_start(
                    out=self.dbg[nm][:, :], in_=t_.ap()[:, :]),
                    reads=[g("d_qn"), g("d_kn"), g("d_v"), g("d_z")], writes=[g("dbg")])
            p.pdma(lambda gg: gg.dma_start(out=self.dbg["dbg_bt"][:, :], in_=self.betaT[:, :]),
                   reads=[g("betaT")], writes=[g("dbg")])
            p.pdma(lambda gg: gg.dma_start(out=self.dbg["dbg_gc"][:, :], in_=self.gcumT[:, :]),
                   reads=[g("gcumT")], writes=[g("dbg")])

    def b4_scan(self):
        p, g = self.p, self.g
        T, Q, J = self.T, self.Q, self.J
        SC = self.s_cols
        gSC = g("s_cols")

        def emit_preA(j):
            par = j % 2
            AB = self.ABF if par == 0 else self.AW
            jc = j * 8

            def ab(a, b, AB=AB):
                return AB.view(a, b)

            def afv(a, b, par=par):
                return self.AFA.view(a + par * 4096, b + par * 4096)

            ctx = {"j": j, "jc": jc, "par": par, "ab": ab, "afv": afv}
            co2 = 96 + par * 48
            ctx["co2"] = co2
            qv, gq = ab(0, 512)
            src = self.qn_sc.ap()[:, j * 128:(j + 1) * 128].rearrange(
                "(h p) t -> p h t", p=128)
            p.hdma(lambda s, src=src, qv=qv: s.dma_start(
                out=qv.rearrange("p (h t) -> p h t", t=128), in_=src),
                writes=[gq], extra=[g("d_qn").w])
            kfm_r, gkfr = ab(1024, 1536)
            src = self.kn_sc.ap()[:, j * 128:(j + 1) * 128].rearrange(
                "(h p) t -> p h t", p=128)
            p.hdma(lambda s, src=src, kfm_r=kfm_r: s.dma_start(
                out=kfm_r.rearrange("p (h t) -> p h t", t=128), in_=src),
                writes=[gkfr], extra=[g("d_kn").w])
            vfm_r, gvfr = ab(2048, 3072)
            src = self.v_sc.ap()[:, j * 128:(j + 1) * 128].rearrange(
                "(u p) t -> p u t", p=128)
            p.hdma(lambda s, src=src, vfm_r=vfm_r: s.dma_start(
                out=vfm_r.rearrange("p (u t) -> p u t", t=128), in_=src),
                writes=[gvfr], extra=[g("d_v").w])
            zfm_r, gzfr = ab(4096, 5120)
            src = self.z_sc.ap()[:, j * 128:(j + 1) * 128].rearrange(
                "(u p) t -> p u t", p=128)
            p.adma(lambda s, src=src, zfm_r=zfm_r: s.dma_start(
                out=zfm_r.rearrange("p (u t) -> p u t", t=128), in_=src),
                writes=[gzfr], extra=[g("d_z").w])
            p.act(lambda s, jc=jc, co2=co2: s.activation(
                SC[:, co2:co2 + 8], self.gcumT[:, jc:jc + 8], AF.Exp, bias=self.c_bias[:, 2:3]),
                  reads=[g("gcumT"), g("c_bias")], writes=[gSC])
            p.pe(lambda t, jc=jc: t.matmul(self.pC[0:1, 896:904], self.c_idf[:, 127:128],
                                           self.gcumT[:, jc:jc + 8], start=True, stop=True),
                 reads=[g("gcumT"), g("c_idf")], writes=[g("pC1")])
            p.dve(lambda v: v.tensor_copy(self.s_rows[0:1, 16:24], self.pC[0:1, 896:904]),
                  reads=[g("pC1")], writes=[g("s_rows")])
            p.pe(lambda t: t.matmul(self.pC[:, 904:912], self.c_onesf[:, :],
                                    self.s_rows[0:1, 16:24], start=True, stop=True),
                 reads=[g("s_rows"), g("c_onesf")], writes=[g("pC1")])
            p.act(lambda s, co2=co2: s.activation(SC[:, co2 + 8:co2 + 16], self.pC[:, 904:912],
                                                  AF.Exp),
                  reads=[g("pC1")], writes=[gSC])
            p.dve(lambda v, jc=jc, co2=co2: v.tensor_tensor(
                SC[:, co2 + 16:co2 + 24], self.pC[:, 904:912],
                self.gcumT[:, jc:jc + 8], OP.subtract),
                  reads=[g("pC1"), g("gcumT")], writes=[gSC])
            p.act(lambda s, co2=co2: s.activation(SC[:, co2 + 16:co2 + 24],
                                                  SC[:, co2 + 16:co2 + 24], AF.Exp),
                  reads=[gSC], writes=[gSC])
            for h in range(4):
                p.pe(lambda t, h=h, kfm_r=kfm_r: t.transpose(
                    self.pT0[:, h * 128:(h + 1) * 128], kfm_r[:, h * 128:(h + 1) * 128],
                    self.c_idb[:, :]),
                    reads=[gkfr, g("c_idb")], writes=[g("pT0")])
            k2, gk2 = afv(3072, 3584)
            p.act(lambda s, k2=k2: s.activation(k2, self.pT0[:, 0:512], AF.Square),
                  reads=[g("pT0")], writes=[gk2])
            p.dve(lambda v, k2=k2: v.reduce_sum(
                SC[:, 0:4], k2.rearrange("p (h a) -> p h a", a=128), axis=AX.X),
                reads=[gk2], writes=[gSC])
            p.dve(lambda v: v.tensor_scalar_add(SC[:, 4:8], SC[:, 0:4], 1e-6),
                  reads=[gSC], writes=[gSC])
            p.act(lambda s: s.activation(SC[:, 4:8], SC[:, 4:8], AF.Sqrt),
                  reads=[gSC], writes=[gSC])
            p.dve(lambda v: v.reciprocal(SC[:, 4:8], SC[:, 4:8]),
                  reads=[gSC], writes=[gSC])
            p.dve(lambda v: v.tensor_copy(
                SC[:, 8:16].rearrange("p (u o) -> p u o", o=2)[:, :, 0:1],
                SC[:, 4:8].rearrange("p (u o) -> p u o", o=1)),
                reads=[gSC], writes=[gSC])
            p.dve(lambda v: v.tensor_copy(
                SC[:, 8:16].rearrange("p (u o) -> p u o", o=2)[:, :, 1:2],
                SC[:, 4:8].rearrange("p (u o) -> p u o", o=1)),
                reads=[gSC], writes=[gSC])
            p.act(lambda s, jc=jc: s.activation(SC[:, 16:24], self.gcumT[:, jc:jc + 8], AF.Exp),
                  reads=[g("gcumT")], writes=[gSC])
            p.dve(lambda v, jc=jc: v.tensor_tensor(SC[:, 48:56], self.betaT[:, jc:jc + 8],
                                                   SC[:, 8:16], OP.mult),
                  reads=[g("betaT"), gSC], writes=[gSC])
            p.dve(lambda v: v.tensor_tensor(SC[:, 56:64], SC[:, 48:56], SC[:, 16:24], OP.mult),
                  reads=[gSC], writes=[gSC])
            p.dve(lambda v, co2=co2: v.tensor_tensor(SC[:, 64:72], SC[:, co2 + 16:co2 + 24],
                                                     SC[:, 8:16], OP.mult),
                  reads=[gSC], writes=[gSC])
            p.dve(lambda v, jc=jc: v.tensor_scalar_mul(SC[:, 72:80], self.gcumT[:, jc:jc + 8],
                                                       -1.0),
                  reads=[g("gcumT")], writes=[gSC])
            kbtm, gkb = ab(6144, 7168)
            kbgtm, gkbg = ab(7168, 8192)
            kdtm, gkd = ab(8192, 9216)
            kntm, gknt = ab(10240, 10752)
            pt0_x = self.ex_pair(self.pT0[:, 0:512])
            p.dve(lambda v, kbtm=kbtm, pt0_x=pt0_x: v.tensor_tensor(
                kbtm.rearrange("p (u t) -> p u t", t=128), pt0_x,
                self.ex_u(SC[:, 48:56]), OP.mult),
                reads=[g("pT0"), gSC], writes=[gkb])
            p.dve(lambda v, kbgtm=kbgtm, pt0_x=pt0_x: v.tensor_tensor(
                kbgtm.rearrange("p (u t) -> p u t", t=128), pt0_x,
                self.ex_u(SC[:, 56:64]), OP.mult),
                reads=[g("pT0"), gSC], writes=[gkbg])
            p.dve(lambda v, kdtm=kdtm, pt0_x=pt0_x: v.tensor_tensor(
                kdtm.rearrange("p (u t) -> p u t", t=128), pt0_x,
                self.ex_u(SC[:, 64:72]), OP.mult),
                reads=[g("pT0"), gSC], writes=[gkd])
            p.dve(lambda v, kntm=kntm: v.tensor_tensor(
                kntm.rearrange("p (h t) -> p h t", t=128),
                self.pT0[:, 0:512].rearrange("p (h t) -> p h t", t=128),
                self.ex_u(SC[:, 4:8], n=4), OP.mult),
                reads=[g("pT0"), gSC], writes=[gknt])
            ctx["kdtm"], ctx["gkd"] = kdtm, gkd
            ctx["kbgtm"], ctx["gkbg"] = kbgtm, gkbg
            vbtm, gvb = ab(9216, 10240)
            for u in range(8):
                p.pe(lambda t, u=u, vfm_r=vfm_r: t.transpose(
                    self.pT1[:, u * 128:(u + 1) * 128], vfm_r[:, u * 128:(u + 1) * 128],
                    self.c_idb[:, :]),
                    reads=[gvfr, g("c_idb")], writes=[g("pT1")])
            p.dve(lambda v, jc=jc, vbtm=vbtm: v.tensor_tensor(
                vbtm.rearrange("p (u t) -> p u t", t=128),
                self.pT1[:, :].rearrange("p (u t) -> p u t", t=128),
                self.ex_u(self.betaT[:, jc:jc + 8]), OP.mult),
                reads=[g("pT1"), g("betaT")], writes=[gvb])
            ctx["vbtm"], ctx["gvb"] = vbtm, gvb
            zsil, gzs = ab(5120, 6144)
            for u in range(8):
                p.pe(lambda t, u=u, zfm_r=zfm_r: t.transpose(
                    self.pT0[:, u * 128:(u + 1) * 128], zfm_r[:, u * 128:(u + 1) * 128],
                    self.c_idb[:, :]),
                    reads=[gzfr, g("c_idb")], writes=[g("pT0")])
            p.act(lambda s, zsil=zsil: s.activation(zsil, self.pT0[:, :], AF.Silu),
                  reads=[g("pT0")], writes=[gzs])
            ctx["zsil"], ctx["gzs"] = zsil, gzs
            kbfm, gkbf = ab(10752, 11776)
            for u in range(8):
                p.pe(lambda t, u=u, kbtm=kbtm: t.transpose(
                    self.pT1[:, u * 128:(u + 1) * 128], kbtm[:, u * 128:(u + 1) * 128],
                    self.c_idb[:, :]),
                    reads=[gkb, g("c_idb")], writes=[g("pT1")])
            p.dve(lambda v, kbfm=kbfm: v.tensor_copy(kbfm, self.pT1[:, :]),
                  reads=[g("pT1")], writes=[gkbf])
            knfm, gknf = ab(11776, 12288)
            for h in range(4):
                p.pe(lambda t, h=h, kntm=kntm: t.transpose(
                    self.pT0[:, h * 128:(h + 1) * 128], kntm[:, h * 128:(h + 1) * 128],
                    self.c_idb[:, :]),
                    reads=[gknt, g("c_idb")], writes=[g("pT0")])
            p.dve(lambda v, knfm=knfm: v.tensor_copy(knfm, self.pT0[:, 0:512]),
                  reads=[g("pT0")], writes=[gknf])
            epre, gep = afv(0, 1024)
            cb, gcb = afv(3072, 4096)
            ones_x = bass.AP(self.c_ones128[:, :].tensor, 0, [[128, 128], [0, 8], [1, 128]])
            p.dve(lambda v, cb=cb, ones_x=ones_x: v.tensor_tensor(
                cb.rearrange("p (u t) -> p u t", t=128), ones_x,
                self.ex_u(SC[:, 72:80]), OP.mult),
                reads=[g("c_ones128"), gSC], writes=[gcb])
            for u in range(8):
                p.pe(lambda t, u=u, cb=cb: t.transpose(
                    self.pB[:, u * 128:(u + 1) * 128], cb[:, u * 128:(u + 1) * 128],
                    self.c_idf[:, :]),
                    reads=[gcb, g("c_idf")], writes=[g("pB0") if u < 4 else g("pB1")])
            p.dve(lambda v, jc=jc, epre=epre: v.tensor_tensor(
                epre.rearrange("p (u t) -> p u t", t=128), self.pB[:, :].rearrange(
                    "p (u t) -> p u t", t=128),
                self.ex_u(self.gcumT[:, jc:jc + 8]), OP.add),
                reads=[g("pB0"), g("pB1"), g("gcumT")], writes=[gep])
            p.dve(lambda v, epre=epre: v.tensor_scalar_min(epre, epre, 0.0),
                  reads=[gep], writes=[gep])
            eexp, gee = ab(12288, 13312)
            p.act(lambda s, epre=epre, eexp=eexp: s.activation(eexp, epre, AF.Exp),
                  reads=[gep], writes=[gee])
            eneg, gen = ab(13312, 14336)
            p.dve(lambda v, eexp=eexp, eneg=eneg: v.tensor_tensor(
                eneg, eexp, self.c_sneg[:, :], OP.mult),
                reads=[gee, g("c_sneg")], writes=[gen])
            for u in range(8):
                p.pe(lambda t, u=u, kbfm=kbfm, knfm=knfm: t.matmul(
                    self.pA[:, u * 128:(u + 1) * 128], kbfm[:, u * 128:(u + 1) * 128],
                    knfm[:, (u // 2) * 128:(u // 2 + 1) * 128], start=True, stop=True),
                    reads=[gkbf, gknf], writes=[g("pA0") if u < 4 else g("pA1")])
            sl, gsl = ab(15360, 16384)
            p.dve(lambda v, sl=sl, eneg=eneg: v.tensor_tensor(sl, self.pA[:, :], eneg, OP.mult),
                  reads=[g("pA0"), g("pA1"), gen], writes=[gsl])
            st_, gst = ab(16384, 17408)
            for u in range(8):
                p.pe(lambda t, u=u, sl=sl: t.transpose(
                    self.pT0[:, u * 128:(u + 1) * 128], sl[:, u * 128:(u + 1) * 128],
                    self.c_idb[:, :]),
                    reads=[gsl, g("c_idb")], writes=[g("pT0")])
            p.dve(lambda v, st_=st_: v.tensor_copy(st_, self.pT0[:, :]),
                  reads=[g("pT0")], writes=[gst])
            eit, geit = ab(14336, 15360)
            for u in range(8):
                p.pe(lambda t, u=u, eneg=eneg: t.transpose(
                    self.pT1[:, u * 128:(u + 1) * 128], eneg[:, u * 128:(u + 1) * 128],
                    self.c_idb[:, :]),
                    reads=[gen, g("c_idb")], writes=[g("pT1")])
            p.dve(lambda v, eit=eit: v.tensor_tensor(eit, self.c_blkI[:, :], self.pT1[:, :],
                                                     OP.subtract),
                  reads=[g("c_blkI"), g("pT1")], writes=[geit])
            for u in range(8):
                p.pe(lambda t, u=u, knfm=knfm, qv=qv: t.matmul(
                    self.pB[:, u * 128:(u + 1) * 128], knfm[:, (u // 2) * 128:(u // 2 + 1) * 128],
                    qv[:, (u // 2) * 128:(u // 2 + 1) * 128], start=True, stop=True),
                    reads=[gknf, gq], writes=[g("pB0") if u < 4 else g("pB1")])
            intq, gint = ab(17408, 18432)
            p.dve(lambda v, intq=intq, eit=eit: v.scalar_tensor_tensor(
                intq, self.pB[:, :], RS128, eit, OP.mult, OP.mult),
                reads=[g("pB0"), g("pB1"), geit], writes=[gint])
            ctx["qv"], ctx["gq"] = qv, gq
            ctx["intq"], ctx["gint"] = intq, gint
            ctx["sl"], ctx["gsl"] = sl, gsl
            ctx["st_"], ctx["gst"] = st_, gst
            return ctx

        def emit_ut(ctx):
            ab = ctx["ab"]
            sl, gsl = ctx["sl"], ctx["gsl"]
            st_, gst = ctx["st_"], ctx["gst"]
            Pb = [ab(22528, 23552), ab(23552, 24576)]
            p.dve(lambda v, P0=Pb[0][0], st_=st_: v.tensor_tensor(
                P0, self.c_blkI[:, :], st_, OP.add),
                  reads=[g("c_blkI"), gst], writes=[Pb[0][1]])
            cur = 0
            spl, gspl = sl, gsl
            spt, gspt = st_, gst
            lv = [(ab(18432, 19456), ab(19456, 20480)), (ab(20480, 21504), ab(21504, 22528))]
            for lev in range(1, 7):
                for u in range(8):
                    p.pe(lambda t, u=u, spl=spl, spt=spt: t.matmul(
                        self.pA[:, u * 128:(u + 1) * 128], spt[:, u * 128:(u + 1) * 128],
                        spl[:, u * 128:(u + 1) * 128], start=True, stop=True),
                        reads=[gspl, gspt], writes=[g("pA0") if u < 4 else g("pA1")])
                (nl, gnl), (nt, gnt) = lv[lev % 2]
                p.act(lambda s, nl=nl: s.activation(nl, self.pA[:, :], AF.Copy),
                      reads=[g("pA0"), g("pA1")], writes=[gnl])
                if lev < 6:
                    for u in range(8):
                        p.pe(lambda t, u=u, spl=spl, spt=spt: t.matmul(
                            self.pC[:, u * 128:(u + 1) * 128], spl[:, u * 128:(u + 1) * 128],
                            spt[:, u * 128:(u + 1) * 128], start=True, stop=True),
                            reads=[gspl, gspt], writes=[g("pC0") if u < 4 else g("pC1")])
                    p.act(lambda s, nt=nt: s.activation(nt, self.pC[:, :], AF.Copy),
                          reads=[g("pC0"), g("pC1")], writes=[gnt])
                for u in range(8):
                    p.pe(lambda t, u=u, nl=nl, Pc=Pb[cur][0]: t.matmul(
                        self.pB[:, u * 128:(u + 1) * 128], nl[:, u * 128:(u + 1) * 128],
                        Pc[:, u * 128:(u + 1) * 128], start=True, stop=True),
                        reads=[gnl, Pb[cur][1]], writes=[g("pB0") if u < 4 else g("pB1")])
                p.dve(lambda v, Pn=Pb[1 - cur][0], Pc=Pb[cur][0]: v.tensor_tensor(
                    Pn[:, :], self.pB[:, :], Pc[:, :], OP.add),
                    reads=[g("pB0"), g("pB1"), Pb[cur][1]], writes=[Pb[1 - cur][1]])
                cur = 1 - cur
                spl, gspl = nl, gnl
                spt, gspt = nt, gnt
            Pf, gPf = Pb[cur]
            ctx["Pf"], ctx["gPf"] = Pf, gPf
            kbgtm, gkbg = ctx["kbgtm"], ctx["gkbg"]
            nkcd, gnk = ab(24576, 25600)
            for u in range(8):
                p.pe(lambda t, u=u, kbgtm=kbgtm, Pf=Pf: t.matmul(
                    self.pB[:, u * 128:(u + 1) * 128], kbgtm[:, u * 128:(u + 1) * 128],
                    Pf[:, u * 128:(u + 1) * 128], start=True, stop=True),
                    reads=[gkbg, gPf], writes=[g("pB0") if u < 4 else g("pB1")])
            p.dve(lambda v, nkcd=nkcd: v.tensor_scalar_mul(nkcd, self.pB[:, :], -1.0),
                  reads=[g("pB0"), g("pB1")], writes=[gnk])
            ctx["nkcd"], ctx["gnk"] = nkcd, gnk

        def emit_post(ctx):
            j, jc, par, ab, afv = ctx["j"], ctx["jc"], ctx["par"], ctx["ab"], ctx["afv"]
            co2 = ctx["co2"]
            qv, gq = ctx["qv"], ctx["gq"]
            zsil, gzs = ctx["zsil"], ctx["gzs"]
            vbtm, gvb = ctx["vbtm"], ctx["gvb"]
            kdtm, gkd = ctx["kdtm"], ctx["gkd"]
            Pf, gPf = ctx["Pf"], ctx["gPf"]
            nkcd, gnk = ctx["nkcd"], ctx["gnk"]
            intq, gint = ctx["intq"], ctx["gint"]
            for u in range(8):
                p.pe(lambda t, u=u, Pf=Pf, vbtm=vbtm: t.matmul(
                    self.pA[:, u * 128:(u + 1) * 128], Pf[:, u * 128:(u + 1) * 128],
                    vbtm[:, u * 128:(u + 1) * 128], start=True, stop=False),
                    reads=[gPf, gvb], writes=[g("pA0") if u < 4 else g("pA1")])
                p.pe(lambda t, u=u, nkcd=nkcd: t.matmul(
                    self.pA[:, u * 128:(u + 1) * 128], nkcd[:, u * 128:(u + 1) * 128],
                    self.s_stb[:, u * 128:(u + 1) * 128], start=False, stop=True),
                    reads=[gnk, g("s_stb")], writes=[g("pA0") if u < 4 else g("pA1")])
            vnew, gvn = ab(25600, 26624)
            p.dve(lambda v, vnew=vnew: v.tensor_copy(vnew, self.pA[:, :]),
                  reads=[g("pA0"), g("pA1")], writes=[gvn])
            interf, gitf = afv(1024, 2048)
            for u in range(8):
                p.pe(lambda t, u=u, qv=qv: t.matmul(
                    self.pB[:, u * 128:(u + 1) * 128], qv[:, (u // 2) * 128:(u // 2 + 1) * 128],
                    self.s_stb[:, u * 128:(u + 1) * 128], start=True, stop=True),
                    reads=[gq, g("s_stb")], writes=[g("pB0") if u < 4 else g("pB1")])
            p.dve(lambda v, interf=interf, co2=co2: v.tensor_tensor(
                interf.rearrange("p (u t) -> p u t", t=128),
                self.pB[:, :].rearrange("p (u t) -> p u t", t=128),
                self.ex_u(SC[:, co2:co2 + 8]), OP.mult),
                reads=[g("pB0"), g("pB1"), gSC], writes=[gitf])
            for u in range(8):
                p.pe(lambda t, u=u, intq=intq, vnew=vnew: t.matmul(
                    self.pC[:, u * 128:(u + 1) * 128], intq[:, u * 128:(u + 1) * 128],
                    vnew[:, u * 128:(u + 1) * 128], start=True, stop=True),
                    reads=[gint, gvn], writes=[g("pC0") if u < 4 else g("pC1")])
            outc, goc = afv(2048, 3072)
            p.dve(lambda v, outc=outc, interf=interf: v.tensor_tensor(
                outc, self.pC[:, :], interf, OP.add),
                reads=[g("pC0"), g("pC1"), gitf], writes=[goc])
            for u in range(8):
                p.pe(lambda t, u=u, kdtm=kdtm, vnew=vnew: t.matmul(
                    self.pA[:, u * 128:(u + 1) * 128], kdtm[:, u * 128:(u + 1) * 128],
                    vnew[:, u * 128:(u + 1) * 128], start=True, stop=True),
                    reads=[gkd, gvn], writes=[g("pA0") if u < 4 else g("pA1")])
            p.dve(lambda v, co2=co2: v.tensor_tensor(
                self.s_st[:, :].rearrange("p (u t) -> p u t", t=128),
                self.s_st[:, :].rearrange("p (u t) -> p u t", t=128),
                self.ex_u(SC[:, co2 + 8:co2 + 16]), OP.mult),
                reads=[gSC], writes=[g("s_st")])
            p.dve(lambda v: v.tensor_tensor(self.s_st[:, :], self.s_st[:, :], self.pA[:, :],
                                            OP.add),
                  reads=[g("pA0"), g("pA1")], writes=[g("s_st")])
            p.dve(lambda v: v.tensor_copy(self.s_stb[:, :], self.s_st[:, :]),
                  reads=[g("s_st")], writes=[g("s_stb")])
            xg, gxg = afv(1024, 2048)
            p.dve(lambda v, outc=outc, zsil=zsil, xg=xg: v.tensor_tensor(
                xg, outc, zsil, OP.mult),
                reads=[goc, gzs], writes=[gxg])
            sqb, gsq = ab(27648, 28672)
            p.act(lambda s, xg=xg, sqb=sqb: s.activation(sqb, xg, AF.Square),
                  reads=[gxg], writes=[gsq])
            p.dve(lambda v, sqb=sqb: v.reduce_sum(
                SC[:, 80:88], sqb.rearrange("p (u l) -> p u l", l=128), axis=AX.X),
                reads=[gsq], writes=[gSC])
            p.dve(lambda v: v.tensor_scalar(SC[:, 88:96], SC[:, 80:88], 1.0 / 128.0, 1e-6,
                                            OP.mult, OP.add),
                  reads=[gSC], writes=[gSC])
            p.act(lambda s: s.activation(SC[:, 88:96], SC[:, 88:96], AF.Sqrt),
                  reads=[gSC], writes=[gSC])
            p.dve(lambda v: v.reciprocal(SC[:, 88:96], SC[:, 88:96]),
                  reads=[gSC], writes=[gSC])
            ntm, gnt2 = ab(28672, 29696)
            p.dve(lambda v, xg=xg, ntm=ntm: v.tensor_tensor(
                ntm.rearrange("p (u t) -> p u t", t=128),
                xg.rearrange("p (u t) -> p u t", t=128),
                self.ex_u(SC[:, 88:96]), OP.mult),
                reads=[gxg, gSC], writes=[gnt2])
            nst, gnst = ab(29696, 30720)
            for u in range(8):
                p.pe(lambda t, u=u, ntm=ntm: t.transpose(
                    self.pT0[:, u * 128:(u + 1) * 128], ntm[:, u * 128:(u + 1) * 128],
                    self.c_idb[:, :]),
                    reads=[gnt2, g("c_idb")], writes=[g("pT0")])
            p.dve(lambda v, nst=nst: v.tensor_scalar(
                nst, self.pT0[:, :], self.c_normw[:, :], None, OP.mult),
                reads=[g("pT0"), g("c_normw")], writes=[gnst])
            dst = self.nrm_sc.ap()[:, j * 128:(j + 1) * 128].rearrange(
                "(u p) l -> p u l", p=128)
            p.pdma(lambda s, dst=dst, nst=nst: s.dma_start(
                out=dst, in_=nst.rearrange("p (u l) -> p u l", l=128)),
                   reads=[gnst], writes=[g("d_nrm")])

        ctxs = {}
        ctxs[0] = emit_preA(0)
        emit_ut(ctxs[0])
        ctxs[1] = emit_preA(1)
        for j in range(J):
            emit_post(ctxs.pop(j))
            if j + 1 < J:
                emit_ut(ctxs[j + 1])
            if j + 2 < J:
                ctxs[j + 2] = emit_preA(j + 2)
        if self.dev_dbg:
            p.pdma(lambda gg: gg.dma_start(out=self.dbg["dbg_nrm"][:, :],
                                           in_=self.nrm_sc.ap()[:, :]),
                   reads=[g("d_nrm")], writes=[g("dbg")])

    def b6_wout(self):
        p, g = self.p, self.g
        T, Q = self.T, self.Q
        wo, gwo = self.AW.view(0, 16384)
        for vdc in range(8):
            src = self.wout[vdc * 128:(vdc + 1) * 128, :]
            p.pdma(lambda gg, src=src, vdc=vdc, wo=wo: gg.dma_start(
                out=wo[:, vdc * 2048:(vdc + 1) * 2048], in_=src), writes=[gwo])
        for tt in range(T // 512):
            par = tt % 2
            t0 = tt * 512
            q, lc = t0 // Q, t0 % Q
            vn, gvn = self.ABF.view(par * 4096, par * 4096 + 4096)
            src = self.nrm_sc.ap()[:, t0:t0 + 512].rearrange("(c p) t -> p c t", p=128)
            p.hdma(lambda s, src=src, vn=vn: s.dma_start(
                out=vn.rearrange("p (c t) -> p c t", t=512), in_=src),
                   writes=[gvn], extra=[g("d_nrm").w])
            yst, gyst = self.AFA.view(0, 8192)
            for dc in range(16):
                half = dc % 2
                for vdc in range(8):
                    p.pe(lambda t, dc=dc, vdc=vdc, half=half, vn=vn, wo=wo: t.matmul(
                        self.pA[:, half * 512:half * 512 + 512],
                        wo[:, vdc * 2048 + dc * 128: vdc * 2048 + (dc + 1) * 128],
                        vn[:, vdc * 512:(vdc + 1) * 512], start=(vdc == 0), stop=(vdc == 7)),
                        reads=[gwo, gvn], writes=[g("pA0") if half == 0 else g("pA1")])
                p.dve(lambda v, dc=dc, half=half, yst=yst: v.tensor_copy(
                    yst[:, dc * 512:(dc + 1) * 512], self.pA[:, half * 512:half * 512 + 512]),
                    reads=[g("pA0") if half == 0 else g("pA1")], writes=[gyst])
            dst = self.ps_in.ap()[q * 2048:(q + 1) * 2048, lc:lc + 512].rearrange(
                "(c p) t -> p c t", p=128)
            p.pdma(lambda s, dst=dst, yst=yst: s.dma_start(
                out=dst, in_=yst.rearrange("p (c t) -> p c t", t=512)),
                   reads=[gyst], writes=[g("d_ps")])
        p.cc(lambda gg: gg.collective_compute(
            "ReduceScatter", OP.add, replica_groups=[[0, 1, 2, 3], [4, 5, 6, 7]],
            ins=[self.ps_in.ap().opt()], outs=[self.s_own.ap().opt()]),
            reads=[g("d_ps")], writes=[g("d_s_own")])
        if self.dev_dbg:
            p.pdma(lambda gg: gg.dma_start(out=self.dbg["dbg_s"][:, :],
                                           in_=self.s_own.ap()[:, :]),
                   reads=[g("d_s_own")], writes=[g("dbg")])

    def b9_combine(self):
        p, g = self.p, self.g
        Q = self.Q
        for tt in range(Q // 512):
            t0 = tt * 512
            for dc in range(16):
                i = (tt * 16 + dc) % 2
                vx, gx = self.AFA.view(i * 2048, i * 2048 + 512)
                vs, gs = self.AFA.view(i * 2048 + 512, i * 2048 + 1024)
                vf, gf = self.AFA.view(i * 2048 + 1024, i * 2048 + 1536)
                vo, go = self.AFA.view(i * 2048 + 1536, i * 2048 + 2048)
                p.adma(lambda s, dc=dc, t0=t0, vx=vx: s.dma_start(
                    out=vx, in_=self.x_fm[dc * 128:(dc + 1) * 128, t0:t0 + 512]),
                    writes=[gx])
                p.adma(lambda s, dc=dc, t0=t0, vs=vs: s.dma_start(
                    out=vs, in_=self.s_own.ap()[dc * 128:(dc + 1) * 128, t0:t0 + 512]),
                    writes=[gs], extra=[g("d_s_own").w])
                p.adma(lambda s, dc=dc, t0=t0, vf=vf: s.dma_start(
                    out=vf, in_=self.ffn_own.ap()[dc * 128:(dc + 1) * 128, t0:t0 + 512]),
                    writes=[gf], extra=[g("d_ffn_own").w])
                p.act(lambda s, vf=vf: s.activation(vf, vf, AF.Silu),
                      reads=[gf], writes=[gf])
                p.dve(lambda v, vo=vo, vs=vs, vf=vf: v.tensor_tensor(vo, vs, vf, OP.mult),
                      reads=[gs, gf], writes=[go])
                p.dve(lambda v, vo=vo, vx=vx: v.tensor_tensor(vo, vo, vx, OP.add),
                      reads=[gx], writes=[go])
                p.hdma(lambda s, dc=dc, t0=t0, vo=vo: s.dma_start(
                    out=self.out_fm[dc * 128:(dc + 1) * 128, t0:t0 + 512], in_=vo),
                    reads=[go], writes=[g("d_out")])

    def build(self):
        self.declare()
        with ExitStack() as es:
            self.alloc(es)
            self.b0_consts()
            self.b1_ln_router()
            self.b2_collectives()
            self.b7_ffn()
            self.b3_proj()
            self.b4_scan()
            self.b6_wout()
            self.b9_combine()
            with ExitStack() as es2:
                sems = {k: es2.enter_context(self.nc.semaphore("sem_" + k))
                        for k in Prog.SEMS}
                blk = es2.enter_context(self.nc.Block("mk"))
                self.p.emit(blk, sems, pe_fifo=self.pe_fifo)
                final = dict(self.p.cnt)

                def tail(eng):
                    for k, vv in final.items():
                        if vv:
                            eng.wait_ge(sems[k], vv)
                blk.sync(tail)
        return self.nc



def _host_inputs(inputs, T):
    Q = T // 4
    x = np.asarray(inputs["L0_attn_hidden"])
    Wqkv = np.asarray(inputs["L0_attn_W_qkv"])
    Wz = np.asarray(inputs["L0_attn_W_z"])
    Wb = np.asarray(inputs["L0_attn_W_b"])
    Wa = np.asarray(inputs["L0_attn_W_a"])
    cw = np.asarray(inputs["L0_attn_conv_weight"])
    dtbias = np.asarray(inputs["L0_attn_dt_bias"])
    Alog = np.asarray(inputs["L0_attn_A_log"])
    nw = np.asarray(inputs["L0_attn_norm_w"])
    Wout = np.asarray(inputs["L0_attn_W_out"])
    gw = np.asarray(inputs["L0_ffn_gate_w"])
    rb = np.asarray(inputs["L0_ffn_router_bias"])
    W1 = np.asarray(inputs["L0_ffn_W1"])
    W3 = np.asarray(inputs["L0_ffn_W3"])
    W2 = np.asarray(inputs["L0_ffn_W2"])
    sw1 = np.asarray(inputs["L0_ffn_sw1"])
    sw3 = np.asarray(inputs["L0_ffn_sw3"])
    sw2 = np.asarray(inputs["L0_ffn_sw2"])

    Xf = np.ascontiguousarray(x.reshape(2 * T, D))
    idf = np.eye(128, dtype=np.float32)
    idb = idf.astype(BFNP)
    onesb = np.ones((1, 128), np.float32).astype(BFNP)
    onebc = np.ones((128, 1), np.float32).astype(BFNP)
    cbias = np.tile(np.array([[1e-5, 1.0, -0.5 * np.log(128.0), 1e-6]], np.float32), (128, 1))
    onesf = np.ones((1, 128), np.float32)
    sneg1 = (-np.tril(np.ones((128, 128), np.float32), -1)).astype(BFNP)
    sneg = np.concatenate([sneg1] * VH, axis=1)
    blkI = np.concatenate([idb] * VH, axis=1)
    gw128 = np.ascontiguousarray(
        gw.T.reshape(16, 128, 8).transpose(1, 0, 2).reshape(128, 128)).astype(np.float32)

    maps = []
    for c in range(8):
        b = c // 4
        hq0 = (c % 4) * QH
        qcols = np.arange(hq0 * 128, (hq0 + QH) * 128)
        kcols = 2048 + qcols
        vcols = 4096 + np.arange(hq0 * 2 * 128, (hq0 + QH) * 2 * 128)
        cols = np.concatenate([qcols, kcols, vcols])
        vh = np.arange(hq0 * 2, (hq0 + QH) * 2)
        vdcols = np.arange(vh[0] * 128, (vh[-1] + 1) * 128)

        x_fm = np.ascontiguousarray(Xf[c * Q:(c + 1) * Q].T)
        wqkv_s = np.ascontiguousarray(Wqkv[:, cols])
        wz_s = np.ascontiguousarray(Wz[:, vdcols])
        wb128 = np.ascontiguousarray(
            Wb[:, vh].reshape(16, 128, VH).transpose(1, 0, 2).reshape(128, 128))
        wa128 = np.ascontiguousarray(
            Wa[:, vh].reshape(16, 128, VH).transpose(1, 0, 2).reshape(128, 128))
        convw_s = np.ascontiguousarray(
            cw[cols].reshape(16, 128, 4).transpose(1, 0, 2).reshape(128, 64))
        m = dict(
            x_fm=x_fm, wqkv=wqkv_s, wz=wz_s, wb128=wb128, wa128=wa128,
            convw=convw_s,
            alog=np.ascontiguousarray(Alog[vh].reshape(VH, 1)),
            dtb=np.ascontiguousarray(dtbias[vh].reshape(VH, 1)),
            normw=np.ascontiguousarray(nw.reshape(128, 1)),
            wout=np.ascontiguousarray(Wout[vdcols, :]),
            gw128=gw128, rbias=np.ascontiguousarray(rb.reshape(1, 8)),
            w1=np.ascontiguousarray(W1[c]), w3=np.ascontiguousarray(W3[c]),
            w2=np.ascontiguousarray(W2[c]),
            sw1T=np.ascontiguousarray(sw1.T[:, c * SHS:(c + 1) * SHS]),
            sw3T=np.ascontiguousarray(sw3.T[:, c * SHS:(c + 1) * SHS]),
            sw2T=np.ascontiguousarray(sw2.T[c * SHS:(c + 1) * SHS, :]),
            idf=idf, idb=idb, onesb=onesb, onebc=onebc, onesf=onesf, sneg=sneg, blkI=blkI,
            cbias=cbias,
        )
        maps.append(m)
    return maps


_NC_CACHE = {}


def kernel(**inputs):
    T = int(np.asarray(inputs["L0_attn_hidden"]).shape[1])
    key = ("nc", T)
    if key not in _NC_CACHE:
        _NC_CACHE[key] = Builder(T).build()
    nc = _NC_CACHE[key]
    maps = _host_inputs(inputs, T)
    res = run_bass_kernel_spmd(nc, maps, core_ids=list(range(8)))
    Q = T // 4
    full = np.concatenate([np.asarray(res.results[c]["out_fm"]) for c in range(8)], axis=1)
    return np.ascontiguousarray(full.T).reshape(2, T, D).astype(np.float32)


# version 11
# speedup vs baseline: 1.1402x; 1.0134x over previous
import math
from contextlib import ExitStack

import numpy as np
import ml_dtypes

import concourse.bass as bass
import concourse.mybir as mybir
from concourse.bass_utils import run_bass_kernel_spmd
from concourse.aot_env import allow_ici_all_to_all

F32, BF16 = mybir.dt.float32, mybir.dt.bfloat16
AF = mybir.ActivationFunctionType
OP = mybir.AluOpType
AX = mybir.AxisListType
BFNP = ml_dtypes.bfloat16

D = 2048
QH, VH = 4, 8
C = 128
INTER = 1408
SHS = 352
LN128 = float(np.log(128.0))
RS128 = float(1.0 / np.sqrt(128.0))


class Guard:
    __slots__ = ("w", "rs")
    def __init__(self, init=()):
        self.w = None
        self.rs = list(init)


class Arena:
    def __init__(self, ap_full):
        self.ap = ap_full
        self.ivs = []

    def view(self, a, b):
        for (x, y, gd) in self.ivs:
            if x == a and y == b:
                return self.ap[:, a:b], gd
        deps, keep = [], []
        for (x, y, gd) in self.ivs:
            if x < b and a < y:
                if gd.w is not None:
                    deps.append(gd.w)
                deps.extend(gd.rs)
            else:
                keep.append((x, y, gd))
        gnew = Guard(deps)
        keep.append((a, b, gnew))
        self.ivs = keep
        return self.ap[:, a:b], gnew


class Prog:
    SECTIONS = ("tensor", "vector", "scalar", "sync", "gpsimd")
    COMPUTE_SEMS = ("pe", "dve", "act", "pool")
    DMA_POOL = {"hdma": 8, "adma": 8, "pdma": 8, "cc": 4}
    SEMS = COMPUTE_SEMS + tuple(
        f"{c}{i}" for c, n in DMA_POOL.items() for i in range(n))

    def __init__(self):
        self.q = {s: [] for s in self.SECTIONS}
        self.cnt = {k: 0 for k in self.SEMS}
        self.rr = {c: 0 for c in self.DMA_POOL}

    def _op(self, section, sem, fn, reads, writes, extra, k):
        deps = list(extra)
        if sem in self.DMA_POOL:
            i = self.rr[sem]
            self.rr[sem] = (i + 1) % self.DMA_POOL[sem]
            sub = f"{sem}{i}"
            if self.cnt[sub]:
                deps.append((sub, self.cnt[sub]))
            sem = sub
        for gd in reads:
            if gd.w is not None:
                deps.append(gd.w)
        for gd in writes:
            if gd.w is not None:
                deps.append(gd.w)
            deps.extend(gd.rs)
        self.cnt[sem] += k
        tok = (sem, self.cnt[sem])
        self.q[section].append((fn, sem, k, tuple(d for d in deps if d is not None)))
        for gd in reads:
            gd.rs.append(tok)
        for gd in writes:
            gd.w = tok
            gd.rs = []
        return tok

    def pe(self, fn, reads=(), writes=(), extra=()):
        return self._op("tensor", "pe", fn, reads, writes, extra, 1)

    def dve(self, fn, reads=(), writes=(), extra=()):
        return self._op("vector", "dve", fn, reads, writes, extra, 1)

    def act(self, fn, reads=(), writes=(), extra=()):
        return self._op("scalar", "act", fn, reads, writes, extra, 1)

    def pool(self, fn, reads=(), writes=(), extra=()):
        return self._op("gpsimd", "pool", fn, reads, writes, extra, 1)

    def hdma(self, fn, reads=(), writes=(), extra=()):
        return self._op("sync", "hdma", fn, reads, writes, extra, 16)

    def adma(self, fn, reads=(), writes=(), extra=()):
        return self._op("scalar", "adma", fn, reads, writes, extra, 16)

    def pdma(self, fn, reads=(), writes=(), extra=()):
        return self._op("gpsimd", "pdma", fn, reads, writes, extra, 16)

    def cc(self, fn, reads=(), writes=(), extra=()):
        return self._op("gpsimd", "cc", fn, reads, writes, extra, 1)

    def emit(self, blk, sems, pe_fifo=True):
        for section in self.SECTIONS:
            ops = self.q[section]
            if not ops:
                continue

            def body(eng, ops=ops, section=section):
                water = {k: 0 for k in self.cnt}
                for fn, sem, k, deps in ops:
                    for dsem, dval in deps:
                        if pe_fifo and dsem == "pe" and section == "tensor":
                            continue
                        if water[dsem] < dval:
                            eng.wait_ge(sems[dsem], dval)
                            water[dsem] = dval
                    fn(eng).then_inc(sems[sem], k)
            getattr(blk, section)(body)


class Builder:
    def __init__(self, T, dev_dbg=False, pe_fifo=True):
        self.pe_fifo = pe_fifo
        self.T = T
        self.Q = T // 4
        self.NH = 2 if self.Q >= 4096 else 1
        self.NSQ = 4 if self.Q >= 4096 else 1
        self.J = T // C
        self.dev_dbg = dev_dbg
        self.nc = bass.Bass("TRN2", target_bir_lowering=False)
        self.p = Prog()
        self.G = {}

    def g(self, name):
        if name not in self.G:
            self.G[name] = Guard()
        return self.G[name]

    @staticmethod
    def ex_u(colap, n=8, rep=128):
        return bass.AP(colap.tensor, colap.offset,
                       [list(colap.ap[0]), [1, n], [0, rep]])

    @staticmethod
    def ex_pair(ap2d, nh=4, r=2, inner=128):
        return bass.AP(ap2d.tensor, ap2d.offset,
                       [list(ap2d.ap[0]), [inner, nh], [0, r], [1, inner]])

    def declare(self):
        nc, T, Q = self.nc, self.T, self.Q
        din = lambda n, s, dt=F32: nc.dram_tensor(n, s, dt, kind="ExternalInput").ap()
        self.x_fm = din("x_fm", [D, Q])
        self.wqkv = din("wqkv", [D, 2048])
        self.wz = din("wz", [D, 1024])
        self.wb128 = din("wb128", [128, 128])
        self.wa128 = din("wa128", [128, 128])
        self.convw = din("convw", [128, 64])
        self.alog = din("alog", [VH, 1])
        self.dtb = din("dtb", [VH, 1])
        self.normw = din("normw", [128, 1])
        self.wout = din("wout", [VH * 128, D])
        self.gw128 = din("gw128", [128, 128])
        self.rbias = din("rbias", [1, 8])
        self.w1 = din("w1", [D, INTER])
        self.w3 = din("w3", [D, INTER])
        self.w2 = din("w2", [INTER, D])
        self.sw1T = din("sw1T", [D, SHS])
        self.sw3T = din("sw3T", [D, SHS])
        self.sw2T = din("sw2T", [SHS, D])
        self.idf_d = din("idf", [128, 128])
        self.idb_d = din("idb", [128, 128], BF16)
        self.onesb_d = din("onesb", [1, 128], BF16)
        self.onebc_d = din("onebc", [128, 1], BF16)
        self.cbias_d = din("cbias", [128, 4])
        self.onesf_d = din("onesf", [1, 128])
        self.sneg_d = din("sneg", [128, VH * 128], BF16)
        self.blkI_d = din("blkI", [128, VH * 128], BF16)

        dr = lambda n, s, dt=F32: nc.dram_tensor(n, s, dt)
        self.xn_blk = [dr("xn_blk%d" % i, [D, Q // self.NH], BF16) for i in range(self.NH)]
        NH, NSQ = self.NH, self.NSQ
        self.xn_bat = [dr("xn_bat%d" % i, [4 * D, Q // NH], BF16) for i in range(NH)]
        self.xn_all = [dr("xn_all%d" % i, [8 * D, Q // NH], BF16) for i in range(NH)]
        self.gates_in = [dr("gates_in%d" % i, [8, Q // self.NH], BF16) for i in range(self.NH)]
        self.gates_x = [dr("gates_x%d" % i, [8, Q // NH], BF16) for i in range(NH)]
        self.qn_sc = dr("qn_sc", [QH * 128, T], BF16)
        self.kn_sc = dr("kn_sc", [QH * 128, T], BF16)
        self.v_sc = dr("v_sc", [VH * 128, T], BF16)
        self.z_sc = dr("z_sc", [VH * 128, T], BF16)
        self.nrm_sc = dr("nrm_sc", [VH * 128, T], BF16)
        self.ps_in = [dr("ps_in%d" % m, [4 * D, Q // NSQ], F32) for m in range(NSQ)]
        self.s_own = [dr("s_own%d" % m, [D, Q // NSQ], F32) for m in range(NSQ)]
        self.y_in = dr("y_in", [8 * D, Q], F32)
        self.ffn_own = dr("ffn_own", [D, Q], F32)
        self.out_fm = nc.dram_tensor("out_fm", [D, Q], F32, kind="ExternalOutput").ap()

        self.dbg = {}
        if self.dev_dbg:
            for nm, shape, dt in [
                ("dbg_xn", [D, Q], BF16), ("dbg_gates", [8, Q], BF16),
                ("dbg_q", [QH * 128, T], BF16), ("dbg_k", [QH * 128, T], BF16),
                ("dbg_v", [VH * 128, T], BF16), ("dbg_z", [VH * 128, T], BF16),
                ("dbg_bt", [128, self.J * 8], F32), ("dbg_gc", [128, self.J * 8], F32),
                ("dbg_nrm", [VH * 128, T], BF16), ("dbg_s", [D, Q], F32),
                ("dbg_ffn", [D, Q], F32),
            ]:
                self.dbg[nm] = nc.dram_tensor(nm, shape, dt, kind="ExternalOutput").ap()

    def alloc(self, es):
        nc = self.nc
        sb = lambda n, s, dt: es.enter_context(nc.sbuf_tensor(n, s, dt))
        ps = lambda n, s, dt: es.enter_context(nc.psum_tensor(n, s, dt))
        self.AW = Arena(sb("AW", [128, 36864], BF16))
        self.ABF = Arena(sb("ABF", [128, 31744], BF16))
        self.AFA = Arena(sb("AFA", [128, 8192], F32))
        self.gcumT = sb("gcumT", [128, self.J * 8], F32)
        self.betaT = sb("betaT", [128, self.J * 8], F32)
        self.gates8 = sb("gates8", [8, self.Q], BF16)
        self.c_idf = sb("c_idf", [128, 128], F32)
        self.c_idb = sb("c_idb", [128, 128], BF16)
        self.c_onesb = sb("c_onesb", [1, 128], BF16)
        self.c_onebc = sb("c_onebc", [128, 1], BF16)
        self.c_bias = sb("c_bias", [128, 4], F32)
        self.c_onesf = sb("c_onesf", [1, 128], F32)
        self.c_ones128 = sb("c_ones128", [128, 128], F32)
        self.c_sneg = sb("c_sneg", [128, VH * 128], BF16)
        self.c_blkI = sb("c_blkI", [128, VH * 128], BF16)
        self.c_convw = sb("c_convw", [128, 64], F32)
        self.c_normw = sb("c_normw", [128, 1], F32)
        self.c_alog = sb("c_alog", [VH, 1], F32)
        self.c_dtb = sb("c_dtb", [VH, 1], F32)
        self.c_rbias = sb("c_rbias", [128, 8], F32)
        self.c_nalog = sb("c_nalog", [VH, 1], F32)
        self.c_zero8 = sb("c_zero8", [8, 1], F32)
        self.s_gw = sb("s_gw", [128, 128], F32)
        self.s_wba = sb("s_wba", [128, 256], BF16)
        self.s_cols = sb("s_cols", [128, 256], F32)
        self.s_rows = sb("s_rows", [1, 1536], F32)
        self.s_rowb = sb("s_rowb", [1, 1024], BF16)
        self.s_st = sb("s_st", [128, 1024], F32)
        self.s_stb = sb("s_stb", [128, 1024], BF16)
        self.pA = ps("pA", [128, 1024], F32)
        self.pB = ps("pB", [128, 1024], F32)
        self.pC = ps("pC", [128, 1024], F32)
        self.pT0 = ps("pT0", [128, 1024], BF16)
        self.pT1 = ps("pT1", [128, 1024], BF16)
        for n in ("pA0", "pA1", "pB0", "pB1", "pC0", "pC1", "pT0", "pT1"):
            self.g(n)

    def b0_consts(self):
        p, g = self.p, self.g
        pairs = [
            (self.c_idf, self.idf_d, "c_idf"), (self.c_idb, self.idb_d, "c_idb"),
            (self.c_onesb, self.onesb_d, "c_onesb"), (self.c_onesf, self.onesf_d, "c_onesf"),
            (self.c_onebc, self.onebc_d, "c_onebc"),
            (self.c_bias, self.cbias_d, "c_bias"),
            (self.c_sneg, self.sneg_d, "c_sneg"), (self.c_blkI, self.blkI_d, "c_blkI"),
            (self.c_convw, self.convw, "c_convw"), (self.c_normw, self.normw, "c_normw"),
            (self.c_alog, self.alog, "c_alog"), (self.c_dtb, self.dtb, "c_dtb"),
            (self.s_gw, self.gw128, "s_gw"),
        ]
        for dst, src, gn in pairs:
            p.pdma(lambda gg, dst=dst, src=src: gg.dma_start(out=dst[:, :], in_=src[:, :]),
                   writes=[g(gn)])
        p.pdma(lambda gg: gg.dma_start(out=self.s_wba[:, 0:128], in_=self.wb128[:, :]),
               writes=[g("s_wba")])
        p.pdma(lambda gg: gg.dma_start(out=self.s_wba[:, 128:256], in_=self.wa128[:, :]),
               writes=[g("s_wba")])
        p.pool(lambda gg: gg.memset(self.c_ones128[:, :], 1.0), writes=[g("c_ones128")])
        p.pool(lambda gg: gg.memset(self.c_zero8[:, :], 0.0), writes=[g("c_zero8")])
        p.pool(lambda gg: gg.memset(self.s_st[:, :], 0.0), writes=[g("s_st")])
        p.pool(lambda gg: gg.memset(self.s_stb[:, :], 0.0), writes=[g("s_stb")])
        p.pdma(lambda gg: gg.dma_start(out=self.s_rows[0:1, 0:8], in_=self.rbias[:, :]),
               writes=[g("s_rows")])
        p.pe(lambda t: t.matmul(self.pC[0:128, 0:8], self.c_onesf[:, :],
                                self.s_rows[0:1, 0:8], start=True, stop=True),
             reads=[g("c_onesf"), g("s_rows")], writes=[g("pC0")])
        p.dve(lambda v: v.tensor_copy(self.c_rbias[:, :], self.pC[0:128, 0:8]),
              reads=[g("pC0")], writes=[g("c_rbias")])
        p.act(lambda s: s.activation(self.c_nalog[:, :], self.c_alog[:, :], AF.Exp),
              reads=[g("c_alog")], writes=[g("c_nalog")])
        p.dve(lambda v: v.tensor_scalar_mul(self.c_nalog[:, :], self.c_nalog[:, :], -1.0),
              reads=[g("c_nalog")], writes=[g("c_nalog")])

    def b1_ln_router(self):
        p, g = self.p, self.g
        B1T = self.Q // 512
        for tt in range(B1T):
            t0 = tt * 512
            vx, gx = self.AFA.view(0, 8192)
            vxb, gxb = self.ABF.view(0, 8192)
            vx2, gx2 = self.ABF.view(8192, 16384)
            vxnb, gxnb = self.ABF.view(16384, 24576)
            src = self.x_fm[:, t0:t0 + 512].rearrange("(c p) t -> p c t", p=128)
            p.hdma(lambda s, src=src, vx=vx: s.dma_start(
                out=vx.rearrange("p (c t) -> p c t", t=512), in_=src), writes=[gx])
            p.pdma(lambda gg, src=src, vxb=vxb: gg.dma_start(
                out=vxb.rearrange("p (c t) -> p c t", t=512), in_=src), writes=[gxb])
            p.act(lambda s, vx=vx, vx2=vx2: s.activation(vx2, vx, AF.Square),
                  reads=[gx], writes=[gx2])
            for dc in range(16):
                p.pe(lambda t, dc=dc, vxb=vxb: t.matmul(
                    self.pC[0:1, 0:512], self.c_onebc[:, :],
                    vxb[:, dc * 512:(dc + 1) * 512], start=(dc == 0), stop=(dc == 15)),
                    reads=[gxb, g("c_onebc")], writes=[g("pC0")])
            for dc in range(16):
                p.pe(lambda t, dc=dc, vx2=vx2: t.matmul(
                    self.pC[0:1, 512:1024], self.c_onebc[:, :],
                    vx2[:, dc * 512:(dc + 1) * 512], start=(dc == 0), stop=(dc == 15)),
                    reads=[gx2, g("c_onebc")], writes=[g("pC1")])
            R, gR = self.s_rows, g("s_rows")
            p.dve(lambda v: v.tensor_scalar_mul(R[0:1, 0:512], self.pC[0:1, 0:512], 1.0 / D),
                  reads=[g("pC0")], writes=[gR])
            p.dve(lambda v: v.tensor_scalar_mul(R[0:1, 512:1024], self.pC[0:1, 512:1024], 1.0 / D),
                  reads=[g("pC1")], writes=[gR])
            p.dve(lambda v: v.tensor_tensor(R[0:1, 1024:1536], R[0:1, 0:512], R[0:1, 0:512], OP.mult),
                  reads=[gR], writes=[gR])
            p.dve(lambda v: v.tensor_tensor(R[0:1, 512:1024], R[0:1, 512:1024],
                                            R[0:1, 1024:1536], OP.subtract),
                  reads=[gR], writes=[gR])
            p.act(lambda s: s.activation(R[0:1, 512:1024], R[0:1, 512:1024], AF.Sqrt,
                                         bias=self.c_bias[0:1, 0:1]),
                  reads=[gR, g("c_bias")], writes=[gR])
            p.dve(lambda v: v.reciprocal(R[0:1, 512:1024], R[0:1, 512:1024]),
                  reads=[gR], writes=[gR])
            p.pe(lambda t: t.matmul(self.pA[:, 0:512], self.c_onesf[:, :], R[0:1, 0:512],
                                    start=True, stop=True),
                 reads=[gR, g("c_onesf")], writes=[g("pA0")])
            p.pe(lambda t: t.matmul(self.pA[:, 512:1024], self.c_onesf[:, :], R[0:1, 512:1024],
                                    start=True, stop=True),
                 reads=[gR, g("c_onesf")], writes=[g("pA1")])
            mean_x = bass.AP(self.pA[:, 0:512].tensor, 0, [[1024, 128], [0, 16], [1, 512]])
            rstd_x = bass.AP(self.pA[:, 0:512].tensor, 512, [[1024, 128], [0, 16], [1, 512]])
            vx3 = vx.rearrange("p (c t) -> p c t", t=512)
            p.dve(lambda v, vx3=vx3, mean_x=mean_x: v.tensor_tensor(
                vx3, vx3, mean_x, OP.subtract), reads=[g("pA0")], writes=[gx])
            p.dve(lambda v, vx3=vx3, rstd_x=rstd_x: v.tensor_tensor(
                vx3, vx3, rstd_x, OP.mult), reads=[g("pA1")], writes=[gx])
            p.dve(lambda v, vx=vx, vxnb=vxnb: v.tensor_copy(vxnb, vx),
                  reads=[gx], writes=[gxnb])
            for sub in range(4):
                for dc in range(16):
                    p.pe(lambda t, sub=sub, dc=dc, vx=vx: t.matmul(
                        self.pB[:, sub * 8:sub * 8 + 8],
                        vx[:, dc * 512 + sub * 128: dc * 512 + sub * 128 + 128],
                        self.s_gw[:, dc * 8:(dc + 1) * 8],
                        start=(dc == 0), stop=(dc == 15)),
                        reads=[gx, g("s_gw")], writes=[g("pB0")])
            SC, gS = self.s_cols, g("s_cols")
            for sub in range(4):
                pg = self.pB[:, sub * 8:sub * 8 + 8]
                p.act(lambda s, pg=pg: s.activation(SC[:, 0:8], pg, AF.Sigmoid),
                      reads=[g("pB0")], writes=[gS])
                p.dve(lambda v: v.tensor_tensor(SC[:, 8:16], SC[:, 0:8], self.c_rbias[:, :], OP.add),
                      reads=[gS, g("c_rbias")], writes=[gS])
                p.dve(lambda v: v.reduce_max(SC[:, 16:17], SC[:, 8:16], axis=AX.X),
                      reads=[gS], writes=[gS])
                p.dve(lambda v: v.tensor_scalar(SC[:, 24:32], SC[:, 8:16], SC[:, 16:17],
                                                None, OP.is_ge),
                      reads=[gS], writes=[gS])
                p.dve(lambda v: v.scalar_tensor_tensor(SC[:, 32:40], SC[:, 24:32], -1e9,
                                                       SC[:, 8:16], OP.mult, OP.add),
                      reads=[gS], writes=[gS])
                p.dve(lambda v: v.reduce_max(SC[:, 40:41], SC[:, 32:40], axis=AX.X),
                      reads=[gS], writes=[gS])
                p.dve(lambda v: v.tensor_scalar(SC[:, 48:56], SC[:, 8:16], SC[:, 40:41],
                                                None, OP.is_ge),
                      reads=[gS], writes=[gS])
                p.dve(lambda v: v.tensor_tensor(SC[:, 56:64], SC[:, 0:8], SC[:, 48:56], OP.mult),
                      reads=[gS], writes=[gS])
                p.dve(lambda v: v.reduce_sum(SC[:, 64:65], SC[:, 56:64], axis=AX.X),
                      reads=[gS], writes=[gS])
                p.dve(lambda v: v.tensor_scalar_max(SC[:, 64:65], SC[:, 64:65], 1e-12),
                      reads=[gS], writes=[gS])
                p.dve(lambda v: v.reciprocal(SC[:, 64:65], SC[:, 64:65]),
                      reads=[gS], writes=[gS])
                p.dve(lambda v: v.tensor_scalar(SC[:, 72:80], SC[:, 56:64], SC[:, 64:65],
                                                None, OP.mult),
                      reads=[gS], writes=[gS])
                vgb, ggb = self.ABF.view(24576, 24584)
                p.dve(lambda v, vgb=vgb: v.tensor_copy(vgb, SC[:, 72:80]),
                      reads=[gS], writes=[ggb])
                p.pe(lambda t, vgb=vgb: t.transpose(self.pT0[0:8, 0:128], vgb, self.c_idb[:, :]),
                     reads=[ggb, g("c_idb")], writes=[g("pT0")])
                vgf, ggf = self.ABF.view(24704, 24832)
                p.dve(lambda v, vgf=vgf: v.tensor_copy(vgf[0:8, :], self.pT0[0:8, 0:128]),
                      reads=[g("pT0")], writes=[ggf])
                gc0 = t0 + sub * 128
                ghh = min(gc0 // (self.Q // self.NH), self.NH - 1)
                dst = self.gates_in[ghh].ap()[:, gc0 - ghh * (self.Q // self.NH):
                                              gc0 - ghh * (self.Q // self.NH) + 128]
                p.adma(lambda s, dst=dst, vgf=vgf: s.dma_start(out=dst, in_=vgf[0:8, :]),
                       reads=[ggf], writes=[g("d_gates_in")])
            xhh = min(t0 // (self.Q // self.NH), self.NH - 1)
            t0h = t0 - xhh * (self.Q // self.NH)
            dst = self.xn_blk[xhh].ap()[:, t0h:t0h + 512].rearrange("(c p) t -> p c t", p=128)
            p.adma(lambda s, dst=dst, vxnb=vxnb: s.dma_start(
                out=dst, in_=vxnb.rearrange("p (c t) -> p c t", t=512)),
                   reads=[gxnb], writes=[g("d_xn_blk")])
            if (self.NH == 2 and tt == B1T // 2 - 1) or tt == B1T - 1:
                hh = 0 if (self.NH == 2 and tt == B1T // 2 - 1) else self.NH - 1
                Qh = self.Q // self.NH
                xin = self.xn_blk[hh].ap()
                gin = self.gates_in[hh].ap()
                p.cc(lambda gg, xin=xin, hh=hh: gg.collective_compute(
                    "AllGather", OP.bypass, replica_groups=[[0, 1, 2, 3], [4, 5, 6, 7]],
                    ins=[xin.opt()], outs=[self.xn_bat[hh].ap().opt()]),
                    reads=[g("d_xn_blk")], writes=[g("d_xn_bat%d" % hh)])
                p.cc(lambda gg, xin=xin, hh=hh: gg.collective_compute(
                    "AllGather", OP.bypass, replica_groups=[list(range(8))],
                    ins=[xin.opt()], outs=[self.xn_all[hh].ap().opt()]),
                    reads=[g("d_xn_blk")], writes=[g("d_xn_all%d" % hh)])

                def _a2a(gg, gin=gin, hh=hh):
                    with allow_ici_all_to_all():
                        return gg.collective_compute(
                            "AllToAll", OP.bypass, replica_groups=[list(range(8))],
                            ins=[gin.opt()], outs=[self.gates_x[hh].ap().opt()])
                p.cc(_a2a, reads=[g("d_gates_in")], writes=[g("d_gates_x%d" % hh)])

    def b2_collectives(self):
        p, g = self.p, self.g
        if self.dev_dbg:
            for hh in range(self.NH):
                p.pdma(lambda gg, hh=hh: gg.dma_start(
                    out=self.dbg["dbg_xn"][:, hh * (self.Q // self.NH):(hh + 1) * (self.Q // self.NH)],
                    in_=self.xn_blk[hh].ap()[:, :]),
                    reads=[g("d_xn_blk")], writes=[g("dbg")])
            for hh in range(self.NH):
                p.pdma(lambda gg, hh=hh: gg.dma_start(
                    out=self.dbg["dbg_gates"][:, hh * (self.Q // self.NH):(hh + 1) * (self.Q // self.NH)],
                    in_=self.gates_x[hh].ap()[:, :]),
                    reads=[g("d_gates_x%d" % hh)], writes=[g("dbg")])

    def b7_ffn(self):
        p, g = self.p, self.g
        Q = self.Q
        B7T = (2 * self.T) // 512
        Qh2 = Q // self.NH
        for hh in range(self.NH):
            p.pdma(lambda gg, hh=hh, Qh2=Qh2: gg.dma_start(
                out=self.gates8[:, hh * Qh2:(hh + 1) * Qh2], in_=self.gates_x[hh].ap()[:, :]),
                reads=[g("d_gates_x%d" % hh)], writes=[g("gates8")])
        sweeps = [("e", 0, 6), ("e", 6, 11), ("sh", 0, 3)]
        for sw, (kind, h0, h1) in enumerate(sweeps):
            nh = h1 - h0
            if kind == "e":
                hszs = [128] * nh
                WROW = nh * 128
                w1v, gw1 = self.AW.view(0, 16 * WROW)
                w3v, gw3 = self.AW.view(16 * WROW, 32 * WROW)
                w2v, gw2 = self.AW.view(32 * WROW, 32 * WROW + nh * 2048)
                for dc in range(16):
                    src = self.w1[dc * 128:(dc + 1) * 128, h0 * 128:h1 * 128]
                    p.pdma(lambda gg, src=src, dc=dc, w1v=w1v, WROW=WROW: gg.dma_start(
                        out=w1v[:, dc * WROW:(dc + 1) * WROW], in_=src), writes=[gw1])
                for dc in range(16):
                    src = self.w3[dc * 128:(dc + 1) * 128, h0 * 128:h1 * 128]
                    p.pdma(lambda gg, src=src, dc=dc, w3v=w3v, WROW=WROW: gg.dma_start(
                        out=w3v[:, dc * WROW:(dc + 1) * WROW], in_=src), writes=[gw3])
                for hc in range(nh):
                    src = self.w2[(h0 + hc) * 128:(h0 + hc + 1) * 128, :]
                    p.pdma(lambda gg, src=src, hc=hc, w2v=w2v: gg.dma_start(
                        out=w2v[:, hc * 2048:(hc + 1) * 2048], in_=src), writes=[gw2])
            else:
                hszs = [128, 128, 96]
                WROW = SHS
                w1v, gw1 = self.AW.view(0, 16 * SHS)
                w3v, gw3 = self.AW.view(16 * SHS, 32 * SHS)
                w2v, gw2 = self.AW.view(32 * SHS, 32 * SHS + 3 * 2048)
                for dc in range(16):
                    src = self.sw1T[dc * 128:(dc + 1) * 128, :]
                    p.pdma(lambda gg, src=src, dc=dc, w1v=w1v: gg.dma_start(
                        out=w1v[:, dc * SHS:(dc + 1) * SHS], in_=src), writes=[gw1])
                for dc in range(16):
                    src = self.sw3T[dc * 128:(dc + 1) * 128, :]
                    p.pdma(lambda gg, src=src, dc=dc, w3v=w3v: gg.dma_start(
                        out=w3v[:, dc * SHS:(dc + 1) * SHS], in_=src), writes=[gw3])
                for hc in range(3):
                    hs = hszs[hc]
                    src = self.sw2T[hc * 128:hc * 128 + hs, :]
                    p.pdma(lambda gg, src=src, hc=hc, hs=hs, w2v=w2v: gg.dma_start(
                        out=w2v[0:hs, hc * 2048:(hc + 1) * 2048], in_=src), writes=[gw2])
            nh_ = len(hszs)
            for gt in range(B7T):
                qq, lc = (gt * 512) // Q, (gt * 512) % Q
                par = gt % 2
                vxn, gxn = self.ABF.view(par * 8192, par * 8192 + 8192)
                hh = 0 if lc < Qh2 else 1
                lch = lc - hh * Qh2
                src = self.xn_all[hh].ap()[qq * 2048:(qq + 1) * 2048,
                                           lch:lch + 512].rearrange("(c p) t -> p c t", p=128)
                p.hdma(lambda s, src=src, vxn=vxn: s.dma_start(
                    out=vxn.rearrange("p (c t) -> p c t", t=512), in_=src),
                    writes=[gxn], extra=[g("d_xn_all%d" % hh).w])
                gate_sb, ggsb = self.ABF.view(19968 + par * 512, 19968 + par * 512 + 512)
                if kind == "e":
                    p.pe(lambda t, qq=qq, lc=lc: t.matmul(
                        self.pB[0:1, 512:1024], self.c_idb[0:8, qq:qq + 1],
                        self.gates8[:, lc:lc + 512], start=True, stop=True),
                        reads=[g("gates8"), g("c_idb")], writes=[g("pB1")])
                    p.dve(lambda v: v.tensor_copy(self.s_rowb[0:1, 0:512],
                                                  self.pB[0:1, 512:1024]),
                          reads=[g("pB1")], writes=[g("s_rowb")])
                    p.pe(lambda t: t.matmul(self.pB[:, 512:1024], self.c_onesb[:, :],
                                            self.s_rowb[0:1, 0:512], start=True, stop=True),
                         reads=[g("s_rowb"), g("c_onesb")], writes=[g("pB1")])
                    p.dve(lambda v, gate_sb=gate_sb: v.tensor_copy(
                        gate_sb, self.pB[:, 512:1024]),
                        reads=[g("pB1")], writes=[ggsb])
                g13v, g13g = self.ABF.view(16896, 16896 + 6 * 512)
                silv, gsil = self.ABF.view(16384, 16896)
                hoff = 0
                for hc in range(nh_):
                    hs = hszs[hc]
                    slot = hc % 2
                    pH = self.pA if slot == 0 else self.pB
                    gh1 = g("pA0") if slot == 0 else g("pB0")
                    gh3 = g("pA1") if slot == 0 else g("pB1")
                    for dc in range(16):
                        p.pe(lambda t, dc=dc, hs=hs, hoff=hoff, vxn=vxn, w1v=w1v,
                             WROW=WROW, pH=pH: t.matmul(
                            pH[0:hs, 0:512], w1v[:, dc * WROW + hoff: dc * WROW + hoff + hs],
                            vxn[:, dc * 512:(dc + 1) * 512], start=(dc == 0), stop=(dc == 15)),
                            reads=[gw1, gxn], writes=[gh1])
                    for dc in range(16):
                        p.pe(lambda t, dc=dc, hs=hs, hoff=hoff, vxn=vxn, w3v=w3v,
                             WROW=WROW, pH=pH: t.matmul(
                            pH[0:hs, 512:1024], w3v[:, dc * WROW + hoff: dc * WROW + hoff + hs],
                            vxn[:, dc * 512:(dc + 1) * 512], start=(dc == 0), stop=(dc == 15)),
                            reads=[gw3, gxn], writes=[gh3])
                    p.act(lambda s, hs=hs, pH=pH: s.activation(
                        silv[0:hs, 0:512], pH[0:hs, 0:512], AF.Silu),
                        reads=[gh1], writes=[gsil])
                    p.dve(lambda v, hs=hs, pH=pH: v.tensor_tensor(
                        silv[0:hs, 0:512], pH[0:hs, 512:1024], silv[0:hs, 0:512], OP.mult),
                        reads=[gh3, gsil], writes=[gsil])
                    if kind == "e":
                        p.dve(lambda v, hc=hc, hs=hs, gate_sb=gate_sb: v.tensor_tensor(
                            g13v[0:hs, hc * 512:(hc + 1) * 512], silv[0:hs, 0:512],
                            gate_sb[0:hs, :], OP.mult),
                            reads=[gsil, ggsb], writes=[g13g])
                    else:
                        p.dve(lambda v, hc=hc, hs=hs: v.tensor_copy(
                            g13v[0:hs, hc * 512:(hc + 1) * 512], silv[0:hs, 0:512]),
                            reads=[gsil], writes=[g13g])
                    hoff += hs
                yst, gyst = self.AFA.view(0, 8192)
                for dc in range(16):
                    half = dc % 2
                    gy = g("pC0") if half == 0 else g("pC1")
                    for hc in range(nh_):
                        hs = hszs[hc]
                        p.pe(lambda t, dc=dc, hc=hc, hs=hs, half=half, w2v=w2v,
                             nh_=nh_: t.matmul(
                            self.pC[:, half * 512:half * 512 + 512],
                            w2v[0:hs, hc * 2048 + dc * 128: hc * 2048 + (dc + 1) * 128],
                            g13v[0:hs, hc * 512:(hc + 1) * 512],
                            start=(hc == 0), stop=(hc == nh_ - 1)),
                            reads=[gw2, g13g], writes=[gy])
                    p.act(lambda s, dc=dc, half=half, yst=yst: s.activation(
                        yst[:, dc * 512:(dc + 1) * 512],
                        self.pC[:, half * 512:half * 512 + 512], AF.Copy),
                        reads=[gy], writes=[gyst])
                dst = self.y_in.ap()[qq * 2048:(qq + 1) * 2048, lc:lc + 512].rearrange(
                    "(c p) t -> p c t", p=128)
                ysr = yst.rearrange("p (c t) -> p c t", t=512)
                if sw == 0:
                    p.pdma(lambda gg, dst=dst, ysr=ysr: gg.dma_start(out=dst, in_=ysr),
                           reads=[gyst], writes=[g("d_y_%d_%d" % (qq, lc))])
                else:
                    p.pdma(lambda gg, dst=dst, ysr=ysr: gg.dma_start(
                        out=dst, in_=ysr, accum_op=OP.add),
                        reads=[gyst], writes=[g("d_y_%d_%d" % (qq, lc))])
        ally = [self.g("d_y_%d_%d" % (qq, lc)) for qq in range(8) for lc in range(0, Q, 512)]
        p.cc(lambda gg: gg.collective_compute(
            "ReduceScatter", OP.add, replica_groups=[list(range(8))],
            ins=[self.y_in.ap().opt()], outs=[self.ffn_own.ap().opt()]),
            reads=ally, writes=[g("d_ffn_own")])
        if self.dev_dbg:
            p.pdma(lambda gg: gg.dma_start(out=self.dbg["dbg_ffn"][:, :],
                                           in_=self.ffn_own.ap()[:, :]),
                   reads=[g("d_ffn_own")], writes=[g("dbg")])

    def b3_proj(self):
        p, g = self.p, self.g
        T, Q = self.T, self.Q
        B3T = T // 512
        wv, gwv = self.AW.view(0, 32768)
        for dc in range(16):
            src = self.wqkv[dc * 128:(dc + 1) * 128, :]
            p.pdma(lambda gg, src=src, dc=dc, wv=wv: gg.dma_start(
                out=wv[:, dc * 2048:(dc + 1) * 2048], in_=src), writes=[gwv])
        rings = []
        for cc in range(16):
            rv, gr = self.ABF.view(16384 + cc * 520, 16384 + cc * 520 + 515)
            rings.append((rv, gr))
            p.pool(lambda gg, rv=rv: gg.memset(rv[:, 0:3], 0.0), writes=[gr])
        for tt in range(B3T):
            par = tt % 2
            t0g = tt * 512
            r, col = t0g // Q, t0g % Q
            vxn, gxn = self.ABF.view(par * 8192, par * 8192 + 8192)
            hh = min(col // (Q // self.NH), self.NH - 1)
            colh = col - hh * (Q // self.NH)
            src = self.xn_bat[hh].ap()[r * 2048:(r + 1) * 2048, colh:colh + 512].rearrange(
                "(c p) t -> p c t", p=128)
            p.hdma(lambda s, src=src, vxn=vxn: s.dma_start(
                out=vxn.rearrange("p (c t) -> p c t", t=512), in_=src),
                   writes=[gxn], extra=[self.g("d_xn_bat%d" % hh).w])
            for cc in range(16):
                a0, ga0 = self.ABF.view(26752, 27264) if cc % 2 == 0 else self.ABF.view(27776, 28288)
                a1, ga1 = self.ABF.view(27264, 27776) if cc % 2 == 0 else self.ABF.view(28288, 28800)
                rv, gr = rings[cc]
                ph = (cc % 2) * 512
                gph = g("pA0") if cc % 2 == 0 else g("pA1")
                for dc in range(16):
                    p.pe(lambda t, dc=dc, cc=cc, ph=ph, vxn=vxn, wv=wv: t.matmul(
                        self.pA[:, ph:ph + 512],
                        wv[:, dc * 2048 + cc * 128: dc * 2048 + (cc + 1) * 128],
                        vxn[:, dc * 512:(dc + 1) * 512], start=(dc == 0), stop=(dc == 15)),
                        reads=[gwv, gxn], writes=[gph])
                p.dve(lambda v, rv=rv, ph=ph: v.tensor_copy(rv[:, 3:515],
                                                            self.pA[:, ph:ph + 512]),
                      reads=[gph], writes=[gr])
                p.dve(lambda v, rv=rv, cc=cc, a0=a0: v.tensor_scalar(
                    a0, rv[:, 0:512], self.c_convw[:, cc * 4:cc * 4 + 1], None, OP.mult),
                    reads=[gr, g("c_convw")], writes=[ga0])
                p.dve(lambda v, rv=rv, cc=cc, a0=a0, a1=a1: v.scalar_tensor_tensor(
                    a1, rv[:, 1:513], self.c_convw[:, cc * 4 + 1:cc * 4 + 2], a0, OP.mult, OP.add),
                    reads=[gr, ga0, g("c_convw")], writes=[ga1])
                p.dve(lambda v, rv=rv, cc=cc, a0=a0, a1=a1: v.scalar_tensor_tensor(
                    a0, rv[:, 2:514], self.c_convw[:, cc * 4 + 2:cc * 4 + 3], a1, OP.mult, OP.add),
                    reads=[gr, ga1, g("c_convw")], writes=[ga0])
                p.dve(lambda v, rv=rv, cc=cc, a0=a0, a1=a1: v.scalar_tensor_tensor(
                    a1, rv[:, 3:515], self.c_convw[:, cc * 4 + 3:cc * 4 + 4], a0, OP.mult, OP.add),
                    reads=[gr, ga0, g("c_convw")], writes=[ga1])
                p.dve(lambda v, rv=rv: v.tensor_copy(rv[:, 0:3], rv[:, 512:515]),
                      reads=[gr], writes=[gr])
                if cc < 4:
                    qf, gqf = self.AFA.view(1024, 1536)
                    p.act(lambda s, qf=qf, a1=a1: s.activation(qf, a1, AF.Silu),
                          reads=[ga1], writes=[gqf])
                    sq, gsq = self.ABF.view(24704, 25216)
                    p.act(lambda s, sq=sq, qf=qf: s.activation(sq, qf, AF.Square),
                          reads=[gqf], writes=[gsq])
                    p.pe(lambda t, sq=sq: t.matmul(self.pC[0:1, 0:512], self.c_onebc[:, :],
                                                   sq, start=True, stop=True),
                         reads=[gsq, g("c_onebc")], writes=[g("pC0")])
                    R, gR = self.s_rows, g("s_rows")
                    p.dve(lambda v: v.tensor_scalar_add(R[0:1, 0:512], self.pC[0:1, 0:512], 1e-6),
                          reads=[g("pC0")], writes=[gR])
                    p.act(lambda s: s.activation(R[0:1, 0:512], R[0:1, 0:512], AF.Sqrt),
                          reads=[gR], writes=[gR])
                    p.dve(lambda v: v.reciprocal(R[0:1, 0:512], R[0:1, 0:512]),
                          reads=[gR], writes=[gR])
                    p.dve(lambda v: v.tensor_copy(self.s_rowb[0:1, 0:512], R[0:1, 0:512]),
                          reads=[gR], writes=[g("s_rowb")])
                    p.pe(lambda t: t.matmul(self.pB[:, 0:512], self.c_onesb[:, :],
                                            self.s_rowb[0:1, 0:512], start=True, stop=True),
                         reads=[g("s_rowb"), g("c_onesb")], writes=[g("pB0")])
                    qn, gqn = self.ABF.view(25216, 25728)
                    p.dve(lambda v, qn=qn, qf=qf: v.tensor_tensor(qn, qf, self.pB[:, 0:512],
                                                                  OP.mult),
                          reads=[gqf, g("pB0")], writes=[gqn])
                    dst = self.qn_sc.ap()[cc * 128:(cc + 1) * 128, t0g:t0g + 512]
                    p.adma(lambda s, dst=dst, qn=qn: s.dma_start(out=dst, in_=qn),
                           reads=[gqn], writes=[g("d_qn")])
                elif cc < 8:
                    kb_, gkb = self.ABF.view(25728, 26240)
                    p.act(lambda s, kb_=kb_, a1=a1: s.activation(kb_, a1, AF.Silu),
                          reads=[ga1], writes=[gkb])
                    dst = self.kn_sc.ap()[(cc - 4) * 128:(cc - 3) * 128, t0g:t0g + 512]
                    p.adma(lambda s, dst=dst, kb_=kb_: s.dma_start(out=dst, in_=kb_),
                           reads=[gkb], writes=[g("d_kn")])
                else:
                    vv, gvv = self.ABF.view(26240, 26752)
                    p.act(lambda s, vv=vv, a1=a1: s.activation(vv, a1, AF.Silu),
                          reads=[ga1], writes=[gvv])
                    dst = self.v_sc.ap()[(cc - 8) * 128:(cc - 7) * 128, t0g:t0g + 512]
                    p.adma(lambda s, dst=dst, vv=vv: s.dma_start(out=dst, in_=vv),
                           reads=[gvv], writes=[g("d_v")])
            for dc in range(16):
                p.pe(lambda t, dc=dc, vxn=vxn: t.matmul(
                    self.pC[0:8, 0:512], self.s_wba[:, dc * 8:(dc + 1) * 8],
                    vxn[:, dc * 512:(dc + 1) * 512], start=(dc == 0), stop=(dc == 15)),
                    reads=[g("s_wba"), gxn], writes=[g("pC0")])
            for dc in range(16):
                p.pe(lambda t, dc=dc, vxn=vxn: t.matmul(
                    self.pC[0:8, 512:1024], self.s_wba[:, 128 + dc * 8:128 + (dc + 1) * 8],
                    vxn[:, dc * 512:(dc + 1) * 512], start=(dc == 0), stop=(dc == 15)),
                    reads=[g("s_wba"), gxn], writes=[g("pC1")])
            bt, gbt = self.AFA.view(1536, 2048)
            ga_, gga = self.AFA.view(2048, 2560)
            tx, gtx = self.AFA.view(2560, 3072)
            ty, gty = self.AFA.view(3072, 3584)
            p.act(lambda s, bt=bt: s.activation(bt[0:8, :], self.pC[0:8, 0:512], AF.Sigmoid),
                  reads=[g("pC0")], writes=[gbt])
            p.dve(lambda v, ga_=ga_: v.tensor_scalar(ga_[0:8, :], self.pC[0:8, 512:1024],
                                                     self.c_dtb[:, :], None, OP.add),
                  reads=[g("pC1"), g("c_dtb")], writes=[gga])
            p.act(lambda s, ga_=ga_, tx=tx: s.activation(tx[0:8, :], ga_[0:8, :], AF.Abs),
                  reads=[gga], writes=[gtx])
            p.act(lambda s, tx=tx: s.activation(tx[0:8, :], tx[0:8, :], AF.Exp, scale=-1.0),
                  reads=[gtx], writes=[gtx])
            p.act(lambda s, tx=tx: s.activation(tx[0:8, :], tx[0:8, :], AF.Ln,
                                                bias=self.c_bias[0:8, 1:2]),
                  reads=[gtx, g("c_bias")], writes=[gtx])
            p.dve(lambda v, ga_=ga_, ty=ty: v.tensor_scalar_max(ty[0:8, :], ga_[0:8, :], 0.0),
                  reads=[gga], writes=[gty])
            p.dve(lambda v, tx=tx, ty=ty: v.tensor_tensor(ty[0:8, :], ty[0:8, :], tx[0:8, :],
                                                          OP.add),
                  reads=[gtx], writes=[gty])
            p.dve(lambda v, ty=ty: v.tensor_scalar(ty[0:8, :], ty[0:8, :], self.c_nalog[:, :],
                                                   None, OP.mult),
                  reads=[g("c_nalog")], writes=[gty])
            gc_, ggc = self.AFA.view(3584, 4096)
            for sub in range(4):
                p.dve(lambda v, sub=sub, ty=ty, gc_=gc_: v.tensor_tensor_scan(
                    gc_[0:8, sub * 128:(sub + 1) * 128], ty[0:8, sub * 128:(sub + 1) * 128],
                    ty[0:8, sub * 128:(sub + 1) * 128], self.c_zero8[:, :], OP.add, OP.bypass),
                    reads=[gty, g("c_zero8")], writes=[ggc])
            for sub in range(4):
                p.pe(lambda t, sub=sub, gc_=gc_: t.transpose(
                    self.pA[:, sub * 8:sub * 8 + 8], gc_[0:8, sub * 128:(sub + 1) * 128],
                    self.c_idf[0:8, 0:8]),
                    reads=[ggc, g("c_idf")], writes=[g("pA0")])
                p.pe(lambda t, sub=sub, bt=bt: t.transpose(
                    self.pA[:, 32 + sub * 8:32 + sub * 8 + 8], bt[0:8, sub * 128:(sub + 1) * 128],
                    self.c_idf[0:8, 0:8]),
                    reads=[gbt, g("c_idf")], writes=[g("pA0")])
            p.dve(lambda v, tt=tt: v.tensor_copy(self.gcumT[:, tt * 32:tt * 32 + 32],
                                                 self.pA[:, 0:32]),
                  reads=[g("pA0")], writes=[g("gcumT")])
            p.dve(lambda v, tt=tt: v.tensor_copy(self.betaT[:, tt * 32:tt * 32 + 32],
                                                 self.pA[:, 32:64]),
                  reads=[g("pA0")], writes=[g("betaT")])
        wzv, gwz = self.AW.view(0, 16384)
        for dc in range(16):
            src = self.wz[dc * 128:(dc + 1) * 128, :]
            p.pdma(lambda gg, src=src, dc=dc, wzv=wzv: gg.dma_start(
                out=wzv[:, dc * 1024:(dc + 1) * 1024], in_=src), writes=[gwz])
        for tt in range(B3T):
            par = tt % 2
            t0g = tt * 512
            r, col = t0g // Q, t0g % Q
            vxn, gxn = self.ABF.view(par * 8192, par * 8192 + 8192)
            hh = min(col // (Q // self.NH), self.NH - 1)
            colh = col - hh * (Q // self.NH)
            src = self.xn_bat[hh].ap()[r * 2048:(r + 1) * 2048, colh:colh + 512].rearrange(
                "(c p) t -> p c t", p=128)
            p.hdma(lambda s, src=src, vxn=vxn: s.dma_start(
                out=vxn.rearrange("p (c t) -> p c t", t=512), in_=src), writes=[gxn])
            for zc in range(8):
                for dc in range(16):
                    p.pe(lambda t, zc=zc, dc=dc, vxn=vxn, wzv=wzv: t.matmul(
                        self.pA[:, (zc % 2) * 512:(zc % 2) * 512 + 512],
                        wzv[:, dc * 1024 + zc * 128: dc * 1024 + (zc + 1) * 128],
                        vxn[:, dc * 512:(dc + 1) * 512], start=(dc == 0), stop=(dc == 15)),
                        reads=[gwz, gxn], writes=[g("pA0") if zc % 2 == 0 else g("pA1")])
                zst, gzst = self.ABF.view(16384 + (zc % 2) * 512, 16384 + (zc % 2) * 512 + 512)
                p.act(lambda s, zst=zst, zc=zc: s.activation(
                    zst, self.pA[:, (zc % 2) * 512:(zc % 2) * 512 + 512], AF.Copy),
                    reads=[g("pA0") if zc % 2 == 0 else g("pA1")], writes=[gzst])
                dst = self.z_sc.ap()[zc * 128:(zc + 1) * 128, t0g:t0g + 512]
                p.adma(lambda s, dst=dst, zst=zst: s.dma_start(out=dst, in_=zst),
                       reads=[gzst], writes=[g("d_z")])
        if self.dev_dbg:
            for nm, t_ in [("dbg_q", self.qn_sc), ("dbg_k", self.kn_sc),
                           ("dbg_v", self.v_sc), ("dbg_z", self.z_sc)]:
                p.pdma(lambda gg, nm=nm, t_=t_: gg.dma_start(
                    out=self.dbg[nm][:, :], in_=t_.ap()[:, :]),
                    reads=[g("d_qn"), g("d_kn"), g("d_v"), g("d_z")], writes=[g("dbg")])
            p.pdma(lambda gg: gg.dma_start(out=self.dbg["dbg_bt"][:, :], in_=self.betaT[:, :]),
                   reads=[g("betaT")], writes=[g("dbg")])
            p.pdma(lambda gg: gg.dma_start(out=self.dbg["dbg_gc"][:, :], in_=self.gcumT[:, :]),
                   reads=[g("gcumT")], writes=[g("dbg")])

    def b4_scan(self):
        p, g = self.p, self.g
        T, Q, J = self.T, self.Q, self.J
        SC = self.s_cols
        gSC = g("s_cols")

        def emit_preA(j):
            par = j % 2
            AB = self.ABF if par == 0 else self.AW
            jc = j * 8

            def ab(a, b, AB=AB):
                return AB.view(a, b)

            def afv(a, b, par=par):
                return self.AFA.view(a + par * 4096, b + par * 4096)

            ctx = {"j": j, "jc": jc, "par": par, "ab": ab, "afv": afv}
            co2 = 96 + par * 48
            ctx["co2"] = co2
            qv, gq = ab(0, 512)
            src = self.qn_sc.ap()[:, j * 128:(j + 1) * 128].rearrange(
                "(h p) t -> p h t", p=128)
            p.hdma(lambda s, src=src, qv=qv: s.dma_start(
                out=qv.rearrange("p (h t) -> p h t", t=128), in_=src),
                writes=[gq], extra=[g("d_qn").w])
            kfm_r, gkfr = ab(1024, 1536)
            src = self.kn_sc.ap()[:, j * 128:(j + 1) * 128].rearrange(
                "(h p) t -> p h t", p=128)
            p.hdma(lambda s, src=src, kfm_r=kfm_r: s.dma_start(
                out=kfm_r.rearrange("p (h t) -> p h t", t=128), in_=src),
                writes=[gkfr], extra=[g("d_kn").w])
            vfm_r, gvfr = ab(2048, 3072)
            src = self.v_sc.ap()[:, j * 128:(j + 1) * 128].rearrange(
                "(u p) t -> p u t", p=128)
            p.hdma(lambda s, src=src, vfm_r=vfm_r: s.dma_start(
                out=vfm_r.rearrange("p (u t) -> p u t", t=128), in_=src),
                writes=[gvfr], extra=[g("d_v").w])
            zfm_r, gzfr = ab(4096, 5120)
            src = self.z_sc.ap()[:, j * 128:(j + 1) * 128].rearrange(
                "(u p) t -> p u t", p=128)
            p.adma(lambda s, src=src, zfm_r=zfm_r: s.dma_start(
                out=zfm_r.rearrange("p (u t) -> p u t", t=128), in_=src),
                writes=[gzfr], extra=[g("d_z").w])
            p.act(lambda s, jc=jc, co2=co2: s.activation(
                SC[:, co2:co2 + 8], self.gcumT[:, jc:jc + 8], AF.Exp, bias=self.c_bias[:, 2:3]),
                  reads=[g("gcumT"), g("c_bias")], writes=[gSC])
            p.pe(lambda t, jc=jc: t.matmul(self.pC[0:1, 896:904], self.c_idf[:, 127:128],
                                           self.gcumT[:, jc:jc + 8], start=True, stop=True),
                 reads=[g("gcumT"), g("c_idf")], writes=[g("pC1")])
            p.dve(lambda v: v.tensor_copy(self.s_rows[0:1, 16:24], self.pC[0:1, 896:904]),
                  reads=[g("pC1")], writes=[g("s_rows")])
            p.pe(lambda t: t.matmul(self.pC[:, 904:912], self.c_onesf[:, :],
                                    self.s_rows[0:1, 16:24], start=True, stop=True),
                 reads=[g("s_rows"), g("c_onesf")], writes=[g("pC1")])
            p.act(lambda s, co2=co2: s.activation(SC[:, co2 + 8:co2 + 16], self.pC[:, 904:912],
                                                  AF.Exp),
                  reads=[g("pC1")], writes=[gSC])
            p.dve(lambda v, jc=jc, co2=co2: v.tensor_tensor(
                SC[:, co2 + 16:co2 + 24], self.pC[:, 904:912],
                self.gcumT[:, jc:jc + 8], OP.subtract),
                  reads=[g("pC1"), g("gcumT")], writes=[gSC])
            p.act(lambda s, co2=co2: s.activation(SC[:, co2 + 16:co2 + 24],
                                                  SC[:, co2 + 16:co2 + 24], AF.Exp),
                  reads=[gSC], writes=[gSC])
            for h in range(4):
                p.pe(lambda t, h=h, kfm_r=kfm_r: t.transpose(
                    self.pT0[:, h * 128:(h + 1) * 128], kfm_r[:, h * 128:(h + 1) * 128],
                    self.c_idb[:, :]),
                    reads=[gkfr, g("c_idb")], writes=[g("pT0")])
            k2, gk2 = afv(3072, 3584)
            p.act(lambda s, k2=k2: s.activation(k2, self.pT0[:, 0:512], AF.Square),
                  reads=[g("pT0")], writes=[gk2])
            p.dve(lambda v, k2=k2: v.reduce_sum(
                SC[:, 0:4], k2.rearrange("p (h a) -> p h a", a=128), axis=AX.X),
                reads=[gk2], writes=[gSC])
            p.dve(lambda v: v.tensor_scalar_add(SC[:, 4:8], SC[:, 0:4], 1e-6),
                  reads=[gSC], writes=[gSC])
            p.act(lambda s: s.activation(SC[:, 4:8], SC[:, 4:8], AF.Sqrt),
                  reads=[gSC], writes=[gSC])
            p.dve(lambda v: v.reciprocal(SC[:, 4:8], SC[:, 4:8]),
                  reads=[gSC], writes=[gSC])
            p.dve(lambda v: v.tensor_copy(
                SC[:, 8:16].rearrange("p (u o) -> p u o", o=2)[:, :, 0:1],
                SC[:, 4:8].rearrange("p (u o) -> p u o", o=1)),
                reads=[gSC], writes=[gSC])
            p.dve(lambda v: v.tensor_copy(
                SC[:, 8:16].rearrange("p (u o) -> p u o", o=2)[:, :, 1:2],
                SC[:, 4:8].rearrange("p (u o) -> p u o", o=1)),
                reads=[gSC], writes=[gSC])
            p.act(lambda s, jc=jc: s.activation(SC[:, 16:24], self.gcumT[:, jc:jc + 8], AF.Exp),
                  reads=[g("gcumT")], writes=[gSC])
            p.dve(lambda v, jc=jc: v.tensor_tensor(SC[:, 48:56], self.betaT[:, jc:jc + 8],
                                                   SC[:, 8:16], OP.mult),
                  reads=[g("betaT"), gSC], writes=[gSC])
            p.dve(lambda v: v.tensor_tensor(SC[:, 56:64], SC[:, 48:56], SC[:, 16:24], OP.mult),
                  reads=[gSC], writes=[gSC])
            p.dve(lambda v, co2=co2: v.tensor_tensor(SC[:, 64:72], SC[:, co2 + 16:co2 + 24],
                                                     SC[:, 8:16], OP.mult),
                  reads=[gSC], writes=[gSC])
            p.dve(lambda v, jc=jc: v.tensor_scalar_mul(SC[:, 72:80], self.gcumT[:, jc:jc + 8],
                                                       -1.0),
                  reads=[g("gcumT")], writes=[gSC])
            kbtm, gkb = ab(6144, 7168)
            kbgtm, gkbg = ab(7168, 8192)
            kdtm, gkd = ab(8192, 9216)
            kntm, gknt = ab(10240, 10752)
            pt0_x = self.ex_pair(self.pT0[:, 0:512])
            p.dve(lambda v, kbtm=kbtm, pt0_x=pt0_x: v.tensor_tensor(
                kbtm.rearrange("p (u t) -> p u t", t=128), pt0_x,
                self.ex_u(SC[:, 48:56]), OP.mult),
                reads=[g("pT0"), gSC], writes=[gkb])
            p.dve(lambda v, kbgtm=kbgtm, pt0_x=pt0_x: v.tensor_tensor(
                kbgtm.rearrange("p (u t) -> p u t", t=128), pt0_x,
                self.ex_u(SC[:, 56:64]), OP.mult),
                reads=[g("pT0"), gSC], writes=[gkbg])
            p.dve(lambda v, kdtm=kdtm, pt0_x=pt0_x: v.tensor_tensor(
                kdtm.rearrange("p (u t) -> p u t", t=128), pt0_x,
                self.ex_u(SC[:, 64:72]), OP.mult),
                reads=[g("pT0"), gSC], writes=[gkd])
            p.dve(lambda v, kntm=kntm: v.tensor_tensor(
                kntm.rearrange("p (h t) -> p h t", t=128),
                self.pT0[:, 0:512].rearrange("p (h t) -> p h t", t=128),
                self.ex_u(SC[:, 4:8], n=4), OP.mult),
                reads=[g("pT0"), gSC], writes=[gknt])
            ctx["kdtm"], ctx["gkd"] = kdtm, gkd
            ctx["kbgtm"], ctx["gkbg"] = kbgtm, gkbg
            vbtm, gvb = ab(9216, 10240)
            for u in range(8):
                p.pe(lambda t, u=u, vfm_r=vfm_r: t.transpose(
                    self.pT1[:, u * 128:(u + 1) * 128], vfm_r[:, u * 128:(u + 1) * 128],
                    self.c_idb[:, :]),
                    reads=[gvfr, g("c_idb")], writes=[g("pT1")])
            p.dve(lambda v, jc=jc, vbtm=vbtm: v.tensor_tensor(
                vbtm.rearrange("p (u t) -> p u t", t=128),
                self.pT1[:, :].rearrange("p (u t) -> p u t", t=128),
                self.ex_u(self.betaT[:, jc:jc + 8]), OP.mult),
                reads=[g("pT1"), g("betaT")], writes=[gvb])
            ctx["vbtm"], ctx["gvb"] = vbtm, gvb
            zsil, gzs = ab(5120, 6144)
            for u in range(8):
                p.pe(lambda t, u=u, zfm_r=zfm_r: t.transpose(
                    self.pT0[:, u * 128:(u + 1) * 128], zfm_r[:, u * 128:(u + 1) * 128],
                    self.c_idb[:, :]),
                    reads=[gzfr, g("c_idb")], writes=[g("pT0")])
            p.act(lambda s, zsil=zsil: s.activation(zsil, self.pT0[:, :], AF.Silu),
                  reads=[g("pT0")], writes=[gzs])
            ctx["zsil"], ctx["gzs"] = zsil, gzs
            kbfm, gkbf = ab(10752, 11776)
            for u in range(8):
                p.pe(lambda t, u=u, kbtm=kbtm: t.transpose(
                    self.pT1[:, u * 128:(u + 1) * 128], kbtm[:, u * 128:(u + 1) * 128],
                    self.c_idb[:, :]),
                    reads=[gkb, g("c_idb")], writes=[g("pT1")])
            p.dve(lambda v, kbfm=kbfm: v.tensor_copy(kbfm, self.pT1[:, :]),
                  reads=[g("pT1")], writes=[gkbf])
            knfm, gknf = ab(11776, 12288)
            for h in range(4):
                p.pe(lambda t, h=h, kntm=kntm: t.transpose(
                    self.pT0[:, h * 128:(h + 1) * 128], kntm[:, h * 128:(h + 1) * 128],
                    self.c_idb[:, :]),
                    reads=[gknt, g("c_idb")], writes=[g("pT0")])
            p.dve(lambda v, knfm=knfm: v.tensor_copy(knfm, self.pT0[:, 0:512]),
                  reads=[g("pT0")], writes=[gknf])
            epre, gep = afv(0, 1024)
            cb, gcb = afv(3072, 4096)
            ones_x = bass.AP(self.c_ones128[:, :].tensor, 0, [[128, 128], [0, 8], [1, 128]])
            p.dve(lambda v, cb=cb, ones_x=ones_x: v.tensor_tensor(
                cb.rearrange("p (u t) -> p u t", t=128), ones_x,
                self.ex_u(SC[:, 72:80]), OP.mult),
                reads=[g("c_ones128"), gSC], writes=[gcb])
            for u in range(8):
                p.pe(lambda t, u=u, cb=cb: t.transpose(
                    self.pB[:, u * 128:(u + 1) * 128], cb[:, u * 128:(u + 1) * 128],
                    self.c_idf[:, :]),
                    reads=[gcb, g("c_idf")], writes=[g("pB0") if u < 4 else g("pB1")])
            p.dve(lambda v, jc=jc, epre=epre: v.tensor_tensor(
                epre.rearrange("p (u t) -> p u t", t=128), self.pB[:, :].rearrange(
                    "p (u t) -> p u t", t=128),
                self.ex_u(self.gcumT[:, jc:jc + 8]), OP.add),
                reads=[g("pB0"), g("pB1"), g("gcumT")], writes=[gep])
            p.dve(lambda v, epre=epre: v.tensor_scalar_min(epre, epre, 0.0),
                  reads=[gep], writes=[gep])
            eexp, gee = ab(12288, 13312)
            p.act(lambda s, epre=epre, eexp=eexp: s.activation(eexp, epre, AF.Exp),
                  reads=[gep], writes=[gee])
            eneg, gen = ab(13312, 14336)
            p.dve(lambda v, eexp=eexp, eneg=eneg: v.tensor_tensor(
                eneg, eexp, self.c_sneg[:, :], OP.mult),
                reads=[gee, g("c_sneg")], writes=[gen])
            for u in range(8):
                p.pe(lambda t, u=u, kbfm=kbfm, knfm=knfm: t.matmul(
                    self.pA[:, u * 128:(u + 1) * 128], kbfm[:, u * 128:(u + 1) * 128],
                    knfm[:, (u // 2) * 128:(u // 2 + 1) * 128], start=True, stop=True),
                    reads=[gkbf, gknf], writes=[g("pA0") if u < 4 else g("pA1")])
            sl, gsl = ab(15360, 16384)
            p.dve(lambda v, sl=sl, eneg=eneg: v.tensor_tensor(sl, self.pA[:, :], eneg, OP.mult),
                  reads=[g("pA0"), g("pA1"), gen], writes=[gsl])
            st_, gst = ab(16384, 17408)
            for u in range(8):
                p.pe(lambda t, u=u, sl=sl: t.transpose(
                    self.pT0[:, u * 128:(u + 1) * 128], sl[:, u * 128:(u + 1) * 128],
                    self.c_idb[:, :]),
                    reads=[gsl, g("c_idb")], writes=[g("pT0")])
            p.dve(lambda v, st_=st_: v.tensor_copy(st_, self.pT0[:, :]),
                  reads=[g("pT0")], writes=[gst])
            eit, geit = ab(14336, 15360)
            for u in range(8):
                p.pe(lambda t, u=u, eneg=eneg: t.transpose(
                    self.pT1[:, u * 128:(u + 1) * 128], eneg[:, u * 128:(u + 1) * 128],
                    self.c_idb[:, :]),
                    reads=[gen, g("c_idb")], writes=[g("pT1")])
            p.dve(lambda v, eit=eit: v.tensor_tensor(eit, self.c_blkI[:, :], self.pT1[:, :],
                                                     OP.subtract),
                  reads=[g("c_blkI"), g("pT1")], writes=[geit])
            for u in range(8):
                p.pe(lambda t, u=u, knfm=knfm, qv=qv: t.matmul(
                    self.pB[:, u * 128:(u + 1) * 128], knfm[:, (u // 2) * 128:(u // 2 + 1) * 128],
                    qv[:, (u // 2) * 128:(u // 2 + 1) * 128], start=True, stop=True),
                    reads=[gknf, gq], writes=[g("pB0") if u < 4 else g("pB1")])
            intq, gint = ab(17408, 18432)
            p.dve(lambda v, intq=intq, eit=eit: v.scalar_tensor_tensor(
                intq, self.pB[:, :], RS128, eit, OP.mult, OP.mult),
                reads=[g("pB0"), g("pB1"), geit], writes=[gint])
            ctx["qv"], ctx["gq"] = qv, gq
            ctx["intq"], ctx["gint"] = intq, gint
            ctx["sl"], ctx["gsl"] = sl, gsl
            ctx["st_"], ctx["gst"] = st_, gst
            return ctx

        def emit_ut(ctx):
            ab = ctx["ab"]
            sl, gsl = ctx["sl"], ctx["gsl"]
            st_, gst = ctx["st_"], ctx["gst"]
            Pb = [ab(22528, 23552), ab(23552, 24576)]
            p.dve(lambda v, P0=Pb[0][0], st_=st_: v.tensor_tensor(
                P0, self.c_blkI[:, :], st_, OP.add),
                  reads=[g("c_blkI"), gst], writes=[Pb[0][1]])
            cur = 0
            spl, gspl = sl, gsl
            spt, gspt = st_, gst
            lv = [(ab(18432, 19456), ab(19456, 20480)), (ab(20480, 21504), ab(21504, 22528))]
            for lev in range(1, 7):
                for u in range(8):
                    p.pe(lambda t, u=u, spl=spl, spt=spt: t.matmul(
                        self.pA[:, u * 128:(u + 1) * 128], spt[:, u * 128:(u + 1) * 128],
                        spl[:, u * 128:(u + 1) * 128], start=True, stop=True),
                        reads=[gspl, gspt], writes=[g("pA0") if u < 4 else g("pA1")])
                (nl, gnl), (nt, gnt) = lv[lev % 2]
                p.act(lambda s, nl=nl: s.activation(nl, self.pA[:, :], AF.Copy),
                      reads=[g("pA0"), g("pA1")], writes=[gnl])
                if lev < 6:
                    for u in range(8):
                        p.pe(lambda t, u=u, spl=spl, spt=spt: t.matmul(
                            self.pC[:, u * 128:(u + 1) * 128], spl[:, u * 128:(u + 1) * 128],
                            spt[:, u * 128:(u + 1) * 128], start=True, stop=True),
                            reads=[gspl, gspt], writes=[g("pC0") if u < 4 else g("pC1")])
                    p.act(lambda s, nt=nt: s.activation(nt, self.pC[:, :], AF.Copy),
                          reads=[g("pC0"), g("pC1")], writes=[gnt])
                for u in range(8):
                    p.pe(lambda t, u=u, nl=nl, Pc=Pb[cur][0]: t.matmul(
                        self.pB[:, u * 128:(u + 1) * 128], nl[:, u * 128:(u + 1) * 128],
                        Pc[:, u * 128:(u + 1) * 128], start=True, stop=True),
                        reads=[gnl, Pb[cur][1]], writes=[g("pB0") if u < 4 else g("pB1")])
                p.dve(lambda v, Pn=Pb[1 - cur][0], Pc=Pb[cur][0]: v.tensor_tensor(
                    Pn[:, :], self.pB[:, :], Pc[:, :], OP.add),
                    reads=[g("pB0"), g("pB1"), Pb[cur][1]], writes=[Pb[1 - cur][1]])
                cur = 1 - cur
                spl, gspl = nl, gnl
                spt, gspt = nt, gnt
            Pf, gPf = Pb[cur]
            ctx["Pf"], ctx["gPf"] = Pf, gPf
            kbgtm, gkbg = ctx["kbgtm"], ctx["gkbg"]
            nkcd, gnk = ab(24576, 25600)
            for u in range(8):
                p.pe(lambda t, u=u, kbgtm=kbgtm, Pf=Pf: t.matmul(
                    self.pB[:, u * 128:(u + 1) * 128], kbgtm[:, u * 128:(u + 1) * 128],
                    Pf[:, u * 128:(u + 1) * 128], start=True, stop=True),
                    reads=[gkbg, gPf], writes=[g("pB0") if u < 4 else g("pB1")])
            p.dve(lambda v, nkcd=nkcd: v.tensor_scalar_mul(nkcd, self.pB[:, :], -1.0),
                  reads=[g("pB0"), g("pB1")], writes=[gnk])
            ctx["nkcd"], ctx["gnk"] = nkcd, gnk

        def emit_post(ctx):
            j, jc, par, ab, afv = ctx["j"], ctx["jc"], ctx["par"], ctx["ab"], ctx["afv"]
            co2 = ctx["co2"]
            qv, gq = ctx["qv"], ctx["gq"]
            zsil, gzs = ctx["zsil"], ctx["gzs"]
            vbtm, gvb = ctx["vbtm"], ctx["gvb"]
            kdtm, gkd = ctx["kdtm"], ctx["gkd"]
            Pf, gPf = ctx["Pf"], ctx["gPf"]
            nkcd, gnk = ctx["nkcd"], ctx["gnk"]
            intq, gint = ctx["intq"], ctx["gint"]
            for u in range(8):
                p.pe(lambda t, u=u, Pf=Pf, vbtm=vbtm: t.matmul(
                    self.pA[:, u * 128:(u + 1) * 128], Pf[:, u * 128:(u + 1) * 128],
                    vbtm[:, u * 128:(u + 1) * 128], start=True, stop=False),
                    reads=[gPf, gvb], writes=[g("pA0") if u < 4 else g("pA1")])
                p.pe(lambda t, u=u, nkcd=nkcd: t.matmul(
                    self.pA[:, u * 128:(u + 1) * 128], nkcd[:, u * 128:(u + 1) * 128],
                    self.s_stb[:, u * 128:(u + 1) * 128], start=False, stop=True),
                    reads=[gnk, g("s_stb")], writes=[g("pA0") if u < 4 else g("pA1")])
            vnew, gvn = ab(25600, 26624)
            p.dve(lambda v, vnew=vnew: v.tensor_copy(vnew, self.pA[:, :]),
                  reads=[g("pA0"), g("pA1")], writes=[gvn])
            interf, gitf = afv(1024, 2048)
            for u in range(8):
                p.pe(lambda t, u=u, qv=qv: t.matmul(
                    self.pB[:, u * 128:(u + 1) * 128], qv[:, (u // 2) * 128:(u // 2 + 1) * 128],
                    self.s_stb[:, u * 128:(u + 1) * 128], start=True, stop=True),
                    reads=[gq, g("s_stb")], writes=[g("pB0") if u < 4 else g("pB1")])
            p.dve(lambda v, interf=interf, co2=co2: v.tensor_tensor(
                interf.rearrange("p (u t) -> p u t", t=128),
                self.pB[:, :].rearrange("p (u t) -> p u t", t=128),
                self.ex_u(SC[:, co2:co2 + 8]), OP.mult),
                reads=[g("pB0"), g("pB1"), gSC], writes=[gitf])
            for u in range(8):
                p.pe(lambda t, u=u, intq=intq, vnew=vnew: t.matmul(
                    self.pC[:, u * 128:(u + 1) * 128], intq[:, u * 128:(u + 1) * 128],
                    vnew[:, u * 128:(u + 1) * 128], start=True, stop=True),
                    reads=[gint, gvn], writes=[g("pC0") if u < 4 else g("pC1")])
            outc, goc = afv(2048, 3072)
            p.dve(lambda v, outc=outc, interf=interf: v.tensor_tensor(
                outc, self.pC[:, :], interf, OP.add),
                reads=[g("pC0"), g("pC1"), gitf], writes=[goc])
            for u in range(8):
                p.pe(lambda t, u=u, kdtm=kdtm, vnew=vnew: t.matmul(
                    self.pA[:, u * 128:(u + 1) * 128], kdtm[:, u * 128:(u + 1) * 128],
                    vnew[:, u * 128:(u + 1) * 128], start=True, stop=True),
                    reads=[gkd, gvn], writes=[g("pA0") if u < 4 else g("pA1")])
            p.dve(lambda v, co2=co2: v.tensor_tensor(
                self.s_st[:, :].rearrange("p (u t) -> p u t", t=128),
                self.s_st[:, :].rearrange("p (u t) -> p u t", t=128),
                self.ex_u(SC[:, co2 + 8:co2 + 16]), OP.mult),
                reads=[gSC], writes=[g("s_st")])
            p.dve(lambda v: v.tensor_tensor(self.s_st[:, :], self.s_st[:, :], self.pA[:, :],
                                            OP.add),
                  reads=[g("pA0"), g("pA1")], writes=[g("s_st")])
            p.dve(lambda v: v.tensor_copy(self.s_stb[:, :], self.s_st[:, :]),
                  reads=[g("s_st")], writes=[g("s_stb")])
            xg, gxg = afv(1024, 2048)
            p.dve(lambda v, outc=outc, zsil=zsil, xg=xg: v.tensor_tensor(
                xg, outc, zsil, OP.mult),
                reads=[goc, gzs], writes=[gxg])
            sqb, gsq = ab(27648, 28672)
            p.act(lambda s, xg=xg, sqb=sqb: s.activation(sqb, xg, AF.Square),
                  reads=[gxg], writes=[gsq])
            p.dve(lambda v, sqb=sqb: v.reduce_sum(
                SC[:, 80:88], sqb.rearrange("p (u l) -> p u l", l=128), axis=AX.X),
                reads=[gsq], writes=[gSC])
            p.dve(lambda v: v.tensor_scalar(SC[:, 88:96], SC[:, 80:88], 1.0 / 128.0, 1e-6,
                                            OP.mult, OP.add),
                  reads=[gSC], writes=[gSC])
            p.act(lambda s: s.activation(SC[:, 88:96], SC[:, 88:96], AF.Sqrt),
                  reads=[gSC], writes=[gSC])
            p.dve(lambda v: v.reciprocal(SC[:, 88:96], SC[:, 88:96]),
                  reads=[gSC], writes=[gSC])
            ntm, gnt2 = ab(28672, 29696)
            p.dve(lambda v, xg=xg, ntm=ntm: v.tensor_tensor(
                ntm.rearrange("p (u t) -> p u t", t=128),
                xg.rearrange("p (u t) -> p u t", t=128),
                self.ex_u(SC[:, 88:96]), OP.mult),
                reads=[gxg, gSC], writes=[gnt2])
            nst, gnst = ab(29696, 30720)
            for u in range(8):
                p.pe(lambda t, u=u, ntm=ntm: t.transpose(
                    self.pT0[:, u * 128:(u + 1) * 128], ntm[:, u * 128:(u + 1) * 128],
                    self.c_idb[:, :]),
                    reads=[gnt2, g("c_idb")], writes=[g("pT0")])
            p.dve(lambda v, nst=nst: v.tensor_scalar(
                nst, self.pT0[:, :], self.c_normw[:, :], None, OP.mult),
                reads=[g("pT0"), g("c_normw")], writes=[gnst])
            dst = self.nrm_sc.ap()[:, j * 128:(j + 1) * 128].rearrange(
                "(u p) l -> p u l", p=128)
            p.pdma(lambda s, dst=dst, nst=nst: s.dma_start(
                out=dst, in_=nst.rearrange("p (u l) -> p u l", l=128)),
                   reads=[gnst], writes=[g("d_nrm")])

        ctxs = {}
        ctxs[0] = emit_preA(0)
        emit_ut(ctxs[0])
        ctxs[1] = emit_preA(1)
        for j in range(J):
            emit_post(ctxs.pop(j))
            if j + 1 < J:
                emit_ut(ctxs[j + 1])
            if j + 2 < J:
                ctxs[j + 2] = emit_preA(j + 2)
        if self.dev_dbg:
            p.pdma(lambda gg: gg.dma_start(out=self.dbg["dbg_nrm"][:, :],
                                           in_=self.nrm_sc.ap()[:, :]),
                   reads=[g("d_nrm")], writes=[g("dbg")])

    def b6_wout(self):
        p, g = self.p, self.g
        T, Q = self.T, self.Q
        wo, gwo = self.AW.view(0, 16384)
        for vdc in range(8):
            src = self.wout[vdc * 128:(vdc + 1) * 128, :]
            p.pdma(lambda gg, src=src, vdc=vdc, wo=wo: gg.dma_start(
                out=wo[:, vdc * 2048:(vdc + 1) * 2048], in_=src), writes=[gwo])
        for tt in range(T // 512):
            par = tt % 2
            t0 = tt * 512
            q, lc = t0 // Q, t0 % Q
            vn, gvn = self.ABF.view(par * 4096, par * 4096 + 4096)
            src = self.nrm_sc.ap()[:, t0:t0 + 512].rearrange("(c p) t -> p c t", p=128)
            p.hdma(lambda s, src=src, vn=vn: s.dma_start(
                out=vn.rearrange("p (c t) -> p c t", t=512), in_=src),
                   writes=[gvn], extra=[g("d_nrm").w])
            yst, gyst = self.AFA.view(0, 8192)
            for dc in range(16):
                half = dc % 2
                for vdc in range(8):
                    p.pe(lambda t, dc=dc, vdc=vdc, half=half, vn=vn, wo=wo: t.matmul(
                        self.pA[:, half * 512:half * 512 + 512],
                        wo[:, vdc * 2048 + dc * 128: vdc * 2048 + (dc + 1) * 128],
                        vn[:, vdc * 512:(vdc + 1) * 512], start=(vdc == 0), stop=(vdc == 7)),
                        reads=[gwo, gvn], writes=[g("pA0") if half == 0 else g("pA1")])
                p.dve(lambda v, dc=dc, half=half, yst=yst: v.tensor_copy(
                    yst[:, dc * 512:(dc + 1) * 512], self.pA[:, half * 512:half * 512 + 512]),
                    reads=[g("pA0") if half == 0 else g("pA1")], writes=[gyst])
            Qm = Q // self.NSQ
            m = lc // Qm
            lcm = lc - m * Qm
            dst = self.ps_in[m].ap()[q * 2048:(q + 1) * 2048, lcm:lcm + 512].rearrange(
                "(c p) t -> p c t", p=128)
            p.pdma(lambda s, dst=dst, yst=yst: s.dma_start(
                out=dst, in_=yst.rearrange("p (c t) -> p c t", t=512)),
                   reads=[gyst], writes=[g("d_ps%d" % m)])
            if q == 3 and lcm + 512 == Qm:
                p.cc(lambda gg, m=m: gg.collective_compute(
                    "ReduceScatter", OP.add, replica_groups=[[0, 1, 2, 3], [4, 5, 6, 7]],
                    ins=[self.ps_in[m].ap().opt()], outs=[self.s_own[m].ap().opt()]),
                    reads=[g("d_ps%d" % m)], writes=[g("d_s_own%d" % m)])
        if self.dev_dbg:
            for m in range(self.NSQ):
                p.pdma(lambda gg, m=m: gg.dma_start(
                    out=self.dbg["dbg_s"][:, m * (self.Q // self.NSQ):(m + 1) * (self.Q // self.NSQ)],
                    in_=self.s_own[m].ap()[:, :]),
                    reads=[g("d_s_own%d" % m)], writes=[g("dbg")])

    def b9_combine(self):
        p, g = self.p, self.g
        Q = self.Q
        for tt in range(Q // 512):
            t0 = tt * 512
            for dc in range(16):
                i = (tt * 16 + dc) % 2
                vx, gx = self.AFA.view(i * 2048, i * 2048 + 512)
                vs, gs = self.AFA.view(i * 2048 + 512, i * 2048 + 1024)
                vf, gf = self.AFA.view(i * 2048 + 1024, i * 2048 + 1536)
                vo, go = self.AFA.view(i * 2048 + 1536, i * 2048 + 2048)
                p.adma(lambda s, dc=dc, t0=t0, vx=vx: s.dma_start(
                    out=vx, in_=self.x_fm[dc * 128:(dc + 1) * 128, t0:t0 + 512]),
                    writes=[gx])
                Qm = Q // self.NSQ
                m = t0 // Qm
                t0m = t0 - m * Qm
                p.adma(lambda s, dc=dc, m=m, t0m=t0m, vs=vs: s.dma_start(
                    out=vs, in_=self.s_own[m].ap()[dc * 128:(dc + 1) * 128, t0m:t0m + 512]),
                    writes=[gs], extra=[g("d_s_own%d" % m).w])
                p.adma(lambda s, dc=dc, t0=t0, vf=vf: s.dma_start(
                    out=vf, in_=self.ffn_own.ap()[dc * 128:(dc + 1) * 128, t0:t0 + 512]),
                    writes=[gf], extra=[g("d_ffn_own").w])
                p.act(lambda s, vf=vf: s.activation(vf, vf, AF.Silu),
                      reads=[gf], writes=[gf])
                p.dve(lambda v, vo=vo, vs=vs, vf=vf: v.tensor_tensor(vo, vs, vf, OP.mult),
                      reads=[gs, gf], writes=[go])
                p.dve(lambda v, vo=vo, vx=vx: v.tensor_tensor(vo, vo, vx, OP.add),
                      reads=[gx], writes=[go])
                p.hdma(lambda s, dc=dc, t0=t0, vo=vo: s.dma_start(
                    out=self.out_fm[dc * 128:(dc + 1) * 128, t0:t0 + 512], in_=vo),
                    reads=[go], writes=[g("d_out")])

    def build(self):
        self.declare()
        with ExitStack() as es:
            self.alloc(es)
            self.b0_consts()
            self.b1_ln_router()
            self.b2_collectives()
            self.b7_ffn()
            self.b3_proj()
            self.b4_scan()
            self.b6_wout()
            self.b9_combine()
            with ExitStack() as es2:
                sems = {k: es2.enter_context(self.nc.semaphore("sem_" + k))
                        for k in Prog.SEMS}
                blk = es2.enter_context(self.nc.Block("mk"))
                self.p.emit(blk, sems, pe_fifo=self.pe_fifo)
                final = dict(self.p.cnt)

                def tail(eng):
                    for k, vv in final.items():
                        if vv:
                            eng.wait_ge(sems[k], vv)
                blk.sync(tail)
        return self.nc



def _host_inputs(inputs, T):
    Q = T // 4
    x = np.asarray(inputs["L0_attn_hidden"])
    Wqkv = np.asarray(inputs["L0_attn_W_qkv"])
    Wz = np.asarray(inputs["L0_attn_W_z"])
    Wb = np.asarray(inputs["L0_attn_W_b"])
    Wa = np.asarray(inputs["L0_attn_W_a"])
    cw = np.asarray(inputs["L0_attn_conv_weight"])
    dtbias = np.asarray(inputs["L0_attn_dt_bias"])
    Alog = np.asarray(inputs["L0_attn_A_log"])
    nw = np.asarray(inputs["L0_attn_norm_w"])
    Wout = np.asarray(inputs["L0_attn_W_out"])
    gw = np.asarray(inputs["L0_ffn_gate_w"])
    rb = np.asarray(inputs["L0_ffn_router_bias"])
    W1 = np.asarray(inputs["L0_ffn_W1"])
    W3 = np.asarray(inputs["L0_ffn_W3"])
    W2 = np.asarray(inputs["L0_ffn_W2"])
    sw1 = np.asarray(inputs["L0_ffn_sw1"])
    sw3 = np.asarray(inputs["L0_ffn_sw3"])
    sw2 = np.asarray(inputs["L0_ffn_sw2"])

    Xf = np.ascontiguousarray(x.reshape(2 * T, D))
    idf = np.eye(128, dtype=np.float32)
    idb = idf.astype(BFNP)
    onesb = np.ones((1, 128), np.float32).astype(BFNP)
    onebc = np.ones((128, 1), np.float32).astype(BFNP)
    cbias = np.tile(np.array([[1e-5, 1.0, -0.5 * np.log(128.0), 1e-6]], np.float32), (128, 1))
    onesf = np.ones((1, 128), np.float32)
    sneg1 = (-np.tril(np.ones((128, 128), np.float32), -1)).astype(BFNP)
    sneg = np.concatenate([sneg1] * VH, axis=1)
    blkI = np.concatenate([idb] * VH, axis=1)
    gw128 = np.ascontiguousarray(
        gw.T.reshape(16, 128, 8).transpose(1, 0, 2).reshape(128, 128)).astype(np.float32)

    maps = []
    for c in range(8):
        b = c // 4
        hq0 = (c % 4) * QH
        qcols = np.arange(hq0 * 128, (hq0 + QH) * 128)
        kcols = 2048 + qcols
        vcols = 4096 + np.arange(hq0 * 2 * 128, (hq0 + QH) * 2 * 128)
        cols = np.concatenate([qcols, kcols, vcols])
        vh = np.arange(hq0 * 2, (hq0 + QH) * 2)
        vdcols = np.arange(vh[0] * 128, (vh[-1] + 1) * 128)

        x_fm = np.ascontiguousarray(Xf[c * Q:(c + 1) * Q].T)
        wqkv_s = np.ascontiguousarray(Wqkv[:, cols])
        wz_s = np.ascontiguousarray(Wz[:, vdcols])
        wb128 = np.ascontiguousarray(
            Wb[:, vh].reshape(16, 128, VH).transpose(1, 0, 2).reshape(128, 128))
        wa128 = np.ascontiguousarray(
            Wa[:, vh].reshape(16, 128, VH).transpose(1, 0, 2).reshape(128, 128))
        convw_s = np.ascontiguousarray(
            cw[cols].reshape(16, 128, 4).transpose(1, 0, 2).reshape(128, 64))
        m = dict(
            x_fm=x_fm, wqkv=wqkv_s, wz=wz_s, wb128=wb128, wa128=wa128,
            convw=convw_s,
            alog=np.ascontiguousarray(Alog[vh].reshape(VH, 1)),
            dtb=np.ascontiguousarray(dtbias[vh].reshape(VH, 1)),
            normw=np.ascontiguousarray(nw.reshape(128, 1)),
            wout=np.ascontiguousarray(Wout[vdcols, :]),
            gw128=gw128, rbias=np.ascontiguousarray(rb.reshape(1, 8)),
            w1=np.ascontiguousarray(W1[c]), w3=np.ascontiguousarray(W3[c]),
            w2=np.ascontiguousarray(W2[c]),
            sw1T=np.ascontiguousarray(sw1.T[:, c * SHS:(c + 1) * SHS]),
            sw3T=np.ascontiguousarray(sw3.T[:, c * SHS:(c + 1) * SHS]),
            sw2T=np.ascontiguousarray(sw2.T[c * SHS:(c + 1) * SHS, :]),
            idf=idf, idb=idb, onesb=onesb, onebc=onebc, onesf=onesf, sneg=sneg, blkI=blkI,
            cbias=cbias,
        )
        maps.append(m)
    return maps


_NC_CACHE = {}


def kernel(**inputs):
    T = int(np.asarray(inputs["L0_attn_hidden"]).shape[1])
    key = ("nc", T)
    if key not in _NC_CACHE:
        _NC_CACHE[key] = Builder(T).build()
    nc = _NC_CACHE[key]
    maps = _host_inputs(inputs, T)
    res = run_bass_kernel_spmd(nc, maps, core_ids=list(range(8)))
    Q = T // 4
    full = np.concatenate([np.asarray(res.results[c]["out_fm"]) for c in range(8)], axis=1)
    return np.ascontiguousarray(full.T).reshape(2, T, D).astype(np.float32)
